# version 14
# speedup vs baseline: 1.2588x; 1.0739x over previous
import sys
sys.path.insert(0, '/opt/trn_rl_repo')
import numpy as np
import ml_dtypes
import concourse.bass as bass
import concourse.mybir as mybir
from concourse.bass_utils import run_bass_kernel_spmd

F32 = mybir.dt.float32
BF16 = mybir.dt.bfloat16
I32 = mybir.dt.int32
I8 = mybir.dt.int8
AF = mybir.ActivationFunctionType
ALU = mybir.AluOpType
NPBF = ml_dtypes.bfloat16

S = 8192
D = 4096
NCORE = 8
BLK = 512
NBLK = S // BLK
ROWS = 2 * BLK
NT = ROWS // 128
EPS = 1e-5


def core_blocks(k):
    return [k, NBLK - 1 - k]


def f32c(x):
    return np.ascontiguousarray(x, dtype=np.float32)


class Sems:
    def __init__(self, nc, names):
        self.h = {n: nc.alloc_semaphore(name=n) for n in names}
        self.c = {n: 0 for n in names}
        self.tag = {}

    def inc(self, instr, n, amt=1):
        self.c[n] += amt
        instr.then_inc(self.h[n], amt)
        return self.c[n]

    def wait(self, eng, n, v=None):
        v = self.c[n] if v is None else v
        if v > 0:
            eng.wait_ge(self.h[n], v)


def gdma(g, sm, n, out, in_, **kw):
    sm.wait(g, n)
    g.dma_start(out=out, in_=in_, **kw).then_inc(sm.h[n], 16)
    sm.c[n] += 16


def fdma(g, sm, n, out, in_, **kw):
    g.dma_start(out=out, in_=in_, **kw).then_inc(sm.h[n], 16)
    sm.c[n] += 16
    return sm.c[n]



def emit_ln_tile(block, sm, xt, out_bf, scr_bf, stat,
                 v_waits=(), a_waits=()):
    @block.vector
    def _(v):
        for s, c in v_waits:
            v.wait_ge(sm.h[s], c)
        ins = v.tensor_scalar(out=scr_bf[:], in0=xt[:], scalar1=1.0,
                              scalar2=0.0, op0=ALU.mult, op1=ALU.add,
                              accum_out=stat[:, 0:1])
        sm.inc(ins, 'v')
        sm.wait(v, 'v')
        ins = v.scalar_tensor_tensor(out=scr_bf[:], in0=xt[:], scalar=1.0,
                                     in1=xt[:], op0=ALU.mult, op1=ALU.mult,
                                     accum_out=stat[:, 1:2])
        sm.inc(ins, 'v')
        sm.wait(v, 'v')
        v.tensor_scalar(out=stat[:, 2:3], in0=stat[:, 0:1],
                        scalar1=1.0 / D, scalar2=None, op0=ALU.mult)
        ins = v.tensor_scalar(out=stat[:, 3:4], in0=stat[:, 1:2],
                              scalar1=1.0 / D, scalar2=None, op0=ALU.mult)
        sm.inc(ins, 'v')
        sm.wait(v, 'v')
        ins = v.tensor_scalar(out=stat[:, 4:5], in0=stat[:, 2:3],
                              scalar1=stat[:, 2:3], scalar2=-1.0,
                              op0=ALU.mult, op1=ALU.mult)
        sm.inc(ins, 'v')
        sm.wait(v, 'v')
        ins = v.scalar_tensor_tensor(out=stat[:, 5:6], in0=stat[:, 3:4],
                                     scalar=EPS, in1=stat[:, 4:5],
                                     op0=ALU.add, op1=ALU.add)
        sm.inc(ins, 'v')

    @block.scalar
    def _(a):
        sm.wait(a, 'v')
        for s, c in a_waits:
            a.wait_ge(sm.h[s], c)
        ins = a.activation(out=stat[:, 6:7], in_=stat[:, 5:6], func=AF.Ln)
        sm.inc(ins, 'a')
        sm.wait(a, 'a')
        ins = a.activation(out=stat[:, 7:8], in_=stat[:, 6:7], func=AF.Exp,
                           scale=-0.5)
        sm.inc(ins, 'a')

    @block.vector
    def _(v):
        sm.wait(v, 'a')
        ins = v.tensor_scalar(out=stat[:, 8:9], in0=stat[:, 2:3],
                              scalar1=stat[:, 7:8], scalar2=-1.0,
                              op0=ALU.mult, op1=ALU.mult)
        sm.inc(ins, 'v')
        sm.wait(v, 'v')
        ins = v.tensor_scalar(out=out_bf[:], in0=xt[:],
                              scalar1=stat[:, 7:8], scalar2=stat[:, 8:9],
                              op0=ALU.mult, op1=ALU.add)
        sm.inc(ins, 'v')


def emit_transpose_tiles(block, sm, src_bf, dst_list, dst_col, ident, p_tr,
                         n_k=32):
    vc = sm.c['v']
    for g0 in range(0, n_k, 4):
        j = (g0 // 4) % 2
        cnt = min(4, n_k - g0)

        @block.tensor
        def _(t, g0=g0, j=j, cnt=cnt):
            if g0 == 0:
                t.wait_ge(sm.h['v'], vc)
            key = f'_trw{j}'
            if key in sm.tag:
                t.wait_ge(sm.h['v'], sm.tag[key])
            for k in range(cnt):
                ins = t.transpose(p_tr[j][:, k * 128:(k + 1) * 128],
                                  src_bf[:, (g0 + k) * 128:(g0 + k + 1) * 128],
                                  ident[:])
                sm.inc(ins, 't')

        @block.vector
        def _(v, g0=g0, j=j, cnt=cnt):
            sm.wait(v, 't')
            for k in range(cnt):
                ins = v.tensor_copy(
                    out=dst_list[g0 + k][:, dst_col:dst_col + 128],
                    in_=p_tr[j][:, k * 128:(k + 1) * 128])
                sm.inc(ins, 'v')
            sm.tag[f'_trw{j}'] = sm.c['v']



def build_ffn0():
    nc = bass.Bass("TRN2", target_bir_lowering=False)
    x_in = nc.dram_tensor("x", [ROWS, D], F32, kind="ExternalInput")
    w0T_in = nc.dram_tensor("w0T", [D, D], F32, kind="ExternalInput")
    bias_in = nc.dram_tensor("bias", [1, D], F32, kind="ExternalInput")
    lAT_in = nc.dram_tensor("lAT", [D, 48], F32, kind="ExternalInput")
    lBT_in = nc.dram_tensor("lBT", [48, D], F32, kind="ExternalInput")
    lsi_in = nc.dram_tensor("lsi", [1, 4], I32, kind="ExternalInput")
    c012_in = nc.dram_tensor("c012", [3, 1], F32, kind="ExternalInput")
    e48_in = nc.dram_tensor("e48", [3, 48], F32, kind="ExternalInput")
    ident_in = nc.dram_tensor("ident", [128, 128], BF16, kind="ExternalInput")
    ones1_in = nc.dram_tensor("ones1", [1, 128], BF16, kind="ExternalInput")
    out = nc.dram_tensor("out", [ROWS, D], F32, kind="ExternalOutput")
    scdr3 = nc.dram_tensor("scdr3", [3, 8], F32, kind="Internal")

    lnT = [nc.alloc_sbuf_tensor(f"lnT{d}", [128, ROWS], BF16)
           for d in range(32)]
    ybf = [nc.alloc_sbuf_tensor(f"y{i}", [128, D], BF16) for i in range(NT)]
    wbuf = nc.alloc_sbuf_tensor("wb", [128, 32 * 256], BF16)
    xbuf = nc.alloc_sbuf_tensor("xb", [128, D], F32)
    lnbf = nc.alloc_sbuf_tensor("lnbf", [128, D], BF16)
    scr_bf = nc.alloc_sbuf_tensor("scr_bf", [128, D], BF16)
    stat = nc.alloc_sbuf_tensor("stat", [128, 12], F32)
    aTbf = nc.alloc_sbuf_tensor("aTbf", [128, 32 * 48], BF16)
    bTbf = nc.alloc_sbuf_tensor("bTbf", [48, D], BF16)
    biasbf = nc.alloc_sbuf_tensor("biasbf", [1, D], BF16)
    onesbf = nc.alloc_sbuf_tensor("onesbf", [1, 128], BF16)
    identbf = nc.alloc_sbuf_tensor("identbf", [128, 128], BF16)
    rbf = nc.alloc_sbuf_tensor("rbf", [48, ROWS], BF16)
    lsi_f = nc.alloc_sbuf_tensor("lsi_f", [1, 4], F32)
    lsi_i = nc.alloc_sbuf_tensor("lsi_i", [1, 4], I32)
    sc = nc.alloc_sbuf_tensor("sc", [1, 8], F32)
    c012 = nc.alloc_sbuf_tensor("c012s", [3, 1], F32)
    e48bf = nc.alloc_sbuf_tensor("e48bf", [3, 48], BF16)
    wsel3 = nc.alloc_sbuf_tensor("wsel3", [3, 1], F32)
    use3 = nc.alloc_sbuf_tensor("use3", [3, 1], F32)
    wsel3b = nc.alloc_sbuf_tensor("wsel3b", [3, 1], BF16)
    wsel48 = nc.alloc_sbuf_tensor("wsel48", [48, 1], F32)

    p_tr = [nc.alloc_psum_tensor(f"ptr{i}", [128, 128], BF16)
            for i in range(2)]
    p_y = [nc.alloc_psum_tensor(f"py{i}", [128, 256], F32) for i in range(2)]
    p_r = nc.alloc_psum_tensor("pr", [48, 512], F32)
    p_s = nc.alloc_psum_tensor("ps", [48, 1], F32)

    sm = Sems(nc, ['g', 'v', 'a', 't', 'o'])

    with nc.Block() as block:
        @block.gpsimd
        def _(g):
            gdma(g, sm, 'g', identbf[:], ident_in[:])
            gdma(g, sm, 'g', onesbf[:], ones1_in[:])
            gdma(g, sm, 'g', biasbf[:], bias_in[:])
            gdma(g, sm, 'g', bass.AP(aTbf, 0, [[32 * 48, 128], [48, 32], [1, 48]]), bass.AP(lAT_in, 0, [[48, 128], [128 * 48, 32], [1, 48]]))
            gdma(g, sm, 'g', bTbf[:], lBT_in[:])
            gdma(g, sm, 'g', lsi_i[:], lsi_in[:])
            gdma(g, sm, 'g', c012[:], c012_in[:])
            gdma(g, sm, 'g', e48bf[:], e48_in[:])

        @block.vector
        def _(v):
            sm.wait(v, 'g')
            v.tensor_copy(out=lsi_f[:], in_=lsi_i[:])
            sm.inc(v.engine_nop(), 'v'); sm.wait(v, 'v')
            v.tensor_single_scalar(out=sc[:, 0:4], in_=lsi_f[:], scalar=0.0,
                                   op=ALU.is_le)
            sm.inc(v.engine_nop(), 'v'); sm.wait(v, 'v')
            v.tensor_scalar(out=sc[:, 4:5], in0=sc[:, 3:4], scalar1=3.0,
                            scalar2=sc[:, 2:3], op0=ALU.add, op1=ALU.mult)
            v.tensor_scalar(out=sc[:, 5:6], in0=sc[:, 0:1], scalar1=-1.0,
                            scalar2=2.0, op0=ALU.mult, op1=ALU.add)
            sm.inc(v.engine_nop(), 'v'); sm.wait(v, 'v')
            v.tensor_scalar(out=sc[:, 5:6], in0=sc[:, 5:6],
                            scalar1=sc[:, 1:2], scalar2=None, op0=ALU.mult)
            sm.inc(v.engine_nop(), 'v'); sm.wait(v, 'v')
            v.tensor_scalar(out=sc[:, 5:6], in0=sc[:, 5:6],
                            scalar1=1.0, scalar2=sc[:, 0:1],
                            op0=ALU.mult, op1=ALU.add)
            v.tensor_scalar(out=sc[:, 6:7], in0=sc[:, 2:3], scalar1=-1.0,
                            scalar2=1.0, op0=ALU.mult, op1=ALU.add)
            sm.inc(v.engine_nop(), 'v'); sm.wait(v, 'v')
            v.tensor_scalar(out=sc[:, 5:6], in0=sc[:, 5:6],
                            scalar1=sc[:, 6:7], scalar2=None, op0=ALU.mult)
            sm.inc(v.engine_nop(), 'v'); sm.wait(v, 'v')
            v.tensor_scalar(out=sc[:, 4:5], in0=sc[:, 4:5],
                            scalar1=1.0, scalar2=sc[:, 5:6],
                            op0=ALU.mult, op1=ALU.add)
            sm.inc(v.engine_nop(), 'v'); sm.wait(v, 'v')
            v.tensor_scalar(out=sc[:, 5:6], in0=sc[:, 4:5], scalar1=-1.0,
                            scalar2=None, op0=ALU.add)
            sm.inc(v.engine_nop(), 'v'); sm.wait(v, 'v')
            v.tensor_single_scalar(out=sc[:, 6:7], in_=sc[:, 5:6], scalar=0.5,
                                   op=ALU.is_gt)
            v.tensor_scalar(out=sc[:, 7:8], in0=sc[:, 5:6], scalar1=-1.0,
                            scalar2=None, op0=ALU.add)
            sm.inc(v.engine_nop(), 'v'); sm.wait(v, 'v')
            ins = v.tensor_scalar(out=sc[:, 7:8], in0=sc[:, 7:8], scalar1=0.0,
                                  scalar2=2.0, op0=ALU.max, op1=ALU.min)
            sm.inc(ins, 'v')

        @block.gpsimd
        def _(g):
            sm.wait(g, 'v')
            for r in range(3):
                gdma(g, sm, 'g', scdr3[r:r + 1, :], sc[:])

        @block.gpsimd
        def _(g):
            sm.wait(g, 'g')
            gdma(g, sm, 'g', wsel3[:], bass.AP(scdr3, 7, [[8, 3], [1, 1]]), allow_slow_non_contiguous=True)
            gdma(g, sm, 'g', use3[:], bass.AP(scdr3, 6, [[8, 3], [1, 1]]), allow_slow_non_contiguous=True)

        @block.vector
        def _(v):
            sm.wait(v, 'g')
            v.tensor_tensor(out=wsel3[:], in0=c012[:], in1=wsel3[:],
                            op=ALU.is_equal)
            sm.inc(v.engine_nop(), 'v'); sm.wait(v, 'v')
            v.tensor_tensor(out=wsel3[:], in0=wsel3[:], in1=use3[:],
                            op=ALU.mult)
            sm.inc(v.engine_nop(), 'v'); sm.wait(v, 'v')
            ins = v.tensor_copy(out=wsel3b[:], in_=wsel3[:])
            sm.inc(ins, 'v')

        @block.tensor
        def _(t):
            sm.wait(t, 'v')
            ins = t.matmul(p_s[:], e48bf[:], wsel3b[:], start=True, stop=True)
            sm.inc(ins, 't')

        @block.scalar
        def _(a):
            sm.wait(a, 't')
            ins = a.activation(out=wsel48[:], in_=p_s[:], func=AF.Copy)
            sm.inc(ins, 'a')
        a_wsel = sm.c['a']

        for i in range(NT):
            @block.gpsimd
            def _(g, i=i):
                if '_lnfree' in sm.tag:
                    g.wait_ge(sm.h['v'], sm.tag['_lnfree'])
                gdma(g, sm, 'g', xbuf[:], x_in[i * 128:(i + 1) * 128, :])

            emit_ln_tile(block, sm, xbuf, lnbf, scr_bf, stat,
                         v_waits=[('g', sm.c['g']), ('t', sm.c['t'])])
            sm.tag['_lnfree'] = sm.c['v']
            emit_transpose_tiles(block, sm, lnbf, lnT, i * 128, identbf, p_tr)

        for c in range(2):
            @block.tensor
            def _(t, c=c):
                sm.wait(t, 'a')
                for d in range(32):
                    ins = t.matmul(p_r[:], aTbf[:, d * 48:(d + 1) * 48],
                                   lnT[d][:, c * 512:(c + 1) * 512],
                                   start=(d == 0), stop=(d == 31))
                sm.inc(ins, 't')

            @block.scalar
            def _(a, c=c):
                sm.wait(a, 't')
                a.wait_ge(sm.h['a'], a_wsel)
                ins = a.activation(out=rbf[:, c * 512:(c + 1) * 512],
                                   in_=p_r[:], func=AF.Copy, scale=wsel48[:])
                sm.inc(ins, 'a')

        CH = 256
        for oc in range(D // CH):
            @block.gpsimd
            def _(g, oc=oc):
                if '_wfree' in sm.tag:
                    g.wait_ge(sm.h['t'], sm.tag['_wfree'])
                for q in range(4):
                    gdma(g, sm, 'g', bass.AP(wbuf, q * 8 * CH,
                                    [[32 * CH, 128], [CH, 8], [1, CH]]), bass.AP(w0T_in, oc * CH + q * 8 * 128 * D,
                                    [[D, 128], [128 * D, 8], [1, CH]]),)
            g_w = sm.c['g']

            for i in range(NT):
                @block.tensor
                def _(t, oc=oc, i=i):
                    t.wait_ge(sm.h['g'], g_w)
                    if oc == 0 and i == 0:
                        sm.wait(t, 'a')
                    it = oc * NT + i
                    pj = it % 2
                    if f'_yfree{pj}' in sm.tag:
                        t.wait_ge(sm.h['a'], sm.tag[f'_yfree{pj}'])
                    t.matmul(p_y[pj][:], onesbf[:],
                             biasbf[:, oc * CH:(oc + 1) * CH],
                             start=True, stop=False)
                    for d in range(32):
                        t.matmul(p_y[pj][:], lnT[d][:, i * 128:(i + 1) * 128],
                                 wbuf[:, d * CH:(d + 1) * CH],
                                 start=False, stop=False)
                    ins = t.matmul(p_y[pj][:], rbf[:, i * 128:(i + 1) * 128],
                                   bTbf[:, oc * CH:(oc + 1) * CH],
                                   start=False, stop=True)
                    sm.inc(ins, 't')

                @block.scalar
                def _(a, oc=oc, i=i):
                    sm.wait(a, 't')
                    pj = (oc * NT + i) % 2
                    ins = a.activation(out=ybf[i][:, oc * CH:(oc + 1) * CH],
                                       in_=p_y[pj][:], func=AF.Copy)
                    sm.tag[f'_yfree{pj}'] = sm.inc(ins, 'a')
            sm.tag['_wfree'] = sm.c['t']

        for i in range(NT):
            @block.gpsimd
            def _(g, i=i):
                sm.wait(g, 'a')
                if '_ofree' in sm.tag:
                    g.wait_ge(sm.h['o'], sm.tag['_ofree'])
                gdma(g, sm, 'g', xbuf[:], x_in[i * 128:(i + 1) * 128, :])

            emit_ln_tile(block, sm, ybf[i], lnbf, scr_bf, stat,
                         v_waits=[('a', sm.c['a'])])

            @block.vector
            def _(v, i=i):
                sm.wait(v, 'g')
                sm.wait(v, 'v')
                ins = v.tensor_tensor(out=xbuf[:], in0=xbuf[:],
                                      in1=lnbf[:], op=ALU.add)
                sm.inc(ins, 'v')

            @block.gpsimd
            def _(g, i=i):
                sm.wait(g, 'v')
                gdma(g, sm, 'o', out[i * 128:(i + 1) * 128, :], xbuf[:])
            sm.tag['_ofree'] = sm.c['o']

        @block.sync
        def _(sy):
            sy.wait_ge(sm.h['o'], sm.c['o'])

    return nc


def ffn0_inmap(x1_rows, inp):
    lA = np.asarray(inp['L0_ffn_lora_A'], np.float32)
    lB = np.asarray(inp['L0_ffn_lora_B'], np.float32)
    e48 = np.zeros((3, 48), np.float32)
    for j in range(3):
        e48[j, 16 * j:16 * j + 16] = 1.0
    return {
        "x": f32c(x1_rows),
        "w0T": f32c(np.asarray(inp['L0_ffn_weight']).T),
        "bias": f32c(np.asarray(inp['L0_ffn_bias']).reshape(1, D)),
        "lAT": f32c(np.transpose(lA, (2, 0, 1)).reshape(D, 48)),
        "lBT": f32c(np.transpose(lB, (0, 2, 1)).reshape(48, D)),
        "lsi": np.ascontiguousarray(
            np.asarray(inp['L0_ffn_lora_start_indices'], np.int32).reshape(1, 4)),
        "c012": np.array([[0.0], [1.0], [2.0]], np.float32),
        "e48": f32c(e48),
        "ident": np.eye(128, dtype=NPBF),
        "ones1": np.ones((1, 128), dtype=NPBF),
    }





NI = 11008 // 128


def build_ffn1():
    nc = bass.Bass("TRN2", target_bir_lowering=False)
    x_in = nc.dram_tensor("x", [ROWS, D], F32, kind="ExternalInput")
    wg_in = nc.dram_tensor("wg", [NI, 128, D], I32, kind="ExternalInput")
    wu_in = nc.dram_tensor("wu", [NI, 128, D], I32, kind="ExternalInput")
    wd_in = nc.dram_tensor("wd", [11008, D], I32, kind="ExternalInput")
    sg_in = nc.dram_tensor("sg", [128, NI], F32, kind="ExternalInput")
    su_in = nc.dram_tensor("su", [128, NI], F32, kind="ExternalInput")
    sd_in = nc.dram_tensor("sd", [1, D], F32, kind="ExternalInput")
    ident_in = nc.dram_tensor("ident", [128, 128], BF16, kind="ExternalInput")
    ones1_in = nc.dram_tensor("ones1", [1, 128], BF16, kind="ExternalInput")
    out = nc.dram_tensor("out", [ROWS, D], F32, kind="ExternalOutput")

    HT = 512
    NTH = HT // 128

    lnT = [nc.alloc_sbuf_tensor(f"lnT{d}", [128, HT], BF16) for d in range(32)]
    hsb = [nc.alloc_sbuf_tensor(f"h{i}", [128, HT], BF16) for i in range(NI)]
    ybf = [nc.alloc_sbuf_tensor(f"y{i}", [128, D], BF16) for i in range(NTH)]
    wgb = nc.alloc_sbuf_tensor("wgb", [128, D], BF16)
    wub = nc.alloc_sbuf_tensor("wub", [128, D], BF16)
    wdb = [nc.alloc_sbuf_tensor(f"wdb{i}", [128, 512], BF16) for i in range(2)]
    gact = nc.alloc_sbuf_tensor("gact", [128, HT], BF16)
    uact = nc.alloc_sbuf_tensor("uact", [128, HT], BF16)
    sgs = nc.alloc_sbuf_tensor("sgs", [128, NI], F32)
    sus = nc.alloc_sbuf_tensor("sus", [128, NI], F32)
    sdb = nc.alloc_sbuf_tensor("sdb", [128, D], BF16)
    sdrow = nc.alloc_sbuf_tensor("sdrow", [1, 512], BF16)
    xbuf = nc.alloc_sbuf_tensor("xb", [128, D], F32)
    lnbf = nc.alloc_sbuf_tensor("lnbf", [128, D], BF16)
    stat = nc.alloc_sbuf_tensor("stat", [128, 12], F32)
    onesbf = nc.alloc_sbuf_tensor("onesbf", [1, 128], BF16)
    identbf = nc.alloc_sbuf_tensor("identbf", [128, 128], BF16)

    p_tr = [nc.alloc_psum_tensor(f"ptr{i}", [128, 128], BF16) for i in range(2)]
    p_g = nc.alloc_psum_tensor("pg", [128, HT], F32)
    p_u = nc.alloc_psum_tensor("pu", [128, HT], F32)
    p_y = [nc.alloc_psum_tensor(f"py{i}", [128, 512], F32) for i in range(NTH)]

    sm = Sems(nc, ['g', 'v', 'a', 't', 'o', 'w'])

    with nc.Block() as block:
        @block.gpsimd
        def _(g):
            gdma(g, sm, 'g', identbf[:], ident_in[:])
            gdma(g, sm, 'g', onesbf[:], ones1_in[:])
            gdma(g, sm, 'g', sgs[:], sg_in[:])
            gdma(g, sm, 'g', sus[:], su_in[:])

        for q in range(8):
            @block.gpsimd
            def _(g, q=q):
                if q > 0:
                    g.wait_ge(sm.h['t'], sm.tag['_sdt'])
                gdma(g, sm, 'g', sdrow[:], sd_in[:, q * 512:(q + 1) * 512])

            @block.tensor
            def _(t, q=q):
                sm.wait(t, 'g')
                if q > 0:
                    t.wait_ge(sm.h['a'], sm.tag['_sdq'])
                ins = t.matmul(p_g[:], onesbf[:], sdrow[:],
                               start=True, stop=True)
                sm.tag['_sdt'] = sm.inc(ins, 't')

            @block.scalar
            def _(a, q=q):
                sm.wait(a, 't')
                ins = a.activation(out=sdb[:, q * 512:(q + 1) * 512],
                                   in_=p_g[:], func=AF.Copy)
                sm.tag['_sdq'] = sm.inc(ins, 'a')

        for half in range(2):
            base = half * HT
            for i in range(NTH):
                @block.gpsimd
                def _(g, i=i):
                    if '_lnfree' in sm.tag:
                        g.wait_ge(sm.h['v'], sm.tag['_lnfree'])
                    if '_ofree' in sm.tag:
                        g.wait_ge(sm.h['o'], sm.tag['_ofree'])
                    gdma(g, sm, 'g', xbuf[:], x_in[base + i * 128:base + (i + 1) * 128, :])
                emit_ln_tile(block, sm, xbuf, lnbf, wgb, stat,
                             v_waits=[('g', sm.c['g']), ('t', sm.c['t'])])
                sm.tag['_lnfree'] = sm.c['v']
                emit_transpose_tiles(block, sm, lnbf, lnT, i * 128, identbf,
                                     p_tr)

            for it in range(NI):
                vc_ln = sm.c['v']

                @block.gpsimd
                def _(g, it=it, vc_ln=vc_ln):
                    if '_gufree' in sm.tag:
                        g.wait_ge(sm.h['t'], sm.tag['_gufree'])
                    if it == 0:
                        g.wait_ge(sm.h['v'], vc_ln)
                    gdma(g, sm, 'g', wgb[:], wg_in[it])
                    gdma(g, sm, 'g', wub[:], wu_in[it])

                @block.tensor
                def _(t, it=it):
                    sm.wait(t, 'g')
                    if it == 0:
                        sm.wait(t, 'a')
                    if '_gufree_a' in sm.tag:
                        t.wait_ge(sm.h['a'], sm.tag['_gufree_a'])
                    for dt in range(32):
                        t.matmul(p_g[:], wgb[:, dt * 128:(dt + 1) * 128],
                                 lnT[dt][:], start=(dt == 0), stop=(dt == 31))
                    for dt in range(32):
                        ins = t.matmul(p_u[:], wub[:, dt * 128:(dt + 1) * 128],
                                       lnT[dt][:], start=(dt == 0),
                                       stop=(dt == 31))
                    sm.tag['_gufree'] = sm.inc(ins, 't')

                @block.scalar
                def _(a, it=it):
                    sm.wait(a, 't')
                    if '_hmul' in sm.tag:
                        a.wait_ge(sm.h['v'], sm.tag['_hmul'])
                    a.activation(out=gact[:], in_=p_g[:], func=AF.Silu,
                                 scale=sgs[:, it:it + 1])
                    ins = a.activation(out=uact[:], in_=p_u[:], func=AF.Copy,
                                       scale=sus[:, it:it + 1])
                    sm.tag['_gufree_a'] = sm.inc(ins, 'a')

                @block.vector
                def _(v, it=it):
                    sm.wait(v, 'a')
                    ins = v.tensor_tensor(out=hsb[it][:], in0=gact[:],
                                          in1=uact[:], op=ALU.mult)
                    sm.tag['_hmul'] = sm.inc(ins, 'v')

            for hc in range(8):
                for it in range(NI):
                    @block.gpsimd
                    def _(g, hc=hc, it=it):
                        j = it % 2
                        if f'_wdfree{j}' in sm.tag:
                            g.wait_ge(sm.h['t'], sm.tag[f'_wdfree{j}'])
                        gdma(g, sm, 'w', wdb[it % 2][:], wd_in[it * 128:(it + 1) * 128,
                                      hc * 512:(hc + 1) * 512])
                    w_c = sm.c['w']

                    @block.tensor
                    def _(t, hc=hc, it=it):
                        t.wait_ge(sm.h['w'], w_c)
                        if it == 0:
                            sm.wait(t, 'v')
                            if '_yevac' in sm.tag:
                                t.wait_ge(sm.h['v'], sm.tag['_yevac'])
                        for tt in range(NTH):
                            ins = t.matmul(p_y[tt][:],
                                           hsb[it][:, tt * 128:(tt + 1) * 128],
                                           wdb[it % 2][:],
                                           start=(it == 0), stop=(it == NI - 1))
                        sm.tag[f'_wdfree{it % 2}'] = sm.inc(ins, 't')

                @block.vector
                def _(v, hc=hc):
                    sm.wait(v, 't')
                    for tt in range(NTH):
                        ins = v.tensor_tensor(
                            out=ybf[tt][:, hc * 512:(hc + 1) * 512],
                            in0=p_y[tt][:], in1=sdb[:, hc * 512:(hc + 1) * 512],
                            op=ALU.mult)
                    sm.tag['_yevac'] = sm.inc(ins, 'v')

            for i in range(NTH):
                @block.gpsimd
                def _(g, i=i):
                    sm.wait(g, 'v')
                    if '_ofree' in sm.tag:
                        g.wait_ge(sm.h['o'], sm.tag['_ofree'])
                    gdma(g, sm, 'g', xbuf[:], x_in[base + i * 128:base + (i + 1) * 128, :])
                emit_ln_tile(block, sm, ybf[i], lnbf, wgb, stat,
                             v_waits=[('v', sm.c['v']), ('g', sm.c['g']),
                                      ('t', sm.c['t'])])

                @block.vector
                def _(v, i=i):
                    sm.wait(v, 'g')
                    sm.wait(v, 'v')
                    ins = v.tensor_tensor(out=xbuf[:], in0=xbuf[:],
                                          in1=lnbf[:], op=ALU.add)
                    sm.inc(ins, 'v')

                @block.gpsimd
                def _(g, i=i):
                    sm.wait(g, 'v')
                    gdma(g, sm, 'o', out[base + i * 128:base + (i + 1) * 128, :], xbuf[:])
                sm.tag['_ofree'] = sm.c['o']

        @block.sync
        def _(sy):
            sy.wait_ge(sm.h['o'], sm.c['o'])

    return nc


def ffn1_inmap(x3_rows, inp):
    wg = np.asarray(inp['L1_ffn_w_gate'])
    wu = np.asarray(inp['L1_ffn_w_up'])
    wd = np.asarray(inp['L1_ffn_w_down'])
    def relayout(w):
        t = w.reshape(NI, 128, 32, 128)
        return np.ascontiguousarray(
            np.transpose(t, (0, 3, 2, 1)).astype(np.int8)).reshape(
                NI, 128, D)

    return {
        "x": f32c(x3_rows),
        "wg": relayout(wg),
        "wu": relayout(wu),
        "wd": np.ascontiguousarray(wd.T.astype(np.int8)),
        "sg": f32c(np.asarray(inp['L1_ffn_s_gate']).reshape(NI, 128).T),
        "su": f32c(np.asarray(inp['L1_ffn_s_up']).reshape(NI, 128).T),
        "sd": f32c(np.asarray(inp['L1_ffn_s_down']).reshape(1, D)),
        "ident": np.eye(128, dtype=NPBF),
        "ones1": np.ones((1, 128), dtype=NPBF),
    }



def build_s4():
    nc = bass.Bass("TRN2", target_bir_lowering=False)
    F = F32
    x_in = nc.dram_tensor("x", [ROWS, D], F, kind="ExternalInput")
    lam_in = nc.dram_tensor("lam", [64, 2], F, kind="ExternalInput")
    pri_in = nc.dram_tensor("pri", [64, 2], F, kind="ExternalInput")
    qri_in = nc.dram_tensor("qri", [64, 2], F, kind="ExternalInput")
    vctre_in = nc.dram_tensor("vctre", [64, 64], F, kind="ExternalInput")
    vctim_in = nc.dram_tensor("vctim", [64, 64], F, kind="ExternalInput")
    ctt_in = nc.dram_tensor("ctt", [64, 2], F, kind="ExternalInput")
    bp_in = nc.dram_tensor("bp", [64, 1], F, kind="ExternalInput")
    dsc_in = nc.dram_tensor("dsc", [1, 1], F, kind="ExternalInput")
    lstep_in = nc.dram_tensor("lstep", [1, 1], F, kind="ExternalInput")
    c1bre_in = nc.dram_tensor("c1bre", [64, D], F, kind="ExternalInput")
    c1bim_in = nc.dram_tensor("c1bim", [64, D], F, kind="ExternalInput")
    c2re_in = nc.dram_tensor("c2re", [64, 64], F, kind="ExternalInput")
    c2im_in = nc.dram_tensor("c2im", [64, 64], F, kind="ExternalInput")
    wc_in = nc.dram_tensor("wc", [64, 64], F, kind="ExternalInput")
    ws_in = nc.dram_tensor("ws", [64, 64], F, kind="ExternalInput")
    wsn_in = nc.dram_tensor("wsn", [64, 64], F, kind="ExternalInput")
    twre_in = nc.dram_tensor("twre", [64, 64], F, kind="ExternalInput")
    twim_in = nc.dram_tensor("twim", [64, 64], F, kind="ExternalInput")
    idf_in = nc.dram_tensor("idf", [64, 64], F, kind="ExternalInput")
    onesf_in = nc.dram_tensor("onesf", [1, 128], F, kind="ExternalInput")
    ident_in = nc.dram_tensor("ident", [128, 128], BF16, kind="ExternalInput")
    out = nc.dram_tensor("out", [ROWS, D], F, kind="ExternalOutput")
    odr = nc.dram_tensor("odr", [16, D], F, kind="Internal")
    kdr = nc.dram_tensor("kdr", [64, 64], F, kind="Internal")

    e_all = nc.alloc_sbuf_tensor("e_all", [128, 35 * 128], BF16)
    kpad = nc.alloc_sbuf_tensor("kpad", [1, 4608], BF16)
    identbf = nc.alloc_sbuf_tensor("identbf", [128, 128], BF16)
    onesf = nc.alloc_sbuf_tensor("onesf_s", [1, 128], F)
    dv = nc.alloc_sbuf_tensor("dv", [128, 1], F)
    sc1 = nc.alloc_sbuf_tensor("sc1", [1, 4], F)

    p_tr = [nc.alloc_psum_tensor(f"ptr{i}", [128, 128], BF16)
            for i in range(2)]
    p_y = [nc.alloc_psum_tensor(f"py{i}", [128, 512], F) for i in range(2)]

    sm = Sems(nc, ['g', 'v', 'a', 't', 'o', 'w', 'm'])

    def dstep(eng, ins, n):
        sm.inc(ins, n)
        sm.wait(eng, n)

    with nc.Block() as block:
        from contextlib import ExitStack
        _es = ExitStack()
        c1re = _es.enter_context(nc.sbuf_tensor("c1re", [64, D], F))
        c1im = _es.enter_context(nc.sbuf_tensor("c1im", [64, D], F))
        dre = _es.enter_context(nc.sbuf_tensor("dre", [64, D], F))
        dim_ = _es.enter_context(nc.sbuf_tensor("dim_", [64, D], F))
        tmpl = _es.enter_context(nc.sbuf_tensor("tmpl", [64, D], F))
        winv = _es.enter_context(nc.sbuf_tensor("winv", [64, D], F))
        wre = _es.enter_context(nc.sbuf_tensor("wre", [64, D], F))
        wim = _es.enter_context(nc.sbuf_tensor("wim", [64, D], F))
        lamv = _es.enter_context(nc.sbuf_tensor("lamv", [64, 2], F))
        priv = _es.enter_context(nc.sbuf_tensor("priv", [64, 2], F))
        qriv = _es.enter_context(nc.sbuf_tensor("qriv", [64, 2], F))
        cttv = _es.enter_context(nc.sbuf_tensor("cttv", [64, 2], F))
        vctre = _es.enter_context(nc.sbuf_tensor("vctre_s", [64, 64], F))
        vctim = _es.enter_context(nc.sbuf_tensor("vctim_s", [64, 64], F))
        bpv = _es.enter_context(nc.sbuf_tensor("bpv", [64, 1], F))
        bv = _es.enter_context(nc.sbuf_tensor("bv", [64, 2], F))
        vstk = _es.enter_context(nc.sbuf_tensor("vstk", [64, 8], F))
        ls1 = _es.enter_context(nc.sbuf_tensor("ls1", [1, 4], F))
        s2v = _es.enter_context(nc.sbuf_tensor("s2v", [64, 1], F))
        ob = _es.enter_context(nc.sbuf_tensor("ob", [8, 512], F))
        km = _es.enter_context(nc.sbuf_tensor("km", [64, 16 * 64], F))
        cx = _es.enter_context(nc.sbuf_tensor("cx", [64, 24 * 64], F))
        wcv = _es.enter_context(nc.sbuf_tensor("wcv", [64, 64], F))
        wsv = _es.enter_context(nc.sbuf_tensor("wsv", [64, 64], F))
        wsnv = _es.enter_context(nc.sbuf_tensor("wsnv", [64, 64], F))
        twrev = _es.enter_context(nc.sbuf_tensor("twrev", [64, 64], F))
        twimv = _es.enter_context(nc.sbuf_tensor("twimv", [64, 64], F))
        c2rev = _es.enter_context(nc.sbuf_tensor("c2rev", [64, 64], F))
        c2imv = _es.enter_context(nc.sbuf_tensor("c2imv", [64, 64], F))
        idfv = _es.enter_context(nc.sbuf_tensor("idfv", [64, 64], F))
        k2v = _es.enter_context(nc.sbuf_tensor("k2v", [64, 64], F))
        p_s = _es.enter_context(nc.psum_tensor("p_s", [64, 2], F))
        p_c = _es.enter_context(nc.psum_tensor("p_c", [8, 512], F))
        p_f1 = _es.enter_context(nc.psum_tensor("p_f1", [64, 64], F))
        p_f2 = _es.enter_context(nc.psum_tensor("p_f2", [64, 64], F))
        if True:
            @block.gpsimd
            def _(g):
                for dst, src_ in [(c1re, c1bre_in), (c1im, c1bim_in),
                                  (lamv, lam_in), (priv, pri_in),
                                  (qriv, qri_in), (cttv, ctt_in),
                                  (vctre, vctre_in), (vctim, vctim_in),
                                  (bpv, bp_in), (wcv, wc_in), (wsv, ws_in),
                                  (wsnv, wsn_in), (twrev, twre_in),
                                  (twimv, twim_in), (c2rev, c2re_in),
                                  (c2imv, c2im_in), (idfv, idf_in),
                                  (onesf, onesf_in), (identbf, ident_in)]:
                    gdma(g, sm, 'g', dst[:], src_[:])
                gdma(g, sm, 'g', ls1[:, 0:1], lstep_in[:])

            @block.scalar
            def _(a):
                sm.wait(a, 'g')
                ins = a.activation(out=ls1[:, 1:2], in_=ls1[:, 0:1], func=AF.Exp)
                dstep(a, ins, 'a')

            @block.vector
            def _(v):
                sm.wait(v, 'a')
                ins = v.reciprocal(out=ls1[:, 2:3], in_=ls1[:, 1:2])
                dstep(v, ins, 'v')
                ins = v.tensor_scalar(out=ls1[:, 3:4], in0=ls1[:, 2:3],
                                      scalar1=2.0, scalar2=None, op0=ALU.mult)
                dstep(v, ins, 'v')

            @block.tensor
            def _(t):
                sm.wait(t, 'v')
                t.matmul(p_s[:, 0:1], onesf[:, 0:64], ls1[:, 3:4],
                         start=True, stop=True)
                t.matmul(p_f1[:, 0:1], vctre[:], bpv[:], start=True, stop=True)
                ins = t.matmul(p_f1[:, 1:2], vctim[:], bpv[:], start=True,
                               stop=True)
                sm.inc(ins, 't')

            @block.scalar
            def _(a):
                sm.wait(a, 't')
                a.activation(out=s2v[:], in_=p_s[:, 0:1], func=AF.Copy)
                ins = a.activation(out=bv[:], in_=p_f1[:, 0:2], func=AF.Copy)
                sm.inc(ins, 'a')

            @block.vector
            def _(v):
                sm.wait(v, 'a')
                v.tensor_scalar(out=dre[:], in0=c1re[:], scalar1=s2v[:],
                                scalar2=lamv[:, 0:1], op0=ALU.mult,
                                op1=ALU.subtract)
                ins = v.tensor_scalar(out=dim_[:], in0=c1im[:], scalar1=s2v[:],
                                      scalar2=lamv[:, 1:2], op0=ALU.mult,
                                      op1=ALU.subtract)
                dstep(v, ins, 'v')
                v.tensor_tensor(out=tmpl[:], in0=dre[:], in1=dre[:],
                                op=ALU.mult)
                ins = v.scalar_tensor_tensor(out=winv[:], in0=dim_[:],
                                             scalar=1.0, in1=dim_[:],
                                             op0=ALU.mult, op1=ALU.mult)
                dstep(v, ins, 'v')
                ins = v.tensor_tensor(out=tmpl[:], in0=tmpl[:], in1=winv[:],
                                      op=ALU.add)
                dstep(v, ins, 'v')
                ins = v.reciprocal(out=winv[:], in_=tmpl[:])
                dstep(v, ins, 'v')
                ins = v.tensor_tensor(out=wre[:], in0=dre[:], in1=winv[:],
                                       op=ALU.mult)
                dstep(v, ins, 'v')
                ins = v.scalar_tensor_tensor(out=wim[:], in0=dim_[:],
                                             scalar=-1.0, in1=winv[:],
                                             op0=ALU.mult, op1=ALU.mult)
                dstep(v, ins, 'v')
                pairs = [(cttv, bv, 0), (cttv, priv, 2), (qriv, bv, 4),
                         (qriv, priv, 6)]
                for (xx, yy, col) in pairs:
                    ins = v.tensor_tensor(out=vstk[:, col:col + 1],
                                          in0=xx[:, 0:1], in1=yy[:, 0:1],
                                          op=ALU.mult)
                    dstep(v, ins, 'v')
                    ins = v.tensor_tensor(out=tmpl[:, 0:1], in0=xx[:, 1:2],
                                          in1=yy[:, 1:2], op=ALU.mult)
                    dstep(v, ins, 'v')
                    ins = v.tensor_tensor(out=vstk[:, col:col + 1],
                                          in0=vstk[:, col:col + 1],
                                          in1=tmpl[:, 0:1], op=ALU.add)
                    dstep(v, ins, 'v')
                    ins = v.tensor_tensor(out=vstk[:, col + 1:col + 2],
                                          in0=xx[:, 0:1], in1=yy[:, 1:2],
                                          op=ALU.mult)
                    dstep(v, ins, 'v')
                    ins = v.tensor_tensor(out=tmpl[:, 0:1], in0=xx[:, 1:2],
                                          in1=yy[:, 0:1], op=ALU.mult)
                    dstep(v, ins, 'v')
                    ins = v.tensor_tensor(out=vstk[:, col + 1:col + 2],
                                          in0=vstk[:, col + 1:col + 2],
                                          in1=tmpl[:, 0:1], op=ALU.subtract)
                    dstep(v, ins, 'v')

            for which, rhs in [(0, wre), (1, wim)]:
                for ch in range(8):
                    @block.tensor
                    def _(t, which=which, ch=ch, rhs=rhs):
                        if which == 0 and ch == 0:
                            sm.wait(t, 'v')
                        else:
                            t.wait_ge(sm.h['a'], sm.tag['_cev'])
                        ins = t.matmul(p_c[:], vstk[:],
                                       rhs[:, ch * 512:(ch + 1) * 512],
                                       start=True, stop=True)
                        sm.inc(ins, 't')

                    @block.scalar
                    def _(a, which=which, ch=ch):
                        sm.wait(a, 't')
                        sm.wait(a, 'w')
                        ins = a.activation(out=ob[:], in_=p_c[:], func=AF.Copy)
                        sm.tag['_cev'] = sm.inc(ins, 'a')

                    @block.gpsimd
                    def _(g, which=which, ch=ch):
                        sm.wait(g, 'a')
                        gdma(g, sm, 'w', odr[which * 8:(which + 1) * 8,
                                            ch * 512:(ch + 1) * 512], ob[:])

            @block.gpsimd
            def _(g):
                sm.wait(g, 'w')
                for c in range(16):
                    gdma(g, sm, 'g', km[:, c * 64:(c + 1) * 64], bass.AP(odr, c * D, [[64, 64], [1, 64]]))

            def KM(i):
                return km[:, i * 64:(i + 1) * 64]

            def CX(i):
                return cx[:, i * 64:(i + 1) * 64]

            @block.vector
            def _(v):
                sm.wait(v, 'g')
                for j in range(4):
                    v.tensor_tensor(out=CX(2 * j), in0=KM(2 * j),
                                    in1=KM(8 + 2 * j + 1), op=ALU.subtract)
                    ins = v.tensor_tensor(out=CX(2 * j + 1), in0=KM(2 * j + 1),
                                          in1=KM(8 + 2 * j), op=ALU.add)
                dstep(v, ins, 'v')
                v.tensor_scalar(out=CX(8), in0=CX(6), scalar1=1.0,
                                scalar2=None, op0=ALU.add)
                ins = v.tensor_copy(out=CX(9), in_=CX(7))
                dstep(v, ins, 'v')
                v.tensor_tensor(out=CX(10), in0=CX(2), in1=CX(4), op=ALU.mult)
                v.tensor_tensor(out=CX(12), in0=CX(3), in1=CX(5), op=ALU.mult)
                v.tensor_tensor(out=CX(11), in0=CX(2), in1=CX(5), op=ALU.mult)
                ins = v.tensor_tensor(out=CX(13), in0=CX(3), in1=CX(4),
                                      op=ALU.mult)
                dstep(v, ins, 'v')
                v.tensor_tensor(out=CX(10), in0=CX(10), in1=CX(12),
                                op=ALU.subtract)
                v.tensor_tensor(out=CX(11), in0=CX(11), in1=CX(13), op=ALU.add)
                v.tensor_tensor(out=CX(22), in0=CX(8), in1=CX(8), op=ALU.mult)
                ins = v.tensor_tensor(out=CX(23), in0=CX(9), in1=CX(9),
                                      op=ALU.mult)
                dstep(v, ins, 'v')
                ins = v.tensor_tensor(out=CX(22), in0=CX(22), in1=CX(23),
                                      op=ALU.add)
                dstep(v, ins, 'v')
                ins = v.reciprocal(out=CX(23), in_=CX(22))
                dstep(v, ins, 'v')
                v.tensor_tensor(out=CX(14), in0=CX(10), in1=CX(8), op=ALU.mult)
                v.tensor_tensor(out=CX(16), in0=CX(11), in1=CX(9), op=ALU.mult)
                v.tensor_tensor(out=CX(15), in0=CX(11), in1=CX(8), op=ALU.mult)
                ins = v.tensor_tensor(out=CX(17), in0=CX(10), in1=CX(9),
                                      op=ALU.mult)
                dstep(v, ins, 'v')
                v.tensor_tensor(out=CX(14), in0=CX(14), in1=CX(16), op=ALU.add)
                ins = v.tensor_tensor(out=CX(15), in0=CX(15), in1=CX(17),
                                      op=ALU.subtract)
                dstep(v, ins, 'v')
                v.tensor_tensor(out=CX(14), in0=CX(14), in1=CX(23), op=ALU.mult)
                ins = v.tensor_tensor(out=CX(15), in0=CX(15), in1=CX(23),
                                      op=ALU.mult)
                dstep(v, ins, 'v')
                v.tensor_tensor(out=CX(16), in0=CX(0), in1=CX(14),
                                op=ALU.subtract)
                ins = v.tensor_tensor(out=CX(17), in0=CX(1), in1=CX(15),
                                op=ALU.subtract)
                dstep(v, ins, 'v')
                v.tensor_tensor(out=CX(18), in0=c2rev[:], in1=CX(16),
                                op=ALU.mult)
                v.tensor_tensor(out=CX(20), in0=c2imv[:], in1=CX(17),
                                op=ALU.mult)
                v.tensor_tensor(out=CX(19), in0=c2rev[:], in1=CX(17),
                                op=ALU.mult)
                ins = v.tensor_tensor(out=CX(21), in0=c2imv[:], in1=CX(16),
                                op=ALU.mult)
                dstep(v, ins, 'v')
                v.tensor_tensor(out=CX(18), in0=CX(18), in1=CX(20),
                                op=ALU.subtract)
                ins = v.tensor_tensor(out=CX(19), in0=CX(19), in1=CX(21),
                                      op=ALU.add)
                sm.inc(ins, 'v')

            @block.tensor
            def _(t):
                sm.wait(t, 'v')
                t.matmul(p_f1[:, 0:64], wcv[:], CX(18), start=True, stop=False)
                t.matmul(p_f1[:, 0:64], wsnv[:], CX(19), start=False, stop=True)
                t.matmul(p_f2[:, 0:64], wsv[:], CX(18), start=True, stop=False)
                ins = t.matmul(p_f2[:, 0:64], wcv[:], CX(19), start=False,
                               stop=True)
                sm.inc(ins, 't')

            @block.scalar
            def _(a):
                sm.wait(a, 't')
                a.activation(out=CX(0), in_=p_f1[:, 0:64], func=AF.Copy)
                ins = a.activation(out=CX(1), in_=p_f2[:, 0:64], func=AF.Copy)
                sm.inc(ins, 'a')

            @block.vector
            def _(v):
                sm.wait(v, 'a')
                v.tensor_tensor(out=CX(2), in0=CX(0), in1=twrev[:], op=ALU.mult)
                v.tensor_tensor(out=CX(4), in0=CX(1), in1=twimv[:], op=ALU.mult)
                v.tensor_tensor(out=CX(3), in0=CX(0), in1=twimv[:], op=ALU.mult)
                ins = v.tensor_tensor(out=CX(5), in0=CX(1), in1=twrev[:], op=ALU.mult)
                dstep(v, ins, 'v')
                v.tensor_tensor(out=CX(2), in0=CX(2), in1=CX(4), op=ALU.subtract)
                ins = v.tensor_tensor(out=CX(3), in0=CX(3), in1=CX(5),
                                      op=ALU.add)
                sm.inc(ins, 'v')

            @block.tensor
            def _(t):
                sm.wait(t, 'v')
                t.transpose(p_f1[:, 0:64], CX(2), idfv[:])
                ins = t.transpose(p_f2[:, 0:64], CX(3), idfv[:])
                sm.inc(ins, 't')

            @block.scalar
            def _(a):
                sm.wait(a, 't')
                a.activation(out=CX(6), in_=p_f1[:, 0:64], func=AF.Copy)
                ins = a.activation(out=CX(7), in_=p_f2[:, 0:64], func=AF.Copy)
                sm.inc(ins, 'a')
            a_tr2 = sm.c['a']

            @block.tensor
            def _(t):
                sm.wait(t, 'a')
                t.matmul(p_f1[:, 0:64], wcv[:], CX(6), start=True, stop=False)
                ins = t.matmul(p_f1[:, 0:64], wsnv[:], CX(7), start=False,
                               stop=True)
                sm.inc(ins, 't')

            @block.scalar
            def _(a):
                sm.wait(a, 't')
                ins = a.activation(out=k2v[:], in_=p_f1[:, 0:64], func=AF.Copy,
                                   scale=1.0 / 4096.0)
                sm.inc(ins, 'a')

            @block.gpsimd
            def _(g):
                sm.wait(g, 'a')
                gdma(g, sm, 'w', kdr[:], k2v[:])
                sm.wait(g, 'w')
                ins = g.memset(kpad[:], 0.0)
                sm.inc(ins, 'm')
                sm.wait(g, 'm')
                gdma(g, sm, 'w', kpad[:, 512:4608],
                     bass.AP(kdr, 0, [[4096, 1], [1, 4096]]))
                sm.wait(g, 'w')
                gdma(g, sm, 'w', e_all[:],
                     bass.AP(kpad, 128, [[4608, 1], [-1, 128], [1, 4480]]))
                sm.wait(g, 'w')

        _es.close()
        uT = [nc.alloc_sbuf_tensor(f"uT{d}", [128, ROWS], BF16)
              for d in range(32)]
        lnnat = [nc.alloc_sbuf_tensor(f"ln{i}", [128, D], BF16)
                 for i in range(NT)]
        ysb = nc.alloc_sbuf_tensor("ysb", [128, D], F)
        xbuf = nc.alloc_sbuf_tensor("xb", [128, D], F)
        lnbf = nc.alloc_sbuf_tensor("lnbf", [128, D], BF16)
        scr_bf = nc.alloc_sbuf_tensor("scr_bf", [128, D], BF16)
        stat = nc.alloc_sbuf_tensor("stat", [128, 12], F)
        for i in range(NT):
            @block.gpsimd
            def _(g, i=i):
                if '_lnfree' in sm.tag:
                    g.wait_ge(sm.h['v'], sm.tag['_lnfree'])
                gdma(g, sm, 'g', xbuf[:], x_in[i * 128:(i + 1) * 128, :])
            emit_ln_tile(block, sm, xbuf, lnnat[i], scr_bf, stat,
                         v_waits=[('g', sm.c['g']), ('t', sm.c['t'])])
            sm.tag['_lnfree'] = sm.c['v']
            emit_transpose_tiles(block, sm, lnnat[i], uT, i * 128, identbf,
                                 p_tr)

        @block.gpsimd
        def _(g):
            gdma(g, sm, 'g', sc1[:, 0:1], dsc_in[:])

        @block.tensor
        def _(t):
            sm.wait(t, 'g')
            ins = t.matmul(p_y[0][:, 0:1], onesf[:], sc1[:, 0:1],
                           start=True, stop=True)
            sm.inc(ins, 't')

        @block.scalar
        def _(a):
            sm.wait(a, 't')
            ins = a.activation(out=dv[:], in_=p_y[0][:, 0:1], func=AF.Copy)
            sm.inc(ins, 'a')

        for i in range(NT):
            for grp in range(8):
                @block.tensor
                def _(t, i=i, grp=grp):
                    if i == 0 and grp == 0:
                        sm.wait(t, 'a')
                        sm.wait(t, 'v')
                        sm.wait(t, 'w')
                    j = grp % 2
                    if f'_pyfree{j}' in sm.tag:
                        t.wait_ge(sm.h['a'], sm.tag[f'_pyfree{j}'])
                    last_c = min(4 * grp + 3, 31)
                    for c in range(last_c + 1):
                        ins = t.matmul(
                            p_y[j][:],
                            uT[c][:, i * 128:(i + 1) * 128],
                            e_all[:, (4 * grp - c + 3) * 128:
                                  (4 * grp - c + 3) * 128 + 512],
                            start=(c == 0), stop=(c == last_c))
                    sm.inc(ins, 't')

                @block.scalar
                def _(a, i=i, grp=grp):
                    sm.wait(a, 't')
                    if grp == 0 and '_ysbfree' in sm.tag:
                        a.wait_ge(sm.h['v'], sm.tag['_ysbfree'])
                    j = grp % 2
                    ins = a.activation(out=ysb[:, grp * 512:(grp + 1) * 512],
                                       in_=p_y[j][:], func=AF.Copy)
                    sm.tag[f'_pyfree{j}'] = sm.inc(ins, 'a')

            @block.vector
            def _(v, i=i):
                sm.wait(v, 'a')
                sm.wait(v, 'v')
                ins = v.scalar_tensor_tensor(out=ysb[:], in0=lnnat[i][:],
                                             scalar=dv[:], in1=ysb[:],
                                             op0=ALU.mult, op1=ALU.add)
                sm.inc(ins, 'v')

            @block.gpsimd
            def _(g, i=i):
                sm.wait(g, 'v')
                if '_ofree' in sm.tag:
                    g.wait_ge(sm.h['o'], sm.tag['_ofree'])
                gdma(g, sm, 'g', xbuf[:], x_in[i * 128:(i + 1) * 128, :])
            emit_ln_tile(block, sm, ysb, lnbf, scr_bf, stat,
                         v_waits=[('g', sm.c['g']), ('v', sm.c['v'])])
            sm.tag['_ysbfree'] = sm.c['v']

            @block.vector
            def _(v, i=i):
                sm.wait(v, 'g')
                sm.wait(v, 'v')
                ins = v.tensor_tensor(out=xbuf[:], in0=xbuf[:], in1=lnbf[:],
                                      op=ALU.add)
                sm.inc(ins, 'v')

            @block.gpsimd
            def _(g, i=i):
                sm.wait(g, 'v')
                gdma(g, sm, 'o', out[i * 128:(i + 1) * 128, :], xbuf[:])
            sm.tag['_ofree'] = sm.c['o']

        @block.sync
        def _(sy):
            sy.wait_ge(sm.h['o'], sm.c['o'])

    return nc


def s4_consts():
    L = 4096
    ang = np.arange(L, dtype=np.float32) / np.float32(L)
    th = (np.float32(-2.0) * np.float32(np.pi) * ang).astype(np.float32)
    Om = (np.cos(th) + 1j * np.sin(th)).astype(np.complex64)
    c1 = ((1 - Om) / (1 + Om)).astype(np.complex64)
    cc = (2.0 / (1 + Om)).astype(np.complex64)
    i64 = np.arange(64)
    w64 = 2.0 * np.pi * np.outer(i64, i64) / 64.0
    wc = np.cos(w64).astype(np.float32)
    ws = np.sin(w64).astype(np.float32)
    l0m0 = 2.0 * np.pi * np.outer(i64, i64) / float(L)
    return {
        "c1bre": f32c(np.tile(c1.real[None, :], (64, 1))),
        "c1bim": f32c(np.tile(c1.imag[None, :], (64, 1))),
        "c2re": f32c(cc.real.reshape(64, 64)),
        "c2im": f32c(cc.imag.reshape(64, 64)),
        "wc": wc, "ws": ws, "wsn": f32c(-ws),
        "twre": f32c(np.cos(l0m0)), "twim": f32c(np.sin(l0m0)),
        "idf": np.eye(64, dtype=np.float32),
        "onesf": np.ones((1, 128), np.float32),
        "ident": np.eye(128, dtype=NPBF),
    }


def s4_inmap(x2_rows, inp):
    m = s4_consts()
    m.update({
        "x": f32c(x2_rows),
        "lam": f32c(inp['L1_attn_Lambda_ri']),
        "pri": f32c(inp['L1_attn_p_ri']),
        "qri": f32c(inp['L1_attn_q_ri']),
        "vctre": f32c(np.asarray(inp['L1_attn_Vc_ri'])[:, :, 0].T),
        "vctim": f32c(np.asarray(inp['L1_attn_Vc_ri'])[:, :, 1].T),
        "ctt": f32c(np.asarray(inp['L1_attn_Ct_ri']).reshape(64, 2)),
        "bp": f32c(np.asarray(inp['L1_attn_Bp']).reshape(64, 1)),
        "dsc": f32c(np.asarray(inp['L1_attn_D']).reshape(1, 1)),
        "lstep": f32c(np.asarray(inp['L1_attn_log_step']).reshape(1, 1)),
    })
    return m



def build_qkv():
    nc = bass.Bass("TRN2", target_bir_lowering=False)
    x_in = nc.dram_tensor("x", [2, 640, D], F32, kind="ExternalInput")
    wqkvT_in = nc.dram_tensor("wqkvT", [D, 1536], F32, kind="ExternalInput")
    qc1L_in = nc.dram_tensor("qc1L", [8, 3, 128, 128], F32, kind="ExternalInput")
    qc2L_in = nc.dram_tensor("qc2L", [8, 3, 128, 1024], F32, kind="ExternalInput")
    kc1L_in = nc.dram_tensor("kc1L", [2, 3, 128, 128], F32, kind="ExternalInput")
    kc2L_in = nc.dram_tensor("kc2L", [2, 3, 128, 256], F32, kind="ExternalInput")
    qc1b_in = nc.dram_tensor("qc1b", [128, 8], F32, kind="ExternalInput")
    qc2b_in = nc.dram_tensor("qc2b", [128, 8], F32, kind="ExternalInput")
    kc1b_in = nc.dram_tensor("kc1b", [128, 2], F32, kind="ExternalInput")
    kc2b_in = nc.dram_tensor("kc2b", [128, 2], F32, kind="ExternalInput")
    mq_in = nc.dram_tensor("mq", [64, 128], F32, kind="ExternalInput")
    mm16_in = nc.dram_tensor("mm16", [128, 64], F32, kind="ExternalInput")
    mm4_in = nc.dram_tensor("mm4", [128, 64], F32, kind="ExternalInput")
    hq_in = nc.dram_tensor("hq", [128, 8 * 16], F32, kind="ExternalInput")
    hk_in = nc.dram_tensor("hk", [128, 2 * 4], F32, kind="ExternalInput")
    bq_in = nc.dram_tensor("bq", [16, 8 * 128], F32, kind="ExternalInput")
    bk_in = nc.dram_tensor("bk", [4, 2 * 128], F32, kind="ExternalInput")
    pswap_in = nc.dram_tensor("pswap", [128, 128], BF16, kind="ExternalInput")
    ce_in = nc.dram_tensor("ce", [2, 128, 512], BF16, kind="ExternalInput")
    se_in = nc.dram_tensor("se", [2, 128, 512], BF16, kind="ExternalInput")
    kt16_in = nc.dram_tensor("kt16", [16, 1], F32, kind="ExternalInput")
    ident_in = nc.dram_tensor("ident", [128, 128], BF16, kind="ExternalInput")
    nz_in = nc.dram_tensor("nz", [128, 2], F32, kind="ExternalInput")
    qh_out = nc.dram_tensor("qh", [2, 1024, 512], BF16, kind="ExternalOutput")
    kh_out = nc.dram_tensor("kh", [2, 256, 512], BF16, kind="ExternalOutput")
    v_out = nc.dram_tensor("v", [2, 512, 256], BF16, kind="ExternalOutput")

    HC = 640

    lnT = [nc.alloc_sbuf_tensor(f"lnT{d}", [128, HC], BF16) for d in range(32)]
    qkvT = [nc.alloc_sbuf_tensor(f"qkvT{i}", [128, HC], BF16) for i in range(12)]
    q1b = [nc.alloc_sbuf_tensor(f"q1b{i}", [128, HC], BF16) for i in range(8)]
    k1b = [nc.alloc_sbuf_tensor(f"k1b{i}", [128, HC], BF16) for i in range(2)]
    q2b = [nc.alloc_sbuf_tensor(f"q2b{i}", [128, 512], BF16) for i in range(8)]
    k2b = [nc.alloc_sbuf_tensor(f"k2b{i}", [128, 512], BF16) for i in range(2)]
    qsq = [nc.alloc_sbuf_tensor(f"qsq{i}", [128, 512], BF16) for i in range(8)]
    ksq = [nc.alloc_sbuf_tensor(f"ksq{i}", [128, 512], BF16) for i in range(2)]
    swp = nc.alloc_sbuf_tensor("swp", [128, 512], BF16)
    rt1 = nc.alloc_sbuf_tensor("rt1", [128, 512], BF16)
    qrb = nc.alloc_sbuf_tensor("qrb", [128, 512], BF16)
    ob = [nc.alloc_sbuf_tensor(f"ob{i}", [128, 512], BF16) for i in range(2)]
    wq = nc.alloc_sbuf_tensor("wq", [128, 32 * 128], BF16)
    c1w = nc.alloc_sbuf_tensor("c1w", [128, 3 * 128], BF16)
    c2wq = nc.alloc_sbuf_tensor("c2wq", [128, 24 * 128], BF16)
    c2wk = nc.alloc_sbuf_tensor("c2wk", [128, 6 * 128], BF16)
    qc1bb = nc.alloc_sbuf_tensor("qc1bb", [128, 8], F32)
    qc2bb = nc.alloc_sbuf_tensor("qc2bb", [128, 8], F32)
    kc1bb = nc.alloc_sbuf_tensor("kc1bb", [128, 2], F32)
    kc2bb = nc.alloc_sbuf_tensor("kc2bb", [128, 2], F32)
    mqb = nc.alloc_sbuf_tensor("mqb", [64, 128], BF16)
    mm16b = nc.alloc_sbuf_tensor("mm16b", [128, 64], BF16)
    mm4b = nc.alloc_sbuf_tensor("mm4b", [128, 64], BF16)
    hqb = nc.alloc_sbuf_tensor("hqb", [128, 8 * 16], BF16)
    hkb = nc.alloc_sbuf_tensor("hkb", [128, 2 * 4], BF16)
    bqb = nc.alloc_sbuf_tensor("bqb", [16, 8 * 128], BF16)
    bkb = nc.alloc_sbuf_tensor("bkb", [4, 2 * 128], BF16)
    pswapb = nc.alloc_sbuf_tensor("pswapb", [128, 128], BF16)
    ceb = nc.alloc_sbuf_tensor("ceb", [128, 512], BF16)
    seb = nc.alloc_sbuf_tensor("seb", [128, 512], BF16)
    rn16 = nc.alloc_sbuf_tensor("rn16", [16, 512], F32)
    rn16b = nc.alloc_sbuf_tensor("rn16b", [16, 512], BF16)
    rn4 = nc.alloc_sbuf_tensor("rn4", [16, 512], F32)
    rn4b = nc.alloc_sbuf_tensor("rn4b", [16, 512], BF16)
    ktv = nc.alloc_sbuf_tensor("ktv", [16, 1], F32)
    qmb = nc.alloc_sbuf_tensor("qmb", [64, 512], BF16)
    kmb = nc.alloc_sbuf_tensor("kmb", [64, 512], BF16)
    xbuf = nc.alloc_sbuf_tensor("xb", [128, D], F32)
    lnbf = nc.alloc_sbuf_tensor("lnbf", [128, D], BF16)
    scr_bf = nc.alloc_sbuf_tensor("scr_bf", [128, D], BF16)
    stat = nc.alloc_sbuf_tensor("stat", [128, 12], F32)
    identbf = nc.alloc_sbuf_tensor("identbf", [128, 128], BF16)
    nzb = nc.alloc_sbuf_tensor("nzb", [128, 2], F32)

    p_tr = [nc.alloc_psum_tensor(f"ptr{i}", [128, 128], BF16) for i in range(2)]
    p_a = nc.alloc_psum_tensor("pa", [128, 512], F32)
    p_b = nc.alloc_psum_tensor("pb", [128, 512], F32)
    p_c = nc.alloc_psum_tensor("pc", [128, 512], F32)
    p_m = nc.alloc_psum_tensor("pm", [64, 512], F32)
    p_h = nc.alloc_psum_tensor("ph", [16, 512], F32)

    sm = Sems(nc, ['g', 'v', 'a', 't', 'o', 'w'])

    with nc.Block() as block:
        @block.gpsimd
        def _(g):
            for dst, src_ in [(qc1bb, qc1b_in), (qc2bb, qc2b_in),
                              (kc1bb, kc1b_in), (kc2bb, kc2b_in),
                              (mqb, mq_in), (mm16b, mm16_in), (mm4b, mm4_in),
                              (hqb, hq_in), (hkb, hk_in), (bqb, bq_in),
                              (bkb, bk_in), (pswapb, pswap_in),
                              (identbf, ident_in), (nzb, nz_in),
                              (ktv, kt16_in)]:
                gdma(g, sm, 'g', dst[:], src_[:])

        for seg in range(2):
            @block.gpsimd
            def _(g, seg=seg):
                if seg > 0:
                    g.wait_ge(sm.h['v'], sm.tag['_rope_done'])
                gdma(g, sm, 'g', ceb[:], ce_in[seg])
                gdma(g, sm, 'g', seb[:], se_in[seg])

            for i in range(5):
                @block.gpsimd
                def _(g, seg=seg, i=i):
                    if '_lnfree' in sm.tag:
                        g.wait_ge(sm.h['v'], sm.tag['_lnfree'])
                    gdma(g, sm, 'g', xbuf[:], x_in[seg, i * 128:(i + 1) * 128, :])
                emit_ln_tile(block, sm, xbuf, lnbf, scr_bf, stat,
                             v_waits=[('g', sm.c['g']), ('t', sm.c['t'])])
                sm.tag['_lnfree'] = sm.c['v']
                emit_transpose_tiles(block, sm, lnbf, lnT, i * 128, identbf,
                                     p_tr)

            for ct in range(12):
                @block.gpsimd
                def _(g, ct=ct):
                    if '_wqfree' in sm.tag:
                        g.wait_ge(sm.h['t'], sm.tag['_wqfree'])
                    gdma(g, sm, 'w', bass.AP(wq, 0, [[32 * 128, 128], [128, 32], [1, 128]]), bass.AP(wqkvT_in, ct * 128,
                                    [[1536, 128], [128 * 1536, 32], [1, 128]]),)
                w_c = sm.c['w']

                for ch, (c0, cn) in enumerate([(0, 512), (512, 128)]):
                    @block.tensor
                    def _(t, ct=ct, ch=ch, c0=c0, cn=cn):
                        t.wait_ge(sm.h['w'], w_c)
                        if ct == 0 and ch == 0:
                            sm.wait(t, 'a')
                            sm.wait(t, 'v')
                        pj = (ct * 2 + ch) % 2
                        pp = [p_a, p_b][pj]
                        if f'_qkvfree{pj}' in sm.tag:
                            t.wait_ge(sm.h['a'], sm.tag[f'_qkvfree{pj}'])
                        for d in range(32):
                            ins = t.matmul(pp[:, 0:cn],
                                           wq[:, d * 128:(d + 1) * 128],
                                           lnT[d][:, c0:c0 + cn],
                                           start=(d == 0), stop=(d == 31))
                        sm.inc(ins, 't')

                    @block.scalar
                    def _(a, ct=ct, ch=ch, c0=c0, cn=cn):
                        sm.wait(a, 't')
                        pj = (ct * 2 + ch) % 2
                        pp = [p_a, p_b][pj]
                        ins = a.activation(out=qkvT[ct][:, c0:c0 + cn],
                                           in_=pp[:, 0:cn], func=AF.Copy)
                        sm.tag[f'_qkvfree{pj}'] = sm.inc(ins, 'a')
                sm.tag['_wqfree'] = sm.c['t']

            @block.tensor
            def _(t, seg=seg):
                sm.wait(t, 'a')
                for it in range(8):
                    t.matmul(p_m[:], mm16b[:], qkvT[it][:, 128:640],
                             start=(it == 0), stop=(it == 7))
                for it in range(2):
                    ins = t.matmul(p_c[0:64, :], mm4b[:],
                                   qkvT[8 + it][:, 128:640],
                                   start=(it == 0), stop=(it == 1))
                sm.inc(ins, 't')

            @block.scalar
            def _(a, seg=seg):
                sm.wait(a, 't')
                a.activation(out=qmb[:], in_=p_m[:], func=AF.Copy)
                ins = a.activation(out=kmb[:], in_=p_c[0:64, :], func=AF.Copy)
                sm.inc(ins, 'a')

            for (ntile, dstl, wsrc, bsrc, ibase) in [
                    (8, q1b, qc1L_in, qc1bb, 0), (2, k1b, kc1L_in, kc1bb, 8)]:
                for ot in range(ntile):
                    @block.gpsimd
                    def _(g, ot=ot, wsrc=wsrc):
                        if '_c1free' in sm.tag:
                            g.wait_ge(sm.h['t'], sm.tag['_c1free'])
                        gdma(g, sm, 'w', bass.AP(c1w, 0, [[3 * 128, 128], [128, 3],
                                                 [1, 128]]), bass.AP(wsrc, ot * 3 * 128 * 128,
                                        [[128, 128], [128 * 128, 3], [1, 128]]),)
                    w_c1 = sm.c['w']

                    for ch, (c0, cn) in enumerate([(124, 512), (636, 4)]):
                        @block.tensor
                        def _(t, ot=ot, ch=ch, c0=c0, cn=cn, ibase=ibase):
                            t.wait_ge(sm.h['w'], w_c1)
                            if '_c1start' not in sm.tag:
                                sm.wait(t, 'a')
                                sm.tag['_c1start'] = 1
                            pj = ch
                            pp = [p_a, p_b][pj]
                            if f'_c1f{pj}' in sm.tag:
                                t.wait_ge(sm.h['v'], sm.tag[f'_c1f{pj}'])
                            for tau in range(3):
                                ins = t.matmul(
                                    pp[:, 0:cn],
                                    c1w[:, tau * 128:(tau + 1) * 128],
                                    qkvT[ibase + ot][:, c0 - 2 + tau:
                                                     c0 - 2 + tau + cn],
                                    start=(tau == 0), stop=(tau == 2))
                            sm.inc(ins, 't')

                        @block.vector
                        def _(v, ot=ot, ch=ch, c0=c0, cn=cn, dstl=dstl,
                              bsrc=bsrc):
                            sm.wait(v, 't')
                            pj = ch
                            pp = [p_a, p_b][pj]
                            ins = v.tensor_scalar(
                                out=dstl[ot][:, c0:c0 + cn], in0=pp[:, 0:cn],
                                scalar1=1.0, scalar2=bsrc[:, ot:ot + 1],
                                op0=ALU.mult, op1=ALU.add)
                            sm.tag[f'_c1f{pj}'] = sm.inc(ins, 'v')
                    sm.tag['_c1free'] = sm.c['t']
            sm.tag.pop('_c1start', None)

            @block.vector
            def _(v, seg=seg):
                sm.wait(v, 'v')
                for dstl, ntile in [(q1b, 8), (k1b, 2)]:
                    for ot in range(ntile):
                        ins = v.tensor_scalar(
                            out=dstl[ot][:, 124:128],
                            in0=dstl[ot][:, 124:128],
                            scalar1=nzb[:, seg:seg + 1], scalar2=None,
                            op0=ALU.mult)
                sm.inc(ins, 'v')

            for ot in range(8):
                @block.gpsimd
                def _(g, ot=ot):
                    if '_c2free' in sm.tag:
                        g.wait_ge(sm.h['t'], sm.tag['_c2free'])
                    gdma(g, sm, 'w', bass.AP(c2wq, 0, [[24 * 128, 128], [128, 24],
                                              [1, 128]]), bass.AP(qc2L_in, ot * 128,
                                    [[1024, 128], [128 * 1024, 24], [1, 128]]),)
                w_c2 = sm.c['w']

                @block.tensor
                def _(t, ot=ot):
                    t.wait_ge(sm.h['w'], w_c2)
                    sm.wait(t, 'v')
                    if '_q2f' in sm.tag:
                        t.wait_ge(sm.h['a'], sm.tag['_q2f'])
                    if '_q2fv' in sm.tag:
                        t.wait_ge(sm.h['v'], sm.tag['_q2fv'])
                    for it in range(8):
                        for tau in range(3):
                            t.matmul(p_c[:],
                                     c2wq[:, (it * 3 + tau) * 128:
                                          (it * 3 + tau + 1) * 128],
                                     q1b[it][:, 126 + tau:126 + tau + 512],
                                     start=(it == 0 and tau == 0), stop=False)
                    ins = t.matmul(p_c[:], mqb[:], kmb[:], start=False,
                                   stop=True)
                    sm.inc(ins, 't')
                sm.tag['_c2free'] = sm.c['t']

                @block.scalar
                def _(a, ot=ot):
                    sm.wait(a, 't')
                    ins = a.activation(out=qsq[ot][:], in_=p_c[:],
                                       func=AF.Square,
                                       bias=qc2bb[:, ot:ot + 1])
                    sm.tag['_q2f'] = sm.inc(ins, 'a')

                @block.vector
                def _(v, ot=ot):
                    sm.wait(v, 'a')
                    ins = v.tensor_scalar(out=q2b[ot][:], in0=p_c[:],
                                          scalar1=1.0,
                                          scalar2=qc2bb[:, ot:ot + 1],
                                          op0=ALU.mult, op1=ALU.add)
                    sm.tag['_q2fv'] = sm.inc(ins, 'v')

            for ot in range(2):
                @block.gpsimd
                def _(g, ot=ot):
                    if '_c2free' in sm.tag:
                        g.wait_ge(sm.h['t'], sm.tag['_c2free'])
                    gdma(g, sm, 'w', bass.AP(c2wk, 0, [[6 * 128, 128], [128, 6],
                                              [1, 128]]), bass.AP(kc2L_in, ot * 128,
                                    [[256, 128], [128 * 256, 6], [1, 128]]),)
                w_k2 = sm.c['w']

                @block.tensor
                def _(t, ot=ot):
                    t.wait_ge(sm.h['w'], w_k2)
                    if '_q2f' in sm.tag:
                        t.wait_ge(sm.h['a'], sm.tag['_q2f'])
                    if '_q2fv' in sm.tag:
                        t.wait_ge(sm.h['v'], sm.tag['_q2fv'])
                    for it in range(2):
                        for tau in range(3):
                            t.matmul(p_c[:],
                                     c2wk[:, (it * 3 + tau) * 128:
                                          (it * 3 + tau + 1) * 128],
                                     k1b[it][:, 126 + tau:126 + tau + 512],
                                     start=(it == 0 and tau == 0), stop=False)
                    ins = t.matmul(p_c[:], mqb[:], qmb[:], start=False,
                                   stop=True)
                    sm.inc(ins, 't')
                sm.tag['_c2free'] = sm.c['t']

                @block.scalar
                def _(a, ot=ot):
                    sm.wait(a, 't')
                    ins = a.activation(out=ksq[ot][:], in_=p_c[:],
                                       func=AF.Square,
                                       bias=kc2bb[:, ot:ot + 1])
                    sm.tag['_q2f'] = sm.inc(ins, 'a')

                @block.vector
                def _(v, ot=ot):
                    sm.wait(v, 'a')
                    ins = v.tensor_scalar(out=k2b[ot][:], in0=p_c[:],
                                          scalar1=1.0,
                                          scalar2=kc2bb[:, ot:ot + 1],
                                          op0=ALU.mult, op1=ALU.add)
                    sm.tag['_q2fv'] = sm.inc(ins, 'v')

            @block.tensor
            def _(t, seg=seg):
                sm.wait(t, 'a')
                sm.wait(t, 'v')
                for ot in range(8):
                    t.matmul(p_h[:], hqb[:, ot * 16:(ot + 1) * 16], qsq[ot][:],
                             start=(ot == 0), stop=(ot == 7))
                t.matmul(p_m[0:4, :], hkb[:, 0:4], ksq[0][:], start=True,
                         stop=False)
                ins = t.matmul(p_m[0:4, :], hkb[:, 4:8], ksq[1][:],
                               start=False, stop=True)
                sm.inc(ins, 't')

            @block.vector
            def _(v, seg=seg):
                sm.wait(v, 't')
                v.tensor_scalar(out=rn16[:], in0=p_h[:], scalar1=1.0,
                                scalar2=1e-24, op0=ALU.mult, op1=ALU.max)
                ins = v.tensor_scalar(out=rn4[0:4, :], in0=p_m[0:4, :],
                                      scalar1=1.0, scalar2=1e-24,
                                      op0=ALU.mult, op1=ALU.max)
                sm.inc(ins, 'v')

            @block.scalar
            def _(a, seg=seg):
                sm.wait(a, 'v')
                a.activation(out=rn16[:], in_=rn16[:], func=AF.Ln)
                ins = a.activation(out=rn4[0:4, :], in_=rn4[0:4, :],
                                   func=AF.Ln)
                sm.inc(ins, 'a')
                sm.wait(a, 'a')
                a.activation(out=rn16[:], in_=rn16[:], func=AF.Exp, scale=-0.5)
                ins = a.activation(out=rn4[0:4, :], in_=rn4[0:4, :],
                                   func=AF.Exp, scale=-0.5)
                sm.inc(ins, 'a')

            @block.vector
            def _(v, seg=seg):
                sm.wait(v, 'a')
                v.tensor_copy(out=rn16b[:], in_=rn16[:])
                ins = v.tensor_scalar(out=rn4[0:4, :], in0=rn4[0:4, :],
                                      scalar1=ktv[0:4, :], scalar2=None,
                                      op0=ALU.mult)
                sm.inc(ins, 'v')
                sm.wait(v, 'v')
                ins = v.tensor_copy(out=rn4b[0:4, :], in_=rn4[0:4, :])
                sm.inc(ins, 'v')

            for (ntile, srcl, bselt, rnt, outdr, nout) in [
                    (8, q2b, bqb, rn16b, qh_out, 16),
                    (2, k2b, bkb, rn4b, kh_out, 4)]:
                for ot in range(ntile):
                    @block.tensor
                    def _(t, ot=ot, srcl=srcl):
                        sm.wait(t, 'v')
                        if '_swf' in sm.tag:
                            t.wait_ge(sm.h['a'], sm.tag['_swf'])
                        ins = t.matmul(p_a[:], pswapb[:], srcl[ot][:],
                                       start=True, stop=True)
                        sm.inc(ins, 't')

                    @block.scalar
                    def _(a, ot=ot):
                        sm.wait(a, 't')
                        if '_ropev' in sm.tag:
                            a.wait_ge(sm.h['v'], sm.tag['_ropev'])
                        ins = a.activation(out=swp[:], in_=p_a[:], func=AF.Copy)
                        sm.tag['_swf'] = sm.inc(ins, 'a')

                    @block.tensor
                    def _(t, ot=ot, bselt=bselt, rnt=rnt, nout=nout):
                        if '_rnbf' in sm.tag:
                            t.wait_ge(sm.h['v'], sm.tag['_rnbf'])
                        ins = t.matmul(p_b[:], bselt[0:nout, ot * 128:
                                                     (ot + 1) * 128],
                                       rnt[0:nout, :], start=True, stop=True)
                        sm.inc(ins, 't')

                    @block.vector
                    def _(v, ot=ot, srcl=srcl):
                        sm.wait(v, 'a')
                        sm.wait(v, 't')
                        oj = ot % 2
                        if f'_obf{oj}' in sm.tag:
                            v.wait_ge(sm.h['o'], sm.tag[f'_obf{oj}'])
                        v.tensor_tensor(out=rt1[:], in0=srcl[ot][:],
                                        in1=ceb[:], op=ALU.mult)
                        ins = v.tensor_tensor(out=qrb[:], in0=swp[:],
                                              in1=seb[:], op=ALU.mult)
                        sm.inc(ins, 'v')
                        sm.wait(v, 'v')
                        ins = v.tensor_tensor(out=qrb[:], in0=rt1[:],
                                              in1=qrb[:], op=ALU.add)
                        sm.inc(ins, 'v')
                        sm.wait(v, 'v')
                        sm.tag['_ropev'] = sm.inc(
                            v.tensor_tensor(out=ob[oj][:], in0=qrb[:],
                                            in1=p_b[:], op=ALU.mult), 'v')
                        sm.tag['_rnbf'] = sm.tag['_ropev']
                    @block.gpsimd
                    def _(g, ot=ot, outdr=outdr, seg=seg):
                        sm.wait(g, 'v')
                        oj = ot % 2
                        gdma(g, sm, 'o', outdr[seg, ot * 128:(ot + 1) * 128, :], ob[oj][:])
                        sm.tag[f'_obf{oj}'] = sm.c['o']

            for st in range(4):
                for half in range(2):
                    @block.tensor
                    def _(t, st=st, half=half):
                        j = (st * 2 + half) % 2
                        key = f'_trw{j}'
                        if key in sm.tag:
                            t.wait_ge(sm.h['a'], sm.tag[key])
                        if st == 0 and half == 0:
                            sm.wait(t, 'a')
                        src_col = 128 + 128 * st - (1 if half == 1 else 0)
                        ins = t.transpose(p_tr[j][:],
                                          qkvT[10 + half][:, src_col:
                                                          src_col + 128],
                                          identbf[:])
                        sm.inc(ins, 't')

                    @block.scalar
                    def _(a, st=st, half=half):
                        sm.wait(a, 't')
                        j = (st * 2 + half) % 2
                        oj = 1 - (st % 2)
                        if f'_obf{oj}' in sm.tag:
                            a.wait_ge(sm.h['o'], sm.tag[f'_obf{oj}'])
                        ins = a.activation(
                            out=ob[oj][:, half * 128:(half + 1) * 128],
                            in_=p_tr[j][:], func=AF.Copy)
                        sm.tag[f'_trw{j}'] = sm.inc(ins, 'a')

                @block.gpsimd
                def _(g, st=st, seg=seg):
                    sm.wait(g, 'a')
                    oj = 1 - (st % 2)
                    gdma(g, sm, 'o', v_out[seg, st * 128:(st + 1) * 128, :], ob[oj][:, 0:256])
                    sm.tag[f'_obf{1 - (st % 2)}'] = sm.c['o']
            sm.tag['_rope_done'] = sm.c['v']

        @block.sync
        def _(sy):
            sy.wait_ge(sm.h['o'], sm.c['o'])

    return nc


def qkv_consts():
    i64h = np.arange(64)
    mq = np.zeros((64, 128), np.float32)
    for c in range(128):
        mq[c % 64, c] = 0.5
    mm16 = np.tile(np.eye(64, dtype=np.float32) / 16.0, (2, 1))
    mm4 = np.tile(np.eye(64, dtype=np.float32) / 4.0, (2, 1))
    hq = np.zeros((128, 8 * 16), np.float32)
    for ot in range(8):
        for i in range(128):
            hq[i, ot * 16 + (ot * 128 + i) // 64] = 1.0
    hk = np.zeros((128, 2 * 4), np.float32)
    for ot in range(2):
        for i in range(128):
            hk[i, ot * 4 + (ot * 128 + i) // 64] = 1.0
    bq = np.zeros((16, 8 * 128), np.float32)
    for ot in range(8):
        for c in range(128):
            bq[(ot * 128 + c) // 64, ot * 128 + c] = 1.0
    bk = np.zeros((4, 2 * 128), np.float32)
    for ot in range(2):
        for c in range(128):
            bk[(ot * 128 + c) // 64, ot * 128 + c] = 1.0
    psw = np.zeros((128, 128), np.float32)
    for i in range(128):
        psw[i ^ 1, i] = 1.0
    return mq, mm16, mm4, hq, hk, bq, bk, psw


def rope_tables(blocks):
    inv = (1.0 / 10000.0 ** (np.arange(0, 64, 2, dtype=np.float32) / 64.0)
           ).astype(np.float32)
    ce = np.zeros((2, 128, 512), np.float32)
    se = np.zeros((2, 128, 512), np.float32)
    for s, b in enumerate(blocks):
        pos = np.arange(b * BLK, (b + 1) * BLK, dtype=np.float32)
        ang = pos[None, :] * inv[:, None]
        c = np.cos(ang)
        sn = np.sin(ang)
        for p in range(128):
            i = (p % 64) // 2
            ce[s, p] = c[i]
            se[s, p] = -sn[i] if (p % 2 == 0) else sn[i]
    return ce, se


def qkv_inmap(core, x_full, inp):
    bs = core_blocks(core)
    xh = np.zeros((2, 640, D), np.float32)
    for s, b in enumerate(bs):
        lo = b * BLK - 128
        seg = x_full[max(lo, 0):(b + 1) * BLK]
        xh[s, 640 - seg.shape[0]:] = seg
    qc1 = np.asarray(inp['L0_attn_qc1_w'], np.float32)
    qc2 = np.asarray(inp['L0_attn_qc2_w'], np.float32)
    kc1 = np.asarray(inp['L0_attn_kc1_w'], np.float32)
    kc2 = np.asarray(inp['L0_attn_kc2_w'], np.float32)
    qc1L = np.zeros((8, 3, 128, 128), np.float32)
    for ot in range(8):
        for c in range(128):
            o = ot * 128 + c
            blk = c // 64
            qc1L[ot, :, blk * 64:blk * 64 + 64, c] = qc1[o].T
    kc1L = np.zeros((2, 3, 128, 128), np.float32)
    for ot in range(2):
        for c in range(128):
            o = ot * 128 + c
            blk = c // 64
            kc1L[ot, :, blk * 64:blk * 64 + 64, c] = kc1[o].T
    qc2L = np.ascontiguousarray(
        np.transpose(qc2.reshape(1024, 8, 128, 3), (1, 3, 2, 0)))
    kc2L = np.ascontiguousarray(
        np.transpose(kc2.reshape(256, 2, 128, 3), (1, 3, 2, 0)))
    mq, mm16, mm4, hq, hk, bq, bk, psw = qkv_consts()
    ce, se = rope_tables(bs)
    nz = np.ones((128, 2), np.float32)
    for s, b in enumerate(bs):
        if b == 0:
            nz[:, s] = 0.0
    return {
        "x": xh,
        "wqkvT": f32c(np.asarray(inp['L0_attn_w_qkv']).T),
        "qc1L": qc1L, "qc2L": qc2L, "kc1L": kc1L, "kc2L": kc2L,
        "qc1b": f32c(np.tile(np.asarray(inp['L0_attn_qc1_b']).reshape(8, 128).T, (1, 1))),
        "qc2b": f32c(np.asarray(inp['L0_attn_qc2_b']).reshape(8, 128).T),
        "kc1b": f32c(np.asarray(inp['L0_attn_kc1_b']).reshape(2, 128).T),
        "kc2b": f32c(np.asarray(inp['L0_attn_kc2_b']).reshape(2, 128).T),
        "mq": mq, "mm16": mm16, "mm4": mm4,
        "hq": hq, "hk": hk, "bq": bq, "bk": bk,
        "pswap": psw.astype(NPBF),
        "ce": ce.astype(NPBF), "se": se.astype(NPBF),
        "kt16": f32c(np.full((16, 1),
                             np.asarray(inp['L0_attn_key_temp']).ravel()[0])),
        "ident": np.eye(128, dtype=NPBF),
        "nz": nz,
    }



def build_attn():
    nc = bass.Bass("TRN2", target_bir_lowering=False)
    qh_in = nc.dram_tensor("qh2", [128, S], BF16, kind="ExternalInput")
    kh_in = nc.dram_tensor("kh1", [64, S], BF16, kind="ExternalInput")
    va_in = nc.dram_tensor("va", [S // 128, 128, 65], BF16, kind="ExternalInput")
    kt_in = nc.dram_tensor("kt128", [128, 1], F32, kind="ExternalInput")
    tri_in = nc.dram_tensor("tri", [128, 128], BF16, kind="ExternalInput")
    ones1_in = nc.dram_tensor("ones1", [1, 128], BF16, kind="ExternalInput")
    ao_out = nc.dram_tensor("ao", [2, 64, S], BF16, kind="ExternalOutput")

    NTL = S // 128
    qha = [nc.alloc_sbuf_tensor(f"qha{i}", [64, S], BF16) for i in range(2)]
    kh1 = nc.alloc_sbuf_tensor("kh1s", [64, S], BF16)
    va = nc.alloc_sbuf_tensor("vas", [128, NTL * 65], BF16)
    pb = [nc.alloc_sbuf_tensor(f"pbuf{i}", [128, 512], BF16) for i in range(2)]
    trib = nc.alloc_sbuf_tensor("trib", [128, 128], BF16)
    aob = [nc.alloc_sbuf_tensor(f"aob{i}", [64, 512], BF16) for i in range(2)]
    lsb = nc.alloc_sbuf_tensor("lsb", [128, 512], F32)
    lbf = nc.alloc_sbuf_tensor("lbf", [128, 512], BF16)
    rls = nc.alloc_sbuf_tensor("rls", [64, 512], F32)
    ktb = nc.alloc_sbuf_tensor("ktb", [128, 1], F32)
    mb = nc.alloc_sbuf_tensor("mb", [128, 1], F32)
    onesbf = nc.alloc_sbuf_tensor("onesbf", [1, 128], BF16)

    p_s = [nc.alloc_psum_tensor(f"ps{i}", [128, 512], F32) for i in range(2)]
    p_o = nc.alloc_psum_tensor("po", [65, 512], F32)
    p_rl = nc.alloc_psum_tensor("prl", [64, 512], F32)

    sm = Sems(nc, ['g', 'v', 'a', 't', 'o'])

    with nc.Block() as block:
        @block.gpsimd
        def _(g):
            gdma(g, sm, 'g', qha[0][:], qh_in[0:64, :])
            gdma(g, sm, 'g', qha[1][:], qh_in[64:128, :])
            gdma(g, sm, 'g', kh1[:], kh_in[:])
            gdma(g, sm, 'g', bass.AP(va, 0, [[NTL * 65, 128], [65, NTL], [1, 65]]), bass.AP(va_in, 0, [[65, 128], [128 * 65, NTL], [1, 65]]))
            gdma(g, sm, 'g', trib[:], tri_in[:])
            gdma(g, sm, 'g', ktb[:], kt_in[:])
            gdma(g, sm, 'g', onesbf[:], ones1_in[:])

        @block.scalar
        def _(a):
            sm.wait(a, 'g')
            ins = a.activation(out=mb[:], in_=ktb[:], func=AF.Abs)
            sm.inc(ins, 'a')

        @block.vector
        def _(v):
            sm.wait(v, 'a')
            ins = v.tensor_scalar(out=mb[:], in0=mb[:], scalar1=-0.1275,
                                  scalar2=None, op0=ALU.mult)
            sm.inc(ins, 'v')

        for qh in range(2):
            for cb in range(16):
                ntl = 4 * cb + 4
                for c in range(ntl):
                    r = c - 4 * cb
                    span0 = max(r, 0) * 128
                    pj = c % 2

                    @block.tensor
                    def _(t, qh=qh, cb=cb, c=c, pj=pj):
                        if qh == 0 and cb == 0 and c == 0:
                            sm.wait(t, 'v')
                        if f'_expf{pj}' in sm.tag:
                            t.wait_ge(sm.h['a'], sm.tag[f'_expf{pj}'])
                        ins = t.matmul(p_s[pj][:],
                                       kh1[:, c * 128:(c + 1) * 128],
                                       qha[qh][:, cb * 512:(cb + 1) * 512],
                                       start=True, stop=True)
                        sm.inc(ins, 't')

                    @block.scalar
                    def _(a, qh=qh, cb=cb, c=c, pj=pj, span0=span0):
                        sm.wait(a, 't')
                        if f'_avf{pj}' in sm.tag:
                            a.wait_ge(sm.h['t'], sm.tag[f'_avf{pj}'])
                        ins = a.activation(out=pb[pj][:, span0:512],
                                           in_=p_s[pj][:, span0:512],
                                           func=AF.Exp, scale=0.125,
                                           bias=mb[:])
                        sm.tag[f'_expf{pj}'] = sm.inc(ins, 'a')

                    if r >= 0:
                        @block.vector
                        def _(v, c=c, pj=pj, span0=span0):
                            sm.wait(v, 'a')
                            ins = v.tensor_tensor(
                                out=pb[pj][:, span0:span0 + 128],
                                in0=pb[pj][:, span0:span0 + 128],
                                in1=trib[:], op=ALU.mult)
                            sm.inc(ins, 'v')

                    @block.tensor
                    def _(t, qh=qh, cb=cb, c=c, pj=pj, span0=span0, r=r,
                          ntl=ntl):
                        if r >= 0:
                            sm.wait(t, 'v')
                        else:
                            sm.wait(t, 'a')
                        if c == 0 and '_pof' in sm.tag:
                            t.wait_ge(sm.h['v'], sm.tag['_pof'])
                        ins = t.matmul(p_o[:, span0:512],
                                       va[:, c * 65:(c + 1) * 65],
                                       pb[pj][:, span0:512],
                                       start=(c == 0), stop=(c == ntl - 1))
                        sm.tag[f'_avf{pj}'] = sm.inc(ins, 't')

                @block.scalar
                def _(a, qh=qh, cb=cb):
                    sm.wait(a, 't')
                    ins = a.activation(out=lsb[64:65, :], in_=p_o[64:65, :],
                                       func=AF.Copy)
                    sm.inc(ins, 'a')

                @block.vector
                def _(v, qh=qh, cb=cb):
                    sm.wait(v, 'a')
                    ins = v.reciprocal(out=lsb[0:1, :], in_=lsb[64:65, :])
                    sm.inc(ins, 'v')
                    sm.wait(v, 'v')
                    ins = v.tensor_copy(out=lbf[0:1, :], in_=lsb[0:1, :])
                    sm.inc(ins, 'v')

                @block.tensor
                def _(t, qh=qh, cb=cb):
                    sm.wait(t, 'v')
                    if '_rlf' in sm.tag:
                        t.wait_ge(sm.h['a'], sm.tag['_rlf'])
                    ins = t.matmul(p_rl[:], onesbf[:, 0:64], lbf[0:1, :],
                                   start=True, stop=True)
                    sm.inc(ins, 't')

                @block.scalar
                def _(a, qh=qh, cb=cb):
                    sm.wait(a, 't')
                    ins = a.activation(out=rls[:], in_=p_rl[:], func=AF.Copy)
                    sm.tag['_rlf'] = sm.inc(ins, 'a')

                @block.vector
                def _(v, qh=qh, cb=cb):
                    sm.wait(v, 'a')
                    oj = cb % 2
                    if f'_aodma{oj}' in sm.tag:
                        v.wait_ge(sm.h['o'], sm.tag[f'_aodma{oj}'])
                    ins = v.tensor_tensor(out=aob[oj][:], in0=p_o[0:64, :],
                                          in1=rls[:], op=ALU.mult)
                    sm.tag['_pof'] = sm.inc(ins, 'v')

                @block.gpsimd
                def _(g, qh=qh, cb=cb):
                    sm.wait(g, 'v')
                    oj = cb % 2
                    gdma(g, sm, 'o', ao_out[qh, :, cb * 512:(cb + 1) * 512], aob[oj][:])
                    sm.tag[f'_aodma{oj}'] = sm.c['o']
                sm.tag['_po_guard'] = sm.c['v']

        @block.sync
        def _(sy):
            sy.wait_ge(sm.h['o'], sm.c['o'])

    return nc


def attn_inmap(core, qf_T, kf_T, v_f, inp):
    kv = core // 2
    va = np.zeros((S // 128, 128, 65), np.float32)
    va[:, :, 0:64] = v_f[:, kv * 64:(kv + 1) * 64].reshape(S // 128, 128, 64)
    va[:, :, 64] = 1.0
    kt = np.asarray(inp['L0_attn_key_temp']).ravel()[0]
    tri = np.triu(np.ones((128, 128), np.float32))
    return {
        "qh2": qf_T[core * 128:(core + 1) * 128, :].astype(NPBF),
        "kh1": kf_T[kv * 64:(kv + 1) * 64, :].astype(NPBF),
        "va": va.astype(NPBF),
        "kt128": f32c(np.full((128, 1), kt)),
        "tri": tri.astype(NPBF),
        "ones1": np.ones((1, 128), dtype=NPBF),
    }



def build_wo():
    nc = bass.Bass("TRN2", target_bir_lowering=False)
    ao_in = nc.dram_tensor("aoc", [1024, ROWS], BF16, kind="ExternalInput")
    woT_in = nc.dram_tensor("woT", [1024, D], F32, kind="ExternalInput")
    x_in = nc.dram_tensor("x", [ROWS, D], F32, kind="ExternalInput")
    out = nc.dram_tensor("out", [ROWS, D], F32, kind="ExternalOutput")

    aoc = [nc.alloc_sbuf_tensor(f"aoc{i}", [128, ROWS], BF16) for i in range(8)]
    ybf = [nc.alloc_sbuf_tensor(f"y{i}", [128, D], BF16) for i in range(NT)]
    wob = nc.alloc_sbuf_tensor("wob", [128, 8 * 512], BF16)
    xbuf = nc.alloc_sbuf_tensor("xb", [128, D], F32)
    lnbf = nc.alloc_sbuf_tensor("lnbf", [128, D], BF16)
    scr_bf = nc.alloc_sbuf_tensor("scr_bf", [128, D], BF16)
    stat = nc.alloc_sbuf_tensor("stat", [128, 12], F32)
    p_y = [nc.alloc_psum_tensor(f"py{i}", [128, 512], F32) for i in range(2)]

    sm = Sems(nc, ['g', 'v', 'a', 't', 'o', 'w'])

    with nc.Block() as block:
        @block.gpsimd
        def _(g):
            for i in range(8):
                gdma(g, sm, 'g', aoc[i][:], ao_in[i * 128:(i + 1) * 128, :])

        for oc in range(8):
            @block.gpsimd
            def _(g, oc=oc):
                if '_wfree' in sm.tag:
                    g.wait_ge(sm.h['t'], sm.tag['_wfree'])
                gdma(g, sm, 'w', bass.AP(wob, 0, [[8 * 512, 128], [512, 8], [1, 512]]), bass.AP(woT_in, oc * 512,
                                [[D, 128], [128 * D, 8], [1, 512]]),)
            w_c = sm.c['w']

            for i in range(NT):
                @block.tensor
                def _(t, oc=oc, i=i):
                    t.wait_ge(sm.h['w'], w_c)
                    if oc == 0 and i == 0:
                        sm.wait(t, 'g')
                    pj = (oc * NT + i) % 2
                    if f'_yfree{pj}' in sm.tag:
                        t.wait_ge(sm.h['a'], sm.tag[f'_yfree{pj}'])
                    for d in range(8):
                        ins = t.matmul(p_y[pj][:],
                                       aoc[d][:, i * 128:(i + 1) * 128],
                                       wob[:, d * 512:(d + 1) * 512],
                                       start=(d == 0), stop=(d == 7))
                    sm.inc(ins, 't')

                @block.scalar
                def _(a, oc=oc, i=i):
                    sm.wait(a, 't')
                    pj = (oc * NT + i) % 2
                    ins = a.activation(out=ybf[i][:, oc * 512:(oc + 1) * 512],
                                       in_=p_y[pj][:], func=AF.Copy)
                    sm.tag[f'_yfree{pj}'] = sm.inc(ins, 'a')
            sm.tag['_wfree'] = sm.c['t']

        for i in range(NT):
            @block.gpsimd
            def _(g, i=i):
                sm.wait(g, 'a')
                if '_ofree' in sm.tag:
                    g.wait_ge(sm.h['o'], sm.tag['_ofree'])
                gdma(g, sm, 'g', xbuf[:], x_in[i * 128:(i + 1) * 128, :])
            emit_ln_tile(block, sm, ybf[i], lnbf, scr_bf, stat,
                         v_waits=[('a', sm.c['a']), ('g', sm.c['g'])])

            @block.vector
            def _(v, i=i):
                sm.wait(v, 'g')
                sm.wait(v, 'v')
                ins = v.tensor_tensor(out=xbuf[:], in0=xbuf[:], in1=lnbf[:],
                                      op=ALU.add)
                sm.inc(ins, 'v')

            @block.gpsimd
            def _(g, i=i):
                sm.wait(g, 'v')
                gdma(g, sm, 'o', out[i * 128:(i + 1) * 128, :], xbuf[:])
            sm.tag['_ofree'] = sm.c['o']

        @block.sync
        def _(sy):
            sy.wait_ge(sm.h['o'], sm.c['o'])

    return nc





def emit_barrier(block, sm, engines=('gpsimd', 'vector', 'scalar', 'tensor')):
    for en in engines:
        @getattr(block, en)
        def _(e):
            for n in sm.h:
                sm.wait(e, n)


def build_fused(stop_after=99):
    from contextlib import ExitStack
    nc = bass.Bass("TRN2", target_bir_lowering=False)
    F = F32
    NSLOT = 16
    SLOTC = 516
    x_in = nc.dram_tensor("x", [2, 640, D], F, kind="ExternalInput")
    xtr_in = nc.dram_tensor("xtr", [D, NSLOT * SLOTC], F, kind="ExternalInput")
    wqkvT_in = nc.dram_tensor("wqkvT", [D, 1536], F, kind="ExternalInput")
    qc1L_in = nc.dram_tensor("qc1L", [8, 3, 128, 128], F, kind="ExternalInput")
    qc2L_in = nc.dram_tensor("qc2L", [8, 3, 128, 1024], F, kind="ExternalInput")
    kc1L_in = nc.dram_tensor("kc1L", [2, 3, 128, 128], F, kind="ExternalInput")
    kc2L_in = nc.dram_tensor("kc2L", [2, 3, 128, 256], F, kind="ExternalInput")
    qc1b_in = nc.dram_tensor("qc1b", [128, 8], F, kind="ExternalInput")
    qc2b_in = nc.dram_tensor("qc2b", [128, 8], F, kind="ExternalInput")
    kc1b_in = nc.dram_tensor("kc1b", [128, 2], F, kind="ExternalInput")
    kc2b_in = nc.dram_tensor("kc2b", [128, 2], F, kind="ExternalInput")
    mq_in = nc.dram_tensor("mq", [64, 128], F, kind="ExternalInput")
    mm16_in = nc.dram_tensor("mm16", [128, 64], F, kind="ExternalInput")
    mm4_in = nc.dram_tensor("mm4", [128, 64], F, kind="ExternalInput")
    hq_in = nc.dram_tensor("hq", [128, 8 * 16], F, kind="ExternalInput")
    hk_in = nc.dram_tensor("hk", [128, 2 * 4], F, kind="ExternalInput")
    bq_in = nc.dram_tensor("bq", [16, 8 * 128], F, kind="ExternalInput")
    bk_in = nc.dram_tensor("bk", [4, 2 * 128], F, kind="ExternalInput")
    pswap_in = nc.dram_tensor("pswap", [128, 128], BF16, kind="ExternalInput")
    ceq_in = nc.dram_tensor("ceq", [2, 128, 512], BF16, kind="ExternalInput")
    seq_in = nc.dram_tensor("seq", [2, 128, 512], BF16, kind="ExternalInput")
    cek_in = nc.dram_tensor("cek", [NSLOT, 128, 512], BF16, kind="ExternalInput")
    sek_in = nc.dram_tensor("sek", [NSLOT, 128, 512], BF16, kind="ExternalInput")
    kt16_in = nc.dram_tensor("kt16", [16, 1], F, kind="ExternalInput")
    kt128_in = nc.dram_tensor("kt128", [128, 1], F, kind="ExternalInput")
    vis_in = nc.dram_tensor("vis", [128, 128], F, kind="ExternalInput")
    nzkv_in = nc.dram_tensor("nzkv", [128, NSLOT], F, kind="ExternalInput")
    nzq_in = nc.dram_tensor("nzq", [128, 2], F, kind="ExternalInput")
    tri_in = nc.dram_tensor("tri", [128, 128], BF16, kind="ExternalInput")
    ident_in = nc.dram_tensor("ident", [128, 128], BF16, kind="ExternalInput")
    ones1_in = nc.dram_tensor("ones1", [1, 128], BF16, kind="ExternalInput")
    onescol_in = nc.dram_tensor("onescol", [128, 1], BF16, kind="ExternalInput")
    woT_in = nc.dram_tensor("woT", [1024, D], F, kind="ExternalInput")

    qdr = nc.dram_tensor("qdr", [1024, ROWS], BF16, kind="Internal")
    aodr = nc.dram_tensor("aodr", [1024, ROWS], BF16, kind="Internal")
    x1dr = nc.dram_tensor("x1dr", [ROWS, D], F, kind="Internal")
    if stop_after <= 1:
        out = nc.dram_tensor("out", [ROWS, D], F, kind="ExternalOutput")
        qdbg = nc.dram_tensor("qdbg", [1024, ROWS], BF16, kind="ExternalOutput")
        aodbg = nc.dram_tensor("aodbg", [1024, ROWS], BF16, kind="ExternalOutput")
        khdbg = nc.dram_tensor("khdbg", [256, S], BF16, kind="ExternalOutput")
        vadbg = nc.dram_tensor("vadbg", [128, 64 * 4 * 65], BF16,
                               kind="ExternalOutput")
        q2dbg = nc.dram_tensor("q2dbg", [2, 1024, 512], BF16,
                               kind="ExternalOutput")
        kmdbg = nc.dram_tensor("kmdbg", [64, 1024], BF16,
                               kind="ExternalOutput")

    sm = Sems(nc, ['g', 'v', 'a', 't', 'o', 'w', 'u', 'm'])

    es_glob = ExitStack()
    sb = lambda n, s, d: es_glob.enter_context(nc.sbuf_tensor(n, s, d))

    identbf = sb("identbf", [128, 128], BF16)
    onesbf = sb("onesbf", [1, 128], BF16)
    onescol = sb("onescol_s", [128, 1], BF16)
    stat = sb("stat", [128, 12], F)

    with nc.Block() as block:
        @block.gpsimd
        def _(g):
            gdma(g, sm, 'g', identbf[:], ident_in[:])
            gdma(g, sm, 'g', onesbf[:], ones1_in[:])
            gdma(g, sm, 'g', onescol[:], onescol_in[:])

        es_kv = ExitStack()
        sbkv = lambda n, s, d: es_kv.enter_context(nc.sbuf_tensor(n, s, d))
        khat = [sbkv(f"khat{i}", [128, S], BF16) for i in range(2)]
        va = sbkv("va", [128, 64 * 4 * 65], BF16)

        es_a = ExitStack()
        sba = lambda n, s, d: es_a.enter_context(nc.sbuf_tensor(n, s, d))
        kmb2 = sba("kmb2", [64, 2 * 512], BF16)
        qc1bb = sba("qc1bb", [128, 8], F)
        qc2bb = sba("qc2bb", [128, 8], F)
        kc1bb = sba("kc1bb", [128, 2], F)
        kc2bb = sba("kc2bb", [128, 2], F)
        mqb = sba("mqb", [64, 128], BF16)
        mm16b = sba("mm16b", [128, 64], BF16)
        mm4b = sba("mm4b", [128, 64], BF16)
        hqb = sba("hqb", [128, 8 * 16], BF16)
        hkb = sba("hkb", [128, 2 * 4], BF16)
        bqb = sba("bqb", [16, 8 * 128], BF16)
        bkb = sba("bkb", [4, 2 * 128], BF16)
        pswapb = sba("pswapb", [128, 128], BF16)
        ktv = sba("ktv", [16, 1], F)
        nzkvb = sba("nzkvb", [128, NSLOT], F)
        nzqb = sba("nzqb", [128, 2], F)
        sdr = nc.dram_tensor("sdr", [2, NSLOT * SLOTC], F, kind="Internal")

        es_s1 = ExitStack()
        sb1 = lambda n, s, d: es_s1.enter_context(nc.sbuf_tensor(n, s, d))
        xtbs = [sb1("xtb", [128, 32 * SLOTC], BF16)] * 2
        wkvm = sb1("wkvm", [128, 32 * 640], BF16)
        kvr = [sb1(f"kvr{i}", [128, SLOTC], BF16) for i in range(5)]
        k1b = [sb1(f"k1b{i}", [128, SLOTC], BF16) for i in range(2)]
        k2b = [sb1(f"k2b{i}", [128, 512], BF16) for i in range(2)]
        ksq = [sb1(f"ksq{i}", [128, 512], BF16) for i in range(2)]
        sqb = [sb1(f"sqb{i}", [128, SLOTC], BF16) for i in range(2)]
        rowb = sb1("rowb", [1, 2 * SLOTC], BF16)
        rbb = sb1("rbb", [128, SLOTC], BF16)
        strow = sb1("strow", [1, 2 * SLOTC], F)
        rowb2 = sb1("rowb2", [1, 2 * SLOTC], F)
        urow = sb1("urow", [1, 640], BF16)
        wqmf = sb1("wqmf", [128, 64], F32)
        c1w = sb1("c1w1", [128, 3 * 128], BF16)
        c2wk = sb1("c2wk", [128, 6 * 128], BF16)
        ceb = sb1("ceb1", [128, 512], BF16)
        seb = sb1("seb1", [128, 512], BF16)
        rn4 = sb1("rn4", [16, 512], F)
        rn4b = sb1("rn4b", [16, 512], BF16)
        swp = sb1("swp1", [128, 512], BF16)
        rt1 = sb1("rt11", [128, 512], BF16)

        ps = ExitStack()
        pst = lambda n, s, d: ps.enter_context(nc.psum_tensor(n, s, d))
        p_tr = [pst(f"ptr{i}", [128, 128], BF16) for i in range(2)]
        p_a = pst("pa", [128, 512], F)
        p_b = pst("pb", [128, 512], F)
        p_c = pst("pc", [128, 512], F)
        p_m = pst("pm", [64, 512], F)
        p_h = pst("ph", [16, 512], F)

        @block.gpsimd
        def _(g):
            for dst, src_ in [(qc1bb, qc1b_in), (qc2bb, qc2b_in),
                              (kc1bb, kc1b_in), (kc2bb, kc2b_in),
                              (mqb, mq_in), (mm16b, mm16_in), (mm4b, mm4_in),
                              (hqb, hq_in), (hkb, hk_in), (bqb, bq_in),
                              (bkb, bk_in), (pswapb, pswap_in),
                              (ktv, kt16_in), (nzkvb, nzkv_in),
                              (nzqb, nzq_in)]:
                gdma(g, sm, 'g', dst[:], src_[:])
            ins = g.memset(va[:], 1.0)
            sm.inc(ins, 'm')

        for dt in range(32):
            @block.gpsimd
            def _(g, dt=dt):
                if '_wqmf' in sm.tag:
                    g.wait_ge(sm.h['v'], sm.tag['_wqmf'])
                gdma(g, sm, 'w', xtbs[0][:, 0:1536],
                     wqkvT_in[dt * 128:(dt + 1) * 128, :])

            @block.vector
            def _(v, dt=dt):
                sm.wait(v, 'w')
                base = dt * 640
                v.tensor_copy(out=wkvm[:, base + 128:base + 640],
                              in_=xtbs[0][:, 1024:1536])
                ins = v.reduce_sum(
                    out=wqmf[:],
                    in_=bass.AP(xtbs[0], 0,
                                [[32 * SLOTC, 128], [1, 64], [64, 16]]),
                    axis=mybir.AxisListType.X)
                sm.inc(ins, 'v')
                sm.wait(v, 'v')
                v.memset(wkvm[:, base + 64:base + 128], 0.0)
                ins = v.tensor_scalar(out=wkvm[:, base:base + 64],
                                      in0=wqmf[:],
                                      scalar1=1.0 / 16.0, scalar2=None,
                                      op0=ALU.mult)
                sm.tag['_wqmf'] = sm.inc(ins, 'v')

        for ch in range(2):
            @block.tensor
            def _(t, ch=ch):
                if ch == 0:
                    sm.wait(t, 'v')
                    sm.wait(t, 'g')
                if '_urowf' in sm.tag:
                    t.wait_ge(sm.h['a'], sm.tag['_urowf'])
                for dt in range(32):
                    ins = t.matmul(p_a[0:1, 0:320], onescol[:],
                                   wkvm[:, dt * 640 + ch * 320:
                                        dt * 640 + ch * 320 + 320],
                                   start=(dt == 0), stop=(dt == 31))
                sm.inc(ins, 't')

            @block.scalar
            def _(a, ch=ch):
                sm.wait(a, 't')
                ins = a.activation(out=urow[:, ch * 320:(ch + 1) * 320],
                                   in_=p_a[0:1, 0:320], func=AF.Copy)
                sm.tag['_urowf'] = sm.inc(ins, 'a')

        CHK = [(0, 512), (512, 4)]

        for slot in range(NSLOT):
            xtb = xtbs[slot % 2]

            @block.gpsimd
            def _(g, slot=slot, xtb=xtb):
                if slot == 0:
                    g.wait_ge(sm.h['v'], sm.tag['_wqmf'])
                if '_xtf0' in sm.tag:
                    g.wait_ge(sm.h['t'], sm.tag['_xtf0'])
                for q4 in range(4):
                    gdma(g, sm, 'w',
                         bass.AP(xtb, q4 * 8 * SLOTC,
                                 [[32 * SLOTC, 128], [SLOTC, 8], [1, SLOTC]]),
                         bass.AP(xtr_in, slot * SLOTC + q4 * 8 * 128 * (NSLOT * SLOTC),
                                 [[NSLOT * SLOTC, 128],
                                  [128 * NSLOT * SLOTC, 8], [1, SLOTC]]))

            for ci, (c0, cn) in enumerate(CHK):
                @block.tensor
                def _(t, slot=slot, ci=ci, c0=c0, cn=cn):
                    sm.wait(t, 'w')
                    if '_strf' in sm.tag:
                        t.wait_ge(sm.h['a'], sm.tag['_strf'])
                    for dt in range(32):
                        ins = t.matmul(p_a[0:1, 0:cn], onescol[:],
                                       xtb[:, dt * SLOTC + c0:dt * SLOTC + c0 + cn],
                                       start=(dt == 0), stop=(dt == 31))
                    sm.inc(ins, 't')

                @block.scalar
                def _(a, slot=slot, ci=ci, c0=c0, cn=cn):
                    sm.wait(a, 't')
                    ins = a.activation(out=strow[:, c0:c0 + cn],
                                       in_=p_a[0:1, 0:cn], func=AF.Copy)
                    sm.tag['_strf'] = sm.inc(ins, 'a')

            for ci, (c0, cn) in enumerate(CHK):
                for dt in range(32):
                    @block.vector
                    def _(v, slot=slot, dt=dt, c0=c0, cn=cn):
                        if dt < 2 and ci == 0:
                            sm.wait(v, 'w')
                        j = dt % 2
                        if f'_sqf{j}' in sm.tag:
                            v.wait_ge(sm.h['t'], sm.tag[f'_sqf{j}'])
                        ins = v.tensor_tensor(
                            out=sqb[j][:, 0:cn],
                            in0=xtb[:, dt * SLOTC + c0:dt * SLOTC + c0 + cn],
                            in1=xtb[:, dt * SLOTC + c0:dt * SLOTC + c0 + cn],
                            op=ALU.mult)
                        sm.inc(ins, 'v')

                    @block.tensor
                    def _(t, slot=slot, dt=dt, c0=c0, cn=cn):
                        sm.wait(t, 'v')
                        if dt == 0 and '_str2f' in sm.tag:
                            t.wait_ge(sm.h['a'], sm.tag['_str2f'])
                        j = dt % 2
                        ins = t.matmul(p_b[0:1, 0:cn], onescol[:],
                                       sqb[j][:, 0:cn],
                                       start=(dt == 0), stop=(dt == 31))
                        sm.tag[f'_sqf{j}'] = sm.inc(ins, 't')

                @block.scalar
                def _(a, slot=slot, ci=ci, c0=c0, cn=cn):
                    sm.wait(a, 't')
                    ins = a.activation(out=strow[:, SLOTC + c0:SLOTC + c0 + cn],
                                       in_=p_b[0:1, 0:cn], func=AF.Copy)
                    sm.tag['_str2f'] = sm.inc(ins, 'a')

            @block.vector
            def _(v, slot=slot):
                sm.wait(v, 'a')
                ins = v.tensor_scalar(out=strow[:, 0:SLOTC],
                                      in0=strow[:, 0:SLOTC],
                                      scalar1=1.0 / D, scalar2=None,
                                      op0=ALU.mult)
                sm.inc(ins, 'v'); sm.wait(v, 'v')
                ins = v.tensor_tensor(out=rowb2[:, 0:SLOTC],
                                      in0=strow[:, 0:SLOTC],
                                      in1=strow[:, 0:SLOTC], op=ALU.mult)
                sm.inc(ins, 'v'); sm.wait(v, 'v')
                ins = v.tensor_scalar(out=strow[:, SLOTC:2 * SLOTC],
                                      in0=strow[:, SLOTC:2 * SLOTC],
                                      scalar1=1.0 / D, scalar2=None,
                                      op0=ALU.mult)
                sm.inc(ins, 'v'); sm.wait(v, 'v')
                ins = v.tensor_tensor(out=strow[:, SLOTC:2 * SLOTC],
                                      in0=strow[:, SLOTC:2 * SLOTC],
                                      in1=rowb2[:, 0:SLOTC], op=ALU.subtract)
                sm.inc(ins, 'v'); sm.wait(v, 'v')
                ins = v.tensor_scalar(out=strow[:, SLOTC:2 * SLOTC],
                                      in0=strow[:, SLOTC:2 * SLOTC],
                                      scalar1=EPS, scalar2=None, op0=ALU.add)
                sm.inc(ins, 'v')

            @block.scalar
            def _(a, slot=slot):
                sm.wait(a, 'v')
                ins = a.activation(out=strow[:, SLOTC:2 * SLOTC],
                                   in_=strow[:, SLOTC:2 * SLOTC], func=AF.Ln)
                sm.inc(ins, 'a'); sm.wait(a, 'a')
                ins = a.activation(out=strow[:, SLOTC:2 * SLOTC],
                                   in_=strow[:, SLOTC:2 * SLOTC], func=AF.Exp,
                                   scale=-0.5)
                sm.inc(ins, 'a')

            @block.vector
            def _(v, slot=slot):
                sm.wait(v, 'a')
                ins = v.tensor_tensor(out=rowb2[:, 0:SLOTC],
                                      in0=strow[:, 0:SLOTC],
                                      in1=strow[:, SLOTC:2 * SLOTC],
                                      op=ALU.mult)
                sm.inc(ins, 'v'); sm.wait(v, 'v')
                v.tensor_scalar(out=rowb[:, 0:SLOTC], in0=rowb2[:, 0:SLOTC],
                                scalar1=-1.0, scalar2=None, op0=ALU.mult)
                ins = v.tensor_copy(out=rowb[:, SLOTC:2 * SLOTC],
                                    in_=strow[:, SLOTC:2 * SLOTC])
                sm.inc(ins, 'v')

            for ci, (c0, cn) in enumerate(CHK):
                @block.tensor
                def _(t, slot=slot, ci=ci, c0=c0, cn=cn):
                    sm.wait(t, 'v')
                    if '_rbf' in sm.tag:
                        t.wait_ge(sm.h['a'], sm.tag['_rbf'])
                    ins = t.matmul(p_b[:, 0:cn], onesbf[:],
                                   rowb[:, SLOTC + c0:SLOTC + c0 + cn],
                                   start=True, stop=True)
                    sm.inc(ins, 't')

                @block.scalar
                def _(a, slot=slot, ci=ci, c0=c0, cn=cn):
                    sm.wait(a, 't')
                    ins = a.activation(out=rbb[:, c0:c0 + cn],
                                       in_=p_b[:, 0:cn], func=AF.Copy)
                    sm.tag['_rbf'] = sm.inc(ins, 'a')

                for ct in range(5):
                    @block.tensor
                    def _(t, slot=slot, ct=ct, c0=c0, cn=cn):
                        if f'_kvrf' in sm.tag:
                            t.wait_ge(sm.h['v'], sm.tag['_kvrf'])
                        t.matmul(p_c[:, 0:cn], urow[:, ct * 128:(ct + 1) * 128],
                                 rowb[:, c0:c0 + cn], start=True, stop=False)
                        for dt in range(32):
                            ins = t.matmul(
                                p_c[:, 0:cn],
                                wkvm[:, dt * 640 + ct * 128:dt * 640 + (ct + 1) * 128],
                                xtb[:, dt * SLOTC + c0:dt * SLOTC + c0 + cn],
                                start=False, stop=(dt == 31))
                        sm.inc(ins, 't')

                    @block.vector
                    def _(v, slot=slot, ct=ct, c0=c0, cn=cn):
                        sm.wait(v, 't')
                        sm.wait(v, 'a')
                        ins = v.tensor_tensor(out=kvr[ct][:, c0:c0 + cn],
                                              in0=p_c[:, 0:cn],
                                              in1=rbb[:, c0:c0 + cn],
                                              op=ALU.mult)
                        sm.tag['_kvrf'] = sm.inc(ins, 'v')

            sm.tag['_xtf0'] = sm.c['t']

            for ot in range(2):
                @block.gpsimd
                def _(g, ot=ot):
                    if '_c1kf' in sm.tag:
                        g.wait_ge(sm.h['t'], sm.tag['_c1kf'])
                    gdma(g, sm, 'u',
                         bass.AP(c1w, 0, [[3 * 128, 128], [128, 3], [1, 128]]),
                         bass.AP(kc1L_in, ot * 3 * 128 * 128,
                                 [[128, 128], [128 * 128, 3], [1, 128]]))

                for ci, (c0, cn) in enumerate([(2, 512), (514, 2)]):
                    @block.tensor
                    def _(t, ot=ot, ci=ci, c0=c0, cn=cn):
                        sm.wait(t, 'u')
                        sm.wait(t, 'v')
                        pj = ci
                        pp = [p_a, p_b][pj]
                        if f'_c1kp{pj}' in sm.tag:
                            t.wait_ge(sm.h['v'], sm.tag[f'_c1kp{pj}'])
                        for tau in range(3):
                            ins = t.matmul(pp[:, 0:cn],
                                           c1w[:, tau * 128:(tau + 1) * 128],
                                           kvr[1 + ot][:, c0 - 2 + tau:
                                                       c0 - 2 + tau + cn],
                                           start=(tau == 0), stop=(tau == 2))
                        sm.inc(ins, 't')

                    @block.vector
                    def _(v, ot=ot, ci=ci, c0=c0, cn=cn, slot=slot):
                        sm.wait(v, 't')
                        pj = ci
                        pp = [p_a, p_b][pj]
                        ins = v.tensor_scalar(out=k1b[ot][:, c0:c0 + cn],
                                              in0=pp[:, 0:cn], scalar1=1.0,
                                              scalar2=kc1bb[:, ot:ot + 1],
                                              op0=ALU.mult, op1=ALU.add)
                        sm.tag[f'_c1kp{pj}'] = sm.inc(ins, 'v')
                sm.tag['_c1kf'] = sm.c['t']

            @block.vector
            def _(v, slot=slot):
                sm.wait(v, 'v')
                for ot in range(2):
                    ins = v.tensor_scalar(out=k1b[ot][:, 2:4],
                                          in0=k1b[ot][:, 2:4],
                                          scalar1=nzkvb[:, slot:slot + 1],
                                          scalar2=None, op0=ALU.mult)
                sm.inc(ins, 'v')

            for ot in range(2):
                @block.gpsimd
                def _(g, ot=ot):
                    if '_c2kf' in sm.tag:
                        g.wait_ge(sm.h['t'], sm.tag['_c2kf'])
                    gdma(g, sm, 'u',
                         bass.AP(c2wk, 0, [[6 * 128, 128], [128, 6], [1, 128]]),
                         bass.AP(kc2L_in, ot * 128,
                                 [[256, 128], [128 * 256, 6], [1, 128]]))

                @block.tensor
                def _(t, ot=ot, slot=slot):
                    sm.wait(t, 'u')
                    sm.wait(t, 'v')
                    if '_k2f' in sm.tag:
                        t.wait_ge(sm.h['a'], sm.tag['_k2f'])
                    if '_k2fv' in sm.tag:
                        t.wait_ge(sm.h['v'], sm.tag['_k2fv'])
                    for it in range(2):
                        for tau in range(3):
                            t.matmul(p_c[:],
                                     c2wk[:, (it * 3 + tau) * 128:
                                          (it * 3 + tau + 1) * 128],
                                     k1b[it][:, 2 + tau:2 + tau + 512],
                                     start=(it == 0 and tau == 0), stop=False)
                    ins = t.matmul(p_c[:], mqb[:], kvr[0][0:64, 4:516],
                                   start=False, stop=True)
                    sm.inc(ins, 't')
                sm.tag['_c2kf'] = sm.c['t']

                @block.scalar
                def _(a, ot=ot):
                    sm.wait(a, 't')
                    ins = a.activation(out=ksq[ot][:], in_=p_c[:],
                                       func=AF.Square,
                                       bias=kc2bb[:, ot:ot + 1])
                    sm.tag['_k2f'] = sm.inc(ins, 'a')

                @block.vector
                def _(v, ot=ot):
                    sm.wait(v, 'a')
                    ins = v.tensor_scalar(out=k2b[ot][:], in0=p_c[:],
                                          scalar1=1.0,
                                          scalar2=kc2bb[:, ot:ot + 1],
                                          op0=ALU.mult, op1=ALU.add)
                    sm.tag['_k2fv'] = sm.inc(ins, 'v')

            if slot < 2:
                @block.tensor
                def _(t, slot=slot):
                    sm.wait(t, 'v')
                    if '_kmf' in sm.tag:
                        t.wait_ge(sm.h['a'], sm.tag['_kmf'])
                    for it in range(2):
                        ins = t.matmul(p_m[:], mm4b[:],
                                       kvr[1 + it][:, 4:516],
                                       start=(it == 0), stop=(it == 1))
                    sm.inc(ins, 't')

                @block.scalar
                def _(a, slot=slot):
                    sm.wait(a, 't')
                    ins = a.activation(out=kmb2[:, slot * 512:(slot + 1) * 512],
                                       in_=p_m[:], func=AF.Copy)
                    sm.tag['_kmf'] = sm.inc(ins, 'a')

            @block.tensor
            def _(t, slot=slot):
                sm.wait(t, 'a')
                sm.wait(t, 'v')
                if '_rn4f' in sm.tag:
                    t.wait_ge(sm.h['v'], sm.tag['_rn4f'])
                t.matmul(p_h[0:4, :], hkb[:, 0:4], ksq[0][:],
                         start=True, stop=False)
                ins = t.matmul(p_h[0:4, :], hkb[:, 4:8], ksq[1][:],
                               start=False, stop=True)
                sm.inc(ins, 't')

            @block.vector
            def _(v, slot=slot):
                sm.wait(v, 't')
                ins = v.tensor_scalar(out=rn4[0:4, :], in0=p_h[0:4, :],
                                      scalar1=1.0, scalar2=1e-24,
                                      op0=ALU.mult, op1=ALU.max)
                sm.tag['_rn4f'] = sm.inc(ins, 'v')

            @block.scalar
            def _(a, slot=slot):
                sm.wait(a, 'v')
                ins = a.activation(out=rn4[0:4, :], in_=rn4[0:4, :], func=AF.Ln)
                sm.inc(ins, 'a'); sm.wait(a, 'a')
                ins = a.activation(out=rn4[0:4, :], in_=rn4[0:4, :],
                                   func=AF.Exp, scale=-0.5)
                sm.inc(ins, 'a')

            @block.vector
            def _(v, slot=slot):
                sm.wait(v, 'a')
                ins = v.tensor_scalar(out=rn4[0:4, :], in0=rn4[0:4, :],
                                      scalar1=ktv[0:4, :], scalar2=None,
                                      op0=ALU.mult)
                sm.inc(ins, 'v'); sm.wait(v, 'v')
                ins = v.tensor_copy(out=rn4b[0:4, :], in_=rn4[0:4, :])
                sm.inc(ins, 'v')

            @block.gpsimd
            def _(g, slot=slot):
                if '_cebf' in sm.tag:
                    g.wait_ge(sm.h['v'], sm.tag['_cebf'])
                gdma(g, sm, 'u', ceb[:], cek_in[slot])
                gdma(g, sm, 'u', seb[:], sek_in[slot])

            for ot in range(2):
                @block.tensor
                def _(t, ot=ot, slot=slot):
                    sm.wait(t, 'v')
                    if '_swkf' in sm.tag:
                        t.wait_ge(sm.h['a'], sm.tag['_swkf'])
                    ins = t.matmul(p_a[:], pswapb[:], k2b[ot][:],
                                   start=True, stop=True)
                    sm.inc(ins, 't')

                @block.scalar
                def _(a, ot=ot):
                    sm.wait(a, 't')
                    if '_ropekv' in sm.tag:
                        a.wait_ge(sm.h['v'], sm.tag['_ropekv'])
                    ins = a.activation(out=swp[:], in_=p_a[:], func=AF.Copy)
                    sm.tag['_swkf'] = sm.inc(ins, 'a')

                @block.tensor
                def _(t, ot=ot, slot=slot):
                    if '_rnbkf' in sm.tag:
                        t.wait_ge(sm.h['v'], sm.tag['_rnbkf'])
                    ins = t.matmul(p_b[:], bkb[0:4, ot * 128:(ot + 1) * 128],
                                   rn4b[0:4, :], start=True, stop=True)
                    sm.inc(ins, 't')

                @block.vector
                def _(v, ot=ot, slot=slot):
                    sm.wait(v, 'a')
                    sm.wait(v, 't')
                    sm.wait(v, 'u')
                    v.tensor_tensor(out=rt1[:], in0=k2b[ot][:], in1=ceb[:],
                                    op=ALU.mult)
                    ins = v.tensor_tensor(out=swp[:], in0=swp[:], in1=seb[:],
                                          op=ALU.mult)
                    sm.inc(ins, 'v'); sm.wait(v, 'v')
                    ins = v.tensor_tensor(out=swp[:], in0=rt1[:], in1=swp[:],
                                          op=ALU.add)
                    sm.inc(ins, 'v'); sm.wait(v, 'v')
                    ins = v.tensor_tensor(
                        out=khat[ot][:, slot * 512:(slot + 1) * 512],
                        in0=swp[:], in1=p_b[:], op=ALU.mult)
                    sm.tag['_ropekv'] = sm.inc(ins, 'v')
                    sm.tag['_rnbkf'] = sm.tag['_ropekv']
            sm.tag['_cebf'] = sm.c['v']

            for st in range(4):
                for half in range(2):
                    @block.tensor
                    def _(t, st=st, half=half, slot=slot):
                        j = (st * 2 + half) % 2
                        key = f'_trv{j}'
                        if key in sm.tag:
                            t.wait_ge(sm.h['v'], sm.tag[key])
                        if st == 0 and half == 0:
                            sm.wait(t, 'v')
                            sm.wait(t, 'm')
                        src_col = 4 + 128 * st - (1 if half == 1 else 0)
                        ins = t.transpose(p_tr[j][:],
                                          kvr[3 + half][:, src_col:src_col + 128],
                                          identbf[:])
                        sm.inc(ins, 't')

                    @block.vector
                    def _(v, st=st, half=half, slot=slot):
                        sm.wait(v, 't')
                        j = (st * 2 + half) % 2
                        T = slot * 4 + st
                        for kvh in range(2):
                            kv = half * 2 + kvh
                            ins = v.tensor_copy(
                                out=va[:, T * 260 + kv * 65:T * 260 + kv * 65 + 64],
                                in_=p_tr[j][0:128, kvh * 64:kvh * 64 + 64])
                            sm.inc(ins, 'v')
                        sm.tag[f'_trv{j}'] = sm.c['v']
            sm.tag['_xtf'] = sm.c['t']

        emit_barrier(block, sm)
        ps.close()
        es_s1.close()
        es_s2 = ExitStack()
        sb2 = lambda n, s, d: es_s2.enter_context(nc.sbuf_tensor(n, s, d))
        lnT = [sb2(f"lnT{d_}", [128, 640], BF16) for d_ in range(32)]
        qkvT = [sb2(f"qkvT{i}", [128, 640], BF16) for i in range(8)]
        q1b = [sb2(f"q1b{i}", [128, SLOTC], BF16) for i in range(8)]
        q2b = [sb2(f"q2b{i}", [128, 512], BF16) for i in range(8)]
        qsq = [sb2(f"qsq{i}", [128, 512], BF16) for i in range(8)]
        wq = sb2("wq2", [128, 32 * 128], BF16)
        c1w = sb2("c1w2", [128, 3 * 128], BF16)
        c2wq = sb2("c2wq", [128, 24 * 128], BF16)
        ceb = sb2("ceb2", [128, 512], BF16)
        seb = sb2("seb2", [128, 512], BF16)
        rn16 = sb2("rn16", [16, 512], F)
        rn16b = sb2("rn16b", [16, 512], BF16)
        swp = sb2("swp2", [128, 512], BF16)
        rt1 = sb2("rt12", [128, 512], BF16)
        qrb = sb2("qrb2", [128, 512], BF16)
        ob2 = [sb2(f"ob2{i}", [128, 512], BF16) for i in range(2)]
        xbuf = sb2("xb1", [128, D], F)
        lnbf = sb2("lnbf1", [128, D], BF16)
        scr_bf = sb2("scr1", [128, D], BF16)
        ps = ExitStack()
        pst = lambda n, s, d: ps.enter_context(nc.psum_tensor(n, s, d))
        p_tr = [pst(f"ptr2{i}", [128, 512], BF16) for i in range(2)]
        p_a = pst("pa2", [128, 512], F)
        p_b = pst("pb2", [128, 512], F)
        p_c = pst("pc2", [128, 512], F)
        p_h = pst("ph2", [16, 512], F)

        for seg in range(2):
            @block.gpsimd
            def _(g, seg=seg):
                if '_cebf' in sm.tag:
                    g.wait_ge(sm.h['v'], sm.tag['_cebf'])
                gdma(g, sm, 'u', ceb[:], ceq_in[seg])
                gdma(g, sm, 'u', seb[:], seq_in[seg])

            for i in range(5):
                @block.gpsimd
                def _(g, seg=seg, i=i):
                    if '_lnfree' in sm.tag:
                        g.wait_ge(sm.h['v'], sm.tag['_lnfree'])
                    gdma(g, sm, 'g', xbuf[:],
                         x_in[seg, i * 128:(i + 1) * 128, :])
                emit_ln_tile(block, sm, xbuf, lnbf, scr_bf, stat,
                             v_waits=[('g', sm.c['g']), ('t', sm.c['t'])])
                sm.tag['_lnfree'] = sm.c['v']
                emit_transpose_tiles(block, sm, lnbf, lnT, i * 128, identbf,
                                     p_tr)

            for ct in range(8):
                @block.gpsimd
                def _(g, ct=ct):
                    if '_wqf2' in sm.tag:
                        g.wait_ge(sm.h['t'], sm.tag['_wqf2'])
                    gdma(g, sm, 'w',
                         bass.AP(wq, 0, [[32 * 128, 128], [128, 32], [1, 128]]),
                         bass.AP(wqkvT_in, ct * 128,
                                 [[1536, 128], [128 * 1536, 32], [1, 128]]))

                for ch, (c0, cn) in enumerate([(0, 512), (512, 128)]):
                    @block.tensor
                    def _(t, ct=ct, ch=ch, c0=c0, cn=cn):
                        sm.wait(t, 'w')
                        if ct == 0 and ch == 0:
                            sm.wait(t, 'a')
                            sm.wait(t, 'v')
                        pj = (ct * 2 + ch) % 2
                        pp = [p_a, p_b][pj]
                        if f'_qkvq{pj}' in sm.tag:
                            t.wait_ge(sm.h['a'], sm.tag[f'_qkvq{pj}'])
                        for dd in range(32):
                            ins = t.matmul(pp[:, 0:cn],
                                           wq[:, dd * 128:(dd + 1) * 128],
                                           lnT[dd][:, c0:c0 + cn],
                                           start=(dd == 0), stop=(dd == 31))
                        sm.inc(ins, 't')

                    @block.scalar
                    def _(a, ct=ct, ch=ch, c0=c0, cn=cn):
                        sm.wait(a, 't')
                        pj = (ct * 2 + ch) % 2
                        pp = [p_a, p_b][pj]
                        ins = a.activation(out=qkvT[ct][:, c0:c0 + cn],
                                           in_=pp[:, 0:cn], func=AF.Copy)
                        sm.tag[f'_qkvq{pj}'] = sm.inc(ins, 'a')
                sm.tag['_wqf2'] = sm.c['t']

            for ot in range(8):
                @block.gpsimd
                def _(g, ot=ot):
                    if '_c1qf' in sm.tag:
                        g.wait_ge(sm.h['t'], sm.tag['_c1qf'])
                    gdma(g, sm, 'u',
                         bass.AP(c1w, 0, [[3 * 128, 128], [128, 3], [1, 128]]),
                         bass.AP(qc1L_in, ot * 3 * 128 * 128,
                                 [[128, 128], [128 * 128, 3], [1, 128]]))

                for ci, (c0, cn) in enumerate([(124, 512), (636, 4)]):
                    @block.tensor
                    def _(t, ot=ot, ci=ci, c0=c0, cn=cn):
                        sm.wait(t, 'u')
                        if ot == 0 and ci == 0:
                            sm.wait(t, 'a')
                        pj = ci
                        pp = [p_a, p_b][pj]
                        if f'_c1qp{pj}' in sm.tag:
                            t.wait_ge(sm.h['v'], sm.tag[f'_c1qp{pj}'])
                        for tau in range(3):
                            ins = t.matmul(pp[:, 0:cn],
                                           c1w[:, tau * 128:(tau + 1) * 128],
                                           qkvT[ot][:, c0 - 2 + tau:
                                                    c0 - 2 + tau + cn],
                                           start=(tau == 0), stop=(tau == 2))
                        sm.inc(ins, 't')

                    @block.vector
                    def _(v, ot=ot, ci=ci, c0=c0, cn=cn, seg=seg):
                        sm.wait(v, 't')
                        pj = ci
                        pp = [p_a, p_b][pj]
                        ins = v.tensor_scalar(out=q1b[ot][:, c0 - 124:
                                                          c0 - 124 + cn],
                                              in0=pp[:, 0:cn], scalar1=1.0,
                                              scalar2=qc1bb[:, ot:ot + 1],
                                              op0=ALU.mult, op1=ALU.add)
                        sm.tag[f'_c1qp{pj}'] = sm.inc(ins, 'v')
                sm.tag['_c1qf'] = sm.c['t']

            @block.vector
            def _(v, seg=seg):
                sm.wait(v, 'v')
                for ot in range(8):
                    ins = v.tensor_scalar(out=q1b[ot][:, 0:4],
                                          in0=q1b[ot][:, 0:4],
                                          scalar1=nzqb[:, seg:seg + 1],
                                          scalar2=None, op0=ALU.mult)
                sm.inc(ins, 'v')

            for ot in range(8):
                @block.gpsimd
                def _(g, ot=ot):
                    if '_c2qf' in sm.tag:
                        g.wait_ge(sm.h['t'], sm.tag['_c2qf'])
                    gdma(g, sm, 'u',
                         bass.AP(c2wq, 0, [[24 * 128, 128], [128, 24], [1, 128]]),
                         bass.AP(qc2L_in, ot * 128,
                                 [[1024, 128], [128 * 1024, 24], [1, 128]]))

                @block.tensor
                def _(t, ot=ot, seg=seg):
                    sm.wait(t, 'u')
                    sm.wait(t, 'v')
                    if '_q2f2' in sm.tag:
                        t.wait_ge(sm.h['a'], sm.tag['_q2f2'])
                    if '_q2fv2' in sm.tag:
                        t.wait_ge(sm.h['v'], sm.tag['_q2fv2'])
                    for it in range(8):
                        for tau in range(3):
                            t.matmul(p_c[:],
                                     c2wq[:, (it * 3 + tau) * 128:
                                          (it * 3 + tau + 1) * 128],
                                     q1b[it][:, 2 + tau:2 + tau + 512],
                                     start=(it == 0 and tau == 0), stop=False)
                    ins = t.matmul(p_c[:], mqb[:],
                                   kmb2[:, seg * 512:(seg + 1) * 512],
                                   start=False, stop=True)
                    sm.inc(ins, 't')
                sm.tag['_c2qf'] = sm.c['t']

                @block.scalar
                def _(a, ot=ot):
                    sm.wait(a, 't')
                    ins = a.activation(out=qsq[ot][:], in_=p_c[:],
                                       func=AF.Square,
                                       bias=qc2bb[:, ot:ot + 1])
                    sm.tag['_q2f2'] = sm.inc(ins, 'a')

                @block.vector
                def _(v, ot=ot):
                    sm.wait(v, 'a')
                    ins = v.tensor_scalar(out=q2b[ot][:], in0=p_c[:],
                                          scalar1=1.0,
                                          scalar2=qc2bb[:, ot:ot + 1],
                                          op0=ALU.mult, op1=ALU.add)
                    sm.tag['_q2fv2'] = sm.inc(ins, 'v')

            if stop_after <= 1:
                @block.gpsimd
                def _(g, seg=seg):
                    sm.wait(g, 'v')
                    for ot in range(8):
                        gdma(g, sm, 'o',
                             q2dbg[seg, ot * 128:(ot + 1) * 128, :],
                             q2b[ot][:])
                    if seg == 0:
                        gdma(g, sm, 'o', kmdbg[:], kmb2[:])

            @block.tensor
            def _(t, seg=seg):
                sm.wait(t, 'a')
                sm.wait(t, 'v')
                if '_rn16f' in sm.tag:
                    t.wait_ge(sm.h['v'], sm.tag['_rn16f'])
                for ot in range(8):
                    ins = t.matmul(p_h[:], hqb[:, ot * 16:(ot + 1) * 16],
                                   qsq[ot][:], start=(ot == 0), stop=(ot == 7))
                sm.inc(ins, 't')

            @block.vector
            def _(v, seg=seg):
                sm.wait(v, 't')
                ins = v.tensor_scalar(out=rn16[:], in0=p_h[:], scalar1=1.0,
                                      scalar2=1e-24, op0=ALU.mult, op1=ALU.max)
                sm.tag['_rn16f'] = sm.inc(ins, 'v')

            @block.scalar
            def _(a, seg=seg):
                sm.wait(a, 'v')
                ins = a.activation(out=rn16[:], in_=rn16[:], func=AF.Ln)
                sm.inc(ins, 'a'); sm.wait(a, 'a')
                ins = a.activation(out=rn16[:], in_=rn16[:], func=AF.Exp,
                                   scale=-0.5)
                sm.inc(ins, 'a')

            @block.vector
            def _(v, seg=seg):
                sm.wait(v, 'a')
                ins = v.tensor_copy(out=rn16b[:], in_=rn16[:])
                sm.inc(ins, 'v')

            for ot in range(8):
                @block.tensor
                def _(t, ot=ot):
                    sm.wait(t, 'v')
                    if '_swqf' in sm.tag:
                        t.wait_ge(sm.h['a'], sm.tag['_swqf'])
                    ins = t.matmul(p_a[:], pswapb[:], q2b[ot][:],
                                   start=True, stop=True)
                    sm.inc(ins, 't')

                @block.scalar
                def _(a, ot=ot):
                    sm.wait(a, 't')
                    if '_ropeq' in sm.tag:
                        a.wait_ge(sm.h['v'], sm.tag['_ropeq'])
                    ins = a.activation(out=swp[:], in_=p_a[:], func=AF.Copy)
                    sm.tag['_swqf'] = sm.inc(ins, 'a')

                @block.tensor
                def _(t, ot=ot):
                    if '_rnbqf' in sm.tag:
                        t.wait_ge(sm.h['v'], sm.tag['_rnbqf'])
                    ins = t.matmul(p_b[:], bqb[0:16, ot * 128:(ot + 1) * 128],
                                   rn16b[:], start=True, stop=True)
                    sm.inc(ins, 't')

                @block.vector
                def _(v, ot=ot, seg=seg):
                    sm.wait(v, 'a')
                    sm.wait(v, 't')
                    sm.wait(v, 'u')
                    oj = ot % 2
                    if f'_qob{oj}' in sm.tag:
                        v.wait_ge(sm.h['o'], sm.tag[f'_qob{oj}'])
                    v.tensor_tensor(out=rt1[:], in0=q2b[ot][:], in1=ceb[:],
                                    op=ALU.mult)
                    ins = v.tensor_tensor(out=qrb[:], in0=swp[:], in1=seb[:],
                                          op=ALU.mult)
                    sm.inc(ins, 'v'); sm.wait(v, 'v')
                    ins = v.tensor_tensor(out=qrb[:], in0=rt1[:], in1=qrb[:],
                                          op=ALU.add)
                    sm.inc(ins, 'v'); sm.wait(v, 'v')
                    ins = v.tensor_tensor(out=ob2[oj][:], in0=qrb[:],
                                          in1=p_b[:], op=ALU.mult)
                    sm.tag['_ropeq'] = sm.inc(ins, 'v')
                    sm.tag['_rnbqf'] = sm.tag['_ropeq']

                @block.gpsimd
                def _(g, ot=ot, seg=seg):
                    sm.wait(g, 'v')
                    oj = ot % 2
                    gdma(g, sm, 'o',
                         qdr[ot * 128:(ot + 1) * 128,
                             seg * 512:(seg + 1) * 512],
                         ob2[oj][:])
                    sm.tag[f'_qob{oj}'] = sm.c['o']
            sm.tag['_cebf'] = sm.c['v']

        emit_barrier(block, sm)
        ps.close()
        es_s2.close()
        es_a.close()

        es_b = ExitStack()
        sbb = lambda n, s, d: es_b.enter_context(nc.sbuf_tensor(n, s, d))
        qn = [sbb(f"qn{j}", [128, ROWS], BF16) for j in range(8)]
        qs_ = [sbb(f"qs{j}", [128, ROWS], BF16) for j in range(8)]
        aoT = [sbb(f"aoT{h}", [64, ROWS], BF16) for h in range(16)]
        trib = sbb("trib", [128, 128], BF16)
        visb = sbb("visb", [128, 128], F)
        btab = sbb("btab", [128, 128], F)
        btmp = sbb("btmp", [128, 128], F)
        kt128b = sbb("kt128b", [128, 1], F)
        mtmp = sbb("mtmp", [128, 1], F)
        pbuf = [sbb(f"pbuf{i}", [128, 512], BF16) for i in range(6)]
        lsb = sbb("lsb", [128, 512], F)
        lbf = sbb("lbf", [128, 512], BF16)
        rls = sbb("rls", [64, 512], F)
        ps2 = ExitStack()
        pst2 = lambda n, s, d: ps2.enter_context(nc.psum_tensor(n, s, d))
        p_s = [pst2(f"ps_{i}", [128, 512], F) for i in range(6)]
        p_o = pst2("po", [65, 512], F)
        p_rl = pst2("prl", [64, 512], F)

        @block.gpsimd
        def _(g):
            gdma(g, sm, 'g', trib[:], tri_in[:])
            gdma(g, sm, 'g', visb[:], vis_in[:])
            gdma(g, sm, 'g', kt128b[:], kt128_in[:])
            for j in range(8):
                gdma(g, sm, 'g', qn[j][:], qdr[j * 128:(j + 1) * 128, :])
            for j in range(7):
                gdma(g, sm, 'g', qs_[j][:],
                     qdr[j * 128 + 64:(j + 1) * 128 + 64, :])
            gdma(g, sm, 'g', qs_[7][0:64, :], qdr[960:1024, :])

        @block.scalar
        def _(a):
            sm.wait(a, 'g')
            ins = a.activation(out=mtmp[:], in_=kt128b[:], func=AF.Abs)
            sm.inc(ins, 'a')

        @block.vector
        def _(v):
            sm.wait(v, 'a')
            ins = v.tensor_scalar(out=mtmp[:], in0=mtmp[:], scalar1=-0.1275,
                                  scalar2=None, op0=ALU.mult)
            sm.inc(ins, 'v'); sm.wait(v, 'v')
            ins = v.tensor_scalar(out=btab[:], in0=visb[:],
                                  scalar1=mtmp[:], scalar2=None, op0=ALU.mult)
            sm.inc(ins, 'v')
            v.tensor_scalar(out=btmp[:], in0=visb[:], scalar1=1e30,
                            scalar2=-1e30, op0=ALU.mult, op1=ALU.add)
            sm.inc(v.tensor_copy(out=lbf[0:1, 0:1], in_=visb[0:1, 0:1]), 'v')
            sm.wait(v, 'v')
            ins = v.tensor_tensor(out=btab[:], in0=btab[:], in1=btmp[:],
                                  op=ALU.add)
            sm.inc(ins, 'v')

        for seg in range(2):
            for qh in range(16):
                kv = qh // 4
                kvt = kv // 2
                kbase = 64 * (kv % 2)
                if qh % 2 == kv % 2:
                    qsrc = qn[qh // 2]
                    qrow = 64 * (qh % 2)
                else:
                    if qh % 2 == 1:
                        qsrc = qs_[(qh - 1) // 2]
                        qrow = 0
                    else:
                        qsrc = qs_[qh // 2 - 1]
                        qrow = 64
                assert qrow == kbase
                nT = 36 if seg == 0 else 64
                for g0 in range(0, nT, 6):
                    Ts = list(range(g0, min(g0 + 6, nT)))
                    dia = {T: ((seg == 0 and T < 4)
                               or (seg == 1 and 4 <= T < 8)) for T in Ts}
                    gdiag = any(dia.values())

                    @block.tensor
                    def _(t, seg=seg, qh=qh, g0=g0, Ts=Ts, qsrc=qsrc,
                          qrow=qrow, kvt=kvt, kbase=kbase):
                        if seg == 0 and qh == 0 and g0 == 0:
                            sm.wait(t, 'v')
                        if '_exgA' in sm.tag:
                            t.wait_ge(sm.h['a'], sm.tag['_exgA'])
                        for T in Ts:
                            ins = t.matmul(
                                p_s[T % 6][:],
                                khat[kvt][kbase:kbase + 64,
                                          T * 128:(T + 1) * 128],
                                qsrc[qrow:qrow + 64,
                                     seg * 512:(seg + 1) * 512],
                                start=True, stop=True)
                        sm.inc(ins, 't', amt=len(Ts))

                    @block.scalar
                    def _(a, seg=seg, Ts=Ts, dia=dia):
                        sm.wait(a, 't')
                        for T in Ts:
                            span0 = 128 * (T % 4) if dia[T] else 0
                            ins = a.activation(
                                out=pbuf[T % 6][:, span0:512],
                                in_=p_s[T % 6][:, span0:512],
                                func=AF.Exp, scale=0.125,
                                bias=btab[:, seg * 64 + T:seg * 64 + T + 1])
                            sm.tag[f'_exb{T % 6}'] = sm.inc(ins, 'a')
                        sm.tag['_exgA'] = sm.c['a']

                    if gdiag:
                        @block.vector
                        def _(v, Ts=Ts, dia=dia):
                            sm.wait(v, 'a')
                            for T in Ts:
                                if not dia[T]:
                                    continue
                                span0 = 128 * (T % 4)
                                ins = v.tensor_tensor(
                                    out=pbuf[T % 6][:, span0:span0 + 128],
                                    in0=pbuf[T % 6][:, span0:span0 + 128],
                                    in1=trib[:], op=ALU.mult)
                                sm.tag[f'_trib{T % 6}'] = sm.inc(ins, 'v')

                    @block.tensor
                    def _(t, seg=seg, qh=qh, g0=g0, Ts=Ts, kv=kv,
                          gdiag=gdiag, dia=dia, nT=nT):
                        if g0 == 0 and '_pofv' in sm.tag:
                            t.wait_ge(sm.h['v'], sm.tag['_pofv'])
                        for T in Ts:
                            span0 = 128 * (T % 4) if dia[T] else 0
                            if dia[T]:
                                t.wait_ge(sm.h['v'], sm.tag[f'_trib{T % 6}'])
                            else:
                                t.wait_ge(sm.h['a'], sm.tag[f'_exb{T % 6}'])
                            ins = t.matmul(p_o[:, span0:512],
                                           va[:, T * 260 + kv * 65:
                                              T * 260 + kv * 65 + 65],
                                           pbuf[T % 6][:, span0:512],
                                           start=(T == 0), stop=(T == nT - 1))
                        sm.inc(ins, 't', amt=len(Ts))

                @block.vector
                def _(v, seg=seg, qh=qh):
                    sm.wait(v, 't')
                    ins = v.reciprocal(out=lsb[0:1, :], in_=p_o[64:65, :])
                    sm.inc(ins, 'v'); sm.wait(v, 'v')
                    ins = v.tensor_copy(out=lbf[0:1, :], in_=lsb[0:1, :])
                    sm.inc(ins, 'v')

                @block.tensor
                def _(t, seg=seg, qh=qh):
                    sm.wait(t, 'v')
                    if '_rlf' in sm.tag:
                        t.wait_ge(sm.h['v'], sm.tag['_rlf'])
                    ins = t.matmul(p_rl[:], onesbf[:, 0:64], lbf[0:1, :],
                                   start=True, stop=True)
                    sm.inc(ins, 't')

                @block.vector
                def _(v, seg=seg, qh=qh):
                    sm.wait(v, 't')
                    ins = v.tensor_copy(out=rls[:], in_=p_rl[:])
                    sm.inc(ins, 'v'); sm.wait(v, 'v')
                    ins = v.tensor_tensor(
                        out=aoT[qh][:, seg * 512:(seg + 1) * 512],
                        in0=p_o[0:64, :], in1=rls[:], op=ALU.mult)
                    sm.tag['_pofv'] = sm.inc(ins, 'v')
                    sm.tag['_rlf'] = sm.c['v']

        @block.gpsimd
        def _(g):
            sm.wait(g, 'v')
            for h in range(16):
                gdma(g, sm, 'o', aodr[h * 64:(h + 1) * 64, :], aoT[h][:])
            if stop_after <= 1:
                gdma(g, sm, 'o', khdbg[0:128, :], khat[0][:])
                gdma(g, sm, 'o', khdbg[128:256, :], khat[1][:])
                gdma(g, sm, 'o', vadbg[:], va[:])
                for j in range(8):
                    gdma(g, sm, 'o', qdbg[j * 128:(j + 1) * 128, :], qn[j][:])
                gdma(g, sm, 'o', aodbg[:], aodr[:])

        emit_barrier(block, sm)
        ps2.close()
        es_b.close()

        es_c = ExitStack()
        sbc = lambda n, s, d: es_c.enter_context(nc.sbuf_tensor(n, s, d))
        aoc = [sbc(f"aoc{j}", [128, ROWS], BF16) for j in range(8)]
        ybf = [sbc(f"ybf{i}", [128, D], BF16) for i in range(NT)]
        wob = sbc("wob", [128, 8 * 512], BF16)
        xbuf2 = sbc("xb2", [128, D], F)
        lnbf2 = sbc("lnbf2", [128, D], BF16)
        scr2 = sbc("scr2", [128, D], BF16)
        ps3 = ExitStack()
        p_y = [ps3.enter_context(nc.psum_tensor(f"py_{i}", [128, 512], F))
               for i in range(2)]

        x1tgt = out if stop_after <= 1 else x1dr

        @block.gpsimd
        def _(g):
            for j in range(8):
                gdma(g, sm, 'g', aoc[j][:], aodr[j * 128:(j + 1) * 128, :])

        for oc in range(8):
            @block.gpsimd
            def _(g, oc=oc):
                if '_wof' in sm.tag:
                    g.wait_ge(sm.h['t'], sm.tag['_wof'])
                gdma(g, sm, 'w',
                     bass.AP(wob, 0, [[8 * 512, 128], [512, 8], [1, 512]]),
                     bass.AP(woT_in, oc * 512,
                             [[D, 128], [128 * D, 8], [1, 512]]))

            for i in range(NT):
                @block.tensor
                def _(t, oc=oc, i=i):
                    sm.wait(t, 'w')
                    if oc == 0 and i == 0:
                        sm.wait(t, 'g')
                    pj = (oc * NT + i) % 2
                    if f'_yf{pj}' in sm.tag:
                        t.wait_ge(sm.h['a'], sm.tag[f'_yf{pj}'])
                    for dd in range(8):
                        ins = t.matmul(p_y[pj][:],
                                       aoc[dd][:, i * 128:(i + 1) * 128],
                                       wob[:, dd * 512:(dd + 1) * 512],
                                       start=(dd == 0), stop=(dd == 7))
                    sm.inc(ins, 't')

                @block.scalar
                def _(a, oc=oc, i=i):
                    sm.wait(a, 't')
                    pj = (oc * NT + i) % 2
                    ins = a.activation(out=ybf[i][:, oc * 512:(oc + 1) * 512],
                                       in_=p_y[pj][:], func=AF.Copy)
                    sm.tag[f'_yf{pj}'] = sm.inc(ins, 'a')
            sm.tag['_wof'] = sm.c['t']

        for i in range(NT):
            @block.gpsimd
            def _(g, i=i):
                sm.wait(g, 'a')
                if '_of1' in sm.tag:
                    g.wait_ge(sm.h['o'], sm.tag['_of1'])
                gdma(g, sm, 'g', xbuf2[:],
                     x_in[i // 4, 128 + (i % 4) * 128:128 + (i % 4 + 1) * 128, :])
            emit_ln_tile(block, sm, ybf[i], lnbf2, scr2, stat,
                         v_waits=[('a', sm.c['a']), ('g', sm.c['g'])])

            @block.vector
            def _(v, i=i):
                sm.wait(v, 'g')
                sm.wait(v, 'v')
                ins = v.tensor_tensor(out=xbuf2[:], in0=xbuf2[:],
                                      in1=lnbf2[:], op=ALU.add)
                sm.inc(ins, 'v')

            @block.gpsimd
            def _(g, i=i, x1tgt=x1tgt):
                sm.wait(g, 'v')
                gdma(g, sm, 'o', x1tgt[i * 128:(i + 1) * 128, :], xbuf2[:])
                sm.tag['_of1'] = sm.c['o']

        emit_barrier(block, sm)
        ps3.close()
        es_c.close()
        es_kv.close()

        if stop_after <= 1:
            @block.sync
            def _(sy):
                for n in sm.h:
                    sm.wait(sy, n)
            es_glob.close()
            return nc

        sm.tag.clear()
        x2dr = nc.dram_tensor("x2dr", [ROWS, D], F, kind="Internal")
        x3dr = nc.dram_tensor("x3dr", [ROWS, D], F, kind="Internal")
        out = nc.dram_tensor("out", [ROWS, D], F, kind="ExternalOutput")

        w0T_in = nc.dram_tensor("w0T", [D, D], F, kind="ExternalInput")
        bias_in = nc.dram_tensor("bias", [1, D], F, kind="ExternalInput")
        lAT_in = nc.dram_tensor("lAT", [D, 48], F, kind="ExternalInput")
        lBT_in = nc.dram_tensor("lBT", [48, D], F, kind="ExternalInput")
        lsi_in = nc.dram_tensor("lsi", [1, 4], I32, kind="ExternalInput")
        c012_in = nc.dram_tensor("c012", [3, 1], F, kind="ExternalInput")
        e48_in = nc.dram_tensor("e48", [3, 48], F, kind="ExternalInput")
        scdr3 = nc.dram_tensor("scdr3", [3, 8], F, kind="Internal")

        es2 = ExitStack()
        s2 = lambda n, s, d: es2.enter_context(nc.sbuf_tensor(n, s, d))
        lnT = [s2(f"p2lnT{d_}", [128, ROWS], BF16) for d_ in range(32)]
        ybf = [s2(f"p2y{i}", [128, D], BF16) for i in range(NT)]
        wbuf = s2("p2wb", [128, 32 * 512], BF16)
        xbuf = s2("p2xb", [128, D], F)
        lnbf = s2("p2lnbf", [128, D], BF16)
        aTbf = s2("p2aT", [128, 32 * 48], BF16)
        bTbf = s2("p2bT", [48, D], BF16)
        biasbf = s2("p2bias", [1, D], BF16)
        rbf = s2("p2r", [48, ROWS], BF16)
        lsi_f = s2("p2lsif", [1, 4], F)
        lsi_i = s2("p2lsii", [1, 4], I32)
        sc = s2("p2sc", [1, 8], F)
        c012 = s2("p2c012", [3, 1], F)
        e48bf = s2("p2e48", [3, 48], BF16)
        wsel3 = s2("p2ws3", [3, 1], F)
        use3 = s2("p2use3", [3, 1], F)
        wsel3b = s2("p2ws3b", [3, 1], BF16)
        wsel48 = s2("p2ws48", [48, 1], F)
        ps = ExitStack()
        pst = lambda n, s, d: ps.enter_context(nc.psum_tensor(n, s, d))
        p_tr = [pst(f"p2ptr{i}", [128, 512], BF16) for i in range(2)]
        p_y = [pst(f"p2py{i}", [128, 512], F) for i in range(2)]
        p_r = pst("p2pr", [48, 512], F)
        p_s2 = pst("p2ps", [48, 1], F)

        @block.gpsimd
        def _(g):
            gdma(g, sm, 'g', biasbf[:], bias_in[:])
            gdma(g, sm, 'g', bass.AP(aTbf, 0, [[32 * 48, 128], [48, 32], [1, 48]]),
                 bass.AP(lAT_in, 0, [[48, 128], [128 * 48, 32], [1, 48]]))
            gdma(g, sm, 'g', bTbf[:], lBT_in[:])
            gdma(g, sm, 'g', lsi_i[:], lsi_in[:])
            gdma(g, sm, 'g', c012[:], c012_in[:])
            gdma(g, sm, 'g', e48bf[:], e48_in[:])

        @block.vector
        def _(v):
            sm.wait(v, 'g')
            v.tensor_copy(out=lsi_f[:], in_=lsi_i[:])
            sm.inc(v.engine_nop(), 'v'); sm.wait(v, 'v')
            v.tensor_single_scalar(out=sc[:, 0:4], in_=lsi_f[:], scalar=0.0,
                                   op=ALU.is_le)
            sm.inc(v.engine_nop(), 'v'); sm.wait(v, 'v')
            v.tensor_scalar(out=sc[:, 4:5], in0=sc[:, 3:4], scalar1=3.0,
                            scalar2=sc[:, 2:3], op0=ALU.add, op1=ALU.mult)
            v.tensor_scalar(out=sc[:, 5:6], in0=sc[:, 0:1], scalar1=-1.0,
                            scalar2=2.0, op0=ALU.mult, op1=ALU.add)
            sm.inc(v.engine_nop(), 'v'); sm.wait(v, 'v')
            v.tensor_scalar(out=sc[:, 5:6], in0=sc[:, 5:6],
                            scalar1=sc[:, 1:2], scalar2=None, op0=ALU.mult)
            sm.inc(v.engine_nop(), 'v'); sm.wait(v, 'v')
            v.tensor_scalar(out=sc[:, 5:6], in0=sc[:, 5:6],
                            scalar1=1.0, scalar2=sc[:, 0:1],
                            op0=ALU.mult, op1=ALU.add)
            v.tensor_scalar(out=sc[:, 6:7], in0=sc[:, 2:3], scalar1=-1.0,
                            scalar2=1.0, op0=ALU.mult, op1=ALU.add)
            sm.inc(v.engine_nop(), 'v'); sm.wait(v, 'v')
            v.tensor_scalar(out=sc[:, 5:6], in0=sc[:, 5:6],
                            scalar1=sc[:, 6:7], scalar2=None, op0=ALU.mult)
            sm.inc(v.engine_nop(), 'v'); sm.wait(v, 'v')
            v.tensor_scalar(out=sc[:, 4:5], in0=sc[:, 4:5],
                            scalar1=1.0, scalar2=sc[:, 5:6],
                            op0=ALU.mult, op1=ALU.add)
            sm.inc(v.engine_nop(), 'v'); sm.wait(v, 'v')
            v.tensor_scalar(out=sc[:, 5:6], in0=sc[:, 4:5], scalar1=-1.0,
                            scalar2=None, op0=ALU.add)
            sm.inc(v.engine_nop(), 'v'); sm.wait(v, 'v')
            v.tensor_single_scalar(out=sc[:, 6:7], in_=sc[:, 5:6], scalar=0.5,
                                   op=ALU.is_gt)
            v.tensor_scalar(out=sc[:, 7:8], in0=sc[:, 5:6], scalar1=-1.0,
                            scalar2=None, op0=ALU.add)
            sm.inc(v.engine_nop(), 'v'); sm.wait(v, 'v')
            ins = v.tensor_scalar(out=sc[:, 7:8], in0=sc[:, 7:8], scalar1=0.0,
                                  scalar2=2.0, op0=ALU.max, op1=ALU.min)
            sm.inc(ins, 'v')

        @block.gpsimd
        def _(g):
            sm.wait(g, 'v')
            for rr in range(3):
                gdma(g, sm, 'g', scdr3[rr:rr + 1, :], sc[:])

        @block.gpsimd
        def _(g):
            gdma(g, sm, 'g', wsel3[:], bass.AP(scdr3, 7, [[8, 3], [1, 1]]),
                 allow_slow_non_contiguous=True)
            gdma(g, sm, 'g', use3[:], bass.AP(scdr3, 6, [[8, 3], [1, 1]]),
                 allow_slow_non_contiguous=True)

        @block.vector
        def _(v):
            sm.wait(v, 'g')
            v.tensor_tensor(out=wsel3[:], in0=c012[:], in1=wsel3[:],
                            op=ALU.is_equal)
            sm.inc(v.engine_nop(), 'v'); sm.wait(v, 'v')
            v.tensor_tensor(out=wsel3[:], in0=wsel3[:], in1=use3[:],
                            op=ALU.mult)
            sm.inc(v.engine_nop(), 'v'); sm.wait(v, 'v')
            ins = v.tensor_copy(out=wsel3b[:], in_=wsel3[:])
            sm.inc(ins, 'v')

        @block.tensor
        def _(t):
            sm.wait(t, 'v')
            ins = t.matmul(p_s2[:], e48bf[:], wsel3b[:], start=True, stop=True)
            sm.inc(ins, 't')

        @block.scalar
        def _(a):
            sm.wait(a, 't')
            ins = a.activation(out=wsel48[:], in_=p_s2[:], func=AF.Copy)
            sm.inc(ins, 'a')
        a_wsel = sm.c['a']

        for i in range(NT):
            @block.gpsimd
            def _(g, i=i):
                if '_lnfree' in sm.tag:
                    g.wait_ge(sm.h['v'], sm.tag['_lnfree'])
                gdma(g, sm, 'g', xbuf[:], x1dr[i * 128:(i + 1) * 128, :])
            emit_ln_tile(block, sm, xbuf, lnbf, lnbf, stat,
                         v_waits=[('g', sm.c['g']), ('t', sm.c['t'])])
            sm.tag['_lnfree'] = sm.c['v']
            emit_transpose_tiles(block, sm, lnbf, lnT, i * 128, identbf, p_tr)

        for cc in range(2):
            @block.tensor
            def _(t, cc=cc):
                sm.wait(t, 'a')
                sm.wait(t, 'v')
                for d_ in range(32):
                    ins = t.matmul(p_r[:], aTbf[:, d_ * 48:(d_ + 1) * 48],
                                   lnT[d_][:, cc * 512:(cc + 1) * 512],
                                   start=(d_ == 0), stop=(d_ == 31))
                sm.inc(ins, 't')

            @block.scalar
            def _(a, cc=cc):
                sm.wait(a, 't')
                a.wait_ge(sm.h['a'], a_wsel)
                ins = a.activation(out=rbf[:, cc * 512:(cc + 1) * 512],
                                   in_=p_r[:], func=AF.Copy, scale=wsel48[:])
                sm.inc(ins, 'a')

        CH2 = 512
        for oc in range(D // CH2):
            wh = {}

            @block.gpsimd
            def _(g, oc=oc, wh=wh):
                for h in range(2):
                    if f'_wbh{h}' in sm.tag:
                        g.wait_ge(sm.h['t'], sm.tag[f'_wbh{h}'])
                    for q2 in range(2):
                        d0 = h * 16 + q2 * 8
                        fdma(g, sm, 'w',
                             bass.AP(wbuf, d0 * CH2,
                                     [[32 * CH2, 128], [CH2, 8], [1, CH2]]),
                             bass.AP(w0T_in, oc * CH2 + d0 * 128 * D,
                                     [[D, 128], [128 * D, 8], [1, CH2]]))
                    wh[h] = sm.c['w']

            for i in range(NT):
                @block.tensor
                def _(t, oc=oc, i=i, wh=wh):
                    t.wait_ge(sm.h['w'], wh[0])
                    if oc == 0 and i == 0:
                        sm.wait(t, 'a')
                        sm.wait(t, 'v')
                    pj = (oc * NT + i) % 2
                    if f'_yfree{pj}' in sm.tag:
                        t.wait_ge(sm.h['a'], sm.tag[f'_yfree{pj}'])
                    t.matmul(p_y[pj][:], onesbf[:],
                             biasbf[:, oc * CH2:(oc + 1) * CH2],
                             start=True, stop=False)
                    for d_ in range(16):
                        ins = t.matmul(p_y[pj][:],
                                       lnT[d_][:, i * 128:(i + 1) * 128],
                                       wbuf[:, d_ * CH2:(d_ + 1) * CH2],
                                       start=False, stop=False)
                    if i == NT - 1:
                        sm.tag['_wbh0'] = sm.inc(ins, 't')

                    t.wait_ge(sm.h['w'], wh[1])
                    for d_ in range(16, 32):
                        t.matmul(p_y[pj][:], lnT[d_][:, i * 128:(i + 1) * 128],
                                 wbuf[:, d_ * CH2:(d_ + 1) * CH2],
                                 start=False, stop=False)
                    ins = t.matmul(p_y[pj][:], rbf[:, i * 128:(i + 1) * 128],
                                   bTbf[:, oc * CH2:(oc + 1) * CH2],
                                   start=False, stop=True)
                    sm.inc(ins, 't')
                    if i == NT - 1:
                        sm.tag['_wbh1'] = sm.c['t']

                @block.scalar
                def _(a, oc=oc, i=i):
                    sm.wait(a, 't')
                    pj = (oc * NT + i) % 2
                    ins = a.activation(out=ybf[i][:, oc * CH2:(oc + 1) * CH2],
                                       in_=p_y[pj][:], func=AF.Copy)
                    sm.tag[f'_yfree{pj}'] = sm.inc(ins, 'a')

        for i in range(NT):
            @block.gpsimd
            def _(g, i=i):
                sm.wait(g, 'a')
                if '_ofree' in sm.tag:
                    g.wait_ge(sm.h['o'], sm.tag['_ofree'])
                gdma(g, sm, 'g', xbuf[:], x1dr[i * 128:(i + 1) * 128, :])
            emit_ln_tile(block, sm, ybf[i], lnbf, lnbf, stat,
                         v_waits=[('a', sm.c['a']), ('g', sm.c['g'])])

            @block.vector
            def _(v, i=i):
                sm.wait(v, 'g')
                sm.wait(v, 'v')
                ins = v.tensor_tensor(out=xbuf[:], in0=xbuf[:], in1=lnbf[:],
                                      op=ALU.add)
                sm.inc(ins, 'v')

            @block.gpsimd
            def _(g, i=i):
                sm.wait(g, 'v')
                tgt = out if stop_after <= 2 else x2dr
                gdma(g, sm, 'o', tgt[i * 128:(i + 1) * 128, :], xbuf[:])
                sm.tag['_ofree'] = sm.c['o']

        emit_barrier(block, sm)
        sm.tag.clear()
        ps.close()
        es2.close()

        if stop_after <= 2:
            @block.sync
            def _(sy):
                for n in sm.h:
                    sm.wait(sy, n)
            es_glob.close()
            return nc

        lam_in = nc.dram_tensor("lam", [64, 2], F, kind="ExternalInput")
        pri_in = nc.dram_tensor("pri", [64, 2], F, kind="ExternalInput")
        qri_in = nc.dram_tensor("qri", [64, 2], F, kind="ExternalInput")
        vctre_in = nc.dram_tensor("vctre", [64, 64], F, kind="ExternalInput")
        vctim_in = nc.dram_tensor("vctim", [64, 64], F, kind="ExternalInput")
        ctt_in = nc.dram_tensor("ctt", [64, 2], F, kind="ExternalInput")
        bp_in = nc.dram_tensor("bp", [64, 1], F, kind="ExternalInput")
        dsc_in = nc.dram_tensor("dsc", [1, 1], F, kind="ExternalInput")
        lstep_in = nc.dram_tensor("lstep", [1, 1], F, kind="ExternalInput")
        c1bre_in = nc.dram_tensor("c1bre", [64, D], F, kind="ExternalInput")
        c1bim_in = nc.dram_tensor("c1bim", [64, D], F, kind="ExternalInput")
        c2re_in = nc.dram_tensor("c2re", [64, 64], F, kind="ExternalInput")
        c2im_in = nc.dram_tensor("c2im", [64, 64], F, kind="ExternalInput")
        wc_in = nc.dram_tensor("wc", [64, 64], F, kind="ExternalInput")
        ws_in = nc.dram_tensor("ws", [64, 64], F, kind="ExternalInput")
        wsn_in = nc.dram_tensor("wsn", [64, 64], F, kind="ExternalInput")
        twre_in = nc.dram_tensor("twre", [64, 64], F, kind="ExternalInput")
        twim_in = nc.dram_tensor("twim", [64, 64], F, kind="ExternalInput")
        idf_in = nc.dram_tensor("idf", [64, 64], F, kind="ExternalInput")
        onesf_in = nc.dram_tensor("onesf", [1, 128], F, kind="ExternalInput")
        kdr = nc.dram_tensor("kdr", [64, 64], F, kind="Internal")
        odr = nc.dram_tensor("odr", [16, D], F, kind="Internal")
        p3tgt = out if stop_after <= 3 else x3dr

        def dstep(eng, ins, n):
            sm.inc(ins, n)
            sm.wait(eng, n)

        es3 = ExitStack()
        def s3alloc(*a, **k):
            return es3.enter_context(nc.sbuf_tensor(*a, **k))
        ps3s = ExitStack()
        def p3alloc(*a, **k):
            return ps3s.enter_context(nc.psum_tensor(*a, **k))
        e_all = s3alloc("p3eall", [128, 35 * 128], BF16)
        kpad = s3alloc("p3kpad", [1, 4608], BF16)
        onesf = s3alloc("p3onesf", [1, 128], F)
        dv = s3alloc("p3dv", [128, 1], F)
        sc1 = s3alloc("p3sc1", [1, 4], F)
        p_tr = [p3alloc(f"p3ptrA{i}", [128, 512], BF16) for i in range(2)]
        p_y = [p3alloc(f"p3pyA{i}", [128, 512], F) for i in range(2)]

        from contextlib import ExitStack
        _es = ExitStack()
        c1re = _es.enter_context(nc.sbuf_tensor("c1re", [64, D], F))
        c1im = _es.enter_context(nc.sbuf_tensor("c1im", [64, D], F))
        dre = _es.enter_context(nc.sbuf_tensor("dre", [64, D], F))
        dim_ = _es.enter_context(nc.sbuf_tensor("dim_", [64, D], F))
        tmpl = _es.enter_context(nc.sbuf_tensor("tmpl", [64, D], F))
        winv = _es.enter_context(nc.sbuf_tensor("winv", [64, D], F))
        wre = _es.enter_context(nc.sbuf_tensor("wre", [64, D], F))
        wim = _es.enter_context(nc.sbuf_tensor("wim", [64, D], F))
        lamv = _es.enter_context(nc.sbuf_tensor("lamv", [64, 2], F))
        priv = _es.enter_context(nc.sbuf_tensor("priv", [64, 2], F))
        qriv = _es.enter_context(nc.sbuf_tensor("qriv", [64, 2], F))
        cttv = _es.enter_context(nc.sbuf_tensor("cttv", [64, 2], F))
        vctre = _es.enter_context(nc.sbuf_tensor("vctre_s", [64, 64], F))
        vctim = _es.enter_context(nc.sbuf_tensor("vctim_s", [64, 64], F))
        bpv = _es.enter_context(nc.sbuf_tensor("bpv", [64, 1], F))
        bv = _es.enter_context(nc.sbuf_tensor("bv", [64, 2], F))
        vstk = _es.enter_context(nc.sbuf_tensor("vstk", [64, 8], F))
        ls1 = _es.enter_context(nc.sbuf_tensor("ls1", [1, 4], F))
        s2v = _es.enter_context(nc.sbuf_tensor("s2v", [64, 1], F))
        ob = _es.enter_context(nc.sbuf_tensor("ob", [8, 512], F))
        km = _es.enter_context(nc.sbuf_tensor("km", [64, 16 * 64], F))
        cx = _es.enter_context(nc.sbuf_tensor("cx", [64, 24 * 64], F))
        wcv = _es.enter_context(nc.sbuf_tensor("wcv", [64, 64], F))
        wsv = _es.enter_context(nc.sbuf_tensor("wsv", [64, 64], F))
        wsnv = _es.enter_context(nc.sbuf_tensor("wsnv", [64, 64], F))
        twrev = _es.enter_context(nc.sbuf_tensor("twrev", [64, 64], F))
        twimv = _es.enter_context(nc.sbuf_tensor("twimv", [64, 64], F))
        c2rev = _es.enter_context(nc.sbuf_tensor("c2rev", [64, 64], F))
        c2imv = _es.enter_context(nc.sbuf_tensor("c2imv", [64, 64], F))
        idfv = _es.enter_context(nc.sbuf_tensor("idfv", [64, 64], F))
        k2v = _es.enter_context(nc.sbuf_tensor("k2v", [64, 64], F))
        p_s = _es.enter_context(nc.psum_tensor("p_s", [64, 2], F))
        p_c = _es.enter_context(nc.psum_tensor("p_c", [8, 512], F))
        p_f1 = _es.enter_context(nc.psum_tensor("p_f1", [64, 64], F))
        p_f2 = _es.enter_context(nc.psum_tensor("p_f2", [64, 64], F))
        if True:
            @block.gpsimd
            def _(g):
                for dst, src_ in [(c1re, c1bre_in), (c1im, c1bim_in),
                                  (lamv, lam_in), (priv, pri_in),
                                  (qriv, qri_in), (cttv, ctt_in),
                                  (vctre, vctre_in), (vctim, vctim_in),
                                  (bpv, bp_in), (wcv, wc_in), (wsv, ws_in),
                                  (wsnv, wsn_in), (twrev, twre_in),
                                  (twimv, twim_in), (c2rev, c2re_in),
                                  (c2imv, c2im_in), (idfv, idf_in),
                                  (onesf, onesf_in)]:
                    gdma(g, sm, 'g', dst[:], src_[:])
                gdma(g, sm, 'g', ls1[:, 0:1], lstep_in[:])

            @block.scalar
            def _(a):
                sm.wait(a, 'g')
                ins = a.activation(out=ls1[:, 1:2], in_=ls1[:, 0:1], func=AF.Exp)
                dstep(a, ins, 'a')

            @block.vector
            def _(v):
                sm.wait(v, 'a')
                ins = v.reciprocal(out=ls1[:, 2:3], in_=ls1[:, 1:2])
                dstep(v, ins, 'v')
                ins = v.tensor_scalar(out=ls1[:, 3:4], in0=ls1[:, 2:3],
                                      scalar1=2.0, scalar2=None, op0=ALU.mult)
                dstep(v, ins, 'v')

            @block.tensor
            def _(t):
                sm.wait(t, 'v')
                t.matmul(p_s[:, 0:1], onesf[:, 0:64], ls1[:, 3:4],
                         start=True, stop=True)
                t.matmul(p_f1[:, 0:1], vctre[:], bpv[:], start=True, stop=True)
                ins = t.matmul(p_f1[:, 1:2], vctim[:], bpv[:], start=True,
                               stop=True)
                sm.inc(ins, 't')

            @block.scalar
            def _(a):
                sm.wait(a, 't')
                a.activation(out=s2v[:], in_=p_s[:, 0:1], func=AF.Copy)
                ins = a.activation(out=bv[:], in_=p_f1[:, 0:2], func=AF.Copy)
                sm.inc(ins, 'a')

            @block.vector
            def _(v):
                sm.wait(v, 'a')
                v.tensor_scalar(out=dre[:], in0=c1re[:], scalar1=s2v[:],
                                scalar2=lamv[:, 0:1], op0=ALU.mult,
                                op1=ALU.subtract)
                ins = v.tensor_scalar(out=dim_[:], in0=c1im[:], scalar1=s2v[:],
                                      scalar2=lamv[:, 1:2], op0=ALU.mult,
                                      op1=ALU.subtract)
                dstep(v, ins, 'v')
                v.tensor_tensor(out=tmpl[:], in0=dre[:], in1=dre[:],
                                op=ALU.mult)
                ins = v.scalar_tensor_tensor(out=winv[:], in0=dim_[:],
                                             scalar=1.0, in1=dim_[:],
                                             op0=ALU.mult, op1=ALU.mult)
                dstep(v, ins, 'v')
                ins = v.tensor_tensor(out=tmpl[:], in0=tmpl[:], in1=winv[:],
                                      op=ALU.add)
                dstep(v, ins, 'v')
                ins = v.reciprocal(out=winv[:], in_=tmpl[:])
                dstep(v, ins, 'v')
                ins = v.tensor_tensor(out=wre[:], in0=dre[:], in1=winv[:],
                                       op=ALU.mult)
                dstep(v, ins, 'v')
                ins = v.scalar_tensor_tensor(out=wim[:], in0=dim_[:],
                                             scalar=-1.0, in1=winv[:],
                                             op0=ALU.mult, op1=ALU.mult)
                dstep(v, ins, 'v')
                pairs = [(cttv, bv, 0), (cttv, priv, 2), (qriv, bv, 4),
                         (qriv, priv, 6)]
                for (xx, yy, col) in pairs:
                    ins = v.tensor_tensor(out=vstk[:, col:col + 1],
                                          in0=xx[:, 0:1], in1=yy[:, 0:1],
                                          op=ALU.mult)
                    dstep(v, ins, 'v')
                    ins = v.tensor_tensor(out=tmpl[:, 0:1], in0=xx[:, 1:2],
                                          in1=yy[:, 1:2], op=ALU.mult)
                    dstep(v, ins, 'v')
                    ins = v.tensor_tensor(out=vstk[:, col:col + 1],
                                          in0=vstk[:, col:col + 1],
                                          in1=tmpl[:, 0:1], op=ALU.add)
                    dstep(v, ins, 'v')
                    ins = v.tensor_tensor(out=vstk[:, col + 1:col + 2],
                                          in0=xx[:, 0:1], in1=yy[:, 1:2],
                                          op=ALU.mult)
                    dstep(v, ins, 'v')
                    ins = v.tensor_tensor(out=tmpl[:, 0:1], in0=xx[:, 1:2],
                                          in1=yy[:, 0:1], op=ALU.mult)
                    dstep(v, ins, 'v')
                    ins = v.tensor_tensor(out=vstk[:, col + 1:col + 2],
                                          in0=vstk[:, col + 1:col + 2],
                                          in1=tmpl[:, 0:1], op=ALU.subtract)
                    dstep(v, ins, 'v')

            for which, rhs in [(0, wre), (1, wim)]:
                for ch in range(8):
                    @block.tensor
                    def _(t, which=which, ch=ch, rhs=rhs):
                        if which == 0 and ch == 0:
                            sm.wait(t, 'v')
                        else:
                            t.wait_ge(sm.h['a'], sm.tag['_cev'])
                        ins = t.matmul(p_c[:], vstk[:],
                                       rhs[:, ch * 512:(ch + 1) * 512],
                                       start=True, stop=True)
                        sm.inc(ins, 't')

                    @block.scalar
                    def _(a, which=which, ch=ch):
                        sm.wait(a, 't')
                        sm.wait(a, 'w')
                        ins = a.activation(out=ob[:], in_=p_c[:], func=AF.Copy)
                        sm.tag['_cev'] = sm.inc(ins, 'a')

                    @block.gpsimd
                    def _(g, which=which, ch=ch):
                        sm.wait(g, 'a')
                        gdma(g, sm, 'w', odr[which * 8:(which + 1) * 8,
                                            ch * 512:(ch + 1) * 512], ob[:])

            @block.gpsimd
            def _(g):
                sm.wait(g, 'w')
                for c in range(16):
                    gdma(g, sm, 'g', km[:, c * 64:(c + 1) * 64], bass.AP(odr, c * D, [[64, 64], [1, 64]]))

            def KM(i):
                return km[:, i * 64:(i + 1) * 64]

            def CX(i):
                return cx[:, i * 64:(i + 1) * 64]

            @block.vector
            def _(v):
                sm.wait(v, 'g')
                for j in range(4):
                    v.tensor_tensor(out=CX(2 * j), in0=KM(2 * j),
                                    in1=KM(8 + 2 * j + 1), op=ALU.subtract)
                    ins = v.tensor_tensor(out=CX(2 * j + 1), in0=KM(2 * j + 1),
                                          in1=KM(8 + 2 * j), op=ALU.add)
                dstep(v, ins, 'v')
                v.tensor_scalar(out=CX(8), in0=CX(6), scalar1=1.0,
                                scalar2=None, op0=ALU.add)
                ins = v.tensor_copy(out=CX(9), in_=CX(7))
                dstep(v, ins, 'v')
                v.tensor_tensor(out=CX(10), in0=CX(2), in1=CX(4), op=ALU.mult)
                v.tensor_tensor(out=CX(12), in0=CX(3), in1=CX(5), op=ALU.mult)
                v.tensor_tensor(out=CX(11), in0=CX(2), in1=CX(5), op=ALU.mult)
                ins = v.tensor_tensor(out=CX(13), in0=CX(3), in1=CX(4),
                                      op=ALU.mult)
                dstep(v, ins, 'v')
                v.tensor_tensor(out=CX(10), in0=CX(10), in1=CX(12),
                                op=ALU.subtract)
                v.tensor_tensor(out=CX(11), in0=CX(11), in1=CX(13), op=ALU.add)
                v.tensor_tensor(out=CX(22), in0=CX(8), in1=CX(8), op=ALU.mult)
                ins = v.tensor_tensor(out=CX(23), in0=CX(9), in1=CX(9),
                                      op=ALU.mult)
                dstep(v, ins, 'v')
                ins = v.tensor_tensor(out=CX(22), in0=CX(22), in1=CX(23),
                                      op=ALU.add)
                dstep(v, ins, 'v')
                ins = v.reciprocal(out=CX(23), in_=CX(22))
                dstep(v, ins, 'v')
                v.tensor_tensor(out=CX(14), in0=CX(10), in1=CX(8), op=ALU.mult)
                v.tensor_tensor(out=CX(16), in0=CX(11), in1=CX(9), op=ALU.mult)
                v.tensor_tensor(out=CX(15), in0=CX(11), in1=CX(8), op=ALU.mult)
                ins = v.tensor_tensor(out=CX(17), in0=CX(10), in1=CX(9),
                                      op=ALU.mult)
                dstep(v, ins, 'v')
                v.tensor_tensor(out=CX(14), in0=CX(14), in1=CX(16), op=ALU.add)
                ins = v.tensor_tensor(out=CX(15), in0=CX(15), in1=CX(17),
                                      op=ALU.subtract)
                dstep(v, ins, 'v')
                v.tensor_tensor(out=CX(14), in0=CX(14), in1=CX(23), op=ALU.mult)
                ins = v.tensor_tensor(out=CX(15), in0=CX(15), in1=CX(23),
                                      op=ALU.mult)
                dstep(v, ins, 'v')
                v.tensor_tensor(out=CX(16), in0=CX(0), in1=CX(14),
                                op=ALU.subtract)
                ins = v.tensor_tensor(out=CX(17), in0=CX(1), in1=CX(15),
                                op=ALU.subtract)
                dstep(v, ins, 'v')
                v.tensor_tensor(out=CX(18), in0=c2rev[:], in1=CX(16),
                                op=ALU.mult)
                v.tensor_tensor(out=CX(20), in0=c2imv[:], in1=CX(17),
                                op=ALU.mult)
                v.tensor_tensor(out=CX(19), in0=c2rev[:], in1=CX(17),
                                op=ALU.mult)
                ins = v.tensor_tensor(out=CX(21), in0=c2imv[:], in1=CX(16),
                                op=ALU.mult)
                dstep(v, ins, 'v')
                v.tensor_tensor(out=CX(18), in0=CX(18), in1=CX(20),
                                op=ALU.subtract)
                ins = v.tensor_tensor(out=CX(19), in0=CX(19), in1=CX(21),
                                      op=ALU.add)
                sm.inc(ins, 'v')

            @block.tensor
            def _(t):
                sm.wait(t, 'v')
                t.matmul(p_f1[:, 0:64], wcv[:], CX(18), start=True, stop=False)
                t.matmul(p_f1[:, 0:64], wsnv[:], CX(19), start=False, stop=True)
                t.matmul(p_f2[:, 0:64], wsv[:], CX(18), start=True, stop=False)
                ins = t.matmul(p_f2[:, 0:64], wcv[:], CX(19), start=False,
                               stop=True)
                sm.inc(ins, 't')

            @block.scalar
            def _(a):
                sm.wait(a, 't')
                a.activation(out=CX(0), in_=p_f1[:, 0:64], func=AF.Copy)
                ins = a.activation(out=CX(1), in_=p_f2[:, 0:64], func=AF.Copy)
                sm.inc(ins, 'a')

            @block.vector
            def _(v):
                sm.wait(v, 'a')
                v.tensor_tensor(out=CX(2), in0=CX(0), in1=twrev[:], op=ALU.mult)
                v.tensor_tensor(out=CX(4), in0=CX(1), in1=twimv[:], op=ALU.mult)
                v.tensor_tensor(out=CX(3), in0=CX(0), in1=twimv[:], op=ALU.mult)
                ins = v.tensor_tensor(out=CX(5), in0=CX(1), in1=twrev[:], op=ALU.mult)
                dstep(v, ins, 'v')
                v.tensor_tensor(out=CX(2), in0=CX(2), in1=CX(4), op=ALU.subtract)
                ins = v.tensor_tensor(out=CX(3), in0=CX(3), in1=CX(5),
                                      op=ALU.add)
                sm.inc(ins, 'v')

            @block.tensor
            def _(t):
                sm.wait(t, 'v')
                t.transpose(p_f1[:, 0:64], CX(2), idfv[:])
                ins = t.transpose(p_f2[:, 0:64], CX(3), idfv[:])
                sm.inc(ins, 't')

            @block.scalar
            def _(a):
                sm.wait(a, 't')
                a.activation(out=CX(6), in_=p_f1[:, 0:64], func=AF.Copy)
                ins = a.activation(out=CX(7), in_=p_f2[:, 0:64], func=AF.Copy)
                sm.inc(ins, 'a')
            a_tr2 = sm.c['a']

            @block.tensor
            def _(t):
                sm.wait(t, 'a')
                t.matmul(p_f1[:, 0:64], wcv[:], CX(6), start=True, stop=False)
                ins = t.matmul(p_f1[:, 0:64], wsnv[:], CX(7), start=False,
                               stop=True)
                sm.inc(ins, 't')

            @block.scalar
            def _(a):
                sm.wait(a, 't')
                ins = a.activation(out=k2v[:], in_=p_f1[:, 0:64], func=AF.Copy,
                                   scale=1.0 / 4096.0)
                sm.inc(ins, 'a')

            @block.gpsimd
            def _(g):
                sm.wait(g, 'a')
                gdma(g, sm, 'w', kdr[:], k2v[:])
                sm.wait(g, 'w')
                ins = g.memset(kpad[:], 0.0)
                sm.inc(ins, 'm')
                sm.wait(g, 'm')
                gdma(g, sm, 'w', kpad[:, 512:4608],
                     bass.AP(kdr, 0, [[4096, 1], [1, 4096]]))
                sm.wait(g, 'w')
                gdma(g, sm, 'w', e_all[:],
                     bass.AP(kpad, 128, [[4608, 1], [-1, 128], [1, 4480]]))
                sm.wait(g, 'w')

        _es.close()
        uT = [s3alloc(f"p3uT{d}", [128, ROWS], BF16)
              for d in range(32)]
        lnnat = [s3alloc(f"p3ln{i}", [128, D], BF16)
                 for i in range(NT)]
        ysb = s3alloc("p3ysb", [128, D], F)
        xbuf = s3alloc("p3xb", [128, D], F)
        lnbf = s3alloc("p3lnbf", [128, D], BF16)
        scr_bf = s3alloc("p3scr", [128, D], BF16)
        stat3 = s3alloc("p3stat", [128, 12], F)
        for i in range(NT):
            @block.gpsimd
            def _(g, i=i):
                if '_lnfree' in sm.tag:
                    g.wait_ge(sm.h['v'], sm.tag['_lnfree'])
                gdma(g, sm, 'g', xbuf[:], x2dr[i * 128:(i + 1) * 128, :])
            emit_ln_tile(block, sm, xbuf, lnnat[i], scr_bf, stat3,
                         v_waits=[('g', sm.c['g']), ('t', sm.c['t'])])
            sm.tag['_lnfree'] = sm.c['v']
            emit_transpose_tiles(block, sm, lnnat[i], uT, i * 128, identbf,
                                 p_tr)

        @block.gpsimd
        def _(g):
            gdma(g, sm, 'g', sc1[:, 0:1], dsc_in[:])

        @block.tensor
        def _(t):
            sm.wait(t, 'g')
            ins = t.matmul(p_y[0][:, 0:1], onesf[:], sc1[:, 0:1],
                           start=True, stop=True)
            sm.inc(ins, 't')

        @block.scalar
        def _(a):
            sm.wait(a, 't')
            ins = a.activation(out=dv[:], in_=p_y[0][:, 0:1], func=AF.Copy)
            sm.inc(ins, 'a')

        for i in range(NT):
            for grp in range(8):
                @block.tensor
                def _(t, i=i, grp=grp):
                    if i == 0 and grp == 0:
                        sm.wait(t, 'a')
                        sm.wait(t, 'v')
                        sm.wait(t, 'w')
                    j = grp % 2
                    if f'_pyfree{j}' in sm.tag:
                        t.wait_ge(sm.h['a'], sm.tag[f'_pyfree{j}'])
                    last_c = min(4 * grp + 3, 31)
                    for c in range(last_c + 1):
                        ins = t.matmul(
                            p_y[j][:],
                            uT[c][:, i * 128:(i + 1) * 128],
                            e_all[:, (4 * grp - c + 3) * 128:
                                  (4 * grp - c + 3) * 128 + 512],
                            start=(c == 0), stop=(c == last_c))
                    sm.inc(ins, 't')

                @block.scalar
                def _(a, i=i, grp=grp):
                    sm.wait(a, 't')
                    if grp == 0 and '_ysbfree' in sm.tag:
                        a.wait_ge(sm.h['v'], sm.tag['_ysbfree'])
                    j = grp % 2
                    ins = a.activation(out=ysb[:, grp * 512:(grp + 1) * 512],
                                       in_=p_y[j][:], func=AF.Copy)
                    sm.tag[f'_pyfree{j}'] = sm.inc(ins, 'a')

            @block.vector
            def _(v, i=i):
                sm.wait(v, 'a')
                sm.wait(v, 'v')
                ins = v.scalar_tensor_tensor(out=ysb[:], in0=lnnat[i][:],
                                             scalar=dv[:], in1=ysb[:],
                                             op0=ALU.mult, op1=ALU.add)
                sm.inc(ins, 'v')

            @block.gpsimd
            def _(g, i=i):
                sm.wait(g, 'v')
                if '_ofree' in sm.tag:
                    g.wait_ge(sm.h['o'], sm.tag['_ofree'])
                gdma(g, sm, 'g', xbuf[:], x2dr[i * 128:(i + 1) * 128, :])
            emit_ln_tile(block, sm, ysb, lnbf, scr_bf, stat3,
                         v_waits=[('g', sm.c['g']), ('v', sm.c['v'])])
            sm.tag['_ysbfree'] = sm.c['v']

            @block.vector
            def _(v, i=i):
                sm.wait(v, 'g')
                sm.wait(v, 'v')
                ins = v.tensor_tensor(out=xbuf[:], in0=xbuf[:], in1=lnbf[:],
                                      op=ALU.add)
                sm.inc(ins, 'v')

            @block.gpsimd
            def _(g, i=i):
                sm.wait(g, 'v')
                gdma(g, sm, 'o', p3tgt[i * 128:(i + 1) * 128, :], xbuf[:])
            sm.tag['_ofree'] = sm.c['o']


        emit_barrier(block, sm)
        sm.tag.clear()
        ps3s.close()
        es3.close()

        if stop_after <= 3:
            @block.sync
            def _(sy):
                for n in sm.h:
                    sm.wait(sy, n)
            es_glob.close()
            return nc

        wg_in = nc.dram_tensor("wg", [NI, 128, D], I8, kind="ExternalInput")
        wu_in = nc.dram_tensor("wu", [NI, 128, D], I8, kind="ExternalInput")
        wd_in = nc.dram_tensor("wd", [11008, D], I8, kind="ExternalInput")
        sg_in = nc.dram_tensor("sg", [128, NI], F32, kind="ExternalInput")
        su_in = nc.dram_tensor("su", [128, NI], F32, kind="ExternalInput")
        sd_in = nc.dram_tensor("sd", [1, D], F32, kind="ExternalInput")

        es4 = ExitStack()
        def s4alloc(*a, **k):
            return es4.enter_context(nc.sbuf_tensor(*a, **k))
        ps4s = ExitStack()
        def p4alloc(*a, **k):
            return ps4s.enter_context(nc.psum_tensor(*a, **k))
        HT = 1024
        NTH = HT // 128

        ybf = [s4alloc(f"p4y{i}", [128, D], BF16) for i in range(NTH)]
        wdb = [s4alloc(f"p4wdb{i}", [128, 512], BF16) for i in range(4)]
        gact = s4alloc("p4gact", [128, HT], BF16)
        uact = s4alloc("p4uact", [128, HT], BF16)
        hbuf = [s4alloc(f"p4hbf{j}", [128, HT], BF16) for j in range(2)]
        sgs = s4alloc("p4sgs", [128, NI], F32)
        sus = s4alloc("p4sus", [128, NI], F32)
        sdb = s4alloc("p4sdb", [128, D], BF16)
        sdrow = s4alloc("p4sdrow", [1, 512], BF16)
        xbuf = s4alloc("p4xb", [128, D], F32)
        lnbf = s4alloc("p4lnbf", [128, D], BF16)
        stat4 = s4alloc("p4stat", [128, 12], F32)
        hdr = nc.dram_tensor("hdr", [NI * 128, HT], BF16, kind="Internal")

        es4b = ExitStack()
        def s4b(*a, **k):
            return es4b.enter_context(nc.sbuf_tensor(*a, **k))
        lnT = [s4b(f"p4lnT{d}", [128, HT], BF16) for d in range(32)]
        wgp = [s4b(f"p4wg{j}", [128, 2048], BF16) for j in range(2)]
        wup = [s4b(f"p4wu{j}", [128, 2048], BF16) for j in range(2)]
        ps4a = ExitStack()
        p_tr = [ps4a.enter_context(
            nc.psum_tensor(f"p4ptr{i}", [128, 512], BF16)) for i in range(2)]

        @block.gpsimd
        def _(g):
            gdma(g, sm, 'g', sgs[:], sg_in[:])
            gdma(g, sm, 'g', sus[:], su_in[:])

        for i in range(NTH):
            @block.gpsimd
            def _(g, i=i):
                if '_lnfree' in sm.tag:
                    g.wait_ge(sm.h['v'], sm.tag['_lnfree'])
                gdma(g, sm, 'g', xbuf[:], x3dr[i * 128:(i + 1) * 128, :])
            emit_ln_tile(block, sm, xbuf, lnbf, lnbf, stat4,
                         v_waits=[('g', sm.c['g']), ('t', sm.c['t'])])
            sm.tag['_lnfree'] = sm.c['v']
            emit_transpose_tiles(block, sm, lnbf, lnT, i * 128, identbf,
                                 p_tr)

        ps4a.close()
        p_g = [p4alloc(f"p4pg{j}", [128, HT], F32) for j in range(2)]
        p_u = p4alloc("p4pu", [128, HT], F32)

        for q in range(8):
            @block.gpsimd
            def _(g, q=q):
                if q > 0:
                    g.wait_ge(sm.h['t'], sm.tag['_sdt'])
                gdma(g, sm, 'g', sdrow[:], sd_in[:, q * 512:(q + 1) * 512])

            @block.tensor
            def _(t, q=q):
                sm.wait(t, 'g')
                if q == 0:
                    sm.wait(t, 'v')
                if q > 0:
                    t.wait_ge(sm.h['v'], sm.tag['_sdq'])
                ins = t.matmul(p_g[0][:, 0:512], onesbf[:], sdrow[:],
                               start=True, stop=True)
                sm.tag['_sdt'] = sm.inc(ins, 't')

            @block.vector
            def _(v, q=q):
                sm.wait(v, 't')
                ins = v.tensor_copy(out=sdb[:, q * 512:(q + 1) * 512],
                                    in_=p_g[0][:, 0:512])
                sm.tag['_sdq'] = sm.inc(ins, 'v')

        for it in range(NI):
            w_slice = {}
            jp = it % 2

            @block.gpsimd
            def _(g, it=it, w_slice=w_slice):
                for sl in range(2):
                    for nm, ring, src in (('g', wgp, wg_in),
                                          ('u', wup, wu_in)):
                        key = f'_wf{nm}{sl}'
                        if key in sm.tag:
                            g.wait_ge(sm.h['t'], sm.tag[key])
                        w_slice[(nm, sl)] = fdma(
                            g, sm, 'w', ring[sl][:],
                            bass.AP(src, it * 128 * D + sl * 2048,
                                    [[D, 128], [1, 2048]]))

            @block.tensor
            def _(t, it=it, jp=jp, w_slice=w_slice):
                if it == 0:
                    sm.wait(t, 'v')
                if f'_pgfree{jp}' in sm.tag:
                    t.wait_ge(sm.h['a'], sm.tag[f'_pgfree{jp}'])
                if f'_pufree' in sm.tag:
                    t.wait_ge(sm.h['v'], sm.tag['_pufree'])
                for sl in range(2):
                    t.wait_ge(sm.h['w'], w_slice[('g', sl)])
                    for dt in range(16):
                        for nh in range(2):
                            ins = t.matmul(
                                p_g[jp][:, nh * 512:(nh + 1) * 512],
                                wgp[sl][:, dt * 128:(dt + 1) * 128],
                                lnT[sl * 16 + dt][:, nh * 512:(nh + 1) * 512],
                                start=(sl == 0 and dt == 0),
                                stop=(sl == 1 and dt == 15))
                    sm.tag[f'_wfg{sl}'] = sm.inc(ins, 't')
                    t.wait_ge(sm.h['w'], w_slice[('u', sl)])
                    for dt in range(16):
                        for nh in range(2):
                            ins = t.matmul(
                                p_u[:, nh * 512:(nh + 1) * 512],
                                wup[sl][:, dt * 128:(dt + 1) * 128],
                                lnT[sl * 16 + dt][:, nh * 512:(nh + 1) * 512],
                                start=(sl == 0 and dt == 0),
                                stop=(sl == 1 and dt == 15))
                    sm.tag[f'_wfu{sl}'] = sm.inc(ins, 't')

            @block.scalar
            def _(a, it=it, jp=jp):
                sm.wait(a, 't')
                if '_hmul' in sm.tag:
                    a.wait_ge(sm.h['v'], sm.tag['_hmul'])
                ins = a.activation(out=gact[:], in_=p_g[jp][:], func=AF.Silu,
                                   scale=sgs[:, it:it + 1])
                sm.tag[f'_pgfree{jp}'] = sm.inc(ins, 'a')

            @block.vector
            def _(v, it=it):
                sm.wait(v, 't')
                if '_hmul' in sm.tag:
                    v.wait_ge(sm.h['v'], sm.tag['_hmul'])
                ins = v.tensor_scalar(out=uact[:], in0=p_u[:],
                                      scalar1=sus[:, it:it + 1],
                                      scalar2=None, op0=ALU.mult)
                sm.tag['_pufree'] = sm.inc(ins, 'v')
                jh = it % 2
                if f'_hdw{jh}' in sm.tag:
                    v.wait_ge(sm.h['u'], sm.tag[f'_hdw{jh}'])
                sm.wait(v, 'a')
                sm.wait(v, 'v')
                ins = v.tensor_tensor(out=hbuf[jh][:], in0=gact[:],
                                      in1=uact[:], op=ALU.mult)
                sm.tag['_hmul'] = sm.inc(ins, 'v')

            @block.gpsimd
            def _(g, it=it):
                sm.wait(g, 'v')
                gdma(g, sm, 'u', hdr[it * 128:(it + 1) * 128, :],
                     hbuf[it % 2][:])
                sm.tag[f'_hdw{it % 2}'] = sm.c['u']

        emit_barrier(block, sm,
                     engines=('gpsimd', 'vector', 'scalar', 'tensor', 'sync'))
        sm.tag.clear()
        es4b.close()
        ps4s.close()
        ps4c = ExitStack()
        p_y = [ps4c.enter_context(nc.psum_tensor(f"p4py{i}", [128, 512], F32))
               for i in range(NTH)]
        es4c = ExitStack()
        hbb = [es4c.enter_context(nc.sbuf_tensor(f"p4hbb{j}", [128, HT],
                                                 BF16)) for j in range(4)]

        for hc in range(8):
            for it in range(NI):
                @block.sync
                def _(s, hc=hc, it=it):
                    jh = it % 4
                    if f'_hbfree{jh}' in sm.tag:
                        s.wait_ge(sm.h['t'], sm.tag[f'_hbfree{jh}'])
                    fdma(s, sm, 'u', hbb[jh][:],
                         hdr[it * 128:(it + 1) * 128, :])
                u_c = sm.c['u']

                @block.gpsimd
                def _(g, hc=hc, it=it):
                    j = it % 4
                    if f'_wdfree{j}' in sm.tag:
                        g.wait_ge(sm.h['t'], sm.tag[f'_wdfree{j}'])
                    fdma(g, sm, 'w', wdb[j][:],
                         wd_in[it * 128:(it + 1) * 128,
                               hc * 512:(hc + 1) * 512])
                w_c = sm.c['w']

                @block.tensor
                def _(t, hc=hc, it=it, w_c=w_c, u_c=u_c):
                    t.wait_ge(sm.h['w'], w_c)
                    t.wait_ge(sm.h['u'], u_c)
                    if it == 0 and '_yevac' in sm.tag:
                        t.wait_ge(sm.h['v'], sm.tag['_yevac'])
                    for tt in range(NTH):
                        ins = t.matmul(p_y[tt][:],
                                       hbb[it % 4][:, tt * 128:(tt + 1) * 128],
                                       wdb[it % 4][:],
                                       start=(it == 0), stop=(it == NI - 1))
                    sm.tag[f'_wdfree{it % 4}'] = sm.inc(ins, 't')
                    sm.tag[f'_hbfree{it % 4}'] = sm.c['t']

            @block.vector
            def _(v, hc=hc):
                sm.wait(v, 't')
                for tt in range(NTH):
                    ins = v.tensor_tensor(
                        out=ybf[tt][:, hc * 512:(hc + 1) * 512],
                        in0=p_y[tt][:], in1=sdb[:, hc * 512:(hc + 1) * 512],
                        op=ALU.mult)
                    sm.inc(ins, 'v')
                sm.tag['_yevac'] = sm.c['v']

        for i in range(NTH):
            @block.gpsimd
            def _(g, i=i):
                sm.wait(g, 'v')
                if '_ofree' in sm.tag:
                    g.wait_ge(sm.h['o'], sm.tag['_ofree'])
                gdma(g, sm, 'g', xbuf[:], x3dr[i * 128:(i + 1) * 128, :])
            emit_ln_tile(block, sm, ybf[i], lnbf, lnbf, stat4,
                         v_waits=[('v', sm.c['v']), ('g', sm.c['g']),
                                  ('t', sm.c['t'])])

            @block.vector
            def _(v, i=i):
                sm.wait(v, 'g')
                sm.wait(v, 'v')
                ins = v.tensor_tensor(out=xbuf[:], in0=xbuf[:],
                                      in1=lnbf[:], op=ALU.add)
                sm.inc(ins, 'v')

            @block.gpsimd
            def _(g, i=i):
                sm.wait(g, 'v')
                gdma(g, sm, 'o', out[i * 128:(i + 1) * 128, :], xbuf[:])
            sm.tag['_ofree'] = sm.c['o']


        @block.sync
        def _(sy):
            for n in sm.h:
                sm.wait(sy, n)
        ps4c.close()
        es4c.close()
        es4.close()
        es_glob.close()
    return nc


NSLOT = 16


def slot_order(core):
    own = core_blocks(core)
    return own + [b for b in range(NBLK) if b not in own]


def fused_inmap(core, inputs):
    x = np.ascontiguousarray(np.asarray(inputs['L0_attn_x'], np.float32)[0])
    m = qkv_inmap(core, x, inputs)
    m.pop('isblk0', None)
    sl = slot_order(core)
    xT = x.T
    xtr = np.zeros((D, NSLOT * 516), np.float32)
    for s, b in enumerate(sl):
        if b > 0:
            xtr[:, s * 516:s * 516 + 4] = xT[:, b * 512 - 4:b * 512]
        xtr[:, s * 516 + 4:(s + 1) * 516] = xT[:, b * 512:(b + 1) * 512]
    inv = (1.0 / 10000.0 ** (np.arange(0, 64, 2, dtype=np.float32) / 64.0)
           ).astype(np.float32)
    cek = np.zeros((NSLOT, 128, 512), np.float32)
    sek = np.zeros((NSLOT, 128, 512), np.float32)
    for s, b in enumerate(sl):
        pos = np.arange(b * BLK, (b + 1) * BLK, dtype=np.float32)
        ang = pos[None, :] * inv[:, None]
        c, sn = np.cos(ang), np.sin(ang)
        for p in range(128):
            i = (p % 64) // 2
            cek[s, p] = c[i]
            sek[s, p] = -sn[i] if (p % 2 == 0) else sn[i]
    own = core_blocks(core)
    vis = np.zeros((128, 128), np.float32)
    for seg in range(2):
        bq = own[seg]
        for T in range(64):
            b = sl[T // 4]
            diag = (seg == 0 and T < 4) or (seg == 1 and 4 <= T < 8)
            v = 1.0 if (diag or b < bq) else 0.0
            vis[:, seg * 64 + T] = v
    nzkv = np.ones((128, NSLOT), np.float32)
    for s, b in enumerate(sl):
        if b == 0:
            nzkv[:, s] = 0.0
    kt = np.asarray(inputs['L0_attn_key_temp']).ravel()[0]
    m.update({
        "xtr": f32c(xtr),
        "cek": cek.astype(NPBF), "sek": sek.astype(NPBF),
        "ceq": m.pop("ce"), "seq": m.pop("se"),
        "vis": vis, "nzkv": nzkv, "nzq": m.pop("nz"),
        "kt128": f32c(np.full((128, 1), kt)),
        "tri": np.triu(np.ones((128, 128), np.float32)).astype(NPBF),
        "onescol": np.ones((128, 1), dtype=NPBF),
        "ones1": np.ones((1, 128), dtype=NPBF),
        "woT": f32c(np.asarray(inputs['L0_attn_w_o']).T),
    })
    return m


_BUILD_CACHE = {}


def _get(name, fn):
    if name not in _BUILD_CACHE:
        _BUILD_CACHE[name] = fn()
    return _BUILD_CACHE[name]


def _rows(core, full):
    return np.concatenate([full[b * BLK:(b + 1) * BLK]
                           for b in core_blocks(core)], 0)


def _scatter(res_list, key, shape, dtype=np.float32):
    out = np.zeros(shape, dtype)
    for c in range(NCORE):
        r = np.asarray(res_list[c][key])
        for s, b in enumerate(core_blocks(c)):
            out[b * BLK:(b + 1) * BLK] = r[s * BLK:(s + 1) * BLK]
    return out


def kernel_multilaunch(**inputs):
    x = np.ascontiguousarray(np.asarray(inputs['L0_attn_x'], np.float32)[0])
    cores = list(range(NCORE))

    nc = _get('qkv', build_qkv)
    res = run_bass_kernel_spmd(nc, [qkv_inmap(c, x, inputs) for c in cores],
                               core_ids=cores).results
    qf_T = np.zeros((1024, S), NPBF)
    kf_T = np.zeros((256, S), NPBF)
    vf = np.zeros((S, 256), NPBF)
    for c in cores:
        for s, b in enumerate(core_blocks(c)):
            sl = slice(b * BLK, (b + 1) * BLK)
            qf_T[:, sl] = np.asarray(res[c]['qh'][s])
            kf_T[:, sl] = np.asarray(res[c]['kh'][s])
            vf[sl, :] = np.asarray(res[c]['v'][s])

    nc = _get('attn', build_attn)
    res = run_bass_kernel_spmd(
        nc, [attn_inmap(c, qf_T, kf_T, vf, inputs) for c in cores],
        core_ids=cores).results
    ao_all = np.zeros((16, 64, S), NPBF)
    for c in cores:
        ao_all[2 * c:2 * c + 2] = np.asarray(res[c]['ao'])

    nc = _get('wo', build_wo)
    woT = f32c(np.asarray(inputs['L0_attn_w_o']).T)
    maps = []
    for c in cores:
        cols = np.concatenate([np.arange(b * BLK, (b + 1) * BLK)
                               for b in core_blocks(c)])
        aoc = np.ascontiguousarray(ao_all[:, :, cols].reshape(1024, ROWS))
        maps.append({"aoc": aoc, "woT": woT, "x": f32c(_rows(c, x))})
    res = run_bass_kernel_spmd(nc, maps, core_ids=cores).results
    x1 = _scatter(res, 'out', (S, D))

    nc = _get('ffn0', build_ffn0)
    res = run_bass_kernel_spmd(nc, [ffn0_inmap(_rows(c, x1), inputs)
                                    for c in cores], core_ids=cores).results
    x2 = _scatter(res, 'out', (S, D))

    nc = _get('s4', build_s4)
    res = run_bass_kernel_spmd(nc, [s4_inmap(_rows(c, x2), inputs)
                                    for c in cores], core_ids=cores).results
    x3 = _scatter(res, 'out', (S, D))

    nc = _get('ffn1', build_ffn1)
    res = run_bass_kernel_spmd(nc, [ffn1_inmap(_rows(c, x3), inputs)
                                    for c in cores], core_ids=cores).results
    x4 = _scatter(res, 'out', (S, D))

    return x4.reshape(1, S, D)

def fused_inmap_all(core, inputs):
    m = fused_inmap(core, inputs)
    dummy = np.zeros((ROWS, D), np.float32)
    for k, v in ffn0_inmap(dummy, inputs).items():
        if k not in ('x', 'ident', 'ones1'):
            m[k] = v
    for k, v in s4_inmap(dummy, inputs).items():
        if k not in ('x', 'ident'):
            m[k] = v
    for k, v in ffn1_inmap(dummy, inputs).items():
        if k not in ('x', 'ident', 'ones1'):
            m[k] = v
    return m


def kernel_fused(**inputs):
    nc = _get('fused', build_fused)
    cores = list(range(NCORE))
    res = run_bass_kernel_spmd(
        nc, [fused_inmap_all(c, inputs) for c in cores],
        core_ids=cores).results
    out = np.zeros((S, D), np.float32)
    for c in cores:
        r = np.asarray(res[c]['out'])
        for s_, b in enumerate(core_blocks(c)):
            out[b * BLK:(b + 1) * BLK] = r[s_ * BLK:(s_ + 1) * BLK]
    return out.reshape(1, S, D)



def kernel(**inputs):
    return kernel_fused(**inputs)


# version 15
# speedup vs baseline: 1.3436x; 1.0673x over previous
import sys
sys.path.insert(0, '/opt/trn_rl_repo')
import numpy as np
import ml_dtypes
import concourse.bass as bass
import concourse.mybir as mybir
from concourse.bass_utils import run_bass_kernel_spmd

F32 = mybir.dt.float32
BF16 = mybir.dt.bfloat16
I32 = mybir.dt.int32
I8 = mybir.dt.int8
AF = mybir.ActivationFunctionType
ALU = mybir.AluOpType
NPBF = ml_dtypes.bfloat16

S = 8192
D = 4096
NCORE = 8
BLK = 512
NBLK = S // BLK
ROWS = 2 * BLK
NT = ROWS // 128
EPS = 1e-5


def core_blocks(k):
    return [k, NBLK - 1 - k]


def f32c(x):
    return np.ascontiguousarray(x, dtype=np.float32)


class Sems:
    def __init__(self, nc, names):
        self.h = {n: nc.alloc_semaphore(name=n) for n in names}
        self.c = {n: 0 for n in names}
        self.tag = {}

    def inc(self, instr, n, amt=1):
        self.c[n] += amt
        instr.then_inc(self.h[n], amt)
        return self.c[n]

    def wait(self, eng, n, v=None):
        v = self.c[n] if v is None else v
        if v > 0:
            eng.wait_ge(self.h[n], v)


def gdma(g, sm, n, out, in_, **kw):
    sm.wait(g, n)
    g.dma_start(out=out, in_=in_, **kw).then_inc(sm.h[n], 16)
    sm.c[n] += 16


def fdma(g, sm, n, out, in_, **kw):
    g.dma_start(out=out, in_=in_, **kw).then_inc(sm.h[n], 16)
    sm.c[n] += 16
    return sm.c[n]



def emit_ln_tile(block, sm, xt, out_bf, scr_bf, stat,
                 v_waits=(), a_waits=()):
    @block.vector
    def _(v):
        for s, c in v_waits:
            v.wait_ge(sm.h[s], c)
        ins = v.tensor_scalar(out=scr_bf[:], in0=xt[:], scalar1=1.0,
                              scalar2=0.0, op0=ALU.mult, op1=ALU.add,
                              accum_out=stat[:, 0:1])
        sm.inc(ins, 'v')
        sm.wait(v, 'v')
        ins = v.scalar_tensor_tensor(out=scr_bf[:], in0=xt[:], scalar=1.0,
                                     in1=xt[:], op0=ALU.mult, op1=ALU.mult,
                                     accum_out=stat[:, 1:2])
        sm.inc(ins, 'v')
        sm.wait(v, 'v')
        v.tensor_scalar(out=stat[:, 2:3], in0=stat[:, 0:1],
                        scalar1=1.0 / D, scalar2=None, op0=ALU.mult)
        ins = v.tensor_scalar(out=stat[:, 3:4], in0=stat[:, 1:2],
                              scalar1=1.0 / D, scalar2=None, op0=ALU.mult)
        sm.inc(ins, 'v')
        sm.wait(v, 'v')
        ins = v.tensor_scalar(out=stat[:, 4:5], in0=stat[:, 2:3],
                              scalar1=stat[:, 2:3], scalar2=-1.0,
                              op0=ALU.mult, op1=ALU.mult)
        sm.inc(ins, 'v')
        sm.wait(v, 'v')
        ins = v.scalar_tensor_tensor(out=stat[:, 5:6], in0=stat[:, 3:4],
                                     scalar=EPS, in1=stat[:, 4:5],
                                     op0=ALU.add, op1=ALU.add)
        sm.inc(ins, 'v')

    @block.scalar
    def _(a):
        sm.wait(a, 'v')
        for s, c in a_waits:
            a.wait_ge(sm.h[s], c)
        ins = a.activation(out=stat[:, 6:7], in_=stat[:, 5:6], func=AF.Ln)
        sm.inc(ins, 'a')
        sm.wait(a, 'a')
        ins = a.activation(out=stat[:, 7:8], in_=stat[:, 6:7], func=AF.Exp,
                           scale=-0.5)
        sm.inc(ins, 'a')

    @block.vector
    def _(v):
        sm.wait(v, 'a')
        ins = v.tensor_scalar(out=stat[:, 8:9], in0=stat[:, 2:3],
                              scalar1=stat[:, 7:8], scalar2=-1.0,
                              op0=ALU.mult, op1=ALU.mult)
        sm.inc(ins, 'v')
        sm.wait(v, 'v')
        ins = v.tensor_scalar(out=out_bf[:], in0=xt[:],
                              scalar1=stat[:, 7:8], scalar2=stat[:, 8:9],
                              op0=ALU.mult, op1=ALU.add)
        sm.inc(ins, 'v')


def emit_transpose_tiles(block, sm, src_bf, dst_list, dst_col, ident, p_tr,
                         n_k=32):
    vc = sm.c['v']
    for g0 in range(0, n_k, 4):
        j = (g0 // 4) % 2
        cnt = min(4, n_k - g0)

        @block.tensor
        def _(t, g0=g0, j=j, cnt=cnt):
            if g0 == 0:
                t.wait_ge(sm.h['v'], vc)
            key = f'_trw{j}'
            if key in sm.tag:
                t.wait_ge(sm.h['v'], sm.tag[key])
            for k in range(cnt):
                ins = t.transpose(p_tr[j][:, k * 128:(k + 1) * 128],
                                  src_bf[:, (g0 + k) * 128:(g0 + k + 1) * 128],
                                  ident[:])
                sm.inc(ins, 't')

        @block.vector
        def _(v, g0=g0, j=j, cnt=cnt):
            sm.wait(v, 't')
            for k in range(cnt):
                ins = v.tensor_copy(
                    out=dst_list[g0 + k][:, dst_col:dst_col + 128],
                    in_=p_tr[j][:, k * 128:(k + 1) * 128])
                sm.inc(ins, 'v')
            sm.tag[f'_trw{j}'] = sm.c['v']



def build_ffn0():
    nc = bass.Bass("TRN2", target_bir_lowering=False)
    x_in = nc.dram_tensor("x", [ROWS, D], F32, kind="ExternalInput")
    w0T_in = nc.dram_tensor("w0T", [D, D], F32, kind="ExternalInput")
    bias_in = nc.dram_tensor("bias", [1, D], F32, kind="ExternalInput")
    lAT_in = nc.dram_tensor("lAT", [D, 48], F32, kind="ExternalInput")
    lBT_in = nc.dram_tensor("lBT", [48, D], F32, kind="ExternalInput")
    lsi_in = nc.dram_tensor("lsi", [1, 4], I32, kind="ExternalInput")
    c012_in = nc.dram_tensor("c012", [3, 1], F32, kind="ExternalInput")
    e48_in = nc.dram_tensor("e48", [3, 48], F32, kind="ExternalInput")
    ident_in = nc.dram_tensor("ident", [128, 128], BF16, kind="ExternalInput")
    ones1_in = nc.dram_tensor("ones1", [1, 128], BF16, kind="ExternalInput")
    out = nc.dram_tensor("out", [ROWS, D], F32, kind="ExternalOutput")
    scdr3 = nc.dram_tensor("scdr3", [3, 8], F32, kind="Internal")

    lnT = [nc.alloc_sbuf_tensor(f"lnT{d}", [128, ROWS], BF16)
           for d in range(32)]
    ybf = [nc.alloc_sbuf_tensor(f"y{i}", [128, D], BF16) for i in range(NT)]
    wbuf = nc.alloc_sbuf_tensor("wb", [128, 32 * 256], BF16)
    xbuf = nc.alloc_sbuf_tensor("xb", [128, D], F32)
    lnbf = nc.alloc_sbuf_tensor("lnbf", [128, D], BF16)
    scr_bf = nc.alloc_sbuf_tensor("scr_bf", [128, D], BF16)
    stat = nc.alloc_sbuf_tensor("stat", [128, 12], F32)
    aTbf = nc.alloc_sbuf_tensor("aTbf", [128, 32 * 48], BF16)
    bTbf = nc.alloc_sbuf_tensor("bTbf", [48, D], BF16)
    biasbf = nc.alloc_sbuf_tensor("biasbf", [1, D], BF16)
    onesbf = nc.alloc_sbuf_tensor("onesbf", [1, 128], BF16)
    identbf = nc.alloc_sbuf_tensor("identbf", [128, 128], BF16)
    rbf = nc.alloc_sbuf_tensor("rbf", [48, ROWS], BF16)
    lsi_f = nc.alloc_sbuf_tensor("lsi_f", [1, 4], F32)
    lsi_i = nc.alloc_sbuf_tensor("lsi_i", [1, 4], I32)
    sc = nc.alloc_sbuf_tensor("sc", [1, 8], F32)
    c012 = nc.alloc_sbuf_tensor("c012s", [3, 1], F32)
    e48bf = nc.alloc_sbuf_tensor("e48bf", [3, 48], BF16)
    wsel3 = nc.alloc_sbuf_tensor("wsel3", [3, 1], F32)
    use3 = nc.alloc_sbuf_tensor("use3", [3, 1], F32)
    wsel3b = nc.alloc_sbuf_tensor("wsel3b", [3, 1], BF16)
    wsel48 = nc.alloc_sbuf_tensor("wsel48", [48, 1], F32)

    p_tr = [nc.alloc_psum_tensor(f"ptr{i}", [128, 128], BF16)
            for i in range(2)]
    p_y = [nc.alloc_psum_tensor(f"py{i}", [128, 256], F32) for i in range(2)]
    p_r = nc.alloc_psum_tensor("pr", [48, 512], F32)
    p_s = nc.alloc_psum_tensor("ps", [48, 1], F32)

    sm = Sems(nc, ['g', 'v', 'a', 't', 'o'])

    with nc.Block() as block:
        @block.gpsimd
        def _(g):
            gdma(g, sm, 'g', identbf[:], ident_in[:])
            gdma(g, sm, 'g', onesbf[:], ones1_in[:])
            gdma(g, sm, 'g', biasbf[:], bias_in[:])
            gdma(g, sm, 'g', bass.AP(aTbf, 0, [[32 * 48, 128], [48, 32], [1, 48]]), bass.AP(lAT_in, 0, [[48, 128], [128 * 48, 32], [1, 48]]))
            gdma(g, sm, 'g', bTbf[:], lBT_in[:])
            gdma(g, sm, 'g', lsi_i[:], lsi_in[:])
            gdma(g, sm, 'g', c012[:], c012_in[:])
            gdma(g, sm, 'g', e48bf[:], e48_in[:])

        @block.vector
        def _(v):
            sm.wait(v, 'g')
            v.tensor_copy(out=lsi_f[:], in_=lsi_i[:])
            sm.inc(v.engine_nop(), 'v'); sm.wait(v, 'v')
            v.tensor_single_scalar(out=sc[:, 0:4], in_=lsi_f[:], scalar=0.0,
                                   op=ALU.is_le)
            sm.inc(v.engine_nop(), 'v'); sm.wait(v, 'v')
            v.tensor_scalar(out=sc[:, 4:5], in0=sc[:, 3:4], scalar1=3.0,
                            scalar2=sc[:, 2:3], op0=ALU.add, op1=ALU.mult)
            v.tensor_scalar(out=sc[:, 5:6], in0=sc[:, 0:1], scalar1=-1.0,
                            scalar2=2.0, op0=ALU.mult, op1=ALU.add)
            sm.inc(v.engine_nop(), 'v'); sm.wait(v, 'v')
            v.tensor_scalar(out=sc[:, 5:6], in0=sc[:, 5:6],
                            scalar1=sc[:, 1:2], scalar2=None, op0=ALU.mult)
            sm.inc(v.engine_nop(), 'v'); sm.wait(v, 'v')
            v.tensor_scalar(out=sc[:, 5:6], in0=sc[:, 5:6],
                            scalar1=1.0, scalar2=sc[:, 0:1],
                            op0=ALU.mult, op1=ALU.add)
            v.tensor_scalar(out=sc[:, 6:7], in0=sc[:, 2:3], scalar1=-1.0,
                            scalar2=1.0, op0=ALU.mult, op1=ALU.add)
            sm.inc(v.engine_nop(), 'v'); sm.wait(v, 'v')
            v.tensor_scalar(out=sc[:, 5:6], in0=sc[:, 5:6],
                            scalar1=sc[:, 6:7], scalar2=None, op0=ALU.mult)
            sm.inc(v.engine_nop(), 'v'); sm.wait(v, 'v')
            v.tensor_scalar(out=sc[:, 4:5], in0=sc[:, 4:5],
                            scalar1=1.0, scalar2=sc[:, 5:6],
                            op0=ALU.mult, op1=ALU.add)
            sm.inc(v.engine_nop(), 'v'); sm.wait(v, 'v')
            v.tensor_scalar(out=sc[:, 5:6], in0=sc[:, 4:5], scalar1=-1.0,
                            scalar2=None, op0=ALU.add)
            sm.inc(v.engine_nop(), 'v'); sm.wait(v, 'v')
            v.tensor_single_scalar(out=sc[:, 6:7], in_=sc[:, 5:6], scalar=0.5,
                                   op=ALU.is_gt)
            v.tensor_scalar(out=sc[:, 7:8], in0=sc[:, 5:6], scalar1=-1.0,
                            scalar2=None, op0=ALU.add)
            sm.inc(v.engine_nop(), 'v'); sm.wait(v, 'v')
            ins = v.tensor_scalar(out=sc[:, 7:8], in0=sc[:, 7:8], scalar1=0.0,
                                  scalar2=2.0, op0=ALU.max, op1=ALU.min)
            sm.inc(ins, 'v')

        @block.gpsimd
        def _(g):
            sm.wait(g, 'v')
            for r in range(3):
                gdma(g, sm, 'g', scdr3[r:r + 1, :], sc[:])

        @block.gpsimd
        def _(g):
            sm.wait(g, 'g')
            gdma(g, sm, 'g', wsel3[:], bass.AP(scdr3, 7, [[8, 3], [1, 1]]), allow_slow_non_contiguous=True)
            gdma(g, sm, 'g', use3[:], bass.AP(scdr3, 6, [[8, 3], [1, 1]]), allow_slow_non_contiguous=True)

        @block.vector
        def _(v):
            sm.wait(v, 'g')
            v.tensor_tensor(out=wsel3[:], in0=c012[:], in1=wsel3[:],
                            op=ALU.is_equal)
            sm.inc(v.engine_nop(), 'v'); sm.wait(v, 'v')
            v.tensor_tensor(out=wsel3[:], in0=wsel3[:], in1=use3[:],
                            op=ALU.mult)
            sm.inc(v.engine_nop(), 'v'); sm.wait(v, 'v')
            ins = v.tensor_copy(out=wsel3b[:], in_=wsel3[:])
            sm.inc(ins, 'v')

        @block.tensor
        def _(t):
            sm.wait(t, 'v')
            ins = t.matmul(p_s[:], e48bf[:], wsel3b[:], start=True, stop=True)
            sm.inc(ins, 't')

        @block.scalar
        def _(a):
            sm.wait(a, 't')
            ins = a.activation(out=wsel48[:], in_=p_s[:], func=AF.Copy)
            sm.inc(ins, 'a')
        a_wsel = sm.c['a']

        for i in range(NT):
            @block.gpsimd
            def _(g, i=i):
                if '_lnfree' in sm.tag:
                    g.wait_ge(sm.h['v'], sm.tag['_lnfree'])
                gdma(g, sm, 'g', xbuf[:], x_in[i * 128:(i + 1) * 128, :])

            emit_ln_tile(block, sm, xbuf, lnbf, scr_bf, stat,
                         v_waits=[('g', sm.c['g']), ('t', sm.c['t'])])
            sm.tag['_lnfree'] = sm.c['v']
            emit_transpose_tiles(block, sm, lnbf, lnT, i * 128, identbf, p_tr)

        for c in range(2):
            @block.tensor
            def _(t, c=c):
                sm.wait(t, 'a')
                for d in range(32):
                    ins = t.matmul(p_r[:], aTbf[:, d * 48:(d + 1) * 48],
                                   lnT[d][:, c * 512:(c + 1) * 512],
                                   start=(d == 0), stop=(d == 31))
                sm.inc(ins, 't')

            @block.scalar
            def _(a, c=c):
                sm.wait(a, 't')
                a.wait_ge(sm.h['a'], a_wsel)
                ins = a.activation(out=rbf[:, c * 512:(c + 1) * 512],
                                   in_=p_r[:], func=AF.Copy, scale=wsel48[:])
                sm.inc(ins, 'a')

        CH = 256
        for oc in range(D // CH):
            @block.gpsimd
            def _(g, oc=oc):
                if '_wfree' in sm.tag:
                    g.wait_ge(sm.h['t'], sm.tag['_wfree'])
                for q in range(4):
                    gdma(g, sm, 'g', bass.AP(wbuf, q * 8 * CH,
                                    [[32 * CH, 128], [CH, 8], [1, CH]]), bass.AP(w0T_in, oc * CH + q * 8 * 128 * D,
                                    [[D, 128], [128 * D, 8], [1, CH]]),)
            g_w = sm.c['g']

            for i in range(NT):
                @block.tensor
                def _(t, oc=oc, i=i):
                    t.wait_ge(sm.h['g'], g_w)
                    if oc == 0 and i == 0:
                        sm.wait(t, 'a')
                    it = oc * NT + i
                    pj = it % 2
                    if f'_yfree{pj}' in sm.tag:
                        t.wait_ge(sm.h['a'], sm.tag[f'_yfree{pj}'])
                    t.matmul(p_y[pj][:], onesbf[:],
                             biasbf[:, oc * CH:(oc + 1) * CH],
                             start=True, stop=False)
                    for d in range(32):
                        t.matmul(p_y[pj][:], lnT[d][:, i * 128:(i + 1) * 128],
                                 wbuf[:, d * CH:(d + 1) * CH],
                                 start=False, stop=False)
                    ins = t.matmul(p_y[pj][:], rbf[:, i * 128:(i + 1) * 128],
                                   bTbf[:, oc * CH:(oc + 1) * CH],
                                   start=False, stop=True)
                    sm.inc(ins, 't')

                @block.scalar
                def _(a, oc=oc, i=i):
                    sm.wait(a, 't')
                    pj = (oc * NT + i) % 2
                    ins = a.activation(out=ybf[i][:, oc * CH:(oc + 1) * CH],
                                       in_=p_y[pj][:], func=AF.Copy)
                    sm.tag[f'_yfree{pj}'] = sm.inc(ins, 'a')
            sm.tag['_wfree'] = sm.c['t']

        for i in range(NT):
            @block.gpsimd
            def _(g, i=i):
                sm.wait(g, 'a')
                if '_ofree' in sm.tag:
                    g.wait_ge(sm.h['o'], sm.tag['_ofree'])
                gdma(g, sm, 'g', xbuf[:], x_in[i * 128:(i + 1) * 128, :])

            emit_ln_tile(block, sm, ybf[i], lnbf, scr_bf, stat,
                         v_waits=[('a', sm.c['a'])])

            @block.vector
            def _(v, i=i):
                sm.wait(v, 'g')
                sm.wait(v, 'v')
                ins = v.tensor_tensor(out=xbuf[:], in0=xbuf[:],
                                      in1=lnbf[:], op=ALU.add)
                sm.inc(ins, 'v')

            @block.gpsimd
            def _(g, i=i):
                sm.wait(g, 'v')
                gdma(g, sm, 'o', out[i * 128:(i + 1) * 128, :], xbuf[:])
            sm.tag['_ofree'] = sm.c['o']

        @block.sync
        def _(sy):
            sy.wait_ge(sm.h['o'], sm.c['o'])

    return nc


def ffn0_inmap(x1_rows, inp):
    lA = np.asarray(inp['L0_ffn_lora_A'], np.float32)
    lB = np.asarray(inp['L0_ffn_lora_B'], np.float32)
    e48 = np.zeros((3, 48), np.float32)
    for j in range(3):
        e48[j, 16 * j:16 * j + 16] = 1.0
    return {
        "x": f32c(x1_rows),
        "w0T": f32c(np.asarray(inp['L0_ffn_weight']).T),
        "bias": f32c(np.asarray(inp['L0_ffn_bias']).reshape(1, D)),
        "lAT": f32c(np.transpose(lA, (2, 0, 1)).reshape(D, 48)),
        "lBT": f32c(np.transpose(lB, (0, 2, 1)).reshape(48, D)),
        "lsi": np.ascontiguousarray(
            np.asarray(inp['L0_ffn_lora_start_indices'], np.int32).reshape(1, 4)),
        "c012": np.array([[0.0], [1.0], [2.0]], np.float32),
        "e48": f32c(e48),
        "ident": np.eye(128, dtype=NPBF),
        "ones1": np.ones((1, 128), dtype=NPBF),
    }





NI = 11008 // 128


def build_ffn1():
    nc = bass.Bass("TRN2", target_bir_lowering=False)
    x_in = nc.dram_tensor("x", [ROWS, D], F32, kind="ExternalInput")
    wg_in = nc.dram_tensor("wg", [NI, 128, D], I32, kind="ExternalInput")
    wu_in = nc.dram_tensor("wu", [NI, 128, D], I32, kind="ExternalInput")
    wd_in = nc.dram_tensor("wd", [11008, D], I32, kind="ExternalInput")
    sg_in = nc.dram_tensor("sg", [128, NI], F32, kind="ExternalInput")
    su_in = nc.dram_tensor("su", [128, NI], F32, kind="ExternalInput")
    sd_in = nc.dram_tensor("sd", [1, D], F32, kind="ExternalInput")
    ident_in = nc.dram_tensor("ident", [128, 128], BF16, kind="ExternalInput")
    ones1_in = nc.dram_tensor("ones1", [1, 128], BF16, kind="ExternalInput")
    out = nc.dram_tensor("out", [ROWS, D], F32, kind="ExternalOutput")

    HT = 512
    NTH = HT // 128

    lnT = [nc.alloc_sbuf_tensor(f"lnT{d}", [128, HT], BF16) for d in range(32)]
    hsb = [nc.alloc_sbuf_tensor(f"h{i}", [128, HT], BF16) for i in range(NI)]
    ybf = [nc.alloc_sbuf_tensor(f"y{i}", [128, D], BF16) for i in range(NTH)]
    wgb = nc.alloc_sbuf_tensor("wgb", [128, D], BF16)
    wub = nc.alloc_sbuf_tensor("wub", [128, D], BF16)
    wdb = [nc.alloc_sbuf_tensor(f"wdb{i}", [128, 512], BF16) for i in range(2)]
    gact = nc.alloc_sbuf_tensor("gact", [128, HT], BF16)
    uact = nc.alloc_sbuf_tensor("uact", [128, HT], BF16)
    sgs = nc.alloc_sbuf_tensor("sgs", [128, NI], F32)
    sus = nc.alloc_sbuf_tensor("sus", [128, NI], F32)
    sdb = nc.alloc_sbuf_tensor("sdb", [128, D], BF16)
    sdrow = nc.alloc_sbuf_tensor("sdrow", [1, 512], BF16)
    xbuf = nc.alloc_sbuf_tensor("xb", [128, D], F32)
    lnbf = nc.alloc_sbuf_tensor("lnbf", [128, D], BF16)
    stat = nc.alloc_sbuf_tensor("stat", [128, 12], F32)
    onesbf = nc.alloc_sbuf_tensor("onesbf", [1, 128], BF16)
    identbf = nc.alloc_sbuf_tensor("identbf", [128, 128], BF16)

    p_tr = [nc.alloc_psum_tensor(f"ptr{i}", [128, 128], BF16) for i in range(2)]
    p_g = nc.alloc_psum_tensor("pg", [128, HT], F32)
    p_u = nc.alloc_psum_tensor("pu", [128, HT], F32)
    p_y = [nc.alloc_psum_tensor(f"py{i}", [128, 512], F32) for i in range(NTH)]

    sm = Sems(nc, ['g', 'v', 'a', 't', 'o', 'w'])

    with nc.Block() as block:
        @block.gpsimd
        def _(g):
            gdma(g, sm, 'g', identbf[:], ident_in[:])
            gdma(g, sm, 'g', onesbf[:], ones1_in[:])
            gdma(g, sm, 'g', sgs[:], sg_in[:])
            gdma(g, sm, 'g', sus[:], su_in[:])

        for q in range(8):
            @block.gpsimd
            def _(g, q=q):
                if q > 0:
                    g.wait_ge(sm.h['t'], sm.tag['_sdt'])
                gdma(g, sm, 'g', sdrow[:], sd_in[:, q * 512:(q + 1) * 512])

            @block.tensor
            def _(t, q=q):
                sm.wait(t, 'g')
                if q > 0:
                    t.wait_ge(sm.h['a'], sm.tag['_sdq'])
                ins = t.matmul(p_g[:], onesbf[:], sdrow[:],
                               start=True, stop=True)
                sm.tag['_sdt'] = sm.inc(ins, 't')

            @block.scalar
            def _(a, q=q):
                sm.wait(a, 't')
                ins = a.activation(out=sdb[:, q * 512:(q + 1) * 512],
                                   in_=p_g[:], func=AF.Copy)
                sm.tag['_sdq'] = sm.inc(ins, 'a')

        for half in range(2):
            base = half * HT
            for i in range(NTH):
                @block.gpsimd
                def _(g, i=i):
                    if '_lnfree' in sm.tag:
                        g.wait_ge(sm.h['v'], sm.tag['_lnfree'])
                    if '_ofree' in sm.tag:
                        g.wait_ge(sm.h['o'], sm.tag['_ofree'])
                    gdma(g, sm, 'g', xbuf[:], x_in[base + i * 128:base + (i + 1) * 128, :])
                emit_ln_tile(block, sm, xbuf, lnbf, wgb, stat,
                             v_waits=[('g', sm.c['g']), ('t', sm.c['t'])])
                sm.tag['_lnfree'] = sm.c['v']
                emit_transpose_tiles(block, sm, lnbf, lnT, i * 128, identbf,
                                     p_tr)

            for it in range(NI):
                vc_ln = sm.c['v']

                @block.gpsimd
                def _(g, it=it, vc_ln=vc_ln):
                    if '_gufree' in sm.tag:
                        g.wait_ge(sm.h['t'], sm.tag['_gufree'])
                    if it == 0:
                        g.wait_ge(sm.h['v'], vc_ln)
                    gdma(g, sm, 'g', wgb[:], wg_in[it])
                    gdma(g, sm, 'g', wub[:], wu_in[it])

                @block.tensor
                def _(t, it=it):
                    sm.wait(t, 'g')
                    if it == 0:
                        sm.wait(t, 'a')
                    if '_gufree_a' in sm.tag:
                        t.wait_ge(sm.h['a'], sm.tag['_gufree_a'])
                    for dt in range(32):
                        t.matmul(p_g[:], wgb[:, dt * 128:(dt + 1) * 128],
                                 lnT[dt][:], start=(dt == 0), stop=(dt == 31))
                    for dt in range(32):
                        ins = t.matmul(p_u[:], wub[:, dt * 128:(dt + 1) * 128],
                                       lnT[dt][:], start=(dt == 0),
                                       stop=(dt == 31))
                    sm.tag['_gufree'] = sm.inc(ins, 't')

                @block.scalar
                def _(a, it=it):
                    sm.wait(a, 't')
                    if '_hmul' in sm.tag:
                        a.wait_ge(sm.h['v'], sm.tag['_hmul'])
                    a.activation(out=gact[:], in_=p_g[:], func=AF.Silu,
                                 scale=sgs[:, it:it + 1])
                    ins = a.activation(out=uact[:], in_=p_u[:], func=AF.Copy,
                                       scale=sus[:, it:it + 1])
                    sm.tag['_gufree_a'] = sm.inc(ins, 'a')

                @block.vector
                def _(v, it=it):
                    sm.wait(v, 'a')
                    ins = v.tensor_tensor(out=hsb[it][:], in0=gact[:],
                                          in1=uact[:], op=ALU.mult)
                    sm.tag['_hmul'] = sm.inc(ins, 'v')

            for hc in range(8):
                for it in range(NI):
                    @block.gpsimd
                    def _(g, hc=hc, it=it):
                        j = it % 2
                        if f'_wdfree{j}' in sm.tag:
                            g.wait_ge(sm.h['t'], sm.tag[f'_wdfree{j}'])
                        gdma(g, sm, 'w', wdb[it % 2][:], wd_in[it * 128:(it + 1) * 128,
                                      hc * 512:(hc + 1) * 512])
                    w_c = sm.c['w']

                    @block.tensor
                    def _(t, hc=hc, it=it):
                        t.wait_ge(sm.h['w'], w_c)
                        if it == 0:
                            sm.wait(t, 'v')
                            if '_yevac' in sm.tag:
                                t.wait_ge(sm.h['v'], sm.tag['_yevac'])
                        for tt in range(NTH):
                            ins = t.matmul(p_y[tt][:],
                                           hsb[it][:, tt * 128:(tt + 1) * 128],
                                           wdb[it % 2][:],
                                           start=(it == 0), stop=(it == NI - 1))
                        sm.tag[f'_wdfree{it % 2}'] = sm.inc(ins, 't')

                @block.vector
                def _(v, hc=hc):
                    sm.wait(v, 't')
                    for tt in range(NTH):
                        ins = v.tensor_tensor(
                            out=ybf[tt][:, hc * 512:(hc + 1) * 512],
                            in0=p_y[tt][:], in1=sdb[:, hc * 512:(hc + 1) * 512],
                            op=ALU.mult)
                    sm.tag['_yevac'] = sm.inc(ins, 'v')

            for i in range(NTH):
                @block.gpsimd
                def _(g, i=i):
                    sm.wait(g, 'v')
                    if '_ofree' in sm.tag:
                        g.wait_ge(sm.h['o'], sm.tag['_ofree'])
                    gdma(g, sm, 'g', xbuf[:], x_in[base + i * 128:base + (i + 1) * 128, :])
                emit_ln_tile(block, sm, ybf[i], lnbf, wgb, stat,
                             v_waits=[('v', sm.c['v']), ('g', sm.c['g']),
                                      ('t', sm.c['t'])])

                @block.vector
                def _(v, i=i):
                    sm.wait(v, 'g')
                    sm.wait(v, 'v')
                    ins = v.tensor_tensor(out=xbuf[:], in0=xbuf[:],
                                          in1=lnbf[:], op=ALU.add)
                    sm.inc(ins, 'v')

                @block.gpsimd
                def _(g, i=i):
                    sm.wait(g, 'v')
                    gdma(g, sm, 'o', out[base + i * 128:base + (i + 1) * 128, :], xbuf[:])
                sm.tag['_ofree'] = sm.c['o']

        @block.sync
        def _(sy):
            sy.wait_ge(sm.h['o'], sm.c['o'])

    return nc


def ffn1_inmap(x3_rows, inp):
    wg = np.asarray(inp['L1_ffn_w_gate'])
    wu = np.asarray(inp['L1_ffn_w_up'])
    wd = np.asarray(inp['L1_ffn_w_down'])
    def relayout(w):
        t = w.reshape(NI, 128, 32, 128)
        return np.ascontiguousarray(
            np.transpose(t, (0, 3, 2, 1)).astype(np.int8)).reshape(
                NI, 128, D)

    return {
        "x": f32c(x3_rows),
        "wg": relayout(wg),
        "wu": relayout(wu),
        "wd": np.ascontiguousarray(wd.T.astype(np.int8)),
        "sg": f32c(np.asarray(inp['L1_ffn_s_gate']).reshape(NI, 128).T),
        "su": f32c(np.asarray(inp['L1_ffn_s_up']).reshape(NI, 128).T),
        "sd": f32c(np.asarray(inp['L1_ffn_s_down']).reshape(1, D)),
        "ident": np.eye(128, dtype=NPBF),
        "ones1": np.ones((1, 128), dtype=NPBF),
    }



def build_s4():
    nc = bass.Bass("TRN2", target_bir_lowering=False)
    F = F32
    x_in = nc.dram_tensor("x", [ROWS, D], F, kind="ExternalInput")
    lam_in = nc.dram_tensor("lam", [64, 2], F, kind="ExternalInput")
    pri_in = nc.dram_tensor("pri", [64, 2], F, kind="ExternalInput")
    qri_in = nc.dram_tensor("qri", [64, 2], F, kind="ExternalInput")
    vctre_in = nc.dram_tensor("vctre", [64, 64], F, kind="ExternalInput")
    vctim_in = nc.dram_tensor("vctim", [64, 64], F, kind="ExternalInput")
    ctt_in = nc.dram_tensor("ctt", [64, 2], F, kind="ExternalInput")
    bp_in = nc.dram_tensor("bp", [64, 1], F, kind="ExternalInput")
    dsc_in = nc.dram_tensor("dsc", [1, 1], F, kind="ExternalInput")
    lstep_in = nc.dram_tensor("lstep", [1, 1], F, kind="ExternalInput")
    c1bre_in = nc.dram_tensor("c1bre", [64, D], F, kind="ExternalInput")
    c1bim_in = nc.dram_tensor("c1bim", [64, D], F, kind="ExternalInput")
    c2re_in = nc.dram_tensor("c2re", [64, 64], F, kind="ExternalInput")
    c2im_in = nc.dram_tensor("c2im", [64, 64], F, kind="ExternalInput")
    wc_in = nc.dram_tensor("wc", [64, 64], F, kind="ExternalInput")
    ws_in = nc.dram_tensor("ws", [64, 64], F, kind="ExternalInput")
    wsn_in = nc.dram_tensor("wsn", [64, 64], F, kind="ExternalInput")
    twre_in = nc.dram_tensor("twre", [64, 64], F, kind="ExternalInput")
    twim_in = nc.dram_tensor("twim", [64, 64], F, kind="ExternalInput")
    idf_in = nc.dram_tensor("idf", [64, 64], F, kind="ExternalInput")
    onesf_in = nc.dram_tensor("onesf", [1, 128], F, kind="ExternalInput")
    ident_in = nc.dram_tensor("ident", [128, 128], BF16, kind="ExternalInput")
    out = nc.dram_tensor("out", [ROWS, D], F, kind="ExternalOutput")
    odr = nc.dram_tensor("odr", [16, D], F, kind="Internal")
    kdr = nc.dram_tensor("kdr", [64, 64], F, kind="Internal")

    e_all = nc.alloc_sbuf_tensor("e_all", [128, 35 * 128], BF16)
    kpad = nc.alloc_sbuf_tensor("kpad", [1, 4608], BF16)
    identbf = nc.alloc_sbuf_tensor("identbf", [128, 128], BF16)
    onesf = nc.alloc_sbuf_tensor("onesf_s", [1, 128], F)
    dv = nc.alloc_sbuf_tensor("dv", [128, 1], F)
    sc1 = nc.alloc_sbuf_tensor("sc1", [1, 4], F)

    p_tr = [nc.alloc_psum_tensor(f"ptr{i}", [128, 128], BF16)
            for i in range(2)]
    p_y = [nc.alloc_psum_tensor(f"py{i}", [128, 512], F) for i in range(2)]

    sm = Sems(nc, ['g', 'v', 'a', 't', 'o', 'w', 'm'])

    def dstep(eng, ins, n):
        sm.inc(ins, n)
        sm.wait(eng, n)

    with nc.Block() as block:
        from contextlib import ExitStack
        _es = ExitStack()
        c1re = _es.enter_context(nc.sbuf_tensor("c1re", [64, D], F))
        c1im = _es.enter_context(nc.sbuf_tensor("c1im", [64, D], F))
        dre = _es.enter_context(nc.sbuf_tensor("dre", [64, D], F))
        dim_ = _es.enter_context(nc.sbuf_tensor("dim_", [64, D], F))
        tmpl = _es.enter_context(nc.sbuf_tensor("tmpl", [64, D], F))
        winv = _es.enter_context(nc.sbuf_tensor("winv", [64, D], F))
        wre = _es.enter_context(nc.sbuf_tensor("wre", [64, D], F))
        wim = _es.enter_context(nc.sbuf_tensor("wim", [64, D], F))
        lamv = _es.enter_context(nc.sbuf_tensor("lamv", [64, 2], F))
        priv = _es.enter_context(nc.sbuf_tensor("priv", [64, 2], F))
        qriv = _es.enter_context(nc.sbuf_tensor("qriv", [64, 2], F))
        cttv = _es.enter_context(nc.sbuf_tensor("cttv", [64, 2], F))
        vctre = _es.enter_context(nc.sbuf_tensor("vctre_s", [64, 64], F))
        vctim = _es.enter_context(nc.sbuf_tensor("vctim_s", [64, 64], F))
        bpv = _es.enter_context(nc.sbuf_tensor("bpv", [64, 1], F))
        bv = _es.enter_context(nc.sbuf_tensor("bv", [64, 2], F))
        vstk = _es.enter_context(nc.sbuf_tensor("vstk", [64, 8], F))
        ls1 = _es.enter_context(nc.sbuf_tensor("ls1", [1, 4], F))
        s2v = _es.enter_context(nc.sbuf_tensor("s2v", [64, 1], F))
        ob = _es.enter_context(nc.sbuf_tensor("ob", [8, 512], F))
        km = _es.enter_context(nc.sbuf_tensor("km", [64, 16 * 64], F))
        cx = _es.enter_context(nc.sbuf_tensor("cx", [64, 24 * 64], F))
        wcv = _es.enter_context(nc.sbuf_tensor("wcv", [64, 64], F))
        wsv = _es.enter_context(nc.sbuf_tensor("wsv", [64, 64], F))
        wsnv = _es.enter_context(nc.sbuf_tensor("wsnv", [64, 64], F))
        twrev = _es.enter_context(nc.sbuf_tensor("twrev", [64, 64], F))
        twimv = _es.enter_context(nc.sbuf_tensor("twimv", [64, 64], F))
        c2rev = _es.enter_context(nc.sbuf_tensor("c2rev", [64, 64], F))
        c2imv = _es.enter_context(nc.sbuf_tensor("c2imv", [64, 64], F))
        idfv = _es.enter_context(nc.sbuf_tensor("idfv", [64, 64], F))
        k2v = _es.enter_context(nc.sbuf_tensor("k2v", [64, 64], F))
        p_s = _es.enter_context(nc.psum_tensor("p_s", [64, 2], F))
        p_c = _es.enter_context(nc.psum_tensor("p_c", [8, 512], F))
        p_f1 = _es.enter_context(nc.psum_tensor("p_f1", [64, 64], F))
        p_f2 = _es.enter_context(nc.psum_tensor("p_f2", [64, 64], F))
        if True:
            @block.gpsimd
            def _(g):
                for dst, src_ in [(c1re, c1bre_in), (c1im, c1bim_in),
                                  (lamv, lam_in), (priv, pri_in),
                                  (qriv, qri_in), (cttv, ctt_in),
                                  (vctre, vctre_in), (vctim, vctim_in),
                                  (bpv, bp_in), (wcv, wc_in), (wsv, ws_in),
                                  (wsnv, wsn_in), (twrev, twre_in),
                                  (twimv, twim_in), (c2rev, c2re_in),
                                  (c2imv, c2im_in), (idfv, idf_in),
                                  (onesf, onesf_in), (identbf, ident_in)]:
                    gdma(g, sm, 'g', dst[:], src_[:])
                gdma(g, sm, 'g', ls1[:, 0:1], lstep_in[:])

            @block.scalar
            def _(a):
                sm.wait(a, 'g')
                ins = a.activation(out=ls1[:, 1:2], in_=ls1[:, 0:1], func=AF.Exp)
                dstep(a, ins, 'a')

            @block.vector
            def _(v):
                sm.wait(v, 'a')
                ins = v.reciprocal(out=ls1[:, 2:3], in_=ls1[:, 1:2])
                dstep(v, ins, 'v')
                ins = v.tensor_scalar(out=ls1[:, 3:4], in0=ls1[:, 2:3],
                                      scalar1=2.0, scalar2=None, op0=ALU.mult)
                dstep(v, ins, 'v')

            @block.tensor
            def _(t):
                sm.wait(t, 'v')
                t.matmul(p_s[:, 0:1], onesf[:, 0:64], ls1[:, 3:4],
                         start=True, stop=True)
                t.matmul(p_f1[:, 0:1], vctre[:], bpv[:], start=True, stop=True)
                ins = t.matmul(p_f1[:, 1:2], vctim[:], bpv[:], start=True,
                               stop=True)
                sm.inc(ins, 't')

            @block.scalar
            def _(a):
                sm.wait(a, 't')
                a.activation(out=s2v[:], in_=p_s[:, 0:1], func=AF.Copy)
                ins = a.activation(out=bv[:], in_=p_f1[:, 0:2], func=AF.Copy)
                sm.inc(ins, 'a')

            @block.vector
            def _(v):
                sm.wait(v, 'a')
                v.tensor_scalar(out=dre[:], in0=c1re[:], scalar1=s2v[:],
                                scalar2=lamv[:, 0:1], op0=ALU.mult,
                                op1=ALU.subtract)
                ins = v.tensor_scalar(out=dim_[:], in0=c1im[:], scalar1=s2v[:],
                                      scalar2=lamv[:, 1:2], op0=ALU.mult,
                                      op1=ALU.subtract)
                dstep(v, ins, 'v')
                v.tensor_tensor(out=tmpl[:], in0=dre[:], in1=dre[:],
                                op=ALU.mult)
                ins = v.scalar_tensor_tensor(out=winv[:], in0=dim_[:],
                                             scalar=1.0, in1=dim_[:],
                                             op0=ALU.mult, op1=ALU.mult)
                dstep(v, ins, 'v')
                ins = v.tensor_tensor(out=tmpl[:], in0=tmpl[:], in1=winv[:],
                                      op=ALU.add)
                dstep(v, ins, 'v')
                ins = v.reciprocal(out=winv[:], in_=tmpl[:])
                dstep(v, ins, 'v')
                ins = v.tensor_tensor(out=wre[:], in0=dre[:], in1=winv[:],
                                       op=ALU.mult)
                dstep(v, ins, 'v')
                ins = v.scalar_tensor_tensor(out=wim[:], in0=dim_[:],
                                             scalar=-1.0, in1=winv[:],
                                             op0=ALU.mult, op1=ALU.mult)
                dstep(v, ins, 'v')
                pairs = [(cttv, bv, 0), (cttv, priv, 2), (qriv, bv, 4),
                         (qriv, priv, 6)]
                for (xx, yy, col) in pairs:
                    ins = v.tensor_tensor(out=vstk[:, col:col + 1],
                                          in0=xx[:, 0:1], in1=yy[:, 0:1],
                                          op=ALU.mult)
                    dstep(v, ins, 'v')
                    ins = v.tensor_tensor(out=tmpl[:, 0:1], in0=xx[:, 1:2],
                                          in1=yy[:, 1:2], op=ALU.mult)
                    dstep(v, ins, 'v')
                    ins = v.tensor_tensor(out=vstk[:, col:col + 1],
                                          in0=vstk[:, col:col + 1],
                                          in1=tmpl[:, 0:1], op=ALU.add)
                    dstep(v, ins, 'v')
                    ins = v.tensor_tensor(out=vstk[:, col + 1:col + 2],
                                          in0=xx[:, 0:1], in1=yy[:, 1:2],
                                          op=ALU.mult)
                    dstep(v, ins, 'v')
                    ins = v.tensor_tensor(out=tmpl[:, 0:1], in0=xx[:, 1:2],
                                          in1=yy[:, 0:1], op=ALU.mult)
                    dstep(v, ins, 'v')
                    ins = v.tensor_tensor(out=vstk[:, col + 1:col + 2],
                                          in0=vstk[:, col + 1:col + 2],
                                          in1=tmpl[:, 0:1], op=ALU.subtract)
                    dstep(v, ins, 'v')

            for which, rhs in [(0, wre), (1, wim)]:
                for ch in range(8):
                    @block.tensor
                    def _(t, which=which, ch=ch, rhs=rhs):
                        if which == 0 and ch == 0:
                            sm.wait(t, 'v')
                        else:
                            t.wait_ge(sm.h['a'], sm.tag['_cev'])
                        ins = t.matmul(p_c[:], vstk[:],
                                       rhs[:, ch * 512:(ch + 1) * 512],
                                       start=True, stop=True)
                        sm.inc(ins, 't')

                    @block.scalar
                    def _(a, which=which, ch=ch):
                        sm.wait(a, 't')
                        sm.wait(a, 'w')
                        ins = a.activation(out=ob[:], in_=p_c[:], func=AF.Copy)
                        sm.tag['_cev'] = sm.inc(ins, 'a')

                    @block.gpsimd
                    def _(g, which=which, ch=ch):
                        sm.wait(g, 'a')
                        gdma(g, sm, 'w', odr[which * 8:(which + 1) * 8,
                                            ch * 512:(ch + 1) * 512], ob[:])

            @block.gpsimd
            def _(g):
                sm.wait(g, 'w')
                for c in range(16):
                    gdma(g, sm, 'g', km[:, c * 64:(c + 1) * 64], bass.AP(odr, c * D, [[64, 64], [1, 64]]))

            def KM(i):
                return km[:, i * 64:(i + 1) * 64]

            def CX(i):
                return cx[:, i * 64:(i + 1) * 64]

            @block.vector
            def _(v):
                sm.wait(v, 'g')
                for j in range(4):
                    v.tensor_tensor(out=CX(2 * j), in0=KM(2 * j),
                                    in1=KM(8 + 2 * j + 1), op=ALU.subtract)
                    ins = v.tensor_tensor(out=CX(2 * j + 1), in0=KM(2 * j + 1),
                                          in1=KM(8 + 2 * j), op=ALU.add)
                dstep(v, ins, 'v')
                v.tensor_scalar(out=CX(8), in0=CX(6), scalar1=1.0,
                                scalar2=None, op0=ALU.add)
                ins = v.tensor_copy(out=CX(9), in_=CX(7))
                dstep(v, ins, 'v')
                v.tensor_tensor(out=CX(10), in0=CX(2), in1=CX(4), op=ALU.mult)
                v.tensor_tensor(out=CX(12), in0=CX(3), in1=CX(5), op=ALU.mult)
                v.tensor_tensor(out=CX(11), in0=CX(2), in1=CX(5), op=ALU.mult)
                ins = v.tensor_tensor(out=CX(13), in0=CX(3), in1=CX(4),
                                      op=ALU.mult)
                dstep(v, ins, 'v')
                v.tensor_tensor(out=CX(10), in0=CX(10), in1=CX(12),
                                op=ALU.subtract)
                v.tensor_tensor(out=CX(11), in0=CX(11), in1=CX(13), op=ALU.add)
                v.tensor_tensor(out=CX(22), in0=CX(8), in1=CX(8), op=ALU.mult)
                ins = v.tensor_tensor(out=CX(23), in0=CX(9), in1=CX(9),
                                      op=ALU.mult)
                dstep(v, ins, 'v')
                ins = v.tensor_tensor(out=CX(22), in0=CX(22), in1=CX(23),
                                      op=ALU.add)
                dstep(v, ins, 'v')
                ins = v.reciprocal(out=CX(23), in_=CX(22))
                dstep(v, ins, 'v')
                v.tensor_tensor(out=CX(14), in0=CX(10), in1=CX(8), op=ALU.mult)
                v.tensor_tensor(out=CX(16), in0=CX(11), in1=CX(9), op=ALU.mult)
                v.tensor_tensor(out=CX(15), in0=CX(11), in1=CX(8), op=ALU.mult)
                ins = v.tensor_tensor(out=CX(17), in0=CX(10), in1=CX(9),
                                      op=ALU.mult)
                dstep(v, ins, 'v')
                v.tensor_tensor(out=CX(14), in0=CX(14), in1=CX(16), op=ALU.add)
                ins = v.tensor_tensor(out=CX(15), in0=CX(15), in1=CX(17),
                                      op=ALU.subtract)
                dstep(v, ins, 'v')
                v.tensor_tensor(out=CX(14), in0=CX(14), in1=CX(23), op=ALU.mult)
                ins = v.tensor_tensor(out=CX(15), in0=CX(15), in1=CX(23),
                                      op=ALU.mult)
                dstep(v, ins, 'v')
                v.tensor_tensor(out=CX(16), in0=CX(0), in1=CX(14),
                                op=ALU.subtract)
                ins = v.tensor_tensor(out=CX(17), in0=CX(1), in1=CX(15),
                                op=ALU.subtract)
                dstep(v, ins, 'v')
                v.tensor_tensor(out=CX(18), in0=c2rev[:], in1=CX(16),
                                op=ALU.mult)
                v.tensor_tensor(out=CX(20), in0=c2imv[:], in1=CX(17),
                                op=ALU.mult)
                v.tensor_tensor(out=CX(19), in0=c2rev[:], in1=CX(17),
                                op=ALU.mult)
                ins = v.tensor_tensor(out=CX(21), in0=c2imv[:], in1=CX(16),
                                op=ALU.mult)
                dstep(v, ins, 'v')
                v.tensor_tensor(out=CX(18), in0=CX(18), in1=CX(20),
                                op=ALU.subtract)
                ins = v.tensor_tensor(out=CX(19), in0=CX(19), in1=CX(21),
                                      op=ALU.add)
                sm.inc(ins, 'v')

            @block.tensor
            def _(t):
                sm.wait(t, 'v')
                t.matmul(p_f1[:, 0:64], wcv[:], CX(18), start=True, stop=False)
                t.matmul(p_f1[:, 0:64], wsnv[:], CX(19), start=False, stop=True)
                t.matmul(p_f2[:, 0:64], wsv[:], CX(18), start=True, stop=False)
                ins = t.matmul(p_f2[:, 0:64], wcv[:], CX(19), start=False,
                               stop=True)
                sm.inc(ins, 't')

            @block.scalar
            def _(a):
                sm.wait(a, 't')
                a.activation(out=CX(0), in_=p_f1[:, 0:64], func=AF.Copy)
                ins = a.activation(out=CX(1), in_=p_f2[:, 0:64], func=AF.Copy)
                sm.inc(ins, 'a')

            @block.vector
            def _(v):
                sm.wait(v, 'a')
                v.tensor_tensor(out=CX(2), in0=CX(0), in1=twrev[:], op=ALU.mult)
                v.tensor_tensor(out=CX(4), in0=CX(1), in1=twimv[:], op=ALU.mult)
                v.tensor_tensor(out=CX(3), in0=CX(0), in1=twimv[:], op=ALU.mult)
                ins = v.tensor_tensor(out=CX(5), in0=CX(1), in1=twrev[:], op=ALU.mult)
                dstep(v, ins, 'v')
                v.tensor_tensor(out=CX(2), in0=CX(2), in1=CX(4), op=ALU.subtract)
                ins = v.tensor_tensor(out=CX(3), in0=CX(3), in1=CX(5),
                                      op=ALU.add)
                sm.inc(ins, 'v')

            @block.tensor
            def _(t):
                sm.wait(t, 'v')
                t.transpose(p_f1[:, 0:64], CX(2), idfv[:])
                ins = t.transpose(p_f2[:, 0:64], CX(3), idfv[:])
                sm.inc(ins, 't')

            @block.scalar
            def _(a):
                sm.wait(a, 't')
                a.activation(out=CX(6), in_=p_f1[:, 0:64], func=AF.Copy)
                ins = a.activation(out=CX(7), in_=p_f2[:, 0:64], func=AF.Copy)
                sm.inc(ins, 'a')
            a_tr2 = sm.c['a']

            @block.tensor
            def _(t):
                sm.wait(t, 'a')
                t.matmul(p_f1[:, 0:64], wcv[:], CX(6), start=True, stop=False)
                ins = t.matmul(p_f1[:, 0:64], wsnv[:], CX(7), start=False,
                               stop=True)
                sm.inc(ins, 't')

            @block.scalar
            def _(a):
                sm.wait(a, 't')
                ins = a.activation(out=k2v[:], in_=p_f1[:, 0:64], func=AF.Copy,
                                   scale=1.0 / 4096.0)
                sm.inc(ins, 'a')

            @block.gpsimd
            def _(g):
                sm.wait(g, 'a')
                gdma(g, sm, 'w', kdr[:], k2v[:])
                sm.wait(g, 'w')
                ins = g.memset(kpad[:], 0.0)
                sm.inc(ins, 'm')
                sm.wait(g, 'm')
                gdma(g, sm, 'w', kpad[:, 512:4608],
                     bass.AP(kdr, 0, [[4096, 1], [1, 4096]]))
                sm.wait(g, 'w')
                gdma(g, sm, 'w', e_all[:],
                     bass.AP(kpad, 128, [[4608, 1], [-1, 128], [1, 4480]]))
                sm.wait(g, 'w')

        _es.close()
        uT = [nc.alloc_sbuf_tensor(f"uT{d}", [128, ROWS], BF16)
              for d in range(32)]
        lnnat = [nc.alloc_sbuf_tensor(f"ln{i}", [128, D], BF16)
                 for i in range(NT)]
        ysb = nc.alloc_sbuf_tensor("ysb", [128, D], F)
        xbuf = nc.alloc_sbuf_tensor("xb", [128, D], F)
        lnbf = nc.alloc_sbuf_tensor("lnbf", [128, D], BF16)
        scr_bf = nc.alloc_sbuf_tensor("scr_bf", [128, D], BF16)
        stat = nc.alloc_sbuf_tensor("stat", [128, 12], F)
        for i in range(NT):
            @block.gpsimd
            def _(g, i=i):
                if '_lnfree' in sm.tag:
                    g.wait_ge(sm.h['v'], sm.tag['_lnfree'])
                gdma(g, sm, 'g', xbuf[:], x_in[i * 128:(i + 1) * 128, :])
            emit_ln_tile(block, sm, xbuf, lnnat[i], scr_bf, stat,
                         v_waits=[('g', sm.c['g']), ('t', sm.c['t'])])
            sm.tag['_lnfree'] = sm.c['v']
            emit_transpose_tiles(block, sm, lnnat[i], uT, i * 128, identbf,
                                 p_tr)

        @block.gpsimd
        def _(g):
            gdma(g, sm, 'g', sc1[:, 0:1], dsc_in[:])

        @block.tensor
        def _(t):
            sm.wait(t, 'g')
            ins = t.matmul(p_y[0][:, 0:1], onesf[:], sc1[:, 0:1],
                           start=True, stop=True)
            sm.inc(ins, 't')

        @block.scalar
        def _(a):
            sm.wait(a, 't')
            ins = a.activation(out=dv[:], in_=p_y[0][:, 0:1], func=AF.Copy)
            sm.inc(ins, 'a')

        for i in range(NT):
            for grp in range(8):
                @block.tensor
                def _(t, i=i, grp=grp):
                    if i == 0 and grp == 0:
                        sm.wait(t, 'a')
                        sm.wait(t, 'v')
                        sm.wait(t, 'w')
                    j = grp % 2
                    if f'_pyfree{j}' in sm.tag:
                        t.wait_ge(sm.h['a'], sm.tag[f'_pyfree{j}'])
                    last_c = min(4 * grp + 3, 31)
                    for c in range(last_c + 1):
                        ins = t.matmul(
                            p_y[j][:],
                            uT[c][:, i * 128:(i + 1) * 128],
                            e_all[:, (4 * grp - c + 3) * 128:
                                  (4 * grp - c + 3) * 128 + 512],
                            start=(c == 0), stop=(c == last_c))
                    sm.inc(ins, 't')

                @block.scalar
                def _(a, i=i, grp=grp):
                    sm.wait(a, 't')
                    if grp == 0 and '_ysbfree' in sm.tag:
                        a.wait_ge(sm.h['v'], sm.tag['_ysbfree'])
                    j = grp % 2
                    ins = a.activation(out=ysb[:, grp * 512:(grp + 1) * 512],
                                       in_=p_y[j][:], func=AF.Copy)
                    sm.tag[f'_pyfree{j}'] = sm.inc(ins, 'a')

            @block.vector
            def _(v, i=i):
                sm.wait(v, 'a')
                sm.wait(v, 'v')
                ins = v.scalar_tensor_tensor(out=ysb[:], in0=lnnat[i][:],
                                             scalar=dv[:], in1=ysb[:],
                                             op0=ALU.mult, op1=ALU.add)
                sm.inc(ins, 'v')

            @block.gpsimd
            def _(g, i=i):
                sm.wait(g, 'v')
                if '_ofree' in sm.tag:
                    g.wait_ge(sm.h['o'], sm.tag['_ofree'])
                gdma(g, sm, 'g', xbuf[:], x_in[i * 128:(i + 1) * 128, :])
            emit_ln_tile(block, sm, ysb, lnbf, scr_bf, stat,
                         v_waits=[('g', sm.c['g']), ('v', sm.c['v'])])
            sm.tag['_ysbfree'] = sm.c['v']

            @block.vector
            def _(v, i=i):
                sm.wait(v, 'g')
                sm.wait(v, 'v')
                ins = v.tensor_tensor(out=xbuf[:], in0=xbuf[:], in1=lnbf[:],
                                      op=ALU.add)
                sm.inc(ins, 'v')

            @block.gpsimd
            def _(g, i=i):
                sm.wait(g, 'v')
                gdma(g, sm, 'o', out[i * 128:(i + 1) * 128, :], xbuf[:])
            sm.tag['_ofree'] = sm.c['o']

        @block.sync
        def _(sy):
            sy.wait_ge(sm.h['o'], sm.c['o'])

    return nc


def s4_consts():
    L = 4096
    ang = np.arange(L, dtype=np.float32) / np.float32(L)
    th = (np.float32(-2.0) * np.float32(np.pi) * ang).astype(np.float32)
    Om = (np.cos(th) + 1j * np.sin(th)).astype(np.complex64)
    c1 = ((1 - Om) / (1 + Om)).astype(np.complex64)
    cc = (2.0 / (1 + Om)).astype(np.complex64)
    i64 = np.arange(64)
    w64 = 2.0 * np.pi * np.outer(i64, i64) / 64.0
    wc = np.cos(w64).astype(np.float32)
    ws = np.sin(w64).astype(np.float32)
    l0m0 = 2.0 * np.pi * np.outer(i64, i64) / float(L)
    return {
        "c1bre": f32c(np.tile(c1.real[None, :], (64, 1))),
        "c1bim": f32c(np.tile(c1.imag[None, :], (64, 1))),
        "c2re": f32c(cc.real.reshape(64, 64)),
        "c2im": f32c(cc.imag.reshape(64, 64)),
        "wc": wc, "ws": ws, "wsn": f32c(-ws),
        "twre": f32c(np.cos(l0m0)), "twim": f32c(np.sin(l0m0)),
        "idf": np.eye(64, dtype=np.float32),
        "onesf": np.ones((1, 128), np.float32),
        "ident": np.eye(128, dtype=NPBF),
    }


def s4_inmap(x2_rows, inp):
    m = s4_consts()
    m.update({
        "x": f32c(x2_rows),
        "lam": f32c(inp['L1_attn_Lambda_ri']),
        "pri": f32c(inp['L1_attn_p_ri']),
        "qri": f32c(inp['L1_attn_q_ri']),
        "vctre": f32c(np.asarray(inp['L1_attn_Vc_ri'])[:, :, 0].T),
        "vctim": f32c(np.asarray(inp['L1_attn_Vc_ri'])[:, :, 1].T),
        "ctt": f32c(np.asarray(inp['L1_attn_Ct_ri']).reshape(64, 2)),
        "bp": f32c(np.asarray(inp['L1_attn_Bp']).reshape(64, 1)),
        "dsc": f32c(np.asarray(inp['L1_attn_D']).reshape(1, 1)),
        "lstep": f32c(np.asarray(inp['L1_attn_log_step']).reshape(1, 1)),
    })
    return m



def build_qkv():
    nc = bass.Bass("TRN2", target_bir_lowering=False)
    x_in = nc.dram_tensor("x", [2, 640, D], F32, kind="ExternalInput")
    wqkvT_in = nc.dram_tensor("wqkvT", [D, 1536], F32, kind="ExternalInput")
    qc1L_in = nc.dram_tensor("qc1L", [8, 3, 128, 128], F32, kind="ExternalInput")
    qc2L_in = nc.dram_tensor("qc2L", [8, 3, 128, 1024], F32, kind="ExternalInput")
    kc1L_in = nc.dram_tensor("kc1L", [2, 3, 128, 128], F32, kind="ExternalInput")
    kc2L_in = nc.dram_tensor("kc2L", [2, 3, 128, 256], F32, kind="ExternalInput")
    qc1b_in = nc.dram_tensor("qc1b", [128, 8], F32, kind="ExternalInput")
    qc2b_in = nc.dram_tensor("qc2b", [128, 8], F32, kind="ExternalInput")
    kc1b_in = nc.dram_tensor("kc1b", [128, 2], F32, kind="ExternalInput")
    kc2b_in = nc.dram_tensor("kc2b", [128, 2], F32, kind="ExternalInput")
    mq_in = nc.dram_tensor("mq", [64, 128], F32, kind="ExternalInput")
    mm16_in = nc.dram_tensor("mm16", [128, 64], F32, kind="ExternalInput")
    mm4_in = nc.dram_tensor("mm4", [128, 64], F32, kind="ExternalInput")
    hq_in = nc.dram_tensor("hq", [128, 8 * 16], F32, kind="ExternalInput")
    hk_in = nc.dram_tensor("hk", [128, 2 * 4], F32, kind="ExternalInput")
    bq_in = nc.dram_tensor("bq", [16, 8 * 128], F32, kind="ExternalInput")
    bk_in = nc.dram_tensor("bk", [4, 2 * 128], F32, kind="ExternalInput")
    pswap_in = nc.dram_tensor("pswap", [128, 128], BF16, kind="ExternalInput")
    ce_in = nc.dram_tensor("ce", [2, 128, 512], BF16, kind="ExternalInput")
    se_in = nc.dram_tensor("se", [2, 128, 512], BF16, kind="ExternalInput")
    kt16_in = nc.dram_tensor("kt16", [16, 1], F32, kind="ExternalInput")
    ident_in = nc.dram_tensor("ident", [128, 128], BF16, kind="ExternalInput")
    nz_in = nc.dram_tensor("nz", [128, 2], F32, kind="ExternalInput")
    qh_out = nc.dram_tensor("qh", [2, 1024, 512], BF16, kind="ExternalOutput")
    kh_out = nc.dram_tensor("kh", [2, 256, 512], BF16, kind="ExternalOutput")
    v_out = nc.dram_tensor("v", [2, 512, 256], BF16, kind="ExternalOutput")

    HC = 640

    lnT = [nc.alloc_sbuf_tensor(f"lnT{d}", [128, HC], BF16) for d in range(32)]
    qkvT = [nc.alloc_sbuf_tensor(f"qkvT{i}", [128, HC], BF16) for i in range(12)]
    q1b = [nc.alloc_sbuf_tensor(f"q1b{i}", [128, HC], BF16) for i in range(8)]
    k1b = [nc.alloc_sbuf_tensor(f"k1b{i}", [128, HC], BF16) for i in range(2)]
    q2b = [nc.alloc_sbuf_tensor(f"q2b{i}", [128, 512], BF16) for i in range(8)]
    k2b = [nc.alloc_sbuf_tensor(f"k2b{i}", [128, 512], BF16) for i in range(2)]
    qsq = [nc.alloc_sbuf_tensor(f"qsq{i}", [128, 512], BF16) for i in range(8)]
    ksq = [nc.alloc_sbuf_tensor(f"ksq{i}", [128, 512], BF16) for i in range(2)]
    swp = nc.alloc_sbuf_tensor("swp", [128, 512], BF16)
    rt1 = nc.alloc_sbuf_tensor("rt1", [128, 512], BF16)
    qrb = nc.alloc_sbuf_tensor("qrb", [128, 512], BF16)
    ob = [nc.alloc_sbuf_tensor(f"ob{i}", [128, 512], BF16) for i in range(2)]
    wq = nc.alloc_sbuf_tensor("wq", [128, 32 * 128], BF16)
    c1w = nc.alloc_sbuf_tensor("c1w", [128, 3 * 128], BF16)
    c2wq = nc.alloc_sbuf_tensor("c2wq", [128, 24 * 128], BF16)
    c2wk = nc.alloc_sbuf_tensor("c2wk", [128, 6 * 128], BF16)
    qc1bb = nc.alloc_sbuf_tensor("qc1bb", [128, 8], F32)
    qc2bb = nc.alloc_sbuf_tensor("qc2bb", [128, 8], F32)
    kc1bb = nc.alloc_sbuf_tensor("kc1bb", [128, 2], F32)
    kc2bb = nc.alloc_sbuf_tensor("kc2bb", [128, 2], F32)
    mqb = nc.alloc_sbuf_tensor("mqb", [64, 128], BF16)
    mm16b = nc.alloc_sbuf_tensor("mm16b", [128, 64], BF16)
    mm4b = nc.alloc_sbuf_tensor("mm4b", [128, 64], BF16)
    hqb = nc.alloc_sbuf_tensor("hqb", [128, 8 * 16], BF16)
    hkb = nc.alloc_sbuf_tensor("hkb", [128, 2 * 4], BF16)
    bqb = nc.alloc_sbuf_tensor("bqb", [16, 8 * 128], BF16)
    bkb = nc.alloc_sbuf_tensor("bkb", [4, 2 * 128], BF16)
    pswapb = nc.alloc_sbuf_tensor("pswapb", [128, 128], BF16)
    ceb = nc.alloc_sbuf_tensor("ceb", [128, 512], BF16)
    seb = nc.alloc_sbuf_tensor("seb", [128, 512], BF16)
    rn16 = nc.alloc_sbuf_tensor("rn16", [16, 512], F32)
    rn16b = nc.alloc_sbuf_tensor("rn16b", [16, 512], BF16)
    rn4 = nc.alloc_sbuf_tensor("rn4", [16, 512], F32)
    rn4b = nc.alloc_sbuf_tensor("rn4b", [16, 512], BF16)
    ktv = nc.alloc_sbuf_tensor("ktv", [16, 1], F32)
    qmb = nc.alloc_sbuf_tensor("qmb", [64, 512], BF16)
    kmb = nc.alloc_sbuf_tensor("kmb", [64, 512], BF16)
    xbuf = nc.alloc_sbuf_tensor("xb", [128, D], F32)
    lnbf = nc.alloc_sbuf_tensor("lnbf", [128, D], BF16)
    scr_bf = nc.alloc_sbuf_tensor("scr_bf", [128, D], BF16)
    stat = nc.alloc_sbuf_tensor("stat", [128, 12], F32)
    identbf = nc.alloc_sbuf_tensor("identbf", [128, 128], BF16)
    nzb = nc.alloc_sbuf_tensor("nzb", [128, 2], F32)

    p_tr = [nc.alloc_psum_tensor(f"ptr{i}", [128, 128], BF16) for i in range(2)]
    p_a = nc.alloc_psum_tensor("pa", [128, 512], F32)
    p_b = nc.alloc_psum_tensor("pb", [128, 512], F32)
    p_c = nc.alloc_psum_tensor("pc", [128, 512], F32)
    p_m = nc.alloc_psum_tensor("pm", [64, 512], F32)
    p_h = nc.alloc_psum_tensor("ph", [16, 512], F32)

    sm = Sems(nc, ['g', 'v', 'a', 't', 'o', 'w'])

    with nc.Block() as block:
        @block.gpsimd
        def _(g):
            for dst, src_ in [(qc1bb, qc1b_in), (qc2bb, qc2b_in),
                              (kc1bb, kc1b_in), (kc2bb, kc2b_in),
                              (mqb, mq_in), (mm16b, mm16_in), (mm4b, mm4_in),
                              (hqb, hq_in), (hkb, hk_in), (bqb, bq_in),
                              (bkb, bk_in), (pswapb, pswap_in),
                              (identbf, ident_in), (nzb, nz_in),
                              (ktv, kt16_in)]:
                gdma(g, sm, 'g', dst[:], src_[:])

        for seg in range(2):
            @block.gpsimd
            def _(g, seg=seg):
                if seg > 0:
                    g.wait_ge(sm.h['v'], sm.tag['_rope_done'])
                gdma(g, sm, 'g', ceb[:], ce_in[seg])
                gdma(g, sm, 'g', seb[:], se_in[seg])

            for i in range(5):
                @block.gpsimd
                def _(g, seg=seg, i=i):
                    if '_lnfree' in sm.tag:
                        g.wait_ge(sm.h['v'], sm.tag['_lnfree'])
                    gdma(g, sm, 'g', xbuf[:], x_in[seg, i * 128:(i + 1) * 128, :])
                emit_ln_tile(block, sm, xbuf, lnbf, scr_bf, stat,
                             v_waits=[('g', sm.c['g']), ('t', sm.c['t'])])
                sm.tag['_lnfree'] = sm.c['v']
                emit_transpose_tiles(block, sm, lnbf, lnT, i * 128, identbf,
                                     p_tr)

            for ct in range(12):
                @block.gpsimd
                def _(g, ct=ct):
                    if '_wqfree' in sm.tag:
                        g.wait_ge(sm.h['t'], sm.tag['_wqfree'])
                    gdma(g, sm, 'w', bass.AP(wq, 0, [[32 * 128, 128], [128, 32], [1, 128]]), bass.AP(wqkvT_in, ct * 128,
                                    [[1536, 128], [128 * 1536, 32], [1, 128]]),)
                w_c = sm.c['w']

                for ch, (c0, cn) in enumerate([(0, 512), (512, 128)]):
                    @block.tensor
                    def _(t, ct=ct, ch=ch, c0=c0, cn=cn):
                        t.wait_ge(sm.h['w'], w_c)
                        if ct == 0 and ch == 0:
                            sm.wait(t, 'a')
                            sm.wait(t, 'v')
                        pj = (ct * 2 + ch) % 2
                        pp = [p_a, p_b][pj]
                        if f'_qkvfree{pj}' in sm.tag:
                            t.wait_ge(sm.h['a'], sm.tag[f'_qkvfree{pj}'])
                        for d in range(32):
                            ins = t.matmul(pp[:, 0:cn],
                                           wq[:, d * 128:(d + 1) * 128],
                                           lnT[d][:, c0:c0 + cn],
                                           start=(d == 0), stop=(d == 31))
                        sm.inc(ins, 't')

                    @block.scalar
                    def _(a, ct=ct, ch=ch, c0=c0, cn=cn):
                        sm.wait(a, 't')
                        pj = (ct * 2 + ch) % 2
                        pp = [p_a, p_b][pj]
                        ins = a.activation(out=qkvT[ct][:, c0:c0 + cn],
                                           in_=pp[:, 0:cn], func=AF.Copy)
                        sm.tag[f'_qkvfree{pj}'] = sm.inc(ins, 'a')
                sm.tag['_wqfree'] = sm.c['t']

            @block.tensor
            def _(t, seg=seg):
                sm.wait(t, 'a')
                for it in range(8):
                    t.matmul(p_m[:], mm16b[:], qkvT[it][:, 128:640],
                             start=(it == 0), stop=(it == 7))
                for it in range(2):
                    ins = t.matmul(p_c[0:64, :], mm4b[:],
                                   qkvT[8 + it][:, 128:640],
                                   start=(it == 0), stop=(it == 1))
                sm.inc(ins, 't')

            @block.scalar
            def _(a, seg=seg):
                sm.wait(a, 't')
                a.activation(out=qmb[:], in_=p_m[:], func=AF.Copy)
                ins = a.activation(out=kmb[:], in_=p_c[0:64, :], func=AF.Copy)
                sm.inc(ins, 'a')

            for (ntile, dstl, wsrc, bsrc, ibase) in [
                    (8, q1b, qc1L_in, qc1bb, 0), (2, k1b, kc1L_in, kc1bb, 8)]:
                for ot in range(ntile):
                    @block.gpsimd
                    def _(g, ot=ot, wsrc=wsrc):
                        if '_c1free' in sm.tag:
                            g.wait_ge(sm.h['t'], sm.tag['_c1free'])
                        gdma(g, sm, 'w', bass.AP(c1w, 0, [[3 * 128, 128], [128, 3],
                                                 [1, 128]]), bass.AP(wsrc, ot * 3 * 128 * 128,
                                        [[128, 128], [128 * 128, 3], [1, 128]]),)
                    w_c1 = sm.c['w']

                    for ch, (c0, cn) in enumerate([(124, 512), (636, 4)]):
                        @block.tensor
                        def _(t, ot=ot, ch=ch, c0=c0, cn=cn, ibase=ibase):
                            t.wait_ge(sm.h['w'], w_c1)
                            if '_c1start' not in sm.tag:
                                sm.wait(t, 'a')
                                sm.tag['_c1start'] = 1
                            pj = ch
                            pp = [p_a, p_b][pj]
                            if f'_c1f{pj}' in sm.tag:
                                t.wait_ge(sm.h['v'], sm.tag[f'_c1f{pj}'])
                            for tau in range(3):
                                ins = t.matmul(
                                    pp[:, 0:cn],
                                    c1w[:, tau * 128:(tau + 1) * 128],
                                    qkvT[ibase + ot][:, c0 - 2 + tau:
                                                     c0 - 2 + tau + cn],
                                    start=(tau == 0), stop=(tau == 2))
                            sm.inc(ins, 't')

                        @block.vector
                        def _(v, ot=ot, ch=ch, c0=c0, cn=cn, dstl=dstl,
                              bsrc=bsrc):
                            sm.wait(v, 't')
                            pj = ch
                            pp = [p_a, p_b][pj]
                            ins = v.tensor_scalar(
                                out=dstl[ot][:, c0:c0 + cn], in0=pp[:, 0:cn],
                                scalar1=1.0, scalar2=bsrc[:, ot:ot + 1],
                                op0=ALU.mult, op1=ALU.add)
                            sm.tag[f'_c1f{pj}'] = sm.inc(ins, 'v')
                    sm.tag['_c1free'] = sm.c['t']
            sm.tag.pop('_c1start', None)

            @block.vector
            def _(v, seg=seg):
                sm.wait(v, 'v')
                for dstl, ntile in [(q1b, 8), (k1b, 2)]:
                    for ot in range(ntile):
                        ins = v.tensor_scalar(
                            out=dstl[ot][:, 124:128],
                            in0=dstl[ot][:, 124:128],
                            scalar1=nzb[:, seg:seg + 1], scalar2=None,
                            op0=ALU.mult)
                sm.inc(ins, 'v')

            for ot in range(8):
                @block.gpsimd
                def _(g, ot=ot):
                    if '_c2free' in sm.tag:
                        g.wait_ge(sm.h['t'], sm.tag['_c2free'])
                    gdma(g, sm, 'w', bass.AP(c2wq, 0, [[24 * 128, 128], [128, 24],
                                              [1, 128]]), bass.AP(qc2L_in, ot * 128,
                                    [[1024, 128], [128 * 1024, 24], [1, 128]]),)
                w_c2 = sm.c['w']

                @block.tensor
                def _(t, ot=ot):
                    t.wait_ge(sm.h['w'], w_c2)
                    sm.wait(t, 'v')
                    if '_q2f' in sm.tag:
                        t.wait_ge(sm.h['a'], sm.tag['_q2f'])
                    if '_q2fv' in sm.tag:
                        t.wait_ge(sm.h['v'], sm.tag['_q2fv'])
                    for it in range(8):
                        for tau in range(3):
                            t.matmul(p_c[:],
                                     c2wq[:, (it * 3 + tau) * 128:
                                          (it * 3 + tau + 1) * 128],
                                     q1b[it][:, 126 + tau:126 + tau + 512],
                                     start=(it == 0 and tau == 0), stop=False)
                    ins = t.matmul(p_c[:], mqb[:], kmb[:], start=False,
                                   stop=True)
                    sm.inc(ins, 't')
                sm.tag['_c2free'] = sm.c['t']

                @block.scalar
                def _(a, ot=ot):
                    sm.wait(a, 't')
                    ins = a.activation(out=qsq[ot][:], in_=p_c[:],
                                       func=AF.Square,
                                       bias=qc2bb[:, ot:ot + 1])
                    sm.tag['_q2f'] = sm.inc(ins, 'a')

                @block.vector
                def _(v, ot=ot):
                    sm.wait(v, 'a')
                    ins = v.tensor_scalar(out=q2b[ot][:], in0=p_c[:],
                                          scalar1=1.0,
                                          scalar2=qc2bb[:, ot:ot + 1],
                                          op0=ALU.mult, op1=ALU.add)
                    sm.tag['_q2fv'] = sm.inc(ins, 'v')

            for ot in range(2):
                @block.gpsimd
                def _(g, ot=ot):
                    if '_c2free' in sm.tag:
                        g.wait_ge(sm.h['t'], sm.tag['_c2free'])
                    gdma(g, sm, 'w', bass.AP(c2wk, 0, [[6 * 128, 128], [128, 6],
                                              [1, 128]]), bass.AP(kc2L_in, ot * 128,
                                    [[256, 128], [128 * 256, 6], [1, 128]]),)
                w_k2 = sm.c['w']

                @block.tensor
                def _(t, ot=ot):
                    t.wait_ge(sm.h['w'], w_k2)
                    if '_q2f' in sm.tag:
                        t.wait_ge(sm.h['a'], sm.tag['_q2f'])
                    if '_q2fv' in sm.tag:
                        t.wait_ge(sm.h['v'], sm.tag['_q2fv'])
                    for it in range(2):
                        for tau in range(3):
                            t.matmul(p_c[:],
                                     c2wk[:, (it * 3 + tau) * 128:
                                          (it * 3 + tau + 1) * 128],
                                     k1b[it][:, 126 + tau:126 + tau + 512],
                                     start=(it == 0 and tau == 0), stop=False)
                    ins = t.matmul(p_c[:], mqb[:], qmb[:], start=False,
                                   stop=True)
                    sm.inc(ins, 't')
                sm.tag['_c2free'] = sm.c['t']

                @block.scalar
                def _(a, ot=ot):
                    sm.wait(a, 't')
                    ins = a.activation(out=ksq[ot][:], in_=p_c[:],
                                       func=AF.Square,
                                       bias=kc2bb[:, ot:ot + 1])
                    sm.tag['_q2f'] = sm.inc(ins, 'a')

                @block.vector
                def _(v, ot=ot):
                    sm.wait(v, 'a')
                    ins = v.tensor_scalar(out=k2b[ot][:], in0=p_c[:],
                                          scalar1=1.0,
                                          scalar2=kc2bb[:, ot:ot + 1],
                                          op0=ALU.mult, op1=ALU.add)
                    sm.tag['_q2fv'] = sm.inc(ins, 'v')

            @block.tensor
            def _(t, seg=seg):
                sm.wait(t, 'a')
                sm.wait(t, 'v')
                for ot in range(8):
                    t.matmul(p_h[:], hqb[:, ot * 16:(ot + 1) * 16], qsq[ot][:],
                             start=(ot == 0), stop=(ot == 7))
                t.matmul(p_m[0:4, :], hkb[:, 0:4], ksq[0][:], start=True,
                         stop=False)
                ins = t.matmul(p_m[0:4, :], hkb[:, 4:8], ksq[1][:],
                               start=False, stop=True)
                sm.inc(ins, 't')

            @block.vector
            def _(v, seg=seg):
                sm.wait(v, 't')
                v.tensor_scalar(out=rn16[:], in0=p_h[:], scalar1=1.0,
                                scalar2=1e-24, op0=ALU.mult, op1=ALU.max)
                ins = v.tensor_scalar(out=rn4[0:4, :], in0=p_m[0:4, :],
                                      scalar1=1.0, scalar2=1e-24,
                                      op0=ALU.mult, op1=ALU.max)
                sm.inc(ins, 'v')

            @block.scalar
            def _(a, seg=seg):
                sm.wait(a, 'v')
                a.activation(out=rn16[:], in_=rn16[:], func=AF.Ln)
                ins = a.activation(out=rn4[0:4, :], in_=rn4[0:4, :],
                                   func=AF.Ln)
                sm.inc(ins, 'a')
                sm.wait(a, 'a')
                a.activation(out=rn16[:], in_=rn16[:], func=AF.Exp, scale=-0.5)
                ins = a.activation(out=rn4[0:4, :], in_=rn4[0:4, :],
                                   func=AF.Exp, scale=-0.5)
                sm.inc(ins, 'a')

            @block.vector
            def _(v, seg=seg):
                sm.wait(v, 'a')
                v.tensor_copy(out=rn16b[:], in_=rn16[:])
                ins = v.tensor_scalar(out=rn4[0:4, :], in0=rn4[0:4, :],
                                      scalar1=ktv[0:4, :], scalar2=None,
                                      op0=ALU.mult)
                sm.inc(ins, 'v')
                sm.wait(v, 'v')
                ins = v.tensor_copy(out=rn4b[0:4, :], in_=rn4[0:4, :])
                sm.inc(ins, 'v')

            for (ntile, srcl, bselt, rnt, outdr, nout) in [
                    (8, q2b, bqb, rn16b, qh_out, 16),
                    (2, k2b, bkb, rn4b, kh_out, 4)]:
                for ot in range(ntile):
                    @block.tensor
                    def _(t, ot=ot, srcl=srcl):
                        sm.wait(t, 'v')
                        if '_swf' in sm.tag:
                            t.wait_ge(sm.h['a'], sm.tag['_swf'])
                        ins = t.matmul(p_a[:], pswapb[:], srcl[ot][:],
                                       start=True, stop=True)
                        sm.inc(ins, 't')

                    @block.scalar
                    def _(a, ot=ot):
                        sm.wait(a, 't')
                        if '_ropev' in sm.tag:
                            a.wait_ge(sm.h['v'], sm.tag['_ropev'])
                        ins = a.activation(out=swp[:], in_=p_a[:], func=AF.Copy)
                        sm.tag['_swf'] = sm.inc(ins, 'a')

                    @block.tensor
                    def _(t, ot=ot, bselt=bselt, rnt=rnt, nout=nout):
                        if '_rnbf' in sm.tag:
                            t.wait_ge(sm.h['v'], sm.tag['_rnbf'])
                        ins = t.matmul(p_b[:], bselt[0:nout, ot * 128:
                                                     (ot + 1) * 128],
                                       rnt[0:nout, :], start=True, stop=True)
                        sm.inc(ins, 't')

                    @block.vector
                    def _(v, ot=ot, srcl=srcl):
                        sm.wait(v, 'a')
                        sm.wait(v, 't')
                        oj = ot % 2
                        if f'_obf{oj}' in sm.tag:
                            v.wait_ge(sm.h['o'], sm.tag[f'_obf{oj}'])
                        v.tensor_tensor(out=rt1[:], in0=srcl[ot][:],
                                        in1=ceb[:], op=ALU.mult)
                        ins = v.tensor_tensor(out=qrb[:], in0=swp[:],
                                              in1=seb[:], op=ALU.mult)
                        sm.inc(ins, 'v')
                        sm.wait(v, 'v')
                        ins = v.tensor_tensor(out=qrb[:], in0=rt1[:],
                                              in1=qrb[:], op=ALU.add)
                        sm.inc(ins, 'v')
                        sm.wait(v, 'v')
                        sm.tag['_ropev'] = sm.inc(
                            v.tensor_tensor(out=ob[oj][:], in0=qrb[:],
                                            in1=p_b[:], op=ALU.mult), 'v')
                        sm.tag['_rnbf'] = sm.tag['_ropev']
                    @block.gpsimd
                    def _(g, ot=ot, outdr=outdr, seg=seg):
                        sm.wait(g, 'v')
                        oj = ot % 2
                        gdma(g, sm, 'o', outdr[seg, ot * 128:(ot + 1) * 128, :], ob[oj][:])
                        sm.tag[f'_obf{oj}'] = sm.c['o']

            for st in range(4):
                for half in range(2):
                    @block.tensor
                    def _(t, st=st, half=half):
                        j = (st * 2 + half) % 2
                        key = f'_trw{j}'
                        if key in sm.tag:
                            t.wait_ge(sm.h['a'], sm.tag[key])
                        if st == 0 and half == 0:
                            sm.wait(t, 'a')
                        src_col = 128 + 128 * st - (1 if half == 1 else 0)
                        ins = t.transpose(p_tr[j][:],
                                          qkvT[10 + half][:, src_col:
                                                          src_col + 128],
                                          identbf[:])
                        sm.inc(ins, 't')

                    @block.scalar
                    def _(a, st=st, half=half):
                        sm.wait(a, 't')
                        j = (st * 2 + half) % 2
                        oj = 1 - (st % 2)
                        if f'_obf{oj}' in sm.tag:
                            a.wait_ge(sm.h['o'], sm.tag[f'_obf{oj}'])
                        ins = a.activation(
                            out=ob[oj][:, half * 128:(half + 1) * 128],
                            in_=p_tr[j][:], func=AF.Copy)
                        sm.tag[f'_trw{j}'] = sm.inc(ins, 'a')

                @block.gpsimd
                def _(g, st=st, seg=seg):
                    sm.wait(g, 'a')
                    oj = 1 - (st % 2)
                    gdma(g, sm, 'o', v_out[seg, st * 128:(st + 1) * 128, :], ob[oj][:, 0:256])
                    sm.tag[f'_obf{1 - (st % 2)}'] = sm.c['o']
            sm.tag['_rope_done'] = sm.c['v']

        @block.sync
        def _(sy):
            sy.wait_ge(sm.h['o'], sm.c['o'])

    return nc


def qkv_consts():
    i64h = np.arange(64)
    mq = np.zeros((64, 128), np.float32)
    for c in range(128):
        mq[c % 64, c] = 0.5
    mm16 = np.tile(np.eye(64, dtype=np.float32) / 16.0, (2, 1))
    mm4 = np.tile(np.eye(64, dtype=np.float32) / 4.0, (2, 1))
    hq = np.zeros((128, 8 * 16), np.float32)
    for ot in range(8):
        for i in range(128):
            hq[i, ot * 16 + (ot * 128 + i) // 64] = 1.0
    hk = np.zeros((128, 2 * 4), np.float32)
    for ot in range(2):
        for i in range(128):
            hk[i, ot * 4 + (ot * 128 + i) // 64] = 1.0
    bq = np.zeros((16, 8 * 128), np.float32)
    for ot in range(8):
        for c in range(128):
            bq[(ot * 128 + c) // 64, ot * 128 + c] = 1.0
    bk = np.zeros((4, 2 * 128), np.float32)
    for ot in range(2):
        for c in range(128):
            bk[(ot * 128 + c) // 64, ot * 128 + c] = 1.0
    psw = np.zeros((128, 128), np.float32)
    for i in range(128):
        psw[i ^ 1, i] = 1.0
    return mq, mm16, mm4, hq, hk, bq, bk, psw


def rope_tables(blocks):
    inv = (1.0 / 10000.0 ** (np.arange(0, 64, 2, dtype=np.float32) / 64.0)
           ).astype(np.float32)
    ce = np.zeros((2, 128, 512), np.float32)
    se = np.zeros((2, 128, 512), np.float32)
    for s, b in enumerate(blocks):
        pos = np.arange(b * BLK, (b + 1) * BLK, dtype=np.float32)
        ang = pos[None, :] * inv[:, None]
        c = np.cos(ang)
        sn = np.sin(ang)
        for p in range(128):
            i = (p % 64) // 2
            ce[s, p] = c[i]
            se[s, p] = -sn[i] if (p % 2 == 0) else sn[i]
    return ce, se


def qkv_inmap(core, x_full, inp):
    bs = core_blocks(core)
    xh = np.zeros((2, 640, D), np.float32)
    for s, b in enumerate(bs):
        lo = b * BLK - 128
        seg = x_full[max(lo, 0):(b + 1) * BLK]
        xh[s, 640 - seg.shape[0]:] = seg
    qc1 = np.asarray(inp['L0_attn_qc1_w'], np.float32)
    qc2 = np.asarray(inp['L0_attn_qc2_w'], np.float32)
    kc1 = np.asarray(inp['L0_attn_kc1_w'], np.float32)
    kc2 = np.asarray(inp['L0_attn_kc2_w'], np.float32)
    qc1L = np.zeros((8, 3, 128, 128), np.float32)
    for ot in range(8):
        for c in range(128):
            o = ot * 128 + c
            blk = c // 64
            qc1L[ot, :, blk * 64:blk * 64 + 64, c] = qc1[o].T
    kc1L = np.zeros((2, 3, 128, 128), np.float32)
    for ot in range(2):
        for c in range(128):
            o = ot * 128 + c
            blk = c // 64
            kc1L[ot, :, blk * 64:blk * 64 + 64, c] = kc1[o].T
    qc2L = np.ascontiguousarray(
        np.transpose(qc2.reshape(1024, 8, 128, 3), (1, 3, 2, 0)))
    kc2L = np.ascontiguousarray(
        np.transpose(kc2.reshape(256, 2, 128, 3), (1, 3, 2, 0)))
    mq, mm16, mm4, hq, hk, bq, bk, psw = qkv_consts()
    ce, se = rope_tables(bs)
    nz = np.ones((128, 2), np.float32)
    for s, b in enumerate(bs):
        if b == 0:
            nz[:, s] = 0.0
    return {
        "x": xh,
        "wqkvT": f32c(np.asarray(inp['L0_attn_w_qkv']).T),
        "qc1L": qc1L, "qc2L": qc2L, "kc1L": kc1L, "kc2L": kc2L,
        "qc1b": f32c(np.tile(np.asarray(inp['L0_attn_qc1_b']).reshape(8, 128).T, (1, 1))),
        "qc2b": f32c(np.asarray(inp['L0_attn_qc2_b']).reshape(8, 128).T),
        "kc1b": f32c(np.asarray(inp['L0_attn_kc1_b']).reshape(2, 128).T),
        "kc2b": f32c(np.asarray(inp['L0_attn_kc2_b']).reshape(2, 128).T),
        "mq": mq, "mm16": mm16, "mm4": mm4,
        "hq": hq, "hk": hk, "bq": bq, "bk": bk,
        "pswap": psw.astype(NPBF),
        "ce": ce.astype(NPBF), "se": se.astype(NPBF),
        "kt16": f32c(np.full((16, 1),
                             np.asarray(inp['L0_attn_key_temp']).ravel()[0])),
        "ident": np.eye(128, dtype=NPBF),
        "nz": nz,
    }



def build_attn():
    nc = bass.Bass("TRN2", target_bir_lowering=False)
    qh_in = nc.dram_tensor("qh2", [128, S], BF16, kind="ExternalInput")
    kh_in = nc.dram_tensor("kh1", [64, S], BF16, kind="ExternalInput")
    va_in = nc.dram_tensor("va", [S // 128, 128, 65], BF16, kind="ExternalInput")
    kt_in = nc.dram_tensor("kt128", [128, 1], F32, kind="ExternalInput")
    tri_in = nc.dram_tensor("tri", [128, 128], BF16, kind="ExternalInput")
    ones1_in = nc.dram_tensor("ones1", [1, 128], BF16, kind="ExternalInput")
    ao_out = nc.dram_tensor("ao", [2, 64, S], BF16, kind="ExternalOutput")

    NTL = S // 128
    qha = [nc.alloc_sbuf_tensor(f"qha{i}", [64, S], BF16) for i in range(2)]
    kh1 = nc.alloc_sbuf_tensor("kh1s", [64, S], BF16)
    va = nc.alloc_sbuf_tensor("vas", [128, NTL * 65], BF16)
    pb = [nc.alloc_sbuf_tensor(f"pbuf{i}", [128, 512], BF16) for i in range(2)]
    trib = nc.alloc_sbuf_tensor("trib", [128, 128], BF16)
    aob = [nc.alloc_sbuf_tensor(f"aob{i}", [64, 512], BF16) for i in range(2)]
    lsb = nc.alloc_sbuf_tensor("lsb", [128, 512], F32)
    lbf = nc.alloc_sbuf_tensor("lbf", [128, 512], BF16)
    rls = nc.alloc_sbuf_tensor("rls", [64, 512], F32)
    ktb = nc.alloc_sbuf_tensor("ktb", [128, 1], F32)
    mb = nc.alloc_sbuf_tensor("mb", [128, 1], F32)
    onesbf = nc.alloc_sbuf_tensor("onesbf", [1, 128], BF16)

    p_s = [nc.alloc_psum_tensor(f"ps{i}", [128, 512], F32) for i in range(2)]
    p_o = nc.alloc_psum_tensor("po", [65, 512], F32)
    p_rl = nc.alloc_psum_tensor("prl", [64, 512], F32)

    sm = Sems(nc, ['g', 'v', 'a', 't', 'o'])

    with nc.Block() as block:
        @block.gpsimd
        def _(g):
            gdma(g, sm, 'g', qha[0][:], qh_in[0:64, :])
            gdma(g, sm, 'g', qha[1][:], qh_in[64:128, :])
            gdma(g, sm, 'g', kh1[:], kh_in[:])
            gdma(g, sm, 'g', bass.AP(va, 0, [[NTL * 65, 128], [65, NTL], [1, 65]]), bass.AP(va_in, 0, [[65, 128], [128 * 65, NTL], [1, 65]]))
            gdma(g, sm, 'g', trib[:], tri_in[:])
            gdma(g, sm, 'g', ktb[:], kt_in[:])
            gdma(g, sm, 'g', onesbf[:], ones1_in[:])

        @block.scalar
        def _(a):
            sm.wait(a, 'g')
            ins = a.activation(out=mb[:], in_=ktb[:], func=AF.Abs)
            sm.inc(ins, 'a')

        @block.vector
        def _(v):
            sm.wait(v, 'a')
            ins = v.tensor_scalar(out=mb[:], in0=mb[:], scalar1=-0.1275,
                                  scalar2=None, op0=ALU.mult)
            sm.inc(ins, 'v')

        for qh in range(2):
            for cb in range(16):
                ntl = 4 * cb + 4
                for c in range(ntl):
                    r = c - 4 * cb
                    span0 = max(r, 0) * 128
                    pj = c % 2

                    @block.tensor
                    def _(t, qh=qh, cb=cb, c=c, pj=pj):
                        if qh == 0 and cb == 0 and c == 0:
                            sm.wait(t, 'v')
                        if f'_expf{pj}' in sm.tag:
                            t.wait_ge(sm.h['a'], sm.tag[f'_expf{pj}'])
                        ins = t.matmul(p_s[pj][:],
                                       kh1[:, c * 128:(c + 1) * 128],
                                       qha[qh][:, cb * 512:(cb + 1) * 512],
                                       start=True, stop=True)
                        sm.inc(ins, 't')

                    @block.scalar
                    def _(a, qh=qh, cb=cb, c=c, pj=pj, span0=span0):
                        sm.wait(a, 't')
                        if f'_avf{pj}' in sm.tag:
                            a.wait_ge(sm.h['t'], sm.tag[f'_avf{pj}'])
                        ins = a.activation(out=pb[pj][:, span0:512],
                                           in_=p_s[pj][:, span0:512],
                                           func=AF.Exp, scale=0.125,
                                           bias=mb[:])
                        sm.tag[f'_expf{pj}'] = sm.inc(ins, 'a')

                    if r >= 0:
                        @block.vector
                        def _(v, c=c, pj=pj, span0=span0):
                            sm.wait(v, 'a')
                            ins = v.tensor_tensor(
                                out=pb[pj][:, span0:span0 + 128],
                                in0=pb[pj][:, span0:span0 + 128],
                                in1=trib[:], op=ALU.mult)
                            sm.inc(ins, 'v')

                    @block.tensor
                    def _(t, qh=qh, cb=cb, c=c, pj=pj, span0=span0, r=r,
                          ntl=ntl):
                        if r >= 0:
                            sm.wait(t, 'v')
                        else:
                            sm.wait(t, 'a')
                        if c == 0 and '_pof' in sm.tag:
                            t.wait_ge(sm.h['v'], sm.tag['_pof'])
                        ins = t.matmul(p_o[:, span0:512],
                                       va[:, c * 65:(c + 1) * 65],
                                       pb[pj][:, span0:512],
                                       start=(c == 0), stop=(c == ntl - 1))
                        sm.tag[f'_avf{pj}'] = sm.inc(ins, 't')

                @block.scalar
                def _(a, qh=qh, cb=cb):
                    sm.wait(a, 't')
                    ins = a.activation(out=lsb[64:65, :], in_=p_o[64:65, :],
                                       func=AF.Copy)
                    sm.inc(ins, 'a')

                @block.vector
                def _(v, qh=qh, cb=cb):
                    sm.wait(v, 'a')
                    ins = v.reciprocal(out=lsb[0:1, :], in_=lsb[64:65, :])
                    sm.inc(ins, 'v')
                    sm.wait(v, 'v')
                    ins = v.tensor_copy(out=lbf[0:1, :], in_=lsb[0:1, :])
                    sm.inc(ins, 'v')

                @block.tensor
                def _(t, qh=qh, cb=cb):
                    sm.wait(t, 'v')
                    if '_rlf' in sm.tag:
                        t.wait_ge(sm.h['a'], sm.tag['_rlf'])
                    ins = t.matmul(p_rl[:], onesbf[:, 0:64], lbf[0:1, :],
                                   start=True, stop=True)
                    sm.inc(ins, 't')

                @block.scalar
                def _(a, qh=qh, cb=cb):
                    sm.wait(a, 't')
                    ins = a.activation(out=rls[:], in_=p_rl[:], func=AF.Copy)
                    sm.tag['_rlf'] = sm.inc(ins, 'a')

                @block.vector
                def _(v, qh=qh, cb=cb):
                    sm.wait(v, 'a')
                    oj = cb % 2
                    if f'_aodma{oj}' in sm.tag:
                        v.wait_ge(sm.h['o'], sm.tag[f'_aodma{oj}'])
                    ins = v.tensor_tensor(out=aob[oj][:], in0=p_o[0:64, :],
                                          in1=rls[:], op=ALU.mult)
                    sm.tag['_pof'] = sm.inc(ins, 'v')

                @block.gpsimd
                def _(g, qh=qh, cb=cb):
                    sm.wait(g, 'v')
                    oj = cb % 2
                    gdma(g, sm, 'o', ao_out[qh, :, cb * 512:(cb + 1) * 512], aob[oj][:])
                    sm.tag[f'_aodma{oj}'] = sm.c['o']
                sm.tag['_po_guard'] = sm.c['v']

        @block.sync
        def _(sy):
            sy.wait_ge(sm.h['o'], sm.c['o'])

    return nc


def attn_inmap(core, qf_T, kf_T, v_f, inp):
    kv = core // 2
    va = np.zeros((S // 128, 128, 65), np.float32)
    va[:, :, 0:64] = v_f[:, kv * 64:(kv + 1) * 64].reshape(S // 128, 128, 64)
    va[:, :, 64] = 1.0
    kt = np.asarray(inp['L0_attn_key_temp']).ravel()[0]
    tri = np.triu(np.ones((128, 128), np.float32))
    return {
        "qh2": qf_T[core * 128:(core + 1) * 128, :].astype(NPBF),
        "kh1": kf_T[kv * 64:(kv + 1) * 64, :].astype(NPBF),
        "va": va.astype(NPBF),
        "kt128": f32c(np.full((128, 1), kt)),
        "tri": tri.astype(NPBF),
        "ones1": np.ones((1, 128), dtype=NPBF),
    }



def build_wo():
    nc = bass.Bass("TRN2", target_bir_lowering=False)
    ao_in = nc.dram_tensor("aoc", [1024, ROWS], BF16, kind="ExternalInput")
    woT_in = nc.dram_tensor("woT", [1024, D], F32, kind="ExternalInput")
    x_in = nc.dram_tensor("x", [ROWS, D], F32, kind="ExternalInput")
    out = nc.dram_tensor("out", [ROWS, D], F32, kind="ExternalOutput")

    aoc = [nc.alloc_sbuf_tensor(f"aoc{i}", [128, ROWS], BF16) for i in range(8)]
    ybf = [nc.alloc_sbuf_tensor(f"y{i}", [128, D], BF16) for i in range(NT)]
    wob = nc.alloc_sbuf_tensor("wob", [128, 8 * 512], BF16)
    xbuf = nc.alloc_sbuf_tensor("xb", [128, D], F32)
    lnbf = nc.alloc_sbuf_tensor("lnbf", [128, D], BF16)
    scr_bf = nc.alloc_sbuf_tensor("scr_bf", [128, D], BF16)
    stat = nc.alloc_sbuf_tensor("stat", [128, 12], F32)
    p_y = [nc.alloc_psum_tensor(f"py{i}", [128, 512], F32) for i in range(2)]

    sm = Sems(nc, ['g', 'v', 'a', 't', 'o', 'w'])

    with nc.Block() as block:
        @block.gpsimd
        def _(g):
            for i in range(8):
                gdma(g, sm, 'g', aoc[i][:], ao_in[i * 128:(i + 1) * 128, :])

        for oc in range(8):
            @block.gpsimd
            def _(g, oc=oc):
                if '_wfree' in sm.tag:
                    g.wait_ge(sm.h['t'], sm.tag['_wfree'])
                gdma(g, sm, 'w', bass.AP(wob, 0, [[8 * 512, 128], [512, 8], [1, 512]]), bass.AP(woT_in, oc * 512,
                                [[D, 128], [128 * D, 8], [1, 512]]),)
            w_c = sm.c['w']

            for i in range(NT):
                @block.tensor
                def _(t, oc=oc, i=i):
                    t.wait_ge(sm.h['w'], w_c)
                    if oc == 0 and i == 0:
                        sm.wait(t, 'g')
                    pj = (oc * NT + i) % 2
                    if f'_yfree{pj}' in sm.tag:
                        t.wait_ge(sm.h['a'], sm.tag[f'_yfree{pj}'])
                    for d in range(8):
                        ins = t.matmul(p_y[pj][:],
                                       aoc[d][:, i * 128:(i + 1) * 128],
                                       wob[:, d * 512:(d + 1) * 512],
                                       start=(d == 0), stop=(d == 7))
                    sm.inc(ins, 't')

                @block.scalar
                def _(a, oc=oc, i=i):
                    sm.wait(a, 't')
                    pj = (oc * NT + i) % 2
                    ins = a.activation(out=ybf[i][:, oc * 512:(oc + 1) * 512],
                                       in_=p_y[pj][:], func=AF.Copy)
                    sm.tag[f'_yfree{pj}'] = sm.inc(ins, 'a')
            sm.tag['_wfree'] = sm.c['t']

        for i in range(NT):
            @block.gpsimd
            def _(g, i=i):
                sm.wait(g, 'a')
                if '_ofree' in sm.tag:
                    g.wait_ge(sm.h['o'], sm.tag['_ofree'])
                gdma(g, sm, 'g', xbuf[:], x_in[i * 128:(i + 1) * 128, :])
            emit_ln_tile(block, sm, ybf[i], lnbf, scr_bf, stat,
                         v_waits=[('a', sm.c['a']), ('g', sm.c['g'])])

            @block.vector
            def _(v, i=i):
                sm.wait(v, 'g')
                sm.wait(v, 'v')
                ins = v.tensor_tensor(out=xbuf[:], in0=xbuf[:], in1=lnbf[:],
                                      op=ALU.add)
                sm.inc(ins, 'v')

            @block.gpsimd
            def _(g, i=i):
                sm.wait(g, 'v')
                gdma(g, sm, 'o', out[i * 128:(i + 1) * 128, :], xbuf[:])
            sm.tag['_ofree'] = sm.c['o']

        @block.sync
        def _(sy):
            sy.wait_ge(sm.h['o'], sm.c['o'])

    return nc





def emit_barrier(block, sm, engines=('gpsimd', 'vector', 'scalar', 'tensor')):
    for en in engines:
        @getattr(block, en)
        def _(e):
            for n in sm.h:
                sm.wait(e, n)


def build_fused(stop_after=99):
    from contextlib import ExitStack
    nc = bass.Bass("TRN2", target_bir_lowering=False)
    F = F32
    NSLOT = 16
    SLOTC = 516
    x_in = nc.dram_tensor("x", [2, 640, D], F, kind="ExternalInput")
    xtr_in = nc.dram_tensor("xtr", [D, NSLOT * SLOTC], F, kind="ExternalInput")
    wqkvT_in = nc.dram_tensor("wqkvT", [D, 1536], F, kind="ExternalInput")
    qc1L_in = nc.dram_tensor("qc1L", [8, 3, 128, 128], F, kind="ExternalInput")
    qc2L_in = nc.dram_tensor("qc2L", [8, 3, 128, 1024], F, kind="ExternalInput")
    kc1L_in = nc.dram_tensor("kc1L", [2, 3, 128, 128], F, kind="ExternalInput")
    kc2L_in = nc.dram_tensor("kc2L", [2, 3, 128, 256], F, kind="ExternalInput")
    qc1b_in = nc.dram_tensor("qc1b", [128, 8], F, kind="ExternalInput")
    qc2b_in = nc.dram_tensor("qc2b", [128, 8], F, kind="ExternalInput")
    kc1b_in = nc.dram_tensor("kc1b", [128, 2], F, kind="ExternalInput")
    kc2b_in = nc.dram_tensor("kc2b", [128, 2], F, kind="ExternalInput")
    mq_in = nc.dram_tensor("mq", [64, 128], F, kind="ExternalInput")
    mm16_in = nc.dram_tensor("mm16", [128, 64], F, kind="ExternalInput")
    mm4_in = nc.dram_tensor("mm4", [128, 64], F, kind="ExternalInput")
    hq_in = nc.dram_tensor("hq", [128, 8 * 16], F, kind="ExternalInput")
    hk_in = nc.dram_tensor("hk", [128, 2 * 4], F, kind="ExternalInput")
    bq_in = nc.dram_tensor("bq", [16, 8 * 128], F, kind="ExternalInput")
    bk_in = nc.dram_tensor("bk", [4, 2 * 128], F, kind="ExternalInput")
    pswap_in = nc.dram_tensor("pswap", [128, 128], BF16, kind="ExternalInput")
    ceq_in = nc.dram_tensor("ceq", [2, 128, 512], BF16, kind="ExternalInput")
    seq_in = nc.dram_tensor("seq", [2, 128, 512], BF16, kind="ExternalInput")
    cek_in = nc.dram_tensor("cek", [NSLOT, 128, 512], BF16, kind="ExternalInput")
    sek_in = nc.dram_tensor("sek", [NSLOT, 128, 512], BF16, kind="ExternalInput")
    kt16_in = nc.dram_tensor("kt16", [16, 1], F, kind="ExternalInput")
    kt128_in = nc.dram_tensor("kt128", [128, 1], F, kind="ExternalInput")
    vis_in = nc.dram_tensor("vis", [128, 128], F, kind="ExternalInput")
    nzkv_in = nc.dram_tensor("nzkv", [128, NSLOT], F, kind="ExternalInput")
    nzq_in = nc.dram_tensor("nzq", [128, 2], F, kind="ExternalInput")
    tri_in = nc.dram_tensor("tri", [128, 128], BF16, kind="ExternalInput")
    ident_in = nc.dram_tensor("ident", [128, 128], BF16, kind="ExternalInput")
    ones1_in = nc.dram_tensor("ones1", [1, 128], BF16, kind="ExternalInput")
    onescol_in = nc.dram_tensor("onescol", [128, 1], BF16, kind="ExternalInput")
    woT_in = nc.dram_tensor("woT", [1024, D], F, kind="ExternalInput")

    qdr = nc.dram_tensor("qdr", [1024, ROWS], BF16, kind="Internal")
    aodr = nc.dram_tensor("aodr", [1024, ROWS], BF16, kind="Internal")
    x1dr = nc.dram_tensor("x1dr", [ROWS, D], F, kind="Internal")
    if stop_after <= 1:
        out = nc.dram_tensor("out", [ROWS, D], F, kind="ExternalOutput")
        qdbg = nc.dram_tensor("qdbg", [1024, ROWS], BF16, kind="ExternalOutput")
        aodbg = nc.dram_tensor("aodbg", [1024, ROWS], BF16, kind="ExternalOutput")
        khdbg = nc.dram_tensor("khdbg", [256, S], BF16, kind="ExternalOutput")
        vadbg = nc.dram_tensor("vadbg", [128, 64 * 4 * 65], BF16,
                               kind="ExternalOutput")
        q2dbg = nc.dram_tensor("q2dbg", [2, 1024, 512], BF16,
                               kind="ExternalOutput")
        kmdbg = nc.dram_tensor("kmdbg", [64, 1024], BF16,
                               kind="ExternalOutput")

    sm = Sems(nc, ['g', 'v', 'a', 't', 'o', 'w', 'u', 'm'])

    es_glob = ExitStack()
    sb = lambda n, s, d: es_glob.enter_context(nc.sbuf_tensor(n, s, d))

    identbf = sb("identbf", [128, 128], BF16)
    onesbf = sb("onesbf", [1, 128], BF16)
    onescol = sb("onescol_s", [128, 1], BF16)
    stat = sb("stat", [128, 12], F)

    with nc.Block() as block:
        @block.gpsimd
        def _(g):
            gdma(g, sm, 'g', identbf[:], ident_in[:])
            gdma(g, sm, 'g', onesbf[:], ones1_in[:])
            gdma(g, sm, 'g', onescol[:], onescol_in[:])

        es_kv = ExitStack()
        sbkv = lambda n, s, d: es_kv.enter_context(nc.sbuf_tensor(n, s, d))
        khat = [sbkv(f"khat{i}", [128, S], BF16) for i in range(2)]
        va = sbkv("va", [128, 64 * 4 * 65], BF16)

        es_a = ExitStack()
        sba = lambda n, s, d: es_a.enter_context(nc.sbuf_tensor(n, s, d))
        kmb2 = sba("kmb2", [64, 2 * 512], BF16)
        qc1bb = sba("qc1bb", [128, 8], F)
        qc2bb = sba("qc2bb", [128, 8], F)
        kc1bb = sba("kc1bb", [128, 2], F)
        kc2bb = sba("kc2bb", [128, 2], F)
        mqb = sba("mqb", [64, 128], BF16)
        mm16b = sba("mm16b", [128, 64], BF16)
        mm4b = sba("mm4b", [128, 64], BF16)
        hqb = sba("hqb", [128, 8 * 16], BF16)
        hkb = sba("hkb", [128, 2 * 4], BF16)
        bqb = sba("bqb", [16, 8 * 128], BF16)
        bkb = sba("bkb", [4, 2 * 128], BF16)
        pswapb = sba("pswapb", [128, 128], BF16)
        ktv = sba("ktv", [16, 1], F)
        nzkvb = sba("nzkvb", [128, NSLOT], F)
        nzqb = sba("nzqb", [128, 2], F)
        sdr = nc.dram_tensor("sdr", [2, NSLOT * SLOTC], F, kind="Internal")

        es_s1 = ExitStack()
        sb1 = lambda n, s, d: es_s1.enter_context(nc.sbuf_tensor(n, s, d))
        xtbs = [sb1("xtb", [128, 32 * SLOTC], BF16)] * 2
        wkvm = sb1("wkvm", [128, 32 * 640], BF16)
        kvr = [sb1(f"kvr{i}", [128, SLOTC], BF16) for i in range(5)]
        k1b = [sb1(f"k1b{i}", [128, SLOTC], BF16) for i in range(2)]
        k2b = [sb1(f"k2b{i}", [128, 512], BF16) for i in range(2)]
        ksq = [sb1(f"ksq{i}", [128, 512], BF16) for i in range(2)]
        sqb = [sb1(f"sqb{i}", [128, SLOTC], BF16) for i in range(2)]
        rowb = sb1("rowb", [1, 2 * SLOTC], BF16)
        rbb = sb1("rbb", [128, SLOTC], BF16)
        strow = sb1("strow", [1, 2 * SLOTC], F)
        rowb2 = sb1("rowb2", [1, 2 * SLOTC], F)
        urow = sb1("urow", [1, 640], BF16)
        wqmf = sb1("wqmf", [128, 64], F32)
        c1w = sb1("c1w1", [128, 3 * 128], BF16)
        c2wk = sb1("c2wk", [128, 6 * 128], BF16)
        ceb = sb1("ceb1", [128, 512], BF16)
        seb = sb1("seb1", [128, 512], BF16)
        rn4 = sb1("rn4", [16, 512], F)
        rn4b = sb1("rn4b", [16, 512], BF16)
        swp = sb1("swp1", [128, 512], BF16)
        rt1 = sb1("rt11", [128, 512], BF16)

        ps = ExitStack()
        pst = lambda n, s, d: ps.enter_context(nc.psum_tensor(n, s, d))
        p_tr = [pst(f"ptr{i}", [128, 128], BF16) for i in range(2)]
        p_a = pst("pa", [128, 512], F)
        p_b = pst("pb", [128, 512], F)
        p_c = pst("pc", [128, 512], F)
        p_m = pst("pm", [64, 512], F)
        p_h = pst("ph", [16, 512], F)

        @block.gpsimd
        def _(g):
            for dst, src_ in [(qc1bb, qc1b_in), (qc2bb, qc2b_in),
                              (kc1bb, kc1b_in), (kc2bb, kc2b_in),
                              (mqb, mq_in), (mm16b, mm16_in), (mm4b, mm4_in),
                              (hqb, hq_in), (hkb, hk_in), (bqb, bq_in),
                              (bkb, bk_in), (pswapb, pswap_in),
                              (ktv, kt16_in), (nzkvb, nzkv_in),
                              (nzqb, nzq_in)]:
                gdma(g, sm, 'g', dst[:], src_[:])
            ins = g.memset(va[:], 1.0)
            sm.inc(ins, 'm')

        for dt in range(32):
            @block.gpsimd
            def _(g, dt=dt):
                if '_wqmf' in sm.tag:
                    g.wait_ge(sm.h['v'], sm.tag['_wqmf'])
                gdma(g, sm, 'w', xtbs[0][:, 0:1536],
                     wqkvT_in[dt * 128:(dt + 1) * 128, :])

            @block.vector
            def _(v, dt=dt):
                sm.wait(v, 'w')
                base = dt * 640
                v.tensor_copy(out=wkvm[:, base + 128:base + 640],
                              in_=xtbs[0][:, 1024:1536])
                ins = v.reduce_sum(
                    out=wqmf[:],
                    in_=bass.AP(xtbs[0], 0,
                                [[32 * SLOTC, 128], [1, 64], [64, 16]]),
                    axis=mybir.AxisListType.X)
                sm.inc(ins, 'v')
                sm.wait(v, 'v')
                v.memset(wkvm[:, base + 64:base + 128], 0.0)
                ins = v.tensor_scalar(out=wkvm[:, base:base + 64],
                                      in0=wqmf[:],
                                      scalar1=1.0 / 16.0, scalar2=None,
                                      op0=ALU.mult)
                sm.tag['_wqmf'] = sm.inc(ins, 'v')

        for ch in range(2):
            @block.tensor
            def _(t, ch=ch):
                if ch == 0:
                    sm.wait(t, 'v')
                    sm.wait(t, 'g')
                if '_urowf' in sm.tag:
                    t.wait_ge(sm.h['a'], sm.tag['_urowf'])
                for dt in range(32):
                    ins = t.matmul(p_a[0:1, 0:320], onescol[:],
                                   wkvm[:, dt * 640 + ch * 320:
                                        dt * 640 + ch * 320 + 320],
                                   start=(dt == 0), stop=(dt == 31))
                sm.inc(ins, 't')

            @block.scalar
            def _(a, ch=ch):
                sm.wait(a, 't')
                ins = a.activation(out=urow[:, ch * 320:(ch + 1) * 320],
                                   in_=p_a[0:1, 0:320], func=AF.Copy)
                sm.tag['_urowf'] = sm.inc(ins, 'a')

        CHK = [(0, 512), (512, 4)]

        for slot in range(NSLOT):
            xtb = xtbs[slot % 2]

            @block.gpsimd
            def _(g, slot=slot, xtb=xtb):
                if slot == 0:
                    g.wait_ge(sm.h['v'], sm.tag['_wqmf'])
                if '_xtf0' in sm.tag:
                    g.wait_ge(sm.h['t'], sm.tag['_xtf0'])
                for q4 in range(4):
                    gdma(g, sm, 'w',
                         bass.AP(xtb, q4 * 8 * SLOTC,
                                 [[32 * SLOTC, 128], [SLOTC, 8], [1, SLOTC]]),
                         bass.AP(xtr_in, slot * SLOTC + q4 * 8 * 128 * (NSLOT * SLOTC),
                                 [[NSLOT * SLOTC, 128],
                                  [128 * NSLOT * SLOTC, 8], [1, SLOTC]]))

            for ci, (c0, cn) in enumerate(CHK):
                @block.tensor
                def _(t, slot=slot, ci=ci, c0=c0, cn=cn):
                    sm.wait(t, 'w')
                    if '_strf' in sm.tag:
                        t.wait_ge(sm.h['a'], sm.tag['_strf'])
                    for dt in range(32):
                        ins = t.matmul(p_a[0:1, 0:cn], onescol[:],
                                       xtb[:, dt * SLOTC + c0:dt * SLOTC + c0 + cn],
                                       start=(dt == 0), stop=(dt == 31))
                    sm.inc(ins, 't')

                @block.scalar
                def _(a, slot=slot, ci=ci, c0=c0, cn=cn):
                    sm.wait(a, 't')
                    ins = a.activation(out=strow[:, c0:c0 + cn],
                                       in_=p_a[0:1, 0:cn], func=AF.Copy)
                    sm.tag['_strf'] = sm.inc(ins, 'a')

            for ci, (c0, cn) in enumerate(CHK):
                for dt in range(32):
                    @block.vector
                    def _(v, slot=slot, dt=dt, c0=c0, cn=cn):
                        if dt < 2 and ci == 0:
                            sm.wait(v, 'w')
                        j = dt % 2
                        if f'_sqf{j}' in sm.tag:
                            v.wait_ge(sm.h['t'], sm.tag[f'_sqf{j}'])
                        ins = v.tensor_tensor(
                            out=sqb[j][:, 0:cn],
                            in0=xtb[:, dt * SLOTC + c0:dt * SLOTC + c0 + cn],
                            in1=xtb[:, dt * SLOTC + c0:dt * SLOTC + c0 + cn],
                            op=ALU.mult)
                        sm.inc(ins, 'v')

                    @block.tensor
                    def _(t, slot=slot, dt=dt, c0=c0, cn=cn):
                        sm.wait(t, 'v')
                        if dt == 0 and '_str2f' in sm.tag:
                            t.wait_ge(sm.h['a'], sm.tag['_str2f'])
                        j = dt % 2
                        ins = t.matmul(p_b[0:1, 0:cn], onescol[:],
                                       sqb[j][:, 0:cn],
                                       start=(dt == 0), stop=(dt == 31))
                        sm.tag[f'_sqf{j}'] = sm.inc(ins, 't')

                @block.scalar
                def _(a, slot=slot, ci=ci, c0=c0, cn=cn):
                    sm.wait(a, 't')
                    ins = a.activation(out=strow[:, SLOTC + c0:SLOTC + c0 + cn],
                                       in_=p_b[0:1, 0:cn], func=AF.Copy)
                    sm.tag['_str2f'] = sm.inc(ins, 'a')

            @block.vector
            def _(v, slot=slot):
                sm.wait(v, 'a')
                ins = v.tensor_scalar(out=strow[:, 0:SLOTC],
                                      in0=strow[:, 0:SLOTC],
                                      scalar1=1.0 / D, scalar2=None,
                                      op0=ALU.mult)
                sm.inc(ins, 'v'); sm.wait(v, 'v')
                ins = v.tensor_tensor(out=rowb2[:, 0:SLOTC],
                                      in0=strow[:, 0:SLOTC],
                                      in1=strow[:, 0:SLOTC], op=ALU.mult)
                sm.inc(ins, 'v'); sm.wait(v, 'v')
                ins = v.tensor_scalar(out=strow[:, SLOTC:2 * SLOTC],
                                      in0=strow[:, SLOTC:2 * SLOTC],
                                      scalar1=1.0 / D, scalar2=None,
                                      op0=ALU.mult)
                sm.inc(ins, 'v'); sm.wait(v, 'v')
                ins = v.tensor_tensor(out=strow[:, SLOTC:2 * SLOTC],
                                      in0=strow[:, SLOTC:2 * SLOTC],
                                      in1=rowb2[:, 0:SLOTC], op=ALU.subtract)
                sm.inc(ins, 'v'); sm.wait(v, 'v')
                ins = v.tensor_scalar(out=strow[:, SLOTC:2 * SLOTC],
                                      in0=strow[:, SLOTC:2 * SLOTC],
                                      scalar1=EPS, scalar2=None, op0=ALU.add)
                sm.inc(ins, 'v')

            @block.scalar
            def _(a, slot=slot):
                sm.wait(a, 'v')
                ins = a.activation(out=strow[:, SLOTC:2 * SLOTC],
                                   in_=strow[:, SLOTC:2 * SLOTC], func=AF.Ln)
                sm.inc(ins, 'a'); sm.wait(a, 'a')
                ins = a.activation(out=strow[:, SLOTC:2 * SLOTC],
                                   in_=strow[:, SLOTC:2 * SLOTC], func=AF.Exp,
                                   scale=-0.5)
                sm.inc(ins, 'a')

            @block.vector
            def _(v, slot=slot):
                sm.wait(v, 'a')
                ins = v.tensor_tensor(out=rowb2[:, 0:SLOTC],
                                      in0=strow[:, 0:SLOTC],
                                      in1=strow[:, SLOTC:2 * SLOTC],
                                      op=ALU.mult)
                sm.inc(ins, 'v'); sm.wait(v, 'v')
                v.tensor_scalar(out=rowb[:, 0:SLOTC], in0=rowb2[:, 0:SLOTC],
                                scalar1=-1.0, scalar2=None, op0=ALU.mult)
                ins = v.tensor_copy(out=rowb[:, SLOTC:2 * SLOTC],
                                    in_=strow[:, SLOTC:2 * SLOTC])
                sm.inc(ins, 'v')

            for ci, (c0, cn) in enumerate(CHK):
                @block.tensor
                def _(t, slot=slot, ci=ci, c0=c0, cn=cn):
                    sm.wait(t, 'v')
                    if '_rbf' in sm.tag:
                        t.wait_ge(sm.h['a'], sm.tag['_rbf'])
                    ins = t.matmul(p_b[:, 0:cn], onesbf[:],
                                   rowb[:, SLOTC + c0:SLOTC + c0 + cn],
                                   start=True, stop=True)
                    sm.inc(ins, 't')

                @block.scalar
                def _(a, slot=slot, ci=ci, c0=c0, cn=cn):
                    sm.wait(a, 't')
                    ins = a.activation(out=rbb[:, c0:c0 + cn],
                                       in_=p_b[:, 0:cn], func=AF.Copy)
                    sm.tag['_rbf'] = sm.inc(ins, 'a')

                for ct in range(5):
                    @block.tensor
                    def _(t, slot=slot, ct=ct, c0=c0, cn=cn):
                        if f'_kvrf' in sm.tag:
                            t.wait_ge(sm.h['v'], sm.tag['_kvrf'])
                        t.matmul(p_c[:, 0:cn], urow[:, ct * 128:(ct + 1) * 128],
                                 rowb[:, c0:c0 + cn], start=True, stop=False)
                        for dt in range(32):
                            ins = t.matmul(
                                p_c[:, 0:cn],
                                wkvm[:, dt * 640 + ct * 128:dt * 640 + (ct + 1) * 128],
                                xtb[:, dt * SLOTC + c0:dt * SLOTC + c0 + cn],
                                start=False, stop=(dt == 31))
                        sm.inc(ins, 't')

                    @block.vector
                    def _(v, slot=slot, ct=ct, c0=c0, cn=cn):
                        sm.wait(v, 't')
                        sm.wait(v, 'a')
                        ins = v.tensor_tensor(out=kvr[ct][:, c0:c0 + cn],
                                              in0=p_c[:, 0:cn],
                                              in1=rbb[:, c0:c0 + cn],
                                              op=ALU.mult)
                        sm.tag['_kvrf'] = sm.inc(ins, 'v')

            sm.tag['_xtf0'] = sm.c['t']

            for ot in range(2):
                @block.gpsimd
                def _(g, ot=ot):
                    if '_c1kf' in sm.tag:
                        g.wait_ge(sm.h['t'], sm.tag['_c1kf'])
                    gdma(g, sm, 'u',
                         bass.AP(c1w, 0, [[3 * 128, 128], [128, 3], [1, 128]]),
                         bass.AP(kc1L_in, ot * 3 * 128 * 128,
                                 [[128, 128], [128 * 128, 3], [1, 128]]))

                for ci, (c0, cn) in enumerate([(2, 512), (514, 2)]):
                    @block.tensor
                    def _(t, ot=ot, ci=ci, c0=c0, cn=cn):
                        sm.wait(t, 'u')
                        sm.wait(t, 'v')
                        pj = ci
                        pp = [p_a, p_b][pj]
                        if f'_c1kp{pj}' in sm.tag:
                            t.wait_ge(sm.h['v'], sm.tag[f'_c1kp{pj}'])
                        for tau in range(3):
                            ins = t.matmul(pp[:, 0:cn],
                                           c1w[:, tau * 128:(tau + 1) * 128],
                                           kvr[1 + ot][:, c0 - 2 + tau:
                                                       c0 - 2 + tau + cn],
                                           start=(tau == 0), stop=(tau == 2))
                        sm.inc(ins, 't')

                    @block.vector
                    def _(v, ot=ot, ci=ci, c0=c0, cn=cn, slot=slot):
                        sm.wait(v, 't')
                        pj = ci
                        pp = [p_a, p_b][pj]
                        ins = v.tensor_scalar(out=k1b[ot][:, c0:c0 + cn],
                                              in0=pp[:, 0:cn], scalar1=1.0,
                                              scalar2=kc1bb[:, ot:ot + 1],
                                              op0=ALU.mult, op1=ALU.add)
                        sm.tag[f'_c1kp{pj}'] = sm.inc(ins, 'v')
                sm.tag['_c1kf'] = sm.c['t']

            @block.vector
            def _(v, slot=slot):
                sm.wait(v, 'v')
                for ot in range(2):
                    ins = v.tensor_scalar(out=k1b[ot][:, 2:4],
                                          in0=k1b[ot][:, 2:4],
                                          scalar1=nzkvb[:, slot:slot + 1],
                                          scalar2=None, op0=ALU.mult)
                sm.inc(ins, 'v')

            for ot in range(2):
                @block.gpsimd
                def _(g, ot=ot):
                    if '_c2kf' in sm.tag:
                        g.wait_ge(sm.h['t'], sm.tag['_c2kf'])
                    gdma(g, sm, 'u',
                         bass.AP(c2wk, 0, [[6 * 128, 128], [128, 6], [1, 128]]),
                         bass.AP(kc2L_in, ot * 128,
                                 [[256, 128], [128 * 256, 6], [1, 128]]))

                @block.tensor
                def _(t, ot=ot, slot=slot):
                    sm.wait(t, 'u')
                    sm.wait(t, 'v')
                    if '_k2f' in sm.tag:
                        t.wait_ge(sm.h['a'], sm.tag['_k2f'])
                    if '_k2fv' in sm.tag:
                        t.wait_ge(sm.h['v'], sm.tag['_k2fv'])
                    for it in range(2):
                        for tau in range(3):
                            t.matmul(p_c[:],
                                     c2wk[:, (it * 3 + tau) * 128:
                                          (it * 3 + tau + 1) * 128],
                                     k1b[it][:, 2 + tau:2 + tau + 512],
                                     start=(it == 0 and tau == 0), stop=False)
                    ins = t.matmul(p_c[:], mqb[:], kvr[0][0:64, 4:516],
                                   start=False, stop=True)
                    sm.inc(ins, 't')
                sm.tag['_c2kf'] = sm.c['t']

                @block.scalar
                def _(a, ot=ot):
                    sm.wait(a, 't')
                    ins = a.activation(out=ksq[ot][:], in_=p_c[:],
                                       func=AF.Square,
                                       bias=kc2bb[:, ot:ot + 1])
                    sm.tag['_k2f'] = sm.inc(ins, 'a')

                @block.vector
                def _(v, ot=ot):
                    sm.wait(v, 'a')
                    ins = v.tensor_scalar(out=k2b[ot][:], in0=p_c[:],
                                          scalar1=1.0,
                                          scalar2=kc2bb[:, ot:ot + 1],
                                          op0=ALU.mult, op1=ALU.add)
                    sm.tag['_k2fv'] = sm.inc(ins, 'v')

            if slot < 2:
                @block.tensor
                def _(t, slot=slot):
                    sm.wait(t, 'v')
                    if '_kmf' in sm.tag:
                        t.wait_ge(sm.h['a'], sm.tag['_kmf'])
                    for it in range(2):
                        ins = t.matmul(p_m[:], mm4b[:],
                                       kvr[1 + it][:, 4:516],
                                       start=(it == 0), stop=(it == 1))
                    sm.inc(ins, 't')

                @block.scalar
                def _(a, slot=slot):
                    sm.wait(a, 't')
                    ins = a.activation(out=kmb2[:, slot * 512:(slot + 1) * 512],
                                       in_=p_m[:], func=AF.Copy)
                    sm.tag['_kmf'] = sm.inc(ins, 'a')

            @block.tensor
            def _(t, slot=slot):
                sm.wait(t, 'a')
                sm.wait(t, 'v')
                if '_rn4f' in sm.tag:
                    t.wait_ge(sm.h['v'], sm.tag['_rn4f'])
                t.matmul(p_h[0:4, :], hkb[:, 0:4], ksq[0][:],
                         start=True, stop=False)
                ins = t.matmul(p_h[0:4, :], hkb[:, 4:8], ksq[1][:],
                               start=False, stop=True)
                sm.inc(ins, 't')

            @block.vector
            def _(v, slot=slot):
                sm.wait(v, 't')
                ins = v.tensor_scalar(out=rn4[0:4, :], in0=p_h[0:4, :],
                                      scalar1=1.0, scalar2=1e-24,
                                      op0=ALU.mult, op1=ALU.max)
                sm.tag['_rn4f'] = sm.inc(ins, 'v')

            @block.scalar
            def _(a, slot=slot):
                sm.wait(a, 'v')
                ins = a.activation(out=rn4[0:4, :], in_=rn4[0:4, :], func=AF.Ln)
                sm.inc(ins, 'a'); sm.wait(a, 'a')
                ins = a.activation(out=rn4[0:4, :], in_=rn4[0:4, :],
                                   func=AF.Exp, scale=-0.5)
                sm.inc(ins, 'a')

            @block.vector
            def _(v, slot=slot):
                sm.wait(v, 'a')
                ins = v.tensor_scalar(out=rn4[0:4, :], in0=rn4[0:4, :],
                                      scalar1=ktv[0:4, :], scalar2=None,
                                      op0=ALU.mult)
                sm.inc(ins, 'v'); sm.wait(v, 'v')
                ins = v.tensor_copy(out=rn4b[0:4, :], in_=rn4[0:4, :])
                sm.inc(ins, 'v')

            @block.gpsimd
            def _(g, slot=slot):
                if '_cebf' in sm.tag:
                    g.wait_ge(sm.h['v'], sm.tag['_cebf'])
                gdma(g, sm, 'u', ceb[:], cek_in[slot])
                gdma(g, sm, 'u', seb[:], sek_in[slot])

            for ot in range(2):
                @block.tensor
                def _(t, ot=ot, slot=slot):
                    sm.wait(t, 'v')
                    if '_swkf' in sm.tag:
                        t.wait_ge(sm.h['a'], sm.tag['_swkf'])
                    ins = t.matmul(p_a[:], pswapb[:], k2b[ot][:],
                                   start=True, stop=True)
                    sm.inc(ins, 't')

                @block.scalar
                def _(a, ot=ot):
                    sm.wait(a, 't')
                    if '_ropekv' in sm.tag:
                        a.wait_ge(sm.h['v'], sm.tag['_ropekv'])
                    ins = a.activation(out=swp[:], in_=p_a[:], func=AF.Copy)
                    sm.tag['_swkf'] = sm.inc(ins, 'a')

                @block.tensor
                def _(t, ot=ot, slot=slot):
                    if '_rnbkf' in sm.tag:
                        t.wait_ge(sm.h['v'], sm.tag['_rnbkf'])
                    ins = t.matmul(p_b[:], bkb[0:4, ot * 128:(ot + 1) * 128],
                                   rn4b[0:4, :], start=True, stop=True)
                    sm.inc(ins, 't')

                @block.vector
                def _(v, ot=ot, slot=slot):
                    sm.wait(v, 'a')
                    sm.wait(v, 't')
                    sm.wait(v, 'u')
                    v.tensor_tensor(out=rt1[:], in0=k2b[ot][:], in1=ceb[:],
                                    op=ALU.mult)
                    ins = v.tensor_tensor(out=swp[:], in0=swp[:], in1=seb[:],
                                          op=ALU.mult)
                    sm.inc(ins, 'v'); sm.wait(v, 'v')
                    ins = v.tensor_tensor(out=swp[:], in0=rt1[:], in1=swp[:],
                                          op=ALU.add)
                    sm.inc(ins, 'v'); sm.wait(v, 'v')
                    ins = v.tensor_tensor(
                        out=khat[ot][:, slot * 512:(slot + 1) * 512],
                        in0=swp[:], in1=p_b[:], op=ALU.mult)
                    sm.tag['_ropekv'] = sm.inc(ins, 'v')
                    sm.tag['_rnbkf'] = sm.tag['_ropekv']
            sm.tag['_cebf'] = sm.c['v']

            for st in range(4):
                for half in range(2):
                    @block.tensor
                    def _(t, st=st, half=half, slot=slot):
                        j = (st * 2 + half) % 2
                        key = f'_trv{j}'
                        if key in sm.tag:
                            t.wait_ge(sm.h['v'], sm.tag[key])
                        if st == 0 and half == 0:
                            sm.wait(t, 'v')
                            sm.wait(t, 'm')
                        src_col = 4 + 128 * st - (1 if half == 1 else 0)
                        ins = t.transpose(p_tr[j][:],
                                          kvr[3 + half][:, src_col:src_col + 128],
                                          identbf[:])
                        sm.inc(ins, 't')

                    @block.vector
                    def _(v, st=st, half=half, slot=slot):
                        sm.wait(v, 't')
                        j = (st * 2 + half) % 2
                        T = slot * 4 + st
                        for kvh in range(2):
                            kv = half * 2 + kvh
                            ins = v.tensor_copy(
                                out=va[:, T * 260 + kv * 65:T * 260 + kv * 65 + 64],
                                in_=p_tr[j][0:128, kvh * 64:kvh * 64 + 64])
                            sm.inc(ins, 'v')
                        sm.tag[f'_trv{j}'] = sm.c['v']
            sm.tag['_xtf'] = sm.c['t']

        emit_barrier(block, sm)
        ps.close()
        es_s1.close()
        es_s2 = ExitStack()
        sb2 = lambda n, s, d: es_s2.enter_context(nc.sbuf_tensor(n, s, d))
        lnT = [sb2(f"lnT{d_}", [128, 640], BF16) for d_ in range(32)]
        qkvT = [sb2(f"qkvT{i}", [128, 640], BF16) for i in range(8)]
        q1b = [sb2(f"q1b{i}", [128, SLOTC], BF16) for i in range(8)]
        q2b = [sb2(f"q2b{i}", [128, 512], BF16) for i in range(8)]
        qsq = [sb2(f"qsq{i}", [128, 512], BF16) for i in range(8)]
        wq = sb2("wq2", [128, 32 * 128], BF16)
        c1w = sb2("c1w2", [128, 3 * 128], BF16)
        c2wq = sb2("c2wq", [128, 24 * 128], BF16)
        ceb = sb2("ceb2", [128, 512], BF16)
        seb = sb2("seb2", [128, 512], BF16)
        rn16 = sb2("rn16", [16, 512], F)
        rn16b = sb2("rn16b", [16, 512], BF16)
        swp = sb2("swp2", [128, 512], BF16)
        rt1 = sb2("rt12", [128, 512], BF16)
        qrb = sb2("qrb2", [128, 512], BF16)
        ob2 = [sb2(f"ob2{i}", [128, 512], BF16) for i in range(2)]
        xbuf = sb2("xb1", [128, D], F)
        lnbf = sb2("lnbf1", [128, D], BF16)
        scr_bf = sb2("scr1", [128, D], BF16)
        ps = ExitStack()
        pst = lambda n, s, d: ps.enter_context(nc.psum_tensor(n, s, d))
        p_tr = [pst(f"ptr2{i}", [128, 512], BF16) for i in range(2)]
        p_a = pst("pa2", [128, 512], F)
        p_b = pst("pb2", [128, 512], F)
        p_c = pst("pc2", [128, 512], F)
        p_h = pst("ph2", [16, 512], F)

        for seg in range(2):
            @block.gpsimd
            def _(g, seg=seg):
                if '_cebf' in sm.tag:
                    g.wait_ge(sm.h['v'], sm.tag['_cebf'])
                gdma(g, sm, 'u', ceb[:], ceq_in[seg])
                gdma(g, sm, 'u', seb[:], seq_in[seg])

            for i in range(5):
                @block.gpsimd
                def _(g, seg=seg, i=i):
                    if '_lnfree' in sm.tag:
                        g.wait_ge(sm.h['v'], sm.tag['_lnfree'])
                    gdma(g, sm, 'g', xbuf[:],
                         x_in[seg, i * 128:(i + 1) * 128, :])
                emit_ln_tile(block, sm, xbuf, lnbf, scr_bf, stat,
                             v_waits=[('g', sm.c['g']), ('t', sm.c['t'])])
                sm.tag['_lnfree'] = sm.c['v']
                emit_transpose_tiles(block, sm, lnbf, lnT, i * 128, identbf,
                                     p_tr)

            for ct in range(8):
                @block.gpsimd
                def _(g, ct=ct):
                    if '_wqf2' in sm.tag:
                        g.wait_ge(sm.h['t'], sm.tag['_wqf2'])
                    gdma(g, sm, 'w',
                         bass.AP(wq, 0, [[32 * 128, 128], [128, 32], [1, 128]]),
                         bass.AP(wqkvT_in, ct * 128,
                                 [[1536, 128], [128 * 1536, 32], [1, 128]]))

                for ch, (c0, cn) in enumerate([(0, 512), (512, 128)]):
                    @block.tensor
                    def _(t, ct=ct, ch=ch, c0=c0, cn=cn):
                        sm.wait(t, 'w')
                        if ct == 0 and ch == 0:
                            sm.wait(t, 'a')
                            sm.wait(t, 'v')
                        pj = (ct * 2 + ch) % 2
                        pp = [p_a, p_b][pj]
                        if f'_qkvq{pj}' in sm.tag:
                            t.wait_ge(sm.h['a'], sm.tag[f'_qkvq{pj}'])
                        for dd in range(32):
                            ins = t.matmul(pp[:, 0:cn],
                                           wq[:, dd * 128:(dd + 1) * 128],
                                           lnT[dd][:, c0:c0 + cn],
                                           start=(dd == 0), stop=(dd == 31))
                        sm.inc(ins, 't')

                    @block.scalar
                    def _(a, ct=ct, ch=ch, c0=c0, cn=cn):
                        sm.wait(a, 't')
                        pj = (ct * 2 + ch) % 2
                        pp = [p_a, p_b][pj]
                        ins = a.activation(out=qkvT[ct][:, c0:c0 + cn],
                                           in_=pp[:, 0:cn], func=AF.Copy)
                        sm.tag[f'_qkvq{pj}'] = sm.inc(ins, 'a')
                sm.tag['_wqf2'] = sm.c['t']

            for ot in range(8):
                @block.gpsimd
                def _(g, ot=ot):
                    if '_c1qf' in sm.tag:
                        g.wait_ge(sm.h['t'], sm.tag['_c1qf'])
                    gdma(g, sm, 'u',
                         bass.AP(c1w, 0, [[3 * 128, 128], [128, 3], [1, 128]]),
                         bass.AP(qc1L_in, ot * 3 * 128 * 128,
                                 [[128, 128], [128 * 128, 3], [1, 128]]))

                for ci, (c0, cn) in enumerate([(124, 512), (636, 4)]):
                    @block.tensor
                    def _(t, ot=ot, ci=ci, c0=c0, cn=cn):
                        sm.wait(t, 'u')
                        if ot == 0 and ci == 0:
                            sm.wait(t, 'a')
                        pj = ci
                        pp = [p_a, p_b][pj]
                        if f'_c1qp{pj}' in sm.tag:
                            t.wait_ge(sm.h['v'], sm.tag[f'_c1qp{pj}'])
                        for tau in range(3):
                            ins = t.matmul(pp[:, 0:cn],
                                           c1w[:, tau * 128:(tau + 1) * 128],
                                           qkvT[ot][:, c0 - 2 + tau:
                                                    c0 - 2 + tau + cn],
                                           start=(tau == 0), stop=(tau == 2))
                        sm.inc(ins, 't')

                    @block.vector
                    def _(v, ot=ot, ci=ci, c0=c0, cn=cn, seg=seg):
                        sm.wait(v, 't')
                        pj = ci
                        pp = [p_a, p_b][pj]
                        ins = v.tensor_scalar(out=q1b[ot][:, c0 - 124:
                                                          c0 - 124 + cn],
                                              in0=pp[:, 0:cn], scalar1=1.0,
                                              scalar2=qc1bb[:, ot:ot + 1],
                                              op0=ALU.mult, op1=ALU.add)
                        sm.tag[f'_c1qp{pj}'] = sm.inc(ins, 'v')
                sm.tag['_c1qf'] = sm.c['t']

            @block.vector
            def _(v, seg=seg):
                sm.wait(v, 'v')
                for ot in range(8):
                    ins = v.tensor_scalar(out=q1b[ot][:, 0:4],
                                          in0=q1b[ot][:, 0:4],
                                          scalar1=nzqb[:, seg:seg + 1],
                                          scalar2=None, op0=ALU.mult)
                sm.inc(ins, 'v')

            for ot in range(8):
                @block.gpsimd
                def _(g, ot=ot):
                    if '_c2qf' in sm.tag:
                        g.wait_ge(sm.h['t'], sm.tag['_c2qf'])
                    gdma(g, sm, 'u',
                         bass.AP(c2wq, 0, [[24 * 128, 128], [128, 24], [1, 128]]),
                         bass.AP(qc2L_in, ot * 128,
                                 [[1024, 128], [128 * 1024, 24], [1, 128]]))

                @block.tensor
                def _(t, ot=ot, seg=seg):
                    sm.wait(t, 'u')
                    sm.wait(t, 'v')
                    if '_q2f2' in sm.tag:
                        t.wait_ge(sm.h['a'], sm.tag['_q2f2'])
                    if '_q2fv2' in sm.tag:
                        t.wait_ge(sm.h['v'], sm.tag['_q2fv2'])
                    for it in range(8):
                        for tau in range(3):
                            t.matmul(p_c[:],
                                     c2wq[:, (it * 3 + tau) * 128:
                                          (it * 3 + tau + 1) * 128],
                                     q1b[it][:, 2 + tau:2 + tau + 512],
                                     start=(it == 0 and tau == 0), stop=False)
                    ins = t.matmul(p_c[:], mqb[:],
                                   kmb2[:, seg * 512:(seg + 1) * 512],
                                   start=False, stop=True)
                    sm.inc(ins, 't')
                sm.tag['_c2qf'] = sm.c['t']

                @block.scalar
                def _(a, ot=ot):
                    sm.wait(a, 't')
                    ins = a.activation(out=qsq[ot][:], in_=p_c[:],
                                       func=AF.Square,
                                       bias=qc2bb[:, ot:ot + 1])
                    sm.tag['_q2f2'] = sm.inc(ins, 'a')

                @block.vector
                def _(v, ot=ot):
                    sm.wait(v, 'a')
                    ins = v.tensor_scalar(out=q2b[ot][:], in0=p_c[:],
                                          scalar1=1.0,
                                          scalar2=qc2bb[:, ot:ot + 1],
                                          op0=ALU.mult, op1=ALU.add)
                    sm.tag['_q2fv2'] = sm.inc(ins, 'v')

            if stop_after <= 1:
                @block.gpsimd
                def _(g, seg=seg):
                    sm.wait(g, 'v')
                    for ot in range(8):
                        gdma(g, sm, 'o',
                             q2dbg[seg, ot * 128:(ot + 1) * 128, :],
                             q2b[ot][:])
                    if seg == 0:
                        gdma(g, sm, 'o', kmdbg[:], kmb2[:])

            @block.tensor
            def _(t, seg=seg):
                sm.wait(t, 'a')
                sm.wait(t, 'v')
                if '_rn16f' in sm.tag:
                    t.wait_ge(sm.h['v'], sm.tag['_rn16f'])
                for ot in range(8):
                    ins = t.matmul(p_h[:], hqb[:, ot * 16:(ot + 1) * 16],
                                   qsq[ot][:], start=(ot == 0), stop=(ot == 7))
                sm.inc(ins, 't')

            @block.vector
            def _(v, seg=seg):
                sm.wait(v, 't')
                ins = v.tensor_scalar(out=rn16[:], in0=p_h[:], scalar1=1.0,
                                      scalar2=1e-24, op0=ALU.mult, op1=ALU.max)
                sm.tag['_rn16f'] = sm.inc(ins, 'v')

            @block.scalar
            def _(a, seg=seg):
                sm.wait(a, 'v')
                ins = a.activation(out=rn16[:], in_=rn16[:], func=AF.Ln)
                sm.inc(ins, 'a'); sm.wait(a, 'a')
                ins = a.activation(out=rn16[:], in_=rn16[:], func=AF.Exp,
                                   scale=-0.5)
                sm.inc(ins, 'a')

            @block.vector
            def _(v, seg=seg):
                sm.wait(v, 'a')
                ins = v.tensor_copy(out=rn16b[:], in_=rn16[:])
                sm.inc(ins, 'v')

            for ot in range(8):
                @block.tensor
                def _(t, ot=ot):
                    sm.wait(t, 'v')
                    if '_swqf' in sm.tag:
                        t.wait_ge(sm.h['a'], sm.tag['_swqf'])
                    ins = t.matmul(p_a[:], pswapb[:], q2b[ot][:],
                                   start=True, stop=True)
                    sm.inc(ins, 't')

                @block.scalar
                def _(a, ot=ot):
                    sm.wait(a, 't')
                    if '_ropeq' in sm.tag:
                        a.wait_ge(sm.h['v'], sm.tag['_ropeq'])
                    ins = a.activation(out=swp[:], in_=p_a[:], func=AF.Copy)
                    sm.tag['_swqf'] = sm.inc(ins, 'a')

                @block.tensor
                def _(t, ot=ot):
                    if '_rnbqf' in sm.tag:
                        t.wait_ge(sm.h['v'], sm.tag['_rnbqf'])
                    ins = t.matmul(p_b[:], bqb[0:16, ot * 128:(ot + 1) * 128],
                                   rn16b[:], start=True, stop=True)
                    sm.inc(ins, 't')

                @block.vector
                def _(v, ot=ot, seg=seg):
                    sm.wait(v, 'a')
                    sm.wait(v, 't')
                    sm.wait(v, 'u')
                    oj = ot % 2
                    if f'_qob{oj}' in sm.tag:
                        v.wait_ge(sm.h['o'], sm.tag[f'_qob{oj}'])
                    v.tensor_tensor(out=rt1[:], in0=q2b[ot][:], in1=ceb[:],
                                    op=ALU.mult)
                    ins = v.tensor_tensor(out=qrb[:], in0=swp[:], in1=seb[:],
                                          op=ALU.mult)
                    sm.inc(ins, 'v'); sm.wait(v, 'v')
                    ins = v.tensor_tensor(out=qrb[:], in0=rt1[:], in1=qrb[:],
                                          op=ALU.add)
                    sm.inc(ins, 'v'); sm.wait(v, 'v')
                    ins = v.tensor_tensor(out=ob2[oj][:], in0=qrb[:],
                                          in1=p_b[:], op=ALU.mult)
                    sm.tag['_ropeq'] = sm.inc(ins, 'v')
                    sm.tag['_rnbqf'] = sm.tag['_ropeq']

                @block.gpsimd
                def _(g, ot=ot, seg=seg):
                    sm.wait(g, 'v')
                    oj = ot % 2
                    gdma(g, sm, 'o',
                         qdr[ot * 128:(ot + 1) * 128,
                             seg * 512:(seg + 1) * 512],
                         ob2[oj][:])
                    sm.tag[f'_qob{oj}'] = sm.c['o']
            sm.tag['_cebf'] = sm.c['v']

        emit_barrier(block, sm)
        ps.close()
        es_s2.close()
        es_a.close()

        es_b = ExitStack()
        sbb = lambda n, s, d: es_b.enter_context(nc.sbuf_tensor(n, s, d))
        qn = [sbb(f"qn{j}", [128, ROWS], BF16) for j in range(8)]
        qs_ = [sbb(f"qs{j}", [128, ROWS], BF16) for j in range(8)]
        aoT = [sbb(f"aoT{h}", [64, ROWS], BF16) for h in range(16)]
        trib = sbb("trib", [128, 128], BF16)
        visb = sbb("visb", [128, 128], F)
        btab = sbb("btab", [128, 128], F)
        btmp = sbb("btmp", [128, 128], F)
        kt128b = sbb("kt128b", [128, 1], F)
        mtmp = sbb("mtmp", [128, 1], F)
        pbuf = [sbb(f"pbuf{i}", [128, 512], BF16) for i in range(6)]
        lsb = sbb("lsb", [128, 512], F)
        lbf = sbb("lbf", [128, 512], BF16)
        rls = sbb("rls", [64, 512], F)
        ps2 = ExitStack()
        pst2 = lambda n, s, d: ps2.enter_context(nc.psum_tensor(n, s, d))
        p_s = [pst2(f"ps_{i}", [128, 512], F) for i in range(6)]
        p_o = pst2("po", [65, 512], F)
        p_rl = pst2("prl", [64, 512], F)

        @block.gpsimd
        def _(g):
            gdma(g, sm, 'g', trib[:], tri_in[:])
            gdma(g, sm, 'g', visb[:], vis_in[:])
            gdma(g, sm, 'g', kt128b[:], kt128_in[:])
            for j in range(8):
                gdma(g, sm, 'g', qn[j][:], qdr[j * 128:(j + 1) * 128, :])
            for j in range(7):
                gdma(g, sm, 'g', qs_[j][:],
                     qdr[j * 128 + 64:(j + 1) * 128 + 64, :])
            gdma(g, sm, 'g', qs_[7][0:64, :], qdr[960:1024, :])

        @block.scalar
        def _(a):
            sm.wait(a, 'g')
            ins = a.activation(out=mtmp[:], in_=kt128b[:], func=AF.Abs)
            sm.inc(ins, 'a')

        @block.vector
        def _(v):
            sm.wait(v, 'a')
            ins = v.tensor_scalar(out=mtmp[:], in0=mtmp[:], scalar1=-0.1275,
                                  scalar2=None, op0=ALU.mult)
            sm.inc(ins, 'v'); sm.wait(v, 'v')
            ins = v.tensor_scalar(out=btab[:], in0=visb[:],
                                  scalar1=mtmp[:], scalar2=None, op0=ALU.mult)
            sm.inc(ins, 'v')
            v.tensor_scalar(out=btmp[:], in0=visb[:], scalar1=1e30,
                            scalar2=-1e30, op0=ALU.mult, op1=ALU.add)
            sm.inc(v.tensor_copy(out=lbf[0:1, 0:1], in_=visb[0:1, 0:1]), 'v')
            sm.wait(v, 'v')
            ins = v.tensor_tensor(out=btab[:], in0=btab[:], in1=btmp[:],
                                  op=ALU.add)
            sm.inc(ins, 'v')

        for seg in range(2):
            for qh in range(16):
                kv = qh // 4
                kvt = kv // 2
                kbase = 64 * (kv % 2)
                if qh % 2 == kv % 2:
                    qsrc = qn[qh // 2]
                    qrow = 64 * (qh % 2)
                else:
                    if qh % 2 == 1:
                        qsrc = qs_[(qh - 1) // 2]
                        qrow = 0
                    else:
                        qsrc = qs_[qh // 2 - 1]
                        qrow = 64
                assert qrow == kbase
                nT = 36 if seg == 0 else 64
                for g0 in range(0, nT, 6):
                    Ts = list(range(g0, min(g0 + 6, nT)))
                    dia = {T: ((seg == 0 and T < 4)
                               or (seg == 1 and 4 <= T < 8)) for T in Ts}
                    gdiag = any(dia.values())

                    @block.tensor
                    def _(t, seg=seg, qh=qh, g0=g0, Ts=Ts, qsrc=qsrc,
                          qrow=qrow, kvt=kvt, kbase=kbase):
                        if seg == 0 and qh == 0 and g0 == 0:
                            sm.wait(t, 'v')
                        if '_exgA' in sm.tag:
                            t.wait_ge(sm.h['a'], sm.tag['_exgA'])
                        for T in Ts:
                            ins = t.matmul(
                                p_s[T % 6][:],
                                khat[kvt][kbase:kbase + 64,
                                          T * 128:(T + 1) * 128],
                                qsrc[qrow:qrow + 64,
                                     seg * 512:(seg + 1) * 512],
                                start=True, stop=True)
                        sm.inc(ins, 't', amt=len(Ts))

                    @block.scalar
                    def _(a, seg=seg, Ts=Ts, dia=dia):
                        sm.wait(a, 't')
                        for T in Ts:
                            span0 = 128 * (T % 4) if dia[T] else 0
                            ins = a.activation(
                                out=pbuf[T % 6][:, span0:512],
                                in_=p_s[T % 6][:, span0:512],
                                func=AF.Exp, scale=0.125,
                                bias=btab[:, seg * 64 + T:seg * 64 + T + 1])
                            sm.tag[f'_exb{T % 6}'] = sm.inc(ins, 'a')
                        sm.tag['_exgA'] = sm.c['a']

                    if gdiag:
                        @block.vector
                        def _(v, Ts=Ts, dia=dia):
                            sm.wait(v, 'a')
                            for T in Ts:
                                if not dia[T]:
                                    continue
                                span0 = 128 * (T % 4)
                                ins = v.tensor_tensor(
                                    out=pbuf[T % 6][:, span0:span0 + 128],
                                    in0=pbuf[T % 6][:, span0:span0 + 128],
                                    in1=trib[:], op=ALU.mult)
                                sm.tag[f'_trib{T % 6}'] = sm.inc(ins, 'v')

                    @block.tensor
                    def _(t, seg=seg, qh=qh, g0=g0, Ts=Ts, kv=kv,
                          gdiag=gdiag, dia=dia, nT=nT):
                        if g0 == 0 and '_pofv' in sm.tag:
                            t.wait_ge(sm.h['v'], sm.tag['_pofv'])
                        for T in Ts:
                            span0 = 128 * (T % 4) if dia[T] else 0
                            if dia[T]:
                                t.wait_ge(sm.h['v'], sm.tag[f'_trib{T % 6}'])
                            else:
                                t.wait_ge(sm.h['a'], sm.tag[f'_exb{T % 6}'])
                            ins = t.matmul(p_o[:, span0:512],
                                           va[:, T * 260 + kv * 65:
                                              T * 260 + kv * 65 + 65],
                                           pbuf[T % 6][:, span0:512],
                                           start=(T == 0), stop=(T == nT - 1))
                        sm.inc(ins, 't', amt=len(Ts))

                @block.vector
                def _(v, seg=seg, qh=qh):
                    sm.wait(v, 't')
                    ins = v.reciprocal(out=lsb[0:1, :], in_=p_o[64:65, :])
                    sm.inc(ins, 'v'); sm.wait(v, 'v')
                    ins = v.tensor_copy(out=lbf[0:1, :], in_=lsb[0:1, :])
                    sm.inc(ins, 'v')

                @block.tensor
                def _(t, seg=seg, qh=qh):
                    sm.wait(t, 'v')
                    if '_rlf' in sm.tag:
                        t.wait_ge(sm.h['v'], sm.tag['_rlf'])
                    ins = t.matmul(p_rl[:], onesbf[:, 0:64], lbf[0:1, :],
                                   start=True, stop=True)
                    sm.inc(ins, 't')

                @block.vector
                def _(v, seg=seg, qh=qh):
                    sm.wait(v, 't')
                    ins = v.tensor_copy(out=rls[:], in_=p_rl[:])
                    sm.inc(ins, 'v'); sm.wait(v, 'v')
                    ins = v.tensor_tensor(
                        out=aoT[qh][:, seg * 512:(seg + 1) * 512],
                        in0=p_o[0:64, :], in1=rls[:], op=ALU.mult)
                    sm.tag['_pofv'] = sm.inc(ins, 'v')
                    sm.tag['_rlf'] = sm.c['v']

        @block.gpsimd
        def _(g):
            sm.wait(g, 'v')
            for h in range(16):
                gdma(g, sm, 'o', aodr[h * 64:(h + 1) * 64, :], aoT[h][:])
            if stop_after <= 1:
                gdma(g, sm, 'o', khdbg[0:128, :], khat[0][:])
                gdma(g, sm, 'o', khdbg[128:256, :], khat[1][:])
                gdma(g, sm, 'o', vadbg[:], va[:])
                for j in range(8):
                    gdma(g, sm, 'o', qdbg[j * 128:(j + 1) * 128, :], qn[j][:])
                gdma(g, sm, 'o', aodbg[:], aodr[:])

        emit_barrier(block, sm)
        ps2.close()
        es_b.close()

        es_c = ExitStack()
        sbc = lambda n, s, d: es_c.enter_context(nc.sbuf_tensor(n, s, d))
        aoc = [sbc(f"aoc{j}", [128, ROWS], BF16) for j in range(8)]
        ybf = [sbc(f"ybf{i}", [128, D], BF16) for i in range(NT)]
        wob = sbc("wob", [128, 8 * 512], BF16)
        xbuf2 = sbc("xb2", [128, D], F)
        lnbf2 = sbc("lnbf2", [128, D], BF16)
        scr2 = sbc("scr2", [128, D], BF16)
        ps3 = ExitStack()
        p_y = [ps3.enter_context(nc.psum_tensor(f"py_{i}", [128, 512], F))
               for i in range(2)]

        x1tgt = out if stop_after <= 1 else x1dr

        @block.gpsimd
        def _(g):
            for j in range(8):
                gdma(g, sm, 'g', aoc[j][:], aodr[j * 128:(j + 1) * 128, :])

        for oc in range(8):
            @block.gpsimd
            def _(g, oc=oc):
                if '_wof' in sm.tag:
                    g.wait_ge(sm.h['t'], sm.tag['_wof'])
                gdma(g, sm, 'w',
                     bass.AP(wob, 0, [[8 * 512, 128], [512, 8], [1, 512]]),
                     bass.AP(woT_in, oc * 512,
                             [[D, 128], [128 * D, 8], [1, 512]]))

            for i in range(NT):
                @block.tensor
                def _(t, oc=oc, i=i):
                    sm.wait(t, 'w')
                    if oc == 0 and i == 0:
                        sm.wait(t, 'g')
                    pj = (oc * NT + i) % 2
                    if f'_yf{pj}' in sm.tag:
                        t.wait_ge(sm.h['a'], sm.tag[f'_yf{pj}'])
                    for dd in range(8):
                        ins = t.matmul(p_y[pj][:],
                                       aoc[dd][:, i * 128:(i + 1) * 128],
                                       wob[:, dd * 512:(dd + 1) * 512],
                                       start=(dd == 0), stop=(dd == 7))
                    sm.inc(ins, 't')

                @block.scalar
                def _(a, oc=oc, i=i):
                    sm.wait(a, 't')
                    pj = (oc * NT + i) % 2
                    ins = a.activation(out=ybf[i][:, oc * 512:(oc + 1) * 512],
                                       in_=p_y[pj][:], func=AF.Copy)
                    sm.tag[f'_yf{pj}'] = sm.inc(ins, 'a')
            sm.tag['_wof'] = sm.c['t']

        for i in range(NT):
            @block.gpsimd
            def _(g, i=i):
                sm.wait(g, 'a')
                if '_of1' in sm.tag:
                    g.wait_ge(sm.h['o'], sm.tag['_of1'])
                gdma(g, sm, 'g', xbuf2[:],
                     x_in[i // 4, 128 + (i % 4) * 128:128 + (i % 4 + 1) * 128, :])
            emit_ln_tile(block, sm, ybf[i], lnbf2, scr2, stat,
                         v_waits=[('a', sm.c['a']), ('g', sm.c['g'])])

            @block.vector
            def _(v, i=i):
                sm.wait(v, 'g')
                sm.wait(v, 'v')
                ins = v.tensor_tensor(out=xbuf2[:], in0=xbuf2[:],
                                      in1=lnbf2[:], op=ALU.add)
                sm.inc(ins, 'v')

            @block.gpsimd
            def _(g, i=i, x1tgt=x1tgt):
                sm.wait(g, 'v')
                gdma(g, sm, 'o', x1tgt[i * 128:(i + 1) * 128, :], xbuf2[:])
                sm.tag['_of1'] = sm.c['o']

        emit_barrier(block, sm)
        ps3.close()
        es_c.close()
        es_kv.close()

        if stop_after <= 1:
            @block.sync
            def _(sy):
                for n in sm.h:
                    sm.wait(sy, n)
            es_glob.close()
            return nc

        sm.tag.clear()
        x2dr = nc.dram_tensor("x2dr", [ROWS, D], F, kind="Internal")
        x3dr = nc.dram_tensor("x3dr", [ROWS, D], F, kind="Internal")
        out = nc.dram_tensor("out", [ROWS, D], F, kind="ExternalOutput")

        w0T_in = nc.dram_tensor("w0T", [D, D], F, kind="ExternalInput")
        bias_in = nc.dram_tensor("bias", [1, D], F, kind="ExternalInput")
        lAT_in = nc.dram_tensor("lAT", [D, 48], F, kind="ExternalInput")
        lBT_in = nc.dram_tensor("lBT", [48, D], F, kind="ExternalInput")
        lsi_in = nc.dram_tensor("lsi", [1, 4], I32, kind="ExternalInput")
        c012_in = nc.dram_tensor("c012", [3, 1], F, kind="ExternalInput")
        e48_in = nc.dram_tensor("e48", [3, 48], F, kind="ExternalInput")
        scdr3 = nc.dram_tensor("scdr3", [3, 8], F, kind="Internal")

        es2 = ExitStack()
        s2 = lambda n, s, d: es2.enter_context(nc.sbuf_tensor(n, s, d))
        lnT = [s2(f"p2lnT{d_}", [128, ROWS], BF16) for d_ in range(32)]
        ybf = [s2(f"p2y{i}", [128, D], BF16) for i in range(NT)]
        wbuf = s2("p2wb", [128, 32 * 512], BF16)
        xbuf = s2("p2xb", [128, D], F)
        lnbf = s2("p2lnbf", [128, D], BF16)
        aTbf = s2("p2aT", [128, 32 * 48], BF16)
        bTbf = s2("p2bT", [48, D], BF16)
        biasbf = s2("p2bias", [1, D], BF16)
        rbf = s2("p2r", [48, ROWS], BF16)
        lsi_f = s2("p2lsif", [1, 4], F)
        lsi_i = s2("p2lsii", [1, 4], I32)
        sc = s2("p2sc", [1, 8], F)
        c012 = s2("p2c012", [3, 1], F)
        e48bf = s2("p2e48", [3, 48], BF16)
        wsel3 = s2("p2ws3", [3, 1], F)
        use3 = s2("p2use3", [3, 1], F)
        wsel3b = s2("p2ws3b", [3, 1], BF16)
        wsel48 = s2("p2ws48", [48, 1], F)
        ps = ExitStack()
        pst = lambda n, s, d: ps.enter_context(nc.psum_tensor(n, s, d))
        p_tr = [pst(f"p2ptr{i}", [128, 512], BF16) for i in range(2)]
        p_y = [pst(f"p2py{i}", [128, 512], F) for i in range(2)]
        p_r = pst("p2pr", [48, 512], F)
        p_s2 = pst("p2ps", [48, 1], F)

        @block.gpsimd
        def _(g):
            gdma(g, sm, 'g', biasbf[:], bias_in[:])
            gdma(g, sm, 'g', bass.AP(aTbf, 0, [[32 * 48, 128], [48, 32], [1, 48]]),
                 bass.AP(lAT_in, 0, [[48, 128], [128 * 48, 32], [1, 48]]))
            gdma(g, sm, 'g', bTbf[:], lBT_in[:])
            gdma(g, sm, 'g', lsi_i[:], lsi_in[:])
            gdma(g, sm, 'g', c012[:], c012_in[:])
            gdma(g, sm, 'g', e48bf[:], e48_in[:])

        @block.vector
        def _(v):
            sm.wait(v, 'g')
            v.tensor_copy(out=lsi_f[:], in_=lsi_i[:])
            sm.inc(v.engine_nop(), 'v'); sm.wait(v, 'v')
            v.tensor_single_scalar(out=sc[:, 0:4], in_=lsi_f[:], scalar=0.0,
                                   op=ALU.is_le)
            sm.inc(v.engine_nop(), 'v'); sm.wait(v, 'v')
            v.tensor_scalar(out=sc[:, 4:5], in0=sc[:, 3:4], scalar1=3.0,
                            scalar2=sc[:, 2:3], op0=ALU.add, op1=ALU.mult)
            v.tensor_scalar(out=sc[:, 5:6], in0=sc[:, 0:1], scalar1=-1.0,
                            scalar2=2.0, op0=ALU.mult, op1=ALU.add)
            sm.inc(v.engine_nop(), 'v'); sm.wait(v, 'v')
            v.tensor_scalar(out=sc[:, 5:6], in0=sc[:, 5:6],
                            scalar1=sc[:, 1:2], scalar2=None, op0=ALU.mult)
            sm.inc(v.engine_nop(), 'v'); sm.wait(v, 'v')
            v.tensor_scalar(out=sc[:, 5:6], in0=sc[:, 5:6],
                            scalar1=1.0, scalar2=sc[:, 0:1],
                            op0=ALU.mult, op1=ALU.add)
            v.tensor_scalar(out=sc[:, 6:7], in0=sc[:, 2:3], scalar1=-1.0,
                            scalar2=1.0, op0=ALU.mult, op1=ALU.add)
            sm.inc(v.engine_nop(), 'v'); sm.wait(v, 'v')
            v.tensor_scalar(out=sc[:, 5:6], in0=sc[:, 5:6],
                            scalar1=sc[:, 6:7], scalar2=None, op0=ALU.mult)
            sm.inc(v.engine_nop(), 'v'); sm.wait(v, 'v')
            v.tensor_scalar(out=sc[:, 4:5], in0=sc[:, 4:5],
                            scalar1=1.0, scalar2=sc[:, 5:6],
                            op0=ALU.mult, op1=ALU.add)
            sm.inc(v.engine_nop(), 'v'); sm.wait(v, 'v')
            v.tensor_scalar(out=sc[:, 5:6], in0=sc[:, 4:5], scalar1=-1.0,
                            scalar2=None, op0=ALU.add)
            sm.inc(v.engine_nop(), 'v'); sm.wait(v, 'v')
            v.tensor_single_scalar(out=sc[:, 6:7], in_=sc[:, 5:6], scalar=0.5,
                                   op=ALU.is_gt)
            v.tensor_scalar(out=sc[:, 7:8], in0=sc[:, 5:6], scalar1=-1.0,
                            scalar2=None, op0=ALU.add)
            sm.inc(v.engine_nop(), 'v'); sm.wait(v, 'v')
            ins = v.tensor_scalar(out=sc[:, 7:8], in0=sc[:, 7:8], scalar1=0.0,
                                  scalar2=2.0, op0=ALU.max, op1=ALU.min)
            sm.inc(ins, 'v')

        @block.gpsimd
        def _(g):
            sm.wait(g, 'v')
            for rr in range(3):
                gdma(g, sm, 'g', scdr3[rr:rr + 1, :], sc[:])

        @block.gpsimd
        def _(g):
            gdma(g, sm, 'g', wsel3[:], bass.AP(scdr3, 7, [[8, 3], [1, 1]]),
                 allow_slow_non_contiguous=True)
            gdma(g, sm, 'g', use3[:], bass.AP(scdr3, 6, [[8, 3], [1, 1]]),
                 allow_slow_non_contiguous=True)

        @block.vector
        def _(v):
            sm.wait(v, 'g')
            v.tensor_tensor(out=wsel3[:], in0=c012[:], in1=wsel3[:],
                            op=ALU.is_equal)
            sm.inc(v.engine_nop(), 'v'); sm.wait(v, 'v')
            v.tensor_tensor(out=wsel3[:], in0=wsel3[:], in1=use3[:],
                            op=ALU.mult)
            sm.inc(v.engine_nop(), 'v'); sm.wait(v, 'v')
            ins = v.tensor_copy(out=wsel3b[:], in_=wsel3[:])
            sm.inc(ins, 'v')

        @block.tensor
        def _(t):
            sm.wait(t, 'v')
            ins = t.matmul(p_s2[:], e48bf[:], wsel3b[:], start=True, stop=True)
            sm.inc(ins, 't')

        @block.scalar
        def _(a):
            sm.wait(a, 't')
            ins = a.activation(out=wsel48[:], in_=p_s2[:], func=AF.Copy)
            sm.inc(ins, 'a')
        a_wsel = sm.c['a']

        for i in range(NT):
            @block.gpsimd
            def _(g, i=i):
                if '_lnfree' in sm.tag:
                    g.wait_ge(sm.h['v'], sm.tag['_lnfree'])
                gdma(g, sm, 'g', xbuf[:], x1dr[i * 128:(i + 1) * 128, :])
            emit_ln_tile(block, sm, xbuf, lnbf, lnbf, stat,
                         v_waits=[('g', sm.c['g']), ('t', sm.c['t'])])
            sm.tag['_lnfree'] = sm.c['v']
            emit_transpose_tiles(block, sm, lnbf, lnT, i * 128, identbf, p_tr)

        for cc in range(2):
            @block.tensor
            def _(t, cc=cc):
                sm.wait(t, 'a')
                sm.wait(t, 'v')
                for d_ in range(32):
                    ins = t.matmul(p_r[:], aTbf[:, d_ * 48:(d_ + 1) * 48],
                                   lnT[d_][:, cc * 512:(cc + 1) * 512],
                                   start=(d_ == 0), stop=(d_ == 31))
                sm.inc(ins, 't')

            @block.scalar
            def _(a, cc=cc):
                sm.wait(a, 't')
                a.wait_ge(sm.h['a'], a_wsel)
                ins = a.activation(out=rbf[:, cc * 512:(cc + 1) * 512],
                                   in_=p_r[:], func=AF.Copy, scale=wsel48[:])
                sm.inc(ins, 'a')

        CH2 = 512
        for oc in range(D // CH2):
            wh = {}

            @block.gpsimd
            def _(g, oc=oc, wh=wh):
                for h in range(2):
                    if f'_wbh{h}' in sm.tag:
                        g.wait_ge(sm.h['t'], sm.tag[f'_wbh{h}'])
                    for q2 in range(2):
                        d0 = h * 16 + q2 * 8
                        fdma(g, sm, 'w',
                             bass.AP(wbuf, d0 * CH2,
                                     [[32 * CH2, 128], [CH2, 8], [1, CH2]]),
                             bass.AP(w0T_in, oc * CH2 + d0 * 128 * D,
                                     [[D, 128], [128 * D, 8], [1, CH2]]))
                    wh[h] = sm.c['w']

            for i in range(NT):
                @block.tensor
                def _(t, oc=oc, i=i, wh=wh):
                    t.wait_ge(sm.h['w'], wh[0])
                    if oc == 0 and i == 0:
                        sm.wait(t, 'a')
                        sm.wait(t, 'v')
                    pj = (oc * NT + i) % 2
                    if f'_yfree{pj}' in sm.tag:
                        t.wait_ge(sm.h['a'], sm.tag[f'_yfree{pj}'])
                    t.matmul(p_y[pj][:], onesbf[:],
                             biasbf[:, oc * CH2:(oc + 1) * CH2],
                             start=True, stop=False)
                    for d_ in range(16):
                        ins = t.matmul(p_y[pj][:],
                                       lnT[d_][:, i * 128:(i + 1) * 128],
                                       wbuf[:, d_ * CH2:(d_ + 1) * CH2],
                                       start=False, stop=False)
                    if i == NT - 1:
                        sm.tag['_wbh0'] = sm.inc(ins, 't')

                    t.wait_ge(sm.h['w'], wh[1])
                    for d_ in range(16, 32):
                        t.matmul(p_y[pj][:], lnT[d_][:, i * 128:(i + 1) * 128],
                                 wbuf[:, d_ * CH2:(d_ + 1) * CH2],
                                 start=False, stop=False)
                    ins = t.matmul(p_y[pj][:], rbf[:, i * 128:(i + 1) * 128],
                                   bTbf[:, oc * CH2:(oc + 1) * CH2],
                                   start=False, stop=True)
                    sm.inc(ins, 't')
                    if i == NT - 1:
                        sm.tag['_wbh1'] = sm.c['t']

                @block.scalar
                def _(a, oc=oc, i=i):
                    sm.wait(a, 't')
                    pj = (oc * NT + i) % 2
                    ins = a.activation(out=ybf[i][:, oc * CH2:(oc + 1) * CH2],
                                       in_=p_y[pj][:], func=AF.Copy)
                    sm.tag[f'_yfree{pj}'] = sm.inc(ins, 'a')

        for i in range(NT):
            @block.gpsimd
            def _(g, i=i):
                sm.wait(g, 'a')
                if '_ofree' in sm.tag:
                    g.wait_ge(sm.h['o'], sm.tag['_ofree'])
                gdma(g, sm, 'g', xbuf[:], x1dr[i * 128:(i + 1) * 128, :])
            emit_ln_tile(block, sm, ybf[i], lnbf, lnbf, stat,
                         v_waits=[('a', sm.c['a']), ('g', sm.c['g'])])

            @block.vector
            def _(v, i=i):
                sm.wait(v, 'g')
                sm.wait(v, 'v')
                ins = v.tensor_tensor(out=xbuf[:], in0=xbuf[:], in1=lnbf[:],
                                      op=ALU.add)
                sm.inc(ins, 'v')

            @block.gpsimd
            def _(g, i=i):
                sm.wait(g, 'v')
                tgt = out if stop_after <= 2 else x2dr
                gdma(g, sm, 'o', tgt[i * 128:(i + 1) * 128, :], xbuf[:])
                sm.tag['_ofree'] = sm.c['o']

        emit_barrier(block, sm)
        sm.tag.clear()
        ps.close()
        es2.close()

        if stop_after <= 2:
            @block.sync
            def _(sy):
                for n in sm.h:
                    sm.wait(sy, n)
            es_glob.close()
            return nc

        lam_in = nc.dram_tensor("lam", [64, 2], F, kind="ExternalInput")
        pri_in = nc.dram_tensor("pri", [64, 2], F, kind="ExternalInput")
        qri_in = nc.dram_tensor("qri", [64, 2], F, kind="ExternalInput")
        vctre_in = nc.dram_tensor("vctre", [64, 64], F, kind="ExternalInput")
        vctim_in = nc.dram_tensor("vctim", [64, 64], F, kind="ExternalInput")
        ctt_in = nc.dram_tensor("ctt", [64, 2], F, kind="ExternalInput")
        bp_in = nc.dram_tensor("bp", [64, 1], F, kind="ExternalInput")
        dsc_in = nc.dram_tensor("dsc", [1, 1], F, kind="ExternalInput")
        lstep_in = nc.dram_tensor("lstep", [1, 1], F, kind="ExternalInput")
        c1bre_in = nc.dram_tensor("c1bre", [64, D], F, kind="ExternalInput")
        c1bim_in = nc.dram_tensor("c1bim", [64, D], F, kind="ExternalInput")
        c2re_in = nc.dram_tensor("c2re", [64, 64], F, kind="ExternalInput")
        c2im_in = nc.dram_tensor("c2im", [64, 64], F, kind="ExternalInput")
        wc_in = nc.dram_tensor("wc", [64, 64], F, kind="ExternalInput")
        ws_in = nc.dram_tensor("ws", [64, 64], F, kind="ExternalInput")
        wsn_in = nc.dram_tensor("wsn", [64, 64], F, kind="ExternalInput")
        twre_in = nc.dram_tensor("twre", [64, 64], F, kind="ExternalInput")
        twim_in = nc.dram_tensor("twim", [64, 64], F, kind="ExternalInput")
        idf_in = nc.dram_tensor("idf", [64, 64], F, kind="ExternalInput")
        onesf_in = nc.dram_tensor("onesf", [1, 128], F, kind="ExternalInput")
        kdr = nc.dram_tensor("kdr", [64, 64], F, kind="Internal")
        odr = nc.dram_tensor("odr", [16, D], F, kind="Internal")
        p3tgt = out if stop_after <= 3 else x3dr

        def dstep(eng, ins, n):
            sm.inc(ins, n)
            sm.wait(eng, n)

        es3 = ExitStack()
        def s3alloc(*a, **k):
            return es3.enter_context(nc.sbuf_tensor(*a, **k))
        ps3s = ExitStack()
        def p3alloc(*a, **k):
            return ps3s.enter_context(nc.psum_tensor(*a, **k))
        e_all = s3alloc("p3eall", [128, 35 * 128], BF16)
        kpad = s3alloc("p3kpad", [1, 4608], BF16)
        onesf = s3alloc("p3onesf", [1, 128], F)
        dv = s3alloc("p3dv", [128, 1], F)
        sc1 = s3alloc("p3sc1", [1, 4], F)
        p_tr = [p3alloc(f"p3ptrA{i}", [128, 512], BF16) for i in range(2)]
        p_y = [p3alloc(f"p3pyA{i}", [128, 512], F) for i in range(2)]

        from contextlib import ExitStack
        _es = ExitStack()
        c1re = _es.enter_context(nc.sbuf_tensor("c1re", [64, D], F))
        c1im = _es.enter_context(nc.sbuf_tensor("c1im", [64, D], F))
        dre = _es.enter_context(nc.sbuf_tensor("dre", [64, D], F))
        dim_ = _es.enter_context(nc.sbuf_tensor("dim_", [64, D], F))
        tmpl = _es.enter_context(nc.sbuf_tensor("tmpl", [64, D], F))
        winv = _es.enter_context(nc.sbuf_tensor("winv", [64, D], F))
        wre = _es.enter_context(nc.sbuf_tensor("wre", [64, D], F))
        wim = _es.enter_context(nc.sbuf_tensor("wim", [64, D], F))
        lamv = _es.enter_context(nc.sbuf_tensor("lamv", [64, 2], F))
        priv = _es.enter_context(nc.sbuf_tensor("priv", [64, 2], F))
        qriv = _es.enter_context(nc.sbuf_tensor("qriv", [64, 2], F))
        cttv = _es.enter_context(nc.sbuf_tensor("cttv", [64, 2], F))
        vctre = _es.enter_context(nc.sbuf_tensor("vctre_s", [64, 64], F))
        vctim = _es.enter_context(nc.sbuf_tensor("vctim_s", [64, 64], F))
        bpv = _es.enter_context(nc.sbuf_tensor("bpv", [64, 1], F))
        bv = _es.enter_context(nc.sbuf_tensor("bv", [64, 2], F))
        vstk = _es.enter_context(nc.sbuf_tensor("vstk", [64, 8], F))
        ls1 = _es.enter_context(nc.sbuf_tensor("ls1", [1, 4], F))
        s2v = _es.enter_context(nc.sbuf_tensor("s2v", [64, 1], F))
        ob = _es.enter_context(nc.sbuf_tensor("ob", [8, 512], F))
        km = _es.enter_context(nc.sbuf_tensor("km", [64, 16 * 64], F))
        cx = _es.enter_context(nc.sbuf_tensor("cx", [64, 24 * 64], F))
        wcv = _es.enter_context(nc.sbuf_tensor("wcv", [64, 64], F))
        wsv = _es.enter_context(nc.sbuf_tensor("wsv", [64, 64], F))
        wsnv = _es.enter_context(nc.sbuf_tensor("wsnv", [64, 64], F))
        twrev = _es.enter_context(nc.sbuf_tensor("twrev", [64, 64], F))
        twimv = _es.enter_context(nc.sbuf_tensor("twimv", [64, 64], F))
        c2rev = _es.enter_context(nc.sbuf_tensor("c2rev", [64, 64], F))
        c2imv = _es.enter_context(nc.sbuf_tensor("c2imv", [64, 64], F))
        idfv = _es.enter_context(nc.sbuf_tensor("idfv", [64, 64], F))
        k2v = _es.enter_context(nc.sbuf_tensor("k2v", [64, 64], F))
        p_s = _es.enter_context(nc.psum_tensor("p_s", [64, 2], F))
        p_c = _es.enter_context(nc.psum_tensor("p_c", [8, 512], F))
        p_f1 = _es.enter_context(nc.psum_tensor("p_f1", [64, 64], F))
        p_f2 = _es.enter_context(nc.psum_tensor("p_f2", [64, 64], F))
        if True:
            @block.gpsimd
            def _(g):
                for dst, src_ in [(c1re, c1bre_in), (c1im, c1bim_in),
                                  (lamv, lam_in), (priv, pri_in),
                                  (qriv, qri_in), (cttv, ctt_in),
                                  (vctre, vctre_in), (vctim, vctim_in),
                                  (bpv, bp_in), (wcv, wc_in), (wsv, ws_in),
                                  (wsnv, wsn_in), (twrev, twre_in),
                                  (twimv, twim_in), (c2rev, c2re_in),
                                  (c2imv, c2im_in), (idfv, idf_in),
                                  (onesf, onesf_in)]:
                    gdma(g, sm, 'g', dst[:], src_[:])
                gdma(g, sm, 'g', ls1[:, 0:1], lstep_in[:])

            @block.scalar
            def _(a):
                sm.wait(a, 'g')
                ins = a.activation(out=ls1[:, 1:2], in_=ls1[:, 0:1], func=AF.Exp)
                dstep(a, ins, 'a')

            @block.vector
            def _(v):
                sm.wait(v, 'a')
                ins = v.reciprocal(out=ls1[:, 2:3], in_=ls1[:, 1:2])
                dstep(v, ins, 'v')
                ins = v.tensor_scalar(out=ls1[:, 3:4], in0=ls1[:, 2:3],
                                      scalar1=2.0, scalar2=None, op0=ALU.mult)
                dstep(v, ins, 'v')

            @block.tensor
            def _(t):
                sm.wait(t, 'v')
                t.matmul(p_s[:, 0:1], onesf[:, 0:64], ls1[:, 3:4],
                         start=True, stop=True)
                t.matmul(p_f1[:, 0:1], vctre[:], bpv[:], start=True, stop=True)
                ins = t.matmul(p_f1[:, 1:2], vctim[:], bpv[:], start=True,
                               stop=True)
                sm.inc(ins, 't')

            @block.scalar
            def _(a):
                sm.wait(a, 't')
                a.activation(out=s2v[:], in_=p_s[:, 0:1], func=AF.Copy)
                ins = a.activation(out=bv[:], in_=p_f1[:, 0:2], func=AF.Copy)
                sm.inc(ins, 'a')

            @block.vector
            def _(v):
                sm.wait(v, 'a')
                v.tensor_scalar(out=dre[:], in0=c1re[:], scalar1=s2v[:],
                                scalar2=lamv[:, 0:1], op0=ALU.mult,
                                op1=ALU.subtract)
                ins = v.tensor_scalar(out=dim_[:], in0=c1im[:], scalar1=s2v[:],
                                      scalar2=lamv[:, 1:2], op0=ALU.mult,
                                      op1=ALU.subtract)
                dstep(v, ins, 'v')
                v.tensor_tensor(out=tmpl[:], in0=dre[:], in1=dre[:],
                                op=ALU.mult)
                ins = v.scalar_tensor_tensor(out=winv[:], in0=dim_[:],
                                             scalar=1.0, in1=dim_[:],
                                             op0=ALU.mult, op1=ALU.mult)
                dstep(v, ins, 'v')
                ins = v.tensor_tensor(out=tmpl[:], in0=tmpl[:], in1=winv[:],
                                      op=ALU.add)
                dstep(v, ins, 'v')
                ins = v.reciprocal(out=winv[:], in_=tmpl[:])
                dstep(v, ins, 'v')
                ins = v.tensor_tensor(out=wre[:], in0=dre[:], in1=winv[:],
                                       op=ALU.mult)
                dstep(v, ins, 'v')
                ins = v.scalar_tensor_tensor(out=wim[:], in0=dim_[:],
                                             scalar=-1.0, in1=winv[:],
                                             op0=ALU.mult, op1=ALU.mult)
                dstep(v, ins, 'v')
                pairs = [(cttv, bv, 0), (cttv, priv, 2), (qriv, bv, 4),
                         (qriv, priv, 6)]
                for (xx, yy, col) in pairs:
                    ins = v.tensor_tensor(out=vstk[:, col:col + 1],
                                          in0=xx[:, 0:1], in1=yy[:, 0:1],
                                          op=ALU.mult)
                    dstep(v, ins, 'v')
                    ins = v.tensor_tensor(out=tmpl[:, 0:1], in0=xx[:, 1:2],
                                          in1=yy[:, 1:2], op=ALU.mult)
                    dstep(v, ins, 'v')
                    ins = v.tensor_tensor(out=vstk[:, col:col + 1],
                                          in0=vstk[:, col:col + 1],
                                          in1=tmpl[:, 0:1], op=ALU.add)
                    dstep(v, ins, 'v')
                    ins = v.tensor_tensor(out=vstk[:, col + 1:col + 2],
                                          in0=xx[:, 0:1], in1=yy[:, 1:2],
                                          op=ALU.mult)
                    dstep(v, ins, 'v')
                    ins = v.tensor_tensor(out=tmpl[:, 0:1], in0=xx[:, 1:2],
                                          in1=yy[:, 0:1], op=ALU.mult)
                    dstep(v, ins, 'v')
                    ins = v.tensor_tensor(out=vstk[:, col + 1:col + 2],
                                          in0=vstk[:, col + 1:col + 2],
                                          in1=tmpl[:, 0:1], op=ALU.subtract)
                    dstep(v, ins, 'v')

            for which, rhs in [(0, wre), (1, wim)]:
                for ch in range(8):
                    @block.tensor
                    def _(t, which=which, ch=ch, rhs=rhs):
                        if which == 0 and ch == 0:
                            sm.wait(t, 'v')
                        else:
                            t.wait_ge(sm.h['a'], sm.tag['_cev'])
                        ins = t.matmul(p_c[:], vstk[:],
                                       rhs[:, ch * 512:(ch + 1) * 512],
                                       start=True, stop=True)
                        sm.inc(ins, 't')

                    @block.scalar
                    def _(a, which=which, ch=ch):
                        sm.wait(a, 't')
                        sm.wait(a, 'w')
                        ins = a.activation(out=ob[:], in_=p_c[:], func=AF.Copy)
                        sm.tag['_cev'] = sm.inc(ins, 'a')

                    @block.gpsimd
                    def _(g, which=which, ch=ch):
                        sm.wait(g, 'a')
                        gdma(g, sm, 'w', odr[which * 8:(which + 1) * 8,
                                            ch * 512:(ch + 1) * 512], ob[:])

            @block.gpsimd
            def _(g):
                sm.wait(g, 'w')
                for c in range(16):
                    gdma(g, sm, 'g', km[:, c * 64:(c + 1) * 64], bass.AP(odr, c * D, [[64, 64], [1, 64]]))

            def KM(i):
                return km[:, i * 64:(i + 1) * 64]

            def CX(i):
                return cx[:, i * 64:(i + 1) * 64]

            @block.vector
            def _(v):
                sm.wait(v, 'g')
                for j in range(4):
                    v.tensor_tensor(out=CX(2 * j), in0=KM(2 * j),
                                    in1=KM(8 + 2 * j + 1), op=ALU.subtract)
                    ins = v.tensor_tensor(out=CX(2 * j + 1), in0=KM(2 * j + 1),
                                          in1=KM(8 + 2 * j), op=ALU.add)
                dstep(v, ins, 'v')
                v.tensor_scalar(out=CX(8), in0=CX(6), scalar1=1.0,
                                scalar2=None, op0=ALU.add)
                ins = v.tensor_copy(out=CX(9), in_=CX(7))
                dstep(v, ins, 'v')
                v.tensor_tensor(out=CX(10), in0=CX(2), in1=CX(4), op=ALU.mult)
                v.tensor_tensor(out=CX(12), in0=CX(3), in1=CX(5), op=ALU.mult)
                v.tensor_tensor(out=CX(11), in0=CX(2), in1=CX(5), op=ALU.mult)
                ins = v.tensor_tensor(out=CX(13), in0=CX(3), in1=CX(4),
                                      op=ALU.mult)
                dstep(v, ins, 'v')
                v.tensor_tensor(out=CX(10), in0=CX(10), in1=CX(12),
                                op=ALU.subtract)
                v.tensor_tensor(out=CX(11), in0=CX(11), in1=CX(13), op=ALU.add)
                v.tensor_tensor(out=CX(22), in0=CX(8), in1=CX(8), op=ALU.mult)
                ins = v.tensor_tensor(out=CX(23), in0=CX(9), in1=CX(9),
                                      op=ALU.mult)
                dstep(v, ins, 'v')
                ins = v.tensor_tensor(out=CX(22), in0=CX(22), in1=CX(23),
                                      op=ALU.add)
                dstep(v, ins, 'v')
                ins = v.reciprocal(out=CX(23), in_=CX(22))
                dstep(v, ins, 'v')
                v.tensor_tensor(out=CX(14), in0=CX(10), in1=CX(8), op=ALU.mult)
                v.tensor_tensor(out=CX(16), in0=CX(11), in1=CX(9), op=ALU.mult)
                v.tensor_tensor(out=CX(15), in0=CX(11), in1=CX(8), op=ALU.mult)
                ins = v.tensor_tensor(out=CX(17), in0=CX(10), in1=CX(9),
                                      op=ALU.mult)
                dstep(v, ins, 'v')
                v.tensor_tensor(out=CX(14), in0=CX(14), in1=CX(16), op=ALU.add)
                ins = v.tensor_tensor(out=CX(15), in0=CX(15), in1=CX(17),
                                      op=ALU.subtract)
                dstep(v, ins, 'v')
                v.tensor_tensor(out=CX(14), in0=CX(14), in1=CX(23), op=ALU.mult)
                ins = v.tensor_tensor(out=CX(15), in0=CX(15), in1=CX(23),
                                      op=ALU.mult)
                dstep(v, ins, 'v')
                v.tensor_tensor(out=CX(16), in0=CX(0), in1=CX(14),
                                op=ALU.subtract)
                ins = v.tensor_tensor(out=CX(17), in0=CX(1), in1=CX(15),
                                op=ALU.subtract)
                dstep(v, ins, 'v')
                v.tensor_tensor(out=CX(18), in0=c2rev[:], in1=CX(16),
                                op=ALU.mult)
                v.tensor_tensor(out=CX(20), in0=c2imv[:], in1=CX(17),
                                op=ALU.mult)
                v.tensor_tensor(out=CX(19), in0=c2rev[:], in1=CX(17),
                                op=ALU.mult)
                ins = v.tensor_tensor(out=CX(21), in0=c2imv[:], in1=CX(16),
                                op=ALU.mult)
                dstep(v, ins, 'v')
                v.tensor_tensor(out=CX(18), in0=CX(18), in1=CX(20),
                                op=ALU.subtract)
                ins = v.tensor_tensor(out=CX(19), in0=CX(19), in1=CX(21),
                                      op=ALU.add)
                sm.inc(ins, 'v')

            @block.tensor
            def _(t):
                sm.wait(t, 'v')
                t.matmul(p_f1[:, 0:64], wcv[:], CX(18), start=True, stop=False)
                t.matmul(p_f1[:, 0:64], wsnv[:], CX(19), start=False, stop=True)
                t.matmul(p_f2[:, 0:64], wsv[:], CX(18), start=True, stop=False)
                ins = t.matmul(p_f2[:, 0:64], wcv[:], CX(19), start=False,
                               stop=True)
                sm.inc(ins, 't')

            @block.scalar
            def _(a):
                sm.wait(a, 't')
                a.activation(out=CX(0), in_=p_f1[:, 0:64], func=AF.Copy)
                ins = a.activation(out=CX(1), in_=p_f2[:, 0:64], func=AF.Copy)
                sm.inc(ins, 'a')

            @block.vector
            def _(v):
                sm.wait(v, 'a')
                v.tensor_tensor(out=CX(2), in0=CX(0), in1=twrev[:], op=ALU.mult)
                v.tensor_tensor(out=CX(4), in0=CX(1), in1=twimv[:], op=ALU.mult)
                v.tensor_tensor(out=CX(3), in0=CX(0), in1=twimv[:], op=ALU.mult)
                ins = v.tensor_tensor(out=CX(5), in0=CX(1), in1=twrev[:], op=ALU.mult)
                dstep(v, ins, 'v')
                v.tensor_tensor(out=CX(2), in0=CX(2), in1=CX(4), op=ALU.subtract)
                ins = v.tensor_tensor(out=CX(3), in0=CX(3), in1=CX(5),
                                      op=ALU.add)
                sm.inc(ins, 'v')

            @block.tensor
            def _(t):
                sm.wait(t, 'v')
                t.transpose(p_f1[:, 0:64], CX(2), idfv[:])
                ins = t.transpose(p_f2[:, 0:64], CX(3), idfv[:])
                sm.inc(ins, 't')

            @block.scalar
            def _(a):
                sm.wait(a, 't')
                a.activation(out=CX(6), in_=p_f1[:, 0:64], func=AF.Copy)
                ins = a.activation(out=CX(7), in_=p_f2[:, 0:64], func=AF.Copy)
                sm.inc(ins, 'a')
            a_tr2 = sm.c['a']

            @block.tensor
            def _(t):
                sm.wait(t, 'a')
                t.matmul(p_f1[:, 0:64], wcv[:], CX(6), start=True, stop=False)
                ins = t.matmul(p_f1[:, 0:64], wsnv[:], CX(7), start=False,
                               stop=True)
                sm.inc(ins, 't')

            @block.scalar
            def _(a):
                sm.wait(a, 't')
                ins = a.activation(out=k2v[:], in_=p_f1[:, 0:64], func=AF.Copy,
                                   scale=1.0 / 4096.0)
                sm.inc(ins, 'a')

            @block.gpsimd
            def _(g):
                sm.wait(g, 'a')
                gdma(g, sm, 'w', kdr[:], k2v[:])
                sm.wait(g, 'w')
                ins = g.memset(kpad[:], 0.0)
                sm.inc(ins, 'm')
                sm.wait(g, 'm')
                gdma(g, sm, 'w', kpad[:, 512:4608],
                     bass.AP(kdr, 0, [[4096, 1], [1, 4096]]))
                sm.wait(g, 'w')
                gdma(g, sm, 'w', e_all[:],
                     bass.AP(kpad, 128, [[4608, 1], [-1, 128], [1, 4480]]))
                sm.wait(g, 'w')

        _es.close()
        uT = [s3alloc(f"p3uT{d}", [128, ROWS], BF16)
              for d in range(32)]
        lnnat = [s3alloc(f"p3ln{i}", [128, D], BF16)
                 for i in range(NT)]
        ysb = s3alloc("p3ysb", [128, D], F)
        xbuf = s3alloc("p3xb", [128, D], F)
        lnbf = s3alloc("p3lnbf", [128, D], BF16)
        scr_bf = s3alloc("p3scr", [128, D], BF16)
        stat3 = s3alloc("p3stat", [128, 12], F)
        for i in range(NT):
            @block.gpsimd
            def _(g, i=i):
                if '_lnfree' in sm.tag:
                    g.wait_ge(sm.h['v'], sm.tag['_lnfree'])
                gdma(g, sm, 'g', xbuf[:], x2dr[i * 128:(i + 1) * 128, :])
            emit_ln_tile(block, sm, xbuf, lnnat[i], scr_bf, stat3,
                         v_waits=[('g', sm.c['g']), ('t', sm.c['t'])])
            sm.tag['_lnfree'] = sm.c['v']
            emit_transpose_tiles(block, sm, lnnat[i], uT, i * 128, identbf,
                                 p_tr)

        @block.gpsimd
        def _(g):
            gdma(g, sm, 'g', sc1[:, 0:1], dsc_in[:])

        @block.tensor
        def _(t):
            sm.wait(t, 'g')
            ins = t.matmul(p_y[0][:, 0:1], onesf[:], sc1[:, 0:1],
                           start=True, stop=True)
            sm.inc(ins, 't')

        @block.scalar
        def _(a):
            sm.wait(a, 't')
            ins = a.activation(out=dv[:], in_=p_y[0][:, 0:1], func=AF.Copy)
            sm.inc(ins, 'a')

        for i in range(NT):
            for grp in range(8):
                @block.tensor
                def _(t, i=i, grp=grp):
                    if i == 0 and grp == 0:
                        sm.wait(t, 'a')
                        sm.wait(t, 'v')
                        sm.wait(t, 'w')
                    j = grp % 2
                    if f'_pyfree{j}' in sm.tag:
                        t.wait_ge(sm.h['a'], sm.tag[f'_pyfree{j}'])
                    last_c = min(4 * grp + 3, 31)
                    for c in range(last_c + 1):
                        ins = t.matmul(
                            p_y[j][:],
                            uT[c][:, i * 128:(i + 1) * 128],
                            e_all[:, (4 * grp - c + 3) * 128:
                                  (4 * grp - c + 3) * 128 + 512],
                            start=(c == 0), stop=(c == last_c))
                    sm.inc(ins, 't')

                @block.scalar
                def _(a, i=i, grp=grp):
                    sm.wait(a, 't')
                    if grp == 0 and '_ysbfree' in sm.tag:
                        a.wait_ge(sm.h['v'], sm.tag['_ysbfree'])
                    j = grp % 2
                    ins = a.activation(out=ysb[:, grp * 512:(grp + 1) * 512],
                                       in_=p_y[j][:], func=AF.Copy)
                    sm.tag[f'_pyfree{j}'] = sm.inc(ins, 'a')

            @block.vector
            def _(v, i=i):
                sm.wait(v, 'a')
                sm.wait(v, 'v')
                ins = v.scalar_tensor_tensor(out=ysb[:], in0=lnnat[i][:],
                                             scalar=dv[:], in1=ysb[:],
                                             op0=ALU.mult, op1=ALU.add)
                sm.inc(ins, 'v')

            @block.gpsimd
            def _(g, i=i):
                sm.wait(g, 'v')
                if '_ofree' in sm.tag:
                    g.wait_ge(sm.h['o'], sm.tag['_ofree'])
                gdma(g, sm, 'g', xbuf[:], x2dr[i * 128:(i + 1) * 128, :])
            emit_ln_tile(block, sm, ysb, lnbf, scr_bf, stat3,
                         v_waits=[('g', sm.c['g']), ('v', sm.c['v'])])
            sm.tag['_ysbfree'] = sm.c['v']

            @block.vector
            def _(v, i=i):
                sm.wait(v, 'g')
                sm.wait(v, 'v')
                ins = v.tensor_tensor(out=xbuf[:], in0=xbuf[:], in1=lnbf[:],
                                      op=ALU.add)
                sm.inc(ins, 'v')

            @block.gpsimd
            def _(g, i=i):
                sm.wait(g, 'v')
                gdma(g, sm, 'o', p3tgt[i * 128:(i + 1) * 128, :], xbuf[:])
            sm.tag['_ofree'] = sm.c['o']


        emit_barrier(block, sm)
        sm.tag.clear()
        ps3s.close()
        es3.close()

        if stop_after <= 3:
            @block.sync
            def _(sy):
                for n in sm.h:
                    sm.wait(sy, n)
            es_glob.close()
            return nc

        wg_in = nc.dram_tensor("wg", [NI, 128, D], I8, kind="ExternalInput")
        wu_in = nc.dram_tensor("wu", [NI, 128, D], I8, kind="ExternalInput")
        wd_in = nc.dram_tensor("wd", [11008, D], I8, kind="ExternalInput")
        sg_in = nc.dram_tensor("sg", [128, NI], F32, kind="ExternalInput")
        su_in = nc.dram_tensor("su", [128, NI], F32, kind="ExternalInput")
        sd_in = nc.dram_tensor("sd", [1, D], F32, kind="ExternalInput")

        es4 = ExitStack()
        def s4alloc(*a, **k):
            return es4.enter_context(nc.sbuf_tensor(*a, **k))
        ps4s = ExitStack()
        def p4alloc(*a, **k):
            return ps4s.enter_context(nc.psum_tensor(*a, **k))
        HT = 1024
        NTH = HT // 128

        ybf = [s4alloc(f"p4y{i}", [128, D], BF16) for i in range(NTH)]
        wdb = [s4alloc(f"p4wdb{i}", [128, 512], BF16) for i in range(4)]
        gact = s4alloc("p4gact", [128, HT], BF16)
        uact = s4alloc("p4uact", [128, HT], BF16)
        hbuf = [s4alloc(f"p4hbf{j}", [128, HT], BF16) for j in range(2)]
        sgs = s4alloc("p4sgs", [128, NI], F32)
        sus = s4alloc("p4sus", [128, NI], F32)
        sdb = s4alloc("p4sdb", [128, D], BF16)
        sdrow = s4alloc("p4sdrow", [1, 512], BF16)
        xbuf = s4alloc("p4xb", [128, D], F32)
        lnbf = s4alloc("p4lnbf", [128, D], BF16)
        stat4 = s4alloc("p4stat", [128, 12], F32)
        hdr = nc.dram_tensor("hdr", [NI * 128, HT], BF16, kind="Internal")

        es4b = ExitStack()
        def s4b(*a, **k):
            return es4b.enter_context(nc.sbuf_tensor(*a, **k))
        lnT = [s4b(f"p4lnT{d}", [128, HT], BF16) for d in range(32)]
        wgp = [s4b(f"p4wg{j}", [128, 2048], BF16) for j in range(2)]
        wup = [s4b(f"p4wu{j}", [128, 2048], BF16) for j in range(2)]
        ps4a = ExitStack()
        p_tr = [ps4a.enter_context(
            nc.psum_tensor(f"p4ptr{i}", [128, 512], BF16)) for i in range(2)]

        @block.gpsimd
        def _(g):
            gdma(g, sm, 'g', sgs[:], sg_in[:])
            gdma(g, sm, 'g', sus[:], su_in[:])

        for i in range(NTH):
            @block.gpsimd
            def _(g, i=i):
                if '_lnfree' in sm.tag:
                    g.wait_ge(sm.h['v'], sm.tag['_lnfree'])
                gdma(g, sm, 'g', xbuf[:], x3dr[i * 128:(i + 1) * 128, :])
            emit_ln_tile(block, sm, xbuf, lnbf, lnbf, stat4,
                         v_waits=[('g', sm.c['g']), ('t', sm.c['t'])])
            sm.tag['_lnfree'] = sm.c['v']
            emit_transpose_tiles(block, sm, lnbf, lnT, i * 128, identbf,
                                 p_tr)

        ps4a.close()
        p_g = [p4alloc(f"p4pg{j}", [128, HT], F32) for j in range(2)]
        p_u = p4alloc("p4pu", [128, HT], F32)

        for q in range(8):
            @block.gpsimd
            def _(g, q=q):
                if q > 0:
                    g.wait_ge(sm.h['t'], sm.tag['_sdt'])
                gdma(g, sm, 'g', sdrow[:], sd_in[:, q * 512:(q + 1) * 512])

            @block.tensor
            def _(t, q=q):
                sm.wait(t, 'g')
                if q == 0:
                    sm.wait(t, 'v')
                if q > 0:
                    t.wait_ge(sm.h['v'], sm.tag['_sdq'])
                ins = t.matmul(p_g[0][:, 0:512], onesbf[:], sdrow[:],
                               start=True, stop=True)
                sm.tag['_sdt'] = sm.inc(ins, 't')

            @block.vector
            def _(v, q=q):
                sm.wait(v, 't')
                ins = v.tensor_copy(out=sdb[:, q * 512:(q + 1) * 512],
                                    in_=p_g[0][:, 0:512])
                sm.tag['_sdq'] = sm.inc(ins, 'v')

        for it in range(NI):
            w_slice = {}
            jp = it % 2

            @block.gpsimd
            def _(g, it=it, w_slice=w_slice):
                for sl in range(2):
                    for nm, ring, src in (('g', wgp, wg_in),
                                          ('u', wup, wu_in)):
                        key = f'_wf{nm}{sl}'
                        if key in sm.tag:
                            g.wait_ge(sm.h['t'], sm.tag[key])
                        w_slice[(nm, sl)] = fdma(
                            g, sm, 'w', ring[sl][:],
                            bass.AP(src, it * 128 * D + sl * 2048,
                                    [[D, 128], [1, 2048]]))

            @block.tensor
            def _(t, it=it, jp=jp, w_slice=w_slice):
                if it == 0:
                    sm.wait(t, 'v')
                if f'_pgfree{jp}' in sm.tag:
                    t.wait_ge(sm.h['a'], sm.tag[f'_pgfree{jp}'])
                if f'_pufree' in sm.tag:
                    t.wait_ge(sm.h['v'], sm.tag['_pufree'])
                for sl in range(2):
                    t.wait_ge(sm.h['w'], w_slice[('g', sl)])
                    for dt in range(16):
                        for nh in range(2):
                            ins = t.matmul(
                                p_g[jp][:, nh * 512:(nh + 1) * 512],
                                wgp[sl][:, dt * 128:(dt + 1) * 128],
                                lnT[sl * 16 + dt][:, nh * 512:(nh + 1) * 512],
                                start=(sl == 0 and dt == 0),
                                stop=(sl == 1 and dt == 15))
                    sm.tag[f'_wfg{sl}'] = sm.inc(ins, 't')
                    t.wait_ge(sm.h['w'], w_slice[('u', sl)])
                    for dt in range(16):
                        for nh in range(2):
                            ins = t.matmul(
                                p_u[:, nh * 512:(nh + 1) * 512],
                                wup[sl][:, dt * 128:(dt + 1) * 128],
                                lnT[sl * 16 + dt][:, nh * 512:(nh + 1) * 512],
                                start=(sl == 0 and dt == 0),
                                stop=(sl == 1 and dt == 15))
                    sm.tag[f'_wfu{sl}'] = sm.inc(ins, 't')

            @block.scalar
            def _(a, it=it, jp=jp):
                sm.wait(a, 't')
                if '_hmul' in sm.tag:
                    a.wait_ge(sm.h['v'], sm.tag['_hmul'])
                ins = a.activation(out=gact[:], in_=p_g[jp][:], func=AF.Silu,
                                   scale=sgs[:, it:it + 1])
                sm.tag[f'_pgfree{jp}'] = sm.inc(ins, 'a')

            @block.vector
            def _(v, it=it):
                sm.wait(v, 't')
                if '_hmul' in sm.tag:
                    v.wait_ge(sm.h['v'], sm.tag['_hmul'])
                ins = v.tensor_scalar(out=uact[:], in0=p_u[:],
                                      scalar1=sus[:, it:it + 1],
                                      scalar2=None, op0=ALU.mult)
                sm.tag['_pufree'] = sm.inc(ins, 'v')
                jh = it % 2
                if f'_hdw{jh}' in sm.tag:
                    v.wait_ge(sm.h['u'], sm.tag[f'_hdw{jh}'])
                sm.wait(v, 'a')
                sm.wait(v, 'v')
                ins = v.tensor_tensor(out=hbuf[jh][:], in0=gact[:],
                                      in1=uact[:], op=ALU.mult)
                sm.tag['_hmul'] = sm.inc(ins, 'v')

            @block.sync
            def _(s, it=it):
                sm.wait(s, 'v')
                gdma(s, sm, 'u', hdr[it * 128:(it + 1) * 128, :],
                     hbuf[it % 2][:])
                sm.tag[f'_hdw{it % 2}'] = sm.c['u']

        emit_barrier(block, sm,
                     engines=('gpsimd', 'vector', 'scalar', 'tensor', 'sync'))
        sm.tag.clear()
        es4b.close()
        ps4s.close()
        ps4c = ExitStack()
        p_y = [ps4c.enter_context(nc.psum_tensor(f"p4py{i}", [128, 512], F32))
               for i in range(NTH)]
        es4c = ExitStack()
        hbb = [es4c.enter_context(nc.sbuf_tensor(f"p4hbb{j}", [128, HT],
                                                 BF16)) for j in range(4)]

        for hc in range(8):
            for it in range(NI):
                @block.sync
                def _(s, hc=hc, it=it):
                    jh = it % 4
                    if f'_hbfree{jh}' in sm.tag:
                        s.wait_ge(sm.h['t'], sm.tag[f'_hbfree{jh}'])
                    fdma(s, sm, 'u', hbb[jh][:],
                         hdr[it * 128:(it + 1) * 128, :])
                u_c = sm.c['u']

                @block.gpsimd
                def _(g, hc=hc, it=it):
                    j = it % 4
                    if f'_wdfree{j}' in sm.tag:
                        g.wait_ge(sm.h['t'], sm.tag[f'_wdfree{j}'])
                    fdma(g, sm, 'w', wdb[j][:],
                         wd_in[it * 128:(it + 1) * 128,
                               hc * 512:(hc + 1) * 512])
                w_c = sm.c['w']

                @block.tensor
                def _(t, hc=hc, it=it, w_c=w_c, u_c=u_c):
                    t.wait_ge(sm.h['w'], w_c)
                    t.wait_ge(sm.h['u'], u_c)
                    if it == 0 and '_yevac' in sm.tag:
                        t.wait_ge(sm.h['v'], sm.tag['_yevac'])
                    for tt in range(NTH):
                        ins = t.matmul(p_y[tt][:],
                                       hbb[it % 4][:, tt * 128:(tt + 1) * 128],
                                       wdb[it % 4][:],
                                       start=(it == 0), stop=(it == NI - 1))
                    sm.tag[f'_wdfree{it % 4}'] = sm.inc(ins, 't')
                    sm.tag[f'_hbfree{it % 4}'] = sm.c['t']

            @block.vector
            def _(v, hc=hc):
                sm.wait(v, 't')
                for tt in range(NTH):
                    ins = v.tensor_tensor(
                        out=ybf[tt][:, hc * 512:(hc + 1) * 512],
                        in0=p_y[tt][:], in1=sdb[:, hc * 512:(hc + 1) * 512],
                        op=ALU.mult)
                    sm.inc(ins, 'v')
                sm.tag['_yevac'] = sm.c['v']

        for i in range(NTH):
            @block.gpsimd
            def _(g, i=i):
                sm.wait(g, 'v')
                if '_ofree' in sm.tag:
                    g.wait_ge(sm.h['o'], sm.tag['_ofree'])
                gdma(g, sm, 'g', xbuf[:], x3dr[i * 128:(i + 1) * 128, :])
            emit_ln_tile(block, sm, ybf[i], lnbf, lnbf, stat4,
                         v_waits=[('v', sm.c['v']), ('g', sm.c['g']),
                                  ('t', sm.c['t'])])

            @block.vector
            def _(v, i=i):
                sm.wait(v, 'g')
                sm.wait(v, 'v')
                ins = v.tensor_tensor(out=xbuf[:], in0=xbuf[:],
                                      in1=lnbf[:], op=ALU.add)
                sm.inc(ins, 'v')

            @block.gpsimd
            def _(g, i=i):
                sm.wait(g, 'v')
                gdma(g, sm, 'o', out[i * 128:(i + 1) * 128, :], xbuf[:])
            sm.tag['_ofree'] = sm.c['o']


        @block.sync
        def _(sy):
            for n in sm.h:
                sm.wait(sy, n)
        ps4c.close()
        es4c.close()
        es4.close()
        es_glob.close()
    return nc


NSLOT = 16


def slot_order(core):
    own = core_blocks(core)
    return own + [b for b in range(NBLK) if b not in own]


def fused_inmap(core, inputs):
    x = np.ascontiguousarray(np.asarray(inputs['L0_attn_x'], np.float32)[0])
    m = qkv_inmap(core, x, inputs)
    m.pop('isblk0', None)
    sl = slot_order(core)
    xT = x.T
    xtr = np.zeros((D, NSLOT * 516), np.float32)
    for s, b in enumerate(sl):
        if b > 0:
            xtr[:, s * 516:s * 516 + 4] = xT[:, b * 512 - 4:b * 512]
        xtr[:, s * 516 + 4:(s + 1) * 516] = xT[:, b * 512:(b + 1) * 512]
    inv = (1.0 / 10000.0 ** (np.arange(0, 64, 2, dtype=np.float32) / 64.0)
           ).astype(np.float32)
    cek = np.zeros((NSLOT, 128, 512), np.float32)
    sek = np.zeros((NSLOT, 128, 512), np.float32)
    for s, b in enumerate(sl):
        pos = np.arange(b * BLK, (b + 1) * BLK, dtype=np.float32)
        ang = pos[None, :] * inv[:, None]
        c, sn = np.cos(ang), np.sin(ang)
        for p in range(128):
            i = (p % 64) // 2
            cek[s, p] = c[i]
            sek[s, p] = -sn[i] if (p % 2 == 0) else sn[i]
    own = core_blocks(core)
    vis = np.zeros((128, 128), np.float32)
    for seg in range(2):
        bq = own[seg]
        for T in range(64):
            b = sl[T // 4]
            diag = (seg == 0 and T < 4) or (seg == 1 and 4 <= T < 8)
            v = 1.0 if (diag or b < bq) else 0.0
            vis[:, seg * 64 + T] = v
    nzkv = np.ones((128, NSLOT), np.float32)
    for s, b in enumerate(sl):
        if b == 0:
            nzkv[:, s] = 0.0
    kt = np.asarray(inputs['L0_attn_key_temp']).ravel()[0]
    m.update({
        "xtr": f32c(xtr),
        "cek": cek.astype(NPBF), "sek": sek.astype(NPBF),
        "ceq": m.pop("ce"), "seq": m.pop("se"),
        "vis": vis, "nzkv": nzkv, "nzq": m.pop("nz"),
        "kt128": f32c(np.full((128, 1), kt)),
        "tri": np.triu(np.ones((128, 128), np.float32)).astype(NPBF),
        "onescol": np.ones((128, 1), dtype=NPBF),
        "ones1": np.ones((1, 128), dtype=NPBF),
        "woT": f32c(np.asarray(inputs['L0_attn_w_o']).T),
    })
    return m


_BUILD_CACHE = {}


def _get(name, fn):
    if name not in _BUILD_CACHE:
        _BUILD_CACHE[name] = fn()
    return _BUILD_CACHE[name]


def _rows(core, full):
    return np.concatenate([full[b * BLK:(b + 1) * BLK]
                           for b in core_blocks(core)], 0)


def _scatter(res_list, key, shape, dtype=np.float32):
    out = np.zeros(shape, dtype)
    for c in range(NCORE):
        r = np.asarray(res_list[c][key])
        for s, b in enumerate(core_blocks(c)):
            out[b * BLK:(b + 1) * BLK] = r[s * BLK:(s + 1) * BLK]
    return out


def kernel_multilaunch(**inputs):
    x = np.ascontiguousarray(np.asarray(inputs['L0_attn_x'], np.float32)[0])
    cores = list(range(NCORE))

    nc = _get('qkv', build_qkv)
    res = run_bass_kernel_spmd(nc, [qkv_inmap(c, x, inputs) for c in cores],
                               core_ids=cores).results
    qf_T = np.zeros((1024, S), NPBF)
    kf_T = np.zeros((256, S), NPBF)
    vf = np.zeros((S, 256), NPBF)
    for c in cores:
        for s, b in enumerate(core_blocks(c)):
            sl = slice(b * BLK, (b + 1) * BLK)
            qf_T[:, sl] = np.asarray(res[c]['qh'][s])
            kf_T[:, sl] = np.asarray(res[c]['kh'][s])
            vf[sl, :] = np.asarray(res[c]['v'][s])

    nc = _get('attn', build_attn)
    res = run_bass_kernel_spmd(
        nc, [attn_inmap(c, qf_T, kf_T, vf, inputs) for c in cores],
        core_ids=cores).results
    ao_all = np.zeros((16, 64, S), NPBF)
    for c in cores:
        ao_all[2 * c:2 * c + 2] = np.asarray(res[c]['ao'])

    nc = _get('wo', build_wo)
    woT = f32c(np.asarray(inputs['L0_attn_w_o']).T)
    maps = []
    for c in cores:
        cols = np.concatenate([np.arange(b * BLK, (b + 1) * BLK)
                               for b in core_blocks(c)])
        aoc = np.ascontiguousarray(ao_all[:, :, cols].reshape(1024, ROWS))
        maps.append({"aoc": aoc, "woT": woT, "x": f32c(_rows(c, x))})
    res = run_bass_kernel_spmd(nc, maps, core_ids=cores).results
    x1 = _scatter(res, 'out', (S, D))

    nc = _get('ffn0', build_ffn0)
    res = run_bass_kernel_spmd(nc, [ffn0_inmap(_rows(c, x1), inputs)
                                    for c in cores], core_ids=cores).results
    x2 = _scatter(res, 'out', (S, D))

    nc = _get('s4', build_s4)
    res = run_bass_kernel_spmd(nc, [s4_inmap(_rows(c, x2), inputs)
                                    for c in cores], core_ids=cores).results
    x3 = _scatter(res, 'out', (S, D))

    nc = _get('ffn1', build_ffn1)
    res = run_bass_kernel_spmd(nc, [ffn1_inmap(_rows(c, x3), inputs)
                                    for c in cores], core_ids=cores).results
    x4 = _scatter(res, 'out', (S, D))

    return x4.reshape(1, S, D)

def fused_inmap_all(core, inputs):
    m = fused_inmap(core, inputs)
    dummy = np.zeros((ROWS, D), np.float32)
    for k, v in ffn0_inmap(dummy, inputs).items():
        if k not in ('x', 'ident', 'ones1'):
            m[k] = v
    for k, v in s4_inmap(dummy, inputs).items():
        if k not in ('x', 'ident'):
            m[k] = v
    for k, v in ffn1_inmap(dummy, inputs).items():
        if k not in ('x', 'ident', 'ones1'):
            m[k] = v
    return m


def kernel_fused(**inputs):
    nc = _get('fused', build_fused)
    cores = list(range(NCORE))
    res = run_bass_kernel_spmd(
        nc, [fused_inmap_all(c, inputs) for c in cores],
        core_ids=cores).results
    out = np.zeros((S, D), np.float32)
    for c in cores:
        r = np.asarray(res[c]['out'])
        for s_, b in enumerate(core_blocks(c)):
            out[b * BLK:(b + 1) * BLK] = r[s_ * BLK:(s_ + 1) * BLK]
    return out.reshape(1, S, D)



def kernel(**inputs):
    return kernel_fused(**inputs)
